# version 15
# baseline (speedup 1.0000x reference)
import numpy as np
import ml_dtypes
from contextlib import ExitStack

import concourse.bass as bass
import concourse.mybir as mybir
from concourse.bass_utils import run_bass_kernel_spmd

F32 = mybir.dt.float32
BF16 = mybir.dt.bfloat16
AF = mybir.ActivationFunctionType
ALU = mybir.AluOpType
AX = mybir.AxisListType

BSZ, SEQ, DM, DH = 4, 2048, 2048, 8192
TOK = 1024
NCH = DM // 128
NEG = -1.0e30
LN_EPS = 1e-5
RMS_EPS = 1.1920929e-07
BF = ml_dtypes.bfloat16
RS128 = float(1.0 / np.sqrt(128.0))

ENGINES = ("sync", "tensor", "vector", "scalar", "gpsimd")


class Bld:

    def __init__(self, nc):
        self.nc = nc
        self.prog = {e: [] for e in ENGINES}
        self.cnt = {}

    def sem(self, name):
        assert name not in self.cnt
        self.cnt[name] = 0
        return name

    def wait(self, eng, sem, val):
        if val > 0:
            self.prog[eng].append(("w", sem, val))

    def op(self, eng, fn, inc=None):
        incs = []
        if inc is not None:
            items = [inc] if isinstance(inc, tuple) else list(inc)
            for s, a in items:
                self.cnt[s] += a
                incs.append((s, a))
        self.prog[eng].append(("o", fn, incs))

    def dma(self, eng, out, in_, sem):
        self.op(eng, lambda e, o=out, i=in_: e.dma_start(out=o, in_=i), inc=(sem, 16))
        return self.cnt[sem]

    def tick(self, sem):
        return self.cnt[sem]

    def guard(self, eng, sem):
        self.wait(eng, sem, self.cnt[sem])

    def realize(self, ctx):
        sems = {n: ctx.enter_context(self.nc.semaphore(n)) for n in self.cnt}
        block = ctx.enter_context(self.nc.Block())
        for name in ENGINES:
            items = self.prog[name]

            def body(eng_obj, items=items):
                for it in items:
                    if it[0] == "w":
                        eng_obj.wait_ge(sems[it[1]], it[2])
                    else:
                        inst = it[1](eng_obj)
                        for (s, a) in it[2]:
                            inst.then_inc(sems[s], a)

            getattr(block, name)(body)


class Stream:

    def __init__(self, b, name, queue, nslots=2):
        self.b = b
        self.queue = queue
        self.n = nslots
        self.sems = [b.sem(f"{name}{k}") for k in range(nslots)]
        self.gate = [None] * nslots
        self.i = 0

    def load(self, out_ap, in_ap):
        s = self.i % self.n
        if self.gate[s] is not None:
            self.b.wait(self.queue, self.gate[s][0], self.gate[s][1])
            self.gate[s] = None
        t = self.b.dma(self.queue, out_ap, in_ap, self.sems[s])
        self.i += 1
        return self.sems[s], t, s

    def consumed(self, slot, sem, tick):
        self.gate[slot] = (sem, tick)


class OutStream:

    def __init__(self, b, name, queue, nslots=2):
        self.b = b
        self.queue = queue
        self.n = nslots
        self.sems = [b.sem(f"{name}{k}") for k in range(nslots)]
        self.i = 0

    def out(self, slot, dst, src):
        sem = self.sems[slot]
        self.b.wait(self.queue, sem, self.b.tick(sem))
        t = self.b.dma(self.queue, dst, src, sem)
        return sem, t

    def drain(self):
        for sem in self.sems:
            self.b.wait(self.queue, sem, self.b.tick(sem))


def ceil_div(a, b):
    return (a + b - 1) // b



def emit_ln_stats_and_xn(b, bank, consts, xsrc_dram, x_stream, xbf_stream, sqbuf,
                         rows, xn, tok, sx, sg, st, sv, ss, xn_out_dram=None,
                         x_keep=None, xn_cols=None):
    nh = tok // 512
    assert tok == 1024
    if xn_cols is None:
        xn_cols = lambda dc: xn[:, dc * tok:(dc + 1) * tok]
    b.wait("tensor", ss, b.tick(ss))
    b.wait("tensor", sv, b.tick(sv))
    ps_sum = bank(0, 1024)[0:1, :]
    ps_sq = bank(2, 1024)[0:1, :]
    sq_ticks = {}
    for dc in range(NCH):
        if x_keep is not None:
            xslot = x_keep[:, dc * tok:(dc + 1) * tok]
            xsem, xt, _ = x_stream.load(xslot, xsrc_dram[128 * dc:128 * (dc + 1), :])
        else:
            xsem, xt, xs = x_stream.load(
                x_stream.buf[:, (dc % x_stream.n) * tok:((dc % x_stream.n) + 1) * tok],
                xsrc_dram[128 * dc:128 * (dc + 1), :])
            xslot = x_stream.buf[:, (dc % x_stream.n) * tok:((dc % x_stream.n) + 1) * tok]
        bsem, bt, bslot = xbf_stream.load(
            xbf_stream.buf[:, (dc % xbf_stream.n) * tok:((dc % xbf_stream.n) + 1) * tok],
            xsrc_dram[128 * dc:128 * (dc + 1), :])
        xbfslot = xbf_stream.buf[:, (dc % xbf_stream.n) * tok:((dc % xbf_stream.n) + 1) * tok]
        b.wait("scalar", xsem, xt)
        sqslot = sqbuf[:, (dc % 2) * tok:((dc % 2) + 1) * tok]
        if dc >= 2:
            b.wait("scalar", st, sq_ticks[dc - 2][1])
        b.op("scalar", lambda e, o=sqslot, i=xslot: e.activation(o, i, AF.Square),
             inc=(ss, 1))
        sq_s = b.tick(ss)
        b.wait("tensor", bsem, bt)
        b.wait("tensor", ss, sq_s)
        if dc == 0:
            b.wait("tensor", sv, consts)
        for h in range(nh):
            b.op("tensor", lambda e, i=xbfslot, h=h, dc=dc: e.matmul(
                ps_sum[:, 512 * h:512 * (h + 1)], ps_ones_bf[b][:, :],
                i[:, 512 * h:512 * (h + 1)],
                start=(dc == 0), stop=(dc == NCH - 1)))
            b.op("tensor", lambda e, i=sqslot, h=h, dc=dc: e.matmul(
                ps_sq[:, 512 * h:512 * (h + 1)], ps_ones_bf[b][:, :],
                i[:, 512 * h:512 * (h + 1)],
                start=(dc == 0), stop=(dc == NCH - 1)),
                inc=(st, 1) if h == nh - 1 else None)
        mm_t = b.tick(st)
        sq_ticks[dc] = (sq_s, mm_t)
        if x_keep is None:
            x_stream.consumed(dc % x_stream.n, "ss", sq_s)
        xbf_stream.consumed(bslot, "st", mm_t)
    stats_t = b.tick(st)

    m_row, ex2, m2, var, sqv, rstd = (rows[k] for k in
                                      ("m_row", "ex2", "m2", "var", "sqv", "rstd"))
    b.wait("scalar", st, stats_t)
    b.op("scalar", lambda e: e.activation(m_row[:, :], ps_sum, AF.Copy,
                                          scale=1.0 / DM), inc=(ss, 1))
    b.op("scalar", lambda e: e.activation(ex2[:, :], ps_sq, AF.Copy,
                                          scale=1.0 / DM), inc=(ss, 1))
    b.guard("scalar", ss)
    b.op("scalar", lambda e: e.activation(m2[:, :], m_row[:, :], AF.Square),
         inc=(ss, 1))
    b.wait("vector", ss, b.tick(ss))
    b.op("vector", lambda e: e.tensor_sub(var[:, :], ex2[:, :], m2[:, :]),
         inc=(sv, 1))
    b.wait("scalar", sv, b.tick(sv))
    b.op("scalar", lambda e: e.activation(sqv[:, :], var[:, :], AF.Sqrt,
                                          bias=eps_row[b][0:1, 0:1]), inc=(ss, 1))
    b.wait("vector", ss, b.tick(ss))
    b.op("vector", lambda e: e.reciprocal(rstd[:, :], sqv[:, :]), inc=(sv, 1))
    rstd_v = b.tick(sv)

    ps_M = bank(4)
    ps_R = bank(5)
    tmp32 = rows["tmp32"]
    b.wait("tensor", sv, rstd_v)
    xn_half_done = {}
    for h in range(nh):
        if h > 0:
            b.wait("tensor", sv, b.tick(sv))
        b.op("tensor", lambda e, h=h: e.matmul(
            ps_M, ps_ones_f[b][:, :],
            m_row[:, 512 * h:512 * (h + 1)], start=True, stop=True))
        b.op("tensor", lambda e, h=h: e.matmul(
            ps_R, ps_ones_f[b][:, :],
            rstd[:, 512 * h:512 * (h + 1)], start=True, stop=True), inc=(st, 1))
        bcast_t = b.tick(st)
        for dc in range(NCH):
            if x_keep is not None:
                xsl = x_keep[:, dc * tok + 512 * h:dc * tok + 512 * (h + 1)]
            else:
                slot = x_stream.i % x_stream.n
                dst = x_stream.buf[:, slot * tok:slot * tok + 512]
                xsem, xt, xs = x_stream.load(
                    dst, xsrc_dram[128 * dc:128 * (dc + 1), 512 * h:512 * (h + 1)])
                xsl = dst
                b.wait("vector", xsem, xt)
            if dc == 0:
                b.wait("vector", st, bcast_t)
            b.guard("vector", sv)
            b.op("vector", lambda e, x_=xsl: e.tensor_sub(
                tmp32[:, 0:512], x_, ps_M), inc=(sv, 1))
            b.guard("vector", sv)
            b.op("vector", lambda e, o=xn_cols(dc), h=h: e.tensor_mul(
                o[:, 512 * h:512 * (h + 1)], tmp32[:, 0:512], ps_R), inc=(sv, 1))
            dc_v = b.tick(sv)
            if x_keep is None:
                x_stream.consumed(xs, "sv", dc_v)
            if xn_out_dram is not None and h == nh - 1:
                b.wait("sync", sv, dc_v)
                b.dma("sync", xn_out_dram[128 * dc:128 * (dc + 1), :],
                      xn_cols(dc), sx)
    return b.tick(sv)


ps_ones_bf = {}
ps_ones_f = {}
eps_row = {}



def build_launch_A():
    nc = bass.Bass("TRN2", target_bir_lowering=False)
    xT = nc.dram_tensor("xT", [DM, TOK], F32, kind="ExternalInput").ap()
    wq = nc.dram_tensor("wq", [DM, DM], F32, kind="ExternalInput").ap()
    wk = nc.dram_tensor("wk", [DM, DM], F32, kind="ExternalInput").ap()
    wv = nc.dram_tensor("wv", [DM, DM], F32, kind="ExternalInput").ap()
    xn_o = nc.dram_tensor("xn_o", [DM, TOK], BF16, kind="ExternalOutput").ap()
    q_o = nc.dram_tensor("q_o", [DM, TOK], BF16, kind="ExternalOutput").ap()
    k_o = nc.dram_tensor("k_o", [DM, TOK], BF16, kind="ExternalOutput").ap()
    v_o = nc.dram_tensor("v_o", [TOK, DM], BF16, kind="ExternalOutput").ap()

    ctx = ExitStack()
    sb = lambda n, s, d: ctx.enter_context(nc.sbuf_tensor(n, s, d))
    xn = sb("xn", [128, NCH * TOK], BF16)
    wv_sb = sb("wv_sb", [128, NCH * DM], BF16)
    x_buf = sb("x_buf", [128, 2 * TOK], F32)
    xbf_buf = sb("xbf_buf", [128, 2 * TOK], BF16)
    sq_buf = sb("sq_buf", [128, 2 * TOK], BF16)
    wqk_buf = sb("wqk_buf", [128, 2 * DM], BF16)
    v_sb = sb("v_sb", [128, 2 * DM], BF16)
    q_sb = sb("q_sb", [128, 2 * TOK], BF16)
    ones_bf = sb("ones_bf", [128, 1], BF16)
    ones_f = sb("ones_f", [1, 128], F32)
    rows = {k: sb(k, [1, TOK], F32) for k in ("m_row", "ex2", "m2", "var", "sqv", "rstd")}
    epst = sb("epst", [128, 2], F32)
    rows["tmp32"] = sb("tmp32", [128, 1024], F32)
    pall = ctx.enter_context(nc.psum_tensor("pall", [128, 8 * 512], F32))

    def bank(i, width=512):
        return pall[:, 512 * i:512 * i + width]

    b = Bld(nc)
    sx = b.sem("sx")
    sg = b.sem("sg")
    st = b.sem("st")
    sv = b.sem("sv")
    ss = b.sem("ss")
    so0 = b.sem("so0")
    so1 = b.sem("so1")
    so = [so0, so1]

    ps_ones_bf[b] = ones_bf
    ps_ones_f[b] = ones_f
    eps_row[b] = epst
    b.op("vector", lambda e: e.memset(ones_bf[:, :], 1.0), inc=(sv, 1))
    b.op("vector", lambda e: e.memset(ones_f[:, :], 1.0), inc=(sv, 1))
    b.op("vector", lambda e: e.memset(epst[:, 0:1], LN_EPS), inc=(sv, 1))
    b.op("vector", lambda e: e.memset(epst[:, 1:2], RMS_EPS), inc=(sv, 1))
    consts = b.tick(sv)

    x_stream = Stream(b, "lx", "sync", 2)
    x_stream.buf = x_buf
    xbf_stream = Stream(b, "lb", "gpsimd", 2)
    xbf_stream.buf = xbf_buf

    XN_V = emit_ln_stats_and_xn(b, bank, consts, xT, x_stream, xbf_stream, sq_buf,
                                rows, xn, TOK, sx, sg, st, sv, ss, xn_out_dram=xn_o)

    wv_sem_t = []
    for dc in range(NCH):
        b.dma("gpsimd", wv_sb[:, dc * DM:(dc + 1) * DM],
              wv[128 * dc:128 * (dc + 1), :], sg)
    WV_G = b.tick(sg)
    b.wait("tensor", sg, WV_G)
    b.wait("tensor", sv, XN_V)
    vidx = 0
    v_evac = {}
    vdma = {}
    for tc in range(TOK // 128):
        for fg in range(4):
            bk = vidx % 2
            if vidx >= 2:
                b.wait("tensor", ss, v_evac[vidx - 2])
            for dc in range(NCH):
                b.op("tensor", lambda e, bk=bk, tc=tc, fg=fg, dc=dc: e.matmul(
                    bank(bk), xn[:, dc * TOK + 128 * tc:dc * TOK + 128 * (tc + 1)],
                    wv_sb[:, dc * DM + 512 * fg:dc * DM + 512 * (fg + 1)],
                    start=(dc == 0), stop=(dc == NCH - 1)),
                    inc=(st, 1) if dc == NCH - 1 else None)
            b.wait("scalar", st, b.tick(st))
            vslot = tc % 2
            if tc >= 2 and fg == 0:
                b.wait("scalar", so[vslot], vdma[tc - 2])
            b.op("scalar", lambda e, bk=bk, vslot=vslot, fg=fg: e.activation(
                v_sb[:, vslot * DM + 512 * fg:vslot * DM + 512 * (fg + 1)],
                bank(bk), AF.Copy), inc=(ss, 1))
            v_evac[vidx] = b.tick(ss)
            vidx += 1
        b.wait("sync", ss, v_evac[vidx - 1])
        b.wait("sync", so[tc % 2], b.tick(so[tc % 2]))
        vdma[tc] = b.dma("sync", v_o[128 * tc:128 * (tc + 1), :],
                         v_sb[:, (tc % 2) * DM:(tc % 2 + 1) * DM], so[tc % 2])

    wstream = Stream(b, "wqk", "gpsimd", 2)
    qk_evac = {}
    qdma = {}
    jobs = [("q", oc) for oc in range(NCH)] + [("k", oc) for oc in range(NCH)]
    for ji, (which, oc) in enumerate(jobs):
        wsrc = wq if which == "q" else wk
        src = wsrc[:, 128 * oc:128 * (oc + 1)].rearrange("(c p) n -> p c n", p=128)
        slot = ji % 2
        dst = wqk_buf[:, slot * DM:(slot + 1) * DM].rearrange("p (c n) -> p c n", n=128)
        wsem, wt, ws = wstream.load(dst, src)
        b.wait("tensor", wsem, wt)
        for h in range(2):
            bk = 2 + h
            if ji >= 1:
                b.wait("tensor", ss, qk_evac[(ji - 1, h)])
            for dc in range(NCH):
                b.op("tensor", lambda e, bk=bk, slot=slot, dc=dc, h=h: e.matmul(
                    bank(bk),
                    wqk_buf[:, slot * DM + 128 * dc:slot * DM + 128 * (dc + 1)],
                    xn[:, dc * TOK + 512 * h:dc * TOK + 512 * (h + 1)],
                    start=(dc == 0), stop=(dc == NCH - 1)),
                    inc=(st, 1) if dc == NCH - 1 else None)
        wstream.consumed(ws, "st", b.tick(st))
        qslot = ji % 2
        b.wait("scalar", st, b.tick(st))
        if ji >= 2:
            b.wait("scalar", so[qslot], qdma[ji - 2])
        scale = RS128 if which == "q" else 1.0
        for h in range(2):
            b.op("scalar", lambda e, qslot=qslot, h=h, scale=scale, bk=2 + h: e.activation(
                q_sb[:, qslot * TOK + 512 * h:qslot * TOK + 512 * (h + 1)],
                bank(bk), AF.Copy, scale=scale), inc=(ss, 1))
            qk_evac[(ji, h)] = b.tick(ss)
        b.wait("sync", ss, qk_evac[(ji, 1)])
        b.wait("sync", so[qslot], b.tick(so[qslot]))
        dst = q_o if which == "q" else k_o
        qdma[ji] = b.dma("sync", dst[128 * oc:128 * (oc + 1), :],
                         q_sb[:, qslot * TOK:(qslot + 1) * TOK], so[qslot])

    b.wait("sync", sx, b.tick(sx))
    b.wait("sync", so0, b.tick(so0))
    b.wait("sync", so1, b.tick(so1))
    b.wait("gpsimd", sg, b.tick(sg))

    b.realize(ctx)
    nc._ctx_keepalive = ctx
    return nc



def shard_tokens_fm(x):
    outs = []
    for c in range(8):
        seq, half = c // 2, c % 2
        sl = x[seq, half * TOK:(half + 1) * TOK, :]
        outs.append(np.ascontiguousarray(np.asarray(sl, dtype=np.float32).T))
    return outs


def _run(nc, in_maps):
    return run_bass_kernel_spmd(nc, in_maps, core_ids=list(range(8)))


def kernel(**inputs):
    return kernel_fused(**inputs)


def kernel_multilaunch(**inputs):
    x = np.asarray(inputs["L0_attn_x"], np.float32)
    xts = shard_tokens_fm(x)
    masks = host_l0_masks()
    h128, h64, perm = host_fwht_consts()
    ident = np.eye(128, dtype=np.float32).astype(BF)
    f32c = lambda a: np.ascontiguousarray(np.asarray(a, np.float32))

    wq0, wk0, wv0, wo0 = (f32c(inputs[k]) for k in
                          ("L0_attn_Wq", "L0_attn_Wk", "L0_attn_Wv", "L0_attn_Wo"))
    w1t = f32c(inputs["L0_ffn_W1"].T)
    w3t = f32c(inputs["L0_ffn_W3"].T)
    w2tp = f32c(np.asarray(inputs["L0_ffn_W2"], np.float32).T[perm])
    wq1 = f32c(inputs["L1_attn_Wq"])
    wv1 = f32c(inputs["L1_attn_Wv"])
    wo1 = f32c(inputs["L1_attn_Wo"])
    wgt = f32c(inputs["L1_ffn_w_gate"].T)
    wut = f32c(inputs["L1_ffn_w_up"].T)
    wdt = f32c(inputs["L1_ffn_w_down"].T)

    resA = _run(build_launch_A(),
                [{"xT": xts[c], "wq": wq0, "wk": wk0, "wv": wv0} for c in range(8)])
    ra = resA.results

    in_b = []
    for c in range(8):
        p = c ^ 1
        half = c % 2
        kc, kp = ra[c]["k_o"], ra[p]["k_o"]
        kfull = np.concatenate([kc, kp] if half == 0 else [kp, kc], axis=1)
        vfull = np.concatenate(
            [ra[c]["v_o"], ra[p]["v_o"]] if half == 0 else
            [ra[p]["v_o"], ra[c]["v_o"]], axis=0)
        vhm = np.ascontiguousarray(
            vfull.reshape(SEQ, 16, 128).transpose(1, 0, 2)).reshape(16 * SEQ, 128)
        in_b.append({"q_i": ra[c]["q_o"], "k_i": np.ascontiguousarray(kfull),
                     "vhm_i": vhm, "mask_i": masks[half], "ident_i": ident})
    resB = _run(build_launch_B(), in_b)

    in_c = [{"xT": xts[c], "xn_i": ra[c]["xn_o"], "o_i": resB.results[c]["o_o"],
             "wo": wo0, "w1t": w1t, "w3t": w3t, "w2tp": w2tp,
             "h128_i": np.asarray(h128), "h64_i": np.asarray(h64)}
            for c in range(8)]
    resC = _run(build_launch_C(), in_c)
    x1 = [resC.results[c]["x1_o"] for c in range(8)]

    in_d = []
    for c in range(8):
        p = c ^ 1
        x1f = np.ascontiguousarray(np.concatenate([x1[c], x1[p]], axis=1))
        in_d.append({"x1f": x1f, "wq": wq1, "wv": wv1, "ident_i": ident})
    resD = _run(build_launch_D(), in_d)
    rd = resD.results

    in_e = [{"q_i": rd[c]["q1_o"], "k_i": rd[c]["k1_o"], "vhm_i": rd[c]["v1hm_o"],
             "ident_i": ident, "wo": wo1} for c in range(8)]
    resE = _run(build_launch_E(), in_e)

    in_f = [{"xn1_i": rd[c]["xn1_o"], "s1n_i": resE.results[c]["s1n_o"],
             "x1_i": x1[c], "wgt": wgt, "wut": wut, "wdt": wdt}
            for c in range(8)]
    resF = _run(build_launch_F(), in_f)

    out = np.empty((BSZ, SEQ, DM), np.float32)
    for c in range(8):
        seq, half = c // 2, c % 2
        out[seq, half * TOK:(half + 1) * TOK, :] = np.asarray(
            resF.results[c]["y_o"], np.float32).T
    return out



def build_launch_B():
    nc = bass.Bass("TRN2", target_bir_lowering=False)
    q_i = nc.dram_tensor("q_i", [DM, TOK], BF16, kind="ExternalInput").ap()
    k_i = nc.dram_tensor("k_i", [DM, SEQ], BF16, kind="ExternalInput").ap()
    vhm_i = nc.dram_tensor("vhm_i", [16 * SEQ, 128], BF16, kind="ExternalInput").ap()
    mask_i = nc.dram_tensor("mask_i", [128, 8 * 9 * 128], BF16, kind="ExternalInput").ap()
    ident_i = nc.dram_tensor("ident_i", [128, 128], BF16, kind="ExternalInput").ap()
    o_o = nc.dram_tensor("o_o", [DM, TOK], BF16, kind="ExternalOutput").ap()

    NQC = TOK // 128
    NKB = SEQ // 128

    ctx = ExitStack()
    sb = lambda n, s, d: ctx.enter_context(nc.sbuf_tensor(n, s, d))
    q_sb = sb("q_sb", [128, NCH * TOK], BF16)
    k_sb = sb("k_sb", [128, 2 * SEQ], BF16)
    v_sb = sb("v_sb", [128, 2 * SEQ], BF16)
    p_sb = sb("p_sb", [128, SEQ], BF16)
    pt_sb = sb("pt_sb", [128, NKB * 128], BF16)
    mask_sb = sb("mask_sb", [128, 8 * 9 * 128], BF16)
    ident = sb("ident", [128, 128], BF16)
    ones_bf = sb("ones_bf", [128, 1], BF16)
    ones_f = sb("ones_f", [1, 128], F32)
    mneg = sb("mneg", [128, 3], F32)
    l_col = sb("l_col", [128, 6], F32)
    linv_col = sb("linv_col", [128, 3], F32)
    pn_sb = sb("pn_sb", [128, SEQ], BF16)
    o_sb = sb("o_sb", [128, 2 * TOK], BF16)
    pall = ctx.enter_context(nc.psum_tensor("pall", [128, 6 * 512], F32))
    ptp = ctx.enter_context(nc.psum_tensor("ptp", [128, 2 * 1024], BF16))

    b = Bld(nc)
    sx = b.sem("sx")
    sg = b.sem("sg")
    st = b.sem("st")
    sv = b.sem("sv")
    ss = b.sem("ss")
    so0 = b.sem("so0")
    so1 = b.sem("so1")
    so = [so0, so1]

    b.op("vector", lambda e: e.memset(ones_bf[:, :], 1.0), inc=(sv, 1))
    b.op("vector", lambda e: e.memset(ones_f[:, :], 1.0), inc=(sv, 1))
    CONSTS_V = b.tick(sv)
    b.dma("sync", mask_sb[:, :], mask_i, sx)
    b.dma("sync", ident[:, :], ident_i, sx)
    for dc in range(NCH):
        b.dma("sync", q_sb[:, dc * TOK:(dc + 1) * TOK],
              q_i[128 * dc:128 * (dc + 1), :], sx)
    QLOAD_X = b.tick(sx)

    kstream = Stream(b, "kst", "sync", 2)
    vstream = Stream(b, "vst", "sync", 2)
    pt_evac_tick = [0, 0]
    ot_evac_tick = [0, 0]
    l_recip_tick = 0
    tr_last = 0
    exp_tick = [0]

    S = pall[:, 0:2048]
    OT = lambda s: pall[:, (4 + s) * 512 + 0: (4 + s) * 512 + 128]
    PTP = lambda j: ptp[:, 1024 * (j % 2): 1024 * (j % 2) + 128]

    for h in range(16):
        ksem, kt, kslot = kstream.load(k_sb[:, (h % 2) * SEQ:(h % 2 + 1) * SEQ],
                                       k_i[128 * h:128 * (h + 1), :])
        vsem, vt, vslot = vstream.load(
            v_sb[:, (h % 2) * SEQ:(h % 2 + 1) * SEQ].rearrange("p (c n) -> p c n", n=128),
            vhm_i[SEQ * h:SEQ * (h + 1), :].rearrange("(c p) n -> p c n", p=128))
        kbase = (h % 2) * SEQ
        vbase = (h % 2) * SEQ
        for qc in range(NQC):
            nkb = qc + 9
            nk = 128 * nkb
            b.wait("tensor", ksem, kt)
            if h == 0 and qc == 0:
                b.wait("tensor", sx, QLOAD_X)
            b.wait("tensor", ss, exp_tick[0])
            for kg in range(ceil_div(nk, 512)):
                w = min(512, nk - 512 * kg)
                b.op("tensor", lambda e, h=h, qc=qc, kg=kg, w=w, kbase=kbase: e.matmul(
                    S[:, 512 * kg:512 * kg + w],
                    q_sb[:, h * TOK + 128 * qc:h * TOK + 128 * (qc + 1)],
                    k_sb[:, kbase + 512 * kg:kbase + 512 * kg + w],
                    start=True, stop=True), inc=(st, 1))
            s_mm = b.tick(st)
            b.wait("vector", st, s_mm)
            if h == 0 and qc == 0:
                b.wait("vector", sx, QLOAD_X)
            for j in range(9):
                kb = qc + j
                b.op("vector", lambda e, qc=qc, j=j, kb=kb: e.tensor_tensor(
                    S[:, 128 * kb:128 * (kb + 1)], S[:, 128 * kb:128 * (kb + 1)],
                    mask_sb[:, (qc * 9 + j) * 128:(qc * 9 + j + 1) * 128], ALU.add),
                    inc=(sv, 1))
            b.guard("vector", sv)
            b.op("vector", lambda e, nk=nk: e.reduce_max(
                mneg[:, :], S[:, 0:nk], axis=AX.X, negate=True), inc=(sv, 1))
            mneg_v = b.tick(sv)
            b.wait("scalar", sv, mneg_v)
            b.wait("scalar", sv, l_recip_tick)
            b.op("scalar", lambda e, nk=nk: e.activation(
                p_sb[:, 0:nk], S[:, 0:nk], AF.Exp, bias=mneg[:, :],
                accum_out=l_col[:, :]), inc=(ss, 1))
            exp_tick[0] = b.tick(ss)
            b.wait("vector", ss, exp_tick[0])
            b.op("vector", lambda e: e.reciprocal(linv_col[:, :], l_col[:, :]),
                 inc=(sv, 1))
            l_recip_tick = b.tick(sv)
            b.guard("vector", sv)
            b.wait("vector", st, tr_last)
            b.op("vector", lambda e, nk=nk: e.tensor_scalar_mul(
                pn_sb[:, 0:nk], p_sb[:, 0:nk], linv_col[:, :]), inc=(sv, 1))
            pn_v = b.tick(sv)
            b.wait("tensor", sv, pn_v)
            for kb in range(nkb):
                j = kb % 2
                b.wait("tensor", ss, pt_evac_tick[j])
                b.op("tensor", lambda e, kb=kb, j=j: e.transpose(
                    PTP(j), pn_sb[:, 128 * kb:128 * (kb + 1)], ident[:, :]),
                    inc=(st, 1))
                tr_t = b.tick(st)
                b.wait("scalar", st, tr_t)
                b.op("scalar", lambda e, kb=kb, j=j: e.copy(
                    pt_sb[:, 128 * kb:128 * (kb + 1)], PTP(j)), inc=(ss, 1))
                pt_evac_tick[j] = b.tick(ss)
            tr_last = b.tick(st)
            ots = qc % 2
            b.wait("tensor", ss, b.tick(ss))
            b.wait("tensor", vsem, vt)
            b.wait("tensor", ss, ot_evac_tick[ots])
            for kb in range(nkb):
                b.op("tensor", lambda e, kb=kb, ots=ots, vbase=vbase, nkb=nkb: e.matmul(
                    OT(ots), v_sb[:, vbase + 128 * kb:vbase + 128 * (kb + 1)],
                    pt_sb[:, 128 * kb:128 * (kb + 1)],
                    start=(kb == 0), stop=(kb == nkb - 1)),
                    inc=(st, 1) if kb == nkb - 1 else None)
            av_t = b.tick(st)
            hs = h % 2
            b.wait("scalar", st, av_t)
            if qc == 0 and h >= 2:
                b.wait("scalar", so[h % 2], odma_tick.get(h - 2, 0))
            b.op("scalar", lambda e, hs=hs, qc=qc, ots=ots: e.copy(
                o_sb[:, hs * TOK + 128 * qc:hs * TOK + 128 * (qc + 1)], OT(ots)),
                inc=(ss, 1))
            ot_evac_tick[ots] = b.tick(ss)
        if h == 0:
            odma_tick = {}
        b.wait("sync", ss, b.tick(ss))
        b.wait("sync", so[h % 2], b.tick(so[h % 2]))
        odma_tick[h] = b.dma("sync", o_o[128 * h:128 * (h + 1), :],
                             o_sb[:, (h % 2) * TOK:(h % 2 + 1) * TOK], so[h % 2])
        kstream.consumed(kslot, "st", s_mm)
        vstream.consumed(vslot, "st", av_t)

    b.wait("sync", sx, b.tick(sx))
    b.wait("sync", so0, b.tick(so0))
    b.wait("sync", so1, b.tick(so1))
    b.realize(ctx)
    nc._ctx_keepalive = ctx
    return nc


def host_l0_masks():
    diag = np.where(np.arange(128)[:, None] >= np.arange(128)[None, :], 0.0, NEG)
    full = np.zeros((128, 128), np.float32)
    neg = np.full((128, 128), NEG, np.float32)
    out = {}
    for par in (0, 1):
        tiles = []
        for qc in range(8):
            for j in range(9):
                if par == 0:
                    tiles.append(diag if j == 0 else neg)
                else:
                    tiles.append(diag if j == 8 else full)
        out[par] = np.concatenate(tiles, axis=1).astype(BF)
    return out



FW_SCALE = float(1.0 / np.sqrt(8192.0))
NHC = DH // 128


def build_launch_C():
    nc = bass.Bass("TRN2", target_bir_lowering=False)
    xT = nc.dram_tensor("xT", [DM, TOK], F32, kind="ExternalInput").ap()
    xn_i = nc.dram_tensor("xn_i", [DM, TOK], BF16, kind="ExternalInput").ap()
    o_i = nc.dram_tensor("o_i", [DM, TOK], BF16, kind="ExternalInput").ap()
    wo = nc.dram_tensor("wo", [DM, DM], F32, kind="ExternalInput").ap()
    w1t = nc.dram_tensor("w1t", [DM, DH], F32, kind="ExternalInput").ap()
    w3t = nc.dram_tensor("w3t", [DM, DH], F32, kind="ExternalInput").ap()
    w2tp = nc.dram_tensor("w2tp", [DH, DM], F32, kind="ExternalInput").ap()
    h128_i = nc.dram_tensor("h128_i", [128, 128], BF16, kind="ExternalInput").ap()
    h64_i = nc.dram_tensor("h64_i", [128, 64], BF16, kind="ExternalInput").ap()
    x1_o = nc.dram_tensor("x1_o", [DM, TOK], F32, kind="ExternalOutput").ap()
    fw_d = nc.dram_tensor("fw_d", [DH, 512], BF16).ap()
    g_d = nc.dram_tensor("g_d", [DM, TOK], F32).ap()

    ctx = ExitStack()
    sb = lambda n, s, d: ctx.enter_context(nc.sbuf_tensor(n, s, d))
    xn = sb("xn", [128, NCH * TOK], BF16)
    big = sb("big", [128, NHC * 512], BF16)
    w1buf = sb("w1buf", [128, 2 * DM], BF16)
    w3buf = sb("w3buf", [128, 2 * DM], BF16)
    w2buf = sb("w2buf", [128, 2 * DH], BF16)
    wobuf = sb("wobuf", [128, 2 * DM], BF16)
    gate_s = sb("gate_s", [128, 2 * 512], F32)
    hid_bf = sb("hid_bf", [128, 2 * 512], BF16)
    h1_bf = sb("h1_bf", [128, 2 * 512], BF16)
    yb_buf = sb("yb_buf", [64, 2 * 512], BF16)
    h128_sb = sb("h128_sb", [128, 128], BF16)
    h64_sb = sb("h64_sb", [128, 64], BF16)
    g_buf = sb("g_buf", [128, 2 * TOK], F32)
    x_buf2 = sb("x_buf2", [128, 2 * TOK], F32)
    tmp_f = sb("tmp_f", [128, TOK], F32)
    x1_buf = sb("x1_buf", [128, 2 * TOK], F32)
    pall = ctx.enter_context(nc.psum_tensor("pall", [128, 8 * 512], F32))

    def bank(i, width=512):
        return pall[:, 512 * i:512 * i + width]

    b = Bld(nc)
    sx = b.sem("sx")
    sg = b.sem("sg")
    st = b.sem("st")
    sv = b.sem("sv")
    ss = b.sem("ss")
    so = [b.sem("so0"), b.sem("so1")]
    sf = [b.sem("sf0"), b.sem("sf1")]

    b.dma("sync", h128_sb[:, :], h128_i, sx)
    b.dma("sync", h64_sb[:, :], h64_i, sx)
    for dc in range(NCH):
        b.dma("sync", xn[:, dc * TOK:(dc + 1) * TOK],
              xn_i[128 * dc:128 * (dc + 1), :], sx)
    XLOAD = b.tick(sx)

    w1s = Stream(b, "w1s", "gpsimd", 2)
    w3s = Stream(b, "w3s", "gpsimd", 2)
    w2s = Stream(b, "w2s", "gpsimd", 2)
    wos = Stream(b, "wos", "gpsimd", 2)
    ybs = Stream(b, "ybs", "sync", 2)
    gs_in = Stream(b, "gsin", "sync", 2)
    xs_in = Stream(b, "xsin", "sync", 2)

    silu_tick = [0, 0]
    mul_tick = [0, 0]
    h1evac_tick = [0, 0]
    fwdma_tick = [0, 0]
    gdma_tick = [0, 0]
    gsil_tick = [0, 0]
    h1_pass_tick = 0

    for tb in range(2):
        if tb == 1:
            b.wait("sync", st, b.tick(st))
        for hc in range(NHC):
            par = hc % 2
            w1sem, w1tk, w1slot = w1s.load(
                w1buf[:, par * DM:(par + 1) * DM].rearrange("p (c n) -> p c n", n=128),
                w1t[:, 128 * hc:128 * (hc + 1)].rearrange("(c p) n -> p c n", p=128))
            w3sem, w3tk, w3slot = w3s.load(
                w3buf[:, par * DM:(par + 1) * DM].rearrange("p (c n) -> p c n", n=128),
                w3t[:, 128 * hc:128 * (hc + 1)].rearrange("(c p) n -> p c n", p=128))
            gb, lb, hb = 4 * par, 4 * par + 1, 4 * par + 2
            b.wait("tensor", w1sem, w1tk)
            b.wait("tensor", w3sem, w3tk)
            if hc < 2 and tb == 0:
                b.wait("tensor", sx, XLOAD)
            b.wait("tensor", ss, silu_tick[par])
            b.wait("tensor", sv, mul_tick[par])
            for dc in range(NCH):
                b.op("tensor", lambda e, gb=gb, par=par, dc=dc, tb=tb: e.matmul(
                    bank(gb), w1buf[:, par * DM + 128 * dc:par * DM + 128 * (dc + 1)],
                    xn[:, dc * TOK + 512 * tb:dc * TOK + 512 * (tb + 1)],
                    start=(dc == 0), stop=(dc == NCH - 1)))
            for dc in range(NCH):
                b.op("tensor", lambda e, lb=lb, par=par, dc=dc, tb=tb: e.matmul(
                    bank(lb), w3buf[:, par * DM + 128 * dc:par * DM + 128 * (dc + 1)],
                    xn[:, dc * TOK + 512 * tb:dc * TOK + 512 * (tb + 1)],
                    start=(dc == 0), stop=(dc == NCH - 1)),
                    inc=(st, 1) if dc == NCH - 1 else None)
            up_t = b.tick(st)
            w1s.consumed(w1slot, "st", up_t)
            w3s.consumed(w3slot, "st", up_t)
            b.wait("scalar", st, up_t)
            b.op("scalar", lambda e, gb=gb, par=par: e.activation(
                gate_s[:, 512 * par:512 * (par + 1)], bank(gb), AF.Silu),
                inc=(ss, 1))
            silu_tick[par] = b.tick(ss)
            b.wait("vector", ss, silu_tick[par])
            b.wait("vector", st, h1_pass_tick)
            b.op("vector", lambda e, lb=lb, par=par: e.tensor_mul(
                hid_bf[:, 512 * par:512 * (par + 1)], bank(lb),
                gate_s[:, 512 * par:512 * (par + 1)]), inc=(sv, 1))
            mul_tick[par] = b.tick(sv)
            b.wait("tensor", sv, mul_tick[par])
            b.wait("tensor", ss, h1evac_tick[par])
            b.op("tensor", lambda e, hb=hb, par=par: e.matmul(
                bank(hb), h128_sb[:, :], hid_bf[:, 512 * par:512 * (par + 1)],
                start=True, stop=True), inc=(st, 1))
            h1_pass_tick = b.tick(st)
            b.wait("scalar", st, h1_pass_tick)
            b.wait("scalar", sf[par], fwdma_tick[par])
            b.op("scalar", lambda e, hb=hb, par=par: e.copy(
                h1_bf[:, 512 * par:512 * (par + 1)], bank(hb)), inc=(ss, 1))
            h1evac_tick[par] = b.tick(ss)
            b.wait("sync", ss, h1evac_tick[par])
            b.wait("sync", sf[par], b.tick(sf[par]))
            b.op("sync", lambda e, hc=hc, par=par: e.dma_start(
                out=fw_d[128 * hc:128 * (hc + 1), :],
                in_=h1_bf[:, 512 * par:512 * (par + 1)]), inc=(sf[par], 16))
            fwdma_tick[par] = b.tick(sf[par])

        b.wait("sync", sf[0], b.tick(sf[0]))
        b.wait("sync", sf[1], b.tick(sf[1]))
        fw3 = fw_d.rearrange("(c p) t -> c p t", p=128)
        p2evac_tick = [0, 0]
        b.wait("tensor", ss, b.tick(ss))
        b.wait("tensor", sv, b.tick(sv))
        for bp in range(128):
            par = bp % 2
            ysem, yt, yslot = ybs.load(
                yb_buf[0:64, par * 512:(par + 1) * 512],
                fw_d[bp::128, :])
            b.wait("tensor", ysem, yt)
            b.wait("tensor", ss, p2evac_tick[par])
            b.op("tensor", lambda e, par=par: e.matmul(
                pall[0:64, 512 * par:512 * (par + 1)], h64_sb[:, :],
                yb_buf[0:64, par * 512:(par + 1) * 512],
                start=True, stop=True), inc=(st, 1))
            p2_t = b.tick(st)
            ybs.consumed(yslot, "st", p2_t)
            cpp = bp // 2
            poff = 64 * (bp % 2)
            b.wait("scalar", st, p2_t)
            b.op("scalar", lambda e, par=par, cpp=cpp, poff=poff: e.activation(
                big[poff:poff + 64, cpp * 512:(cpp + 1) * 512],
                pall[0:64, 512 * par:512 * (par + 1)], AF.Copy, scale=FW_SCALE),
                inc=(ss, 1))
            p2evac_tick[par] = b.tick(ss)
        HW_S = b.tick(ss)

        for dc in range(NCH):
            par = dc % 2
            w2sem, w2tk, w2slot = w2s.load(
                w2buf[:, par * DH:(par + 1) * DH].rearrange("p (c n) -> p c n", n=128),
                w2tp[:, 128 * dc:128 * (dc + 1)].rearrange("(c p) n -> p c n", p=128))
            db = 4 * par + 3
            b.wait("tensor", w2sem, w2tk)
            if dc == 0:
                b.wait("tensor", ss, HW_S)
            b.wait("tensor", ss, gsil_tick[par])
            for hcc in range(NHC):
                b.op("tensor", lambda e, db=db, hcc=hcc, par=par: e.matmul(
                    bank(db),
                    w2buf[:, par * DH + hcc * 128:par * DH + hcc * 128 + 128],
                    big[:, hcc * 512:(hcc + 1) * 512],
                    start=(hcc == 0), stop=(hcc == NHC - 1)),
                    inc=(st, 1) if hcc == NHC - 1 else None)
            down_t = b.tick(st)
            w2s.consumed(w2slot, "st", down_t)
            b.wait("scalar", st, down_t)
            b.wait("scalar", sf[par], gdma_tick[par])
            b.op("scalar", lambda e, db=db, par=par: e.activation(
                g_buf[:, par * TOK:par * TOK + 512], bank(db), AF.Silu),
                inc=(ss, 1))
            gsil_tick[par] = b.tick(ss)
            b.wait("sync", ss, gsil_tick[par])
            b.wait("sync", sf[par], b.tick(sf[par]))
            b.op("sync", lambda e, dc=dc, tb=tb, par=par: e.dma_start(
                out=g_d[128 * dc:128 * (dc + 1), 512 * tb:512 * (tb + 1)],
                in_=g_buf[:, par * TOK:par * TOK + 512]), inc=(sf[par], 16))
            gdma_tick[par] = b.tick(sf[par])

    b.wait("sync", sf[0], b.tick(sf[0]))
    b.wait("sync", sf[1], b.tick(sf[1]))
    o_sb = big
    b.wait("sync", st, b.tick(st))
    for dc in range(NCH):
        b.dma("sync", o_sb[:, dc * TOK:(dc + 1) * TOK],
              o_i[128 * dc:128 * (dc + 1), :], sx)
    OLOAD = b.tick(sx)
    x1dma_tick = [0, 0]
    sres_tick = [0, 0]
    for oc in range(NCH):
        par = oc % 2
        wsem, wtk, wslot = wos.load(
            wobuf[:, par * DM:(par + 1) * DM].rearrange("p (c n) -> p c n", n=128),
            wo[:, 128 * oc:128 * (oc + 1)].rearrange("(c p) n -> p c n", p=128))
        gsem, gt, gslot = gs_in.load(g_buf[:, par * TOK:(par + 1) * TOK],
                                     g_d[128 * oc:128 * (oc + 1), :])
        xsem, xt, xslot = xs_in.load(x_buf2[:, par * TOK:(par + 1) * TOK],
                                     xT[128 * oc:128 * (oc + 1), :])
        sb0 = 4 * par
        b.wait("tensor", wsem, wtk)
        if oc < 2:
            b.wait("tensor", sx, OLOAD)
        b.wait("tensor", sv, sres_tick[par])
        for h in range(2):
            for dc in range(NCH):
                b.op("tensor", lambda e, sbk=sb0 + h, par=par, dc=dc, h=h: e.matmul(
                    bank(sbk), wobuf[:, par * DM + 128 * dc:par * DM + 128 * (dc + 1)],
                    o_sb[:, dc * TOK + 512 * h:dc * TOK + 512 * (h + 1)],
                    start=(dc == 0), stop=(dc == NCH - 1)),
                    inc=(st, 1) if dc == NCH - 1 else None)
        s_t = b.tick(st)
        wos.consumed(wslot, "st", s_t)
        b.wait("vector", st, s_t)
        b.wait("vector", gsem, gt)
        b.wait("vector", xsem, xt)
        if oc >= 2:
            b.wait("vector", so[par], x1dma_tick[par])
        for h in range(2):
            b.op("vector", lambda e, sbk=sb0 + h, h=h, par=par: e.tensor_mul(
                tmp_f[:, 512 * h:512 * (h + 1)], bank(sbk),
                g_buf[:, par * TOK + 512 * h:par * TOK + 512 * (h + 1)]), inc=(sv, 1))
            b.guard("vector", sv)
            b.op("vector", lambda e, h=h, par=par: e.tensor_add(
                x1_buf[:, par * TOK + 512 * h:par * TOK + 512 * (h + 1)],
                tmp_f[:, 512 * h:512 * (h + 1)],
                x_buf2[:, par * TOK + 512 * h:par * TOK + 512 * (h + 1)]), inc=(sv, 1))
        res_v = b.tick(sv)
        sres_tick[par] = res_v
        gs_in.consumed(gslot, "sv", res_v)
        xs_in.consumed(xslot, "sv", res_v)
        b.wait("sync", sv, res_v)
        b.wait("sync", so[par], b.tick(so[par]))
        b.op("sync", lambda e, oc=oc, par=par: e.dma_start(
            out=x1_o[128 * oc:128 * (oc + 1), :],
            in_=x1_buf[:, par * TOK:(par + 1) * TOK]), inc=(so[par], 16))
        x1dma_tick[par] = b.tick(so[par])

    b.wait("sync", sx, b.tick(sx))
    b.wait("sync", so[0], b.tick(so[0]))
    b.wait("sync", so[1], b.tick(so[1]))
    b.realize(ctx)
    nc._ctx_keepalive = ctx
    return nc


def host_fwht_consts():
    def hmat(n):
        h = np.array([[1.0]])
        while h.shape[0] < n:
            h = np.block([[h, h], [h, -h]])
        return h
    h128 = hmat(128).astype(BF)
    h64 = hmat(64).astype(BF)
    perm = np.empty(DH, np.int64)
    for cpp in range(64):
        for rr in range(128):
            bprime = 2 * cpp + (1 if rr >= 64 else 0)
            aprime = rr % 64
            perm[cpp * 128 + rr] = aprime * 128 + bprime
    return h128, h64, perm



def build_launch_D():
    nc = bass.Bass("TRN2", target_bir_lowering=False)
    x1f = nc.dram_tensor("x1f", [DM, SEQ], F32, kind="ExternalInput").ap()
    wq = nc.dram_tensor("wq", [DM, DM], F32, kind="ExternalInput").ap()
    wv = nc.dram_tensor("wv", [DM, 512], F32, kind="ExternalInput").ap()
    ident_i = nc.dram_tensor("ident_i", [128, 128], BF16, kind="ExternalInput").ap()
    xn1_o = nc.dram_tensor("xn1_o", [DM, TOK], BF16, kind="ExternalOutput").ap()
    q1_o = nc.dram_tensor("q1_o", [DM, TOK], BF16, kind="ExternalOutput").ap()
    k1_o = nc.dram_tensor("k1_o", [512, SEQ], BF16, kind="ExternalOutput").ap()
    v1hm_o = nc.dram_tensor("v1hm_o", [4 * SEQ, 128], BF16, kind="ExternalOutput").ap()

    ctx = ExitStack()
    sb = lambda n, s, d: ctx.enter_context(nc.sbuf_tensor(n, s, d))
    xn = sb("xn", [128, NCH * SEQ], BF16)
    wv_sb = sb("wv_sb", [128, NCH * 512], BF16)
    wq_buf = sb("wq_buf", [128, 2 * DM], BF16)
    x_buf = sb("x_buf", [128, 2 * TOK], F32)
    xbf_buf = sb("xbf_buf", [128, 2 * TOK], BF16)
    sq_buf = sb("sq_buf", [128, 2 * TOK], BF16)
    v_f = sb("v_f", [128, 2 * 512], F32)
    v_bf = sb("v_bf", [128, 2 * 512], BF16)
    k_bf = sb("k_bf", [128, 2 * 512], BF16)
    kt_bf = sb("kt_bf", [128, 2 * 512], BF16)
    sqv_f = sb("sqv_f", [128, 128], F32)
    vsum = sb("vsum", [128, 8], F32)
    vsd = sb("vsd", [128, 8], F32)
    vrstd = sb("vrstd", [128, 8], F32)
    qsq = sb("qsq", [128, TOK], BF16)
    q_bf = sb("q_bf", [128, 2 * TOK], BF16)
    rstd_bc = sb("rstd_bc", [128, TOK], F32)
    ident = sb("ident", [128, 128], BF16)
    ones_bf = sb("ones_bf", [128, 1], BF16)
    ones_f = sb("ones_f", [1, 128], F32)
    epst = sb("epst", [128, 2], F32)
    rows = {k: sb(k, [1, TOK], F32) for k in ("m_row", "ex2", "m2", "var", "sqv", "rstd")}
    rows["tmp32"] = sb("tmp32", [128, 1024], F32)
    qrow = {k: sb("q" + k, [1, TOK], F32) for k in ("sqv", "rstd")}
    pall = ctx.enter_context(nc.psum_tensor("pall", [128, 6 * 512], F32))
    ptp = ctx.enter_context(nc.psum_tensor("ptp", [128, 2 * 1024], BF16))

    def bank(i, width=512):
        return pall[:, 512 * i:512 * i + width]

    b = Bld(nc)
    sx = b.sem("sx")
    sg = b.sem("sg")
    st = b.sem("st")
    sv = b.sem("sv")
    ss = b.sem("ss")
    so = [b.sem("so0"), b.sem("so1")]
    sid = b.sem("sid")

    ps_ones_bf[b] = ones_bf
    ps_ones_f[b] = ones_f
    eps_row[b] = epst
    b.op("vector", lambda e: e.memset(ones_bf[:, :], 1.0), inc=(sv, 1))
    b.op("vector", lambda e: e.memset(ones_f[:, :], 1.0), inc=(sv, 1))
    b.op("vector", lambda e: e.memset(epst[:, 0:1], LN_EPS), inc=(sv, 1))
    b.op("vector", lambda e: e.memset(epst[:, 1:2], RMS_EPS), inc=(sv, 1))
    consts = b.tick(sv)
    b.dma("sync", ident[:, :], ident_i, sid)
    IDLOAD = b.tick(sid)

    x_stream = Stream(b, "lx", "sync", 2)
    x_stream.buf = x_buf
    xbf_stream = Stream(b, "lb", "gpsimd", 2)
    xbf_stream.buf = xbf_buf

    for th in range(2):
        xt_v = emit_ln_stats_and_xn(
            b, bank, consts, x1f[:, TOK * th:TOK * (th + 1)], x_stream, xbf_stream,
            sq_buf, rows, xn, TOK, sx, sg, st, sv, ss,
            xn_out_dram=(xn1_o if th == 0 else None),
            xn_cols=lambda dc, th=th: xn[:, dc * SEQ + TOK * th: dc * SEQ + TOK * (th + 1)])
    XN_V = b.tick(sv)

    for dc in range(NCH):
        b.dma("gpsimd", wv_sb[:, dc * 512:(dc + 1) * 512],
              wv[128 * dc:128 * (dc + 1), :], sg)
    WVL = b.tick(sg)
    b.wait("tensor", sg, WVL)
    b.wait("tensor", sv, XN_V)
    vevac_tick = [0, 0]
    kdma_tick = [0, 0]
    vdma_tick = [0, 0]
    trevac_tick = [0, 0]
    kmul_tick = [0, 0]
    ktr_tick = [0, 0]
    for tc in range(SEQ // 128):
        par = tc % 2
        vb = par
        b.wait("tensor", ss, vevac_tick[par])
        for dc in range(NCH):
            b.op("tensor", lambda e, vb=vb, tc=tc, dc=dc: e.matmul(
                bank(vb), xn[:, dc * SEQ + 128 * tc:dc * SEQ + 128 * (tc + 1)],
                wv_sb[:, dc * 512:(dc + 1) * 512],
                start=(dc == 0), stop=(dc == NCH - 1)),
                inc=(st, 1) if dc == NCH - 1 else None)
        v_t = b.tick(st)
        b.wait("scalar", st, v_t)
        b.wait("scalar", so[0], vdma_tick[par])
        b.wait("scalar", sv, kmul_tick[par])
        b.op("scalar", lambda e, vb=vb, par=par: e.copy(
            v_f[:, 512 * par:512 * (par + 1)], bank(vb)), inc=(ss, 1))
        b.guard("scalar", ss)
        b.op("scalar", lambda e, vb=vb, par=par: e.copy(
            v_bf[:, 512 * par:512 * (par + 1)], bank(vb)), inc=(ss, 1))
        vevac_tick[par] = b.tick(ss)
        VE = b.tick(ss)
        b.wait("scalar", ss, VE)
        for kvh in range(4):
            col = 512 * par + 128 * kvh
            b.op("scalar", lambda e, col=col: e.activation(
                sqv_f[:, 0:128], v_f[:, col:col + 128], AF.Square), inc=(ss, 1))
            sq_t = b.tick(ss)
            b.wait("vector", ss, sq_t)
            b.op("vector", lambda e, kvh=kvh: e.reduce_sum(
                vsum[:, kvh:kvh + 1], sqv_f[:, 0:128], axis=AX.X), inc=(sv, 1))
            sum_v = b.tick(sv)
            b.wait("scalar", sv, sum_v)
            b.op("scalar", lambda e, kvh=kvh: e.activation(
                vsd[:, kvh:kvh + 1], vsum[:, kvh:kvh + 1], AF.Sqrt,
                scale=1.0 / 128.0, bias=epst[:, 1:2]), inc=(ss, 1))
            sd_t = b.tick(ss)
            b.wait("vector", ss, sd_t)
            b.op("vector", lambda e, kvh=kvh: e.reciprocal(
                vrstd[:, kvh:kvh + 1], vsd[:, kvh:kvh + 1]), inc=(sv, 1))
            b.guard("vector", sv)
            if kvh == 0:
                b.wait("vector", st, ktr_tick[par])
            b.op("vector", lambda e, col=col, kvh=kvh: e.tensor_scalar_mul(
                k_bf[:, col:col + 128], v_f[:, col:col + 128],
                vrstd[:, kvh:kvh + 1]), inc=(sv, 1))
        krow_v = b.tick(sv)
        kmul_tick[par] = krow_v
        b.wait("tensor", sv, krow_v)
        if tc == 0:
            b.wait("tensor", sid, IDLOAD)
        for kvh in range(4):
            j = kvh % 2
            b.wait("tensor", ss, trevac_tick[j])
            b.op("tensor", lambda e, par=par, kvh=kvh, j=j: e.transpose(
                ptp[:, 1024 * j:1024 * j + 128],
                k_bf[:, 512 * par + 128 * kvh:512 * par + 128 * (kvh + 1)],
                ident[:, :]), inc=(st, 1))
            tr_t = b.tick(st)
            b.wait("scalar", st, tr_t)
            b.wait("scalar", so[1], kdma_tick[j])
            b.op("scalar", lambda e, j=j: e.copy(
                kt_bf[:, 512 * j:512 * j + 128], ptp[:, 1024 * j:1024 * j + 128]),
                inc=(ss, 1))
            trevac_tick[j] = b.tick(ss)
            b.wait("sync", ss, trevac_tick[j])
            b.wait("sync", so[1], b.tick(so[1]))
            b.op("sync", lambda e, kvh=kvh, tc=tc, j=j: e.dma_start(
                out=k1_o[128 * kvh:128 * (kvh + 1), 128 * tc:128 * (tc + 1)],
                in_=kt_bf[:, 512 * j:512 * j + 128]), inc=(so[1], 16))
            kdma_tick[j] = b.tick(so[1])
        ktr_tick[par] = b.tick(st)
        b.wait("sync", ss, VE)
        b.wait("sync", so[0], b.tick(so[0]))
        for kvh in range(4):
            b.op("sync", lambda e, kvh=kvh, tc=tc, par=par: e.dma_start(
                out=v1hm_o[SEQ * kvh + 128 * tc:SEQ * kvh + 128 * (tc + 1), :],
                in_=v_bf[:, 512 * par + 128 * kvh:512 * par + 128 * (kvh + 1)]),
                inc=(so[0], 16))
        vdma_tick[par] = b.tick(so[0])

    wqs = Stream(b, "wqs", "gpsimd", 2)
    qdma_tick = [0, 0]
    qmul_tick = 0
    for qh in range(16):
        par = qh % 2
        wsem, wtk, wslot = wqs.load(
            wq_buf[:, par * DM:(par + 1) * DM].rearrange("p (c n) -> p c n", n=128),
            wq[:, 128 * qh:128 * (qh + 1)].rearrange("(c p) n -> p c n", p=128))
        b.wait("tensor", wsem, wtk)
        b.wait("tensor", sv, qmul_tick)
        b.wait("tensor", ss, b.tick(ss))
        for h in range(2):
            for dc in range(NCH):
                b.op("tensor", lambda e, h=h, par=par, dc=dc: e.matmul(
                    bank(h), wq_buf[:, par * DM + 128 * dc:par * DM + 128 * (dc + 1)],
                    xn[:, dc * SEQ + 512 * h:dc * SEQ + 512 * (h + 1)],
                    start=(dc == 0), stop=(dc == NCH - 1)),
                    inc=(st, 1) if dc == NCH - 1 else None)
        q_t = b.tick(st)
        wqs.consumed(wslot, "st", q_t)
        b.wait("scalar", st, q_t)
        b.op("scalar", lambda e: e.activation(qsq[:, :], pall[:, 0:1024], AF.Square),
             inc=(ss, 1))
        qsq_t = b.tick(ss)
        b.wait("tensor", ss, qsq_t)
        for h in range(2):
            b.op("tensor", lambda e, h=h: e.matmul(
                pall[0:1, 512 * (2 + h):512 * (3 + h)], ones_bf[:, :],
                qsq[:, 512 * h:512 * (h + 1)], start=True, stop=True),
                inc=(st, 1) if h == 1 else None)
        qs_t = b.tick(st)
        b.wait("scalar", st, qs_t)
        b.op("scalar", lambda e: e.activation(
            qrow["sqv"][:, :], pall[0:1, 1024:2048], AF.Sqrt, scale=1.0 / 128.0,
            bias=epst[0:1, 1:2]), inc=(ss, 1))
        b.wait("vector", ss, b.tick(ss))
        b.op("vector", lambda e: e.reciprocal(qrow["rstd"][:, :], qrow["sqv"][:, :]),
             inc=(sv, 1))
        qr_v = b.tick(sv)
        b.wait("tensor", sv, qr_v)
        b.wait("tensor", sv, qmul_tick)
        for h in range(2):
            b.op("tensor", lambda e, h=h: e.matmul(
                pall[:, 512 * (4 + h):512 * (5 + h)], ones_f[:, :],
                qrow["rstd"][:, 512 * h:512 * (h + 1)], start=True, stop=True),
                inc=(st, 1) if h == 1 else None)
        bc_t = b.tick(st)
        b.wait("scalar", st, bc_t)
        b.wait("scalar", sv, qmul_tick)
        b.op("scalar", lambda e: e.copy(rstd_bc[:, :], pall[:, 2048:3072]),
             inc=(ss, 1))
        bce_t = b.tick(ss)
        b.wait("vector", ss, bce_t)
        if qh >= 2:
            b.wait("vector", so[par], qdma_tick[par])
        b.op("vector", lambda e, par=par: e.tensor_mul(
            q_bf[:, par * TOK:(par + 1) * TOK], pall[:, 0:1024], rstd_bc[:, :]),
            inc=(sv, 1))
        qmul_tick = b.tick(sv)
        b.wait("sync", sv, qmul_tick)
        b.wait("sync", so[par], b.tick(so[par]))
        b.op("sync", lambda e, qh=qh, par=par: e.dma_start(
            out=q1_o[128 * qh:128 * (qh + 1), :],
            in_=q_bf[:, par * TOK:(par + 1) * TOK]), inc=(so[par], 16))
        qdma_tick[par] = b.tick(so[par])

    b.wait("sync", sx, b.tick(sx))
    b.wait("sync", so[0], b.tick(so[0]))
    b.wait("sync", so[1], b.tick(so[1]))
    b.wait("gpsimd", sg, b.tick(sg))
    b.realize(ctx)
    nc._ctx_keepalive = ctx
    return nc



def build_launch_E():
    nc = bass.Bass("TRN2", target_bir_lowering=False)
    q_i = nc.dram_tensor("q_i", [DM, TOK], BF16, kind="ExternalInput").ap()
    k_i = nc.dram_tensor("k_i", [512, SEQ], BF16, kind="ExternalInput").ap()
    vhm_i = nc.dram_tensor("vhm_i", [4 * SEQ, 128], BF16, kind="ExternalInput").ap()
    ident_i = nc.dram_tensor("ident_i", [128, 128], BF16, kind="ExternalInput").ap()
    wo = nc.dram_tensor("wo", [DM, DM], F32, kind="ExternalInput").ap()
    s1n_o = nc.dram_tensor("s1n_o", [DM, TOK], F32, kind="ExternalOutput").ap()

    NQC = TOK // 128
    NKB = SEQ // 128

    ctx = ExitStack()
    sb = lambda n, s, d: ctx.enter_context(nc.sbuf_tensor(n, s, d))
    q_sb = sb("q_sb", [128, NCH * TOK], BF16)
    k_sb = sb("k_sb", [128, 4 * SEQ], BF16)
    v_sb = sb("v_sb", [128, 4 * SEQ], BF16)
    y_sb = sb("y_sb", [128, NCH * TOK], BF16)
    s_sb = sb("s_sb", [128, NCH * TOK], F32)
    p_sb = sb("p_sb", [128, SEQ], BF16)
    pn_sb = sb("pn_sb", [128, SEQ], BF16)
    pt_sb = sb("pt_sb", [128, NKB * 128], BF16)
    wobuf = sb("wobuf", [128, 2 * DM], BF16)
    sq_bf = sb("sq_bf", [128, TOK], BF16)
    s1n_buf = sb("s1n_buf", [128, 2 * TOK], F32)
    rstd_bc = sb("rstd_bc", [128, TOK], F32)
    ident = sb("ident", [128, 128], BF16)
    ones_bf = sb("ones_bf", [128, 1], BF16)
    ones_f = sb("ones_f", [1, 128], F32)
    epst = sb("epst", [128, 2], F32)
    mneg = sb("mneg", [128, 3], F32)
    l_col = sb("l_col", [128, 6], F32)
    linv_col = sb("linv_col", [128, 3], F32)
    srow = {k: sb("s" + k, [1, TOK], F32) for k in ("sqv", "rstd")}
    pall = ctx.enter_context(nc.psum_tensor("pall", [128, 6 * 512], F32))
    ptp = ctx.enter_context(nc.psum_tensor("ptp", [128, 2 * 1024], BF16))

    b = Bld(nc)
    sx = b.sem("sx")
    sg = b.sem("sg")
    st = b.sem("st")
    sv = b.sem("sv")
    ss = b.sem("ss")
    so = [b.sem("so0"), b.sem("so1")]

    b.op("vector", lambda e: e.memset(ones_bf[:, :], 1.0), inc=(sv, 1))
    b.op("vector", lambda e: e.memset(ones_f[:, :], 1.0), inc=(sv, 1))
    b.op("vector", lambda e: e.memset(epst[:, 1:2], RMS_EPS), inc=(sv, 1))
    CONSTS_V = b.tick(sv)
    b.dma("sync", ident[:, :], ident_i, sx)
    for dc in range(NCH):
        b.dma("sync", q_sb[:, dc * TOK:(dc + 1) * TOK],
              q_i[128 * dc:128 * (dc + 1), :], sx)
    for kvh in range(4):
        b.dma("sync", k_sb[:, kvh * SEQ:(kvh + 1) * SEQ],
              k_i[128 * kvh:128 * (kvh + 1), :], sx)
        b.dma("sync",
              v_sb[:, kvh * SEQ:(kvh + 1) * SEQ].rearrange("p (c n) -> p c n", n=128),
              vhm_i[kvh * SEQ:(kvh + 1) * SEQ, :].rearrange("(c p) n -> p c n", p=128),
              sx)
    LOAD_X = b.tick(sx)

    S = pall[:, 0:2048]
    OT = lambda s: pall[:, (4 + s) * 512: (4 + s) * 512 + 128]
    PTP = lambda j: ptp[:, 1024 * (j % 2): 1024 * (j % 2) + 128]

    pt_evac_tick = [0, 0]
    ot_evac_tick = [0, 0]
    l_recip_tick = 0
    tr_last = 0
    exp_tick = [0]

    for qh in range(16):
        kvh = qh // 4
        kbase = kvh * SEQ
        for qc in range(NQC):
            b.wait("tensor", ss, exp_tick[0])
            if qh == 0 and qc == 0:
                b.wait("tensor", sx, LOAD_X)
            for kg in range(4):
                b.op("tensor", lambda e, qh=qh, qc=qc, kg=kg, kbase=kbase: e.matmul(
                    S[:, 512 * kg:512 * (kg + 1)],
                    q_sb[:, qh * TOK + 128 * qc:qh * TOK + 128 * (qc + 1)],
                    k_sb[:, kbase + 512 * kg:kbase + 512 * (kg + 1)],
                    start=True, stop=True), inc=(st, 1))
            s_mm = b.tick(st)
            b.wait("vector", st, s_mm)
            b.guard("vector", sv)
            b.op("vector", lambda e: e.reduce_max(
                mneg[:, :], S[:, :], axis=AX.X, negate=True), inc=(sv, 1))
            mneg_v = b.tick(sv)
            b.wait("scalar", sv, mneg_v)
            b.wait("scalar", sv, l_recip_tick)
            b.op("scalar", lambda e: e.activation(
                p_sb[:, :], S[:, :], AF.Exp, bias=mneg[:, :],
                accum_out=l_col[:, :]), inc=(ss, 1))
            exp_tick[0] = b.tick(ss)
            b.wait("vector", ss, exp_tick[0])
            b.op("vector", lambda e: e.reciprocal(linv_col[:, :], l_col[:, :]),
                 inc=(sv, 1))
            l_recip_tick = b.tick(sv)
            b.guard("vector", sv)
            b.wait("vector", st, tr_last)
            b.op("vector", lambda e: e.tensor_scalar_mul(
                pn_sb[:, :], p_sb[:, :], linv_col[:, :]), inc=(sv, 1))
            pn_v = b.tick(sv)
            b.wait("tensor", sv, pn_v)
            for kb in range(NKB):
                j = kb % 2
                b.wait("tensor", ss, pt_evac_tick[j])
                b.op("tensor", lambda e, kb=kb, j=j: e.transpose(
                    PTP(j), pn_sb[:, 128 * kb:128 * (kb + 1)], ident[:, :]),
                    inc=(st, 1))
                tr_t = b.tick(st)
                b.wait("scalar", st, tr_t)
                b.op("scalar", lambda e, kb=kb, j=j: e.copy(
                    pt_sb[:, 128 * kb:128 * (kb + 1)], PTP(j)), inc=(ss, 1))
                pt_evac_tick[j] = b.tick(ss)
            tr_last = b.tick(st)
            ots = qc % 2
            b.wait("tensor", ss, b.tick(ss))
            b.wait("tensor", ss, ot_evac_tick[ots])
            for kb in range(NKB):
                b.op("tensor", lambda e, kb=kb, ots=ots, kbase=kbase: e.matmul(
                    OT(ots), v_sb[:, kbase + 128 * kb:kbase + 128 * (kb + 1)],
                    pt_sb[:, 128 * kb:128 * (kb + 1)],
                    start=(kb == 0), stop=(kb == NKB - 1)),
                    inc=(st, 1) if kb == NKB - 1 else None)
            av_t = b.tick(st)
            b.wait("scalar", st, av_t)
            b.op("scalar", lambda e, qh=qh, qc=qc, ots=ots: e.copy(
                y_sb[:, qh * TOK + 128 * qc:qh * TOK + 128 * (qc + 1)], OT(ots)),
                inc=(ss, 1))
            ot_evac_tick[ots] = b.tick(ss)
    Y_S = b.tick(ss)

    wos = Stream(b, "wos", "gpsimd", 2)
    STAT0 = pall[0:1, 4 * 512:4 * 512 + 512]
    STAT1 = pall[0:1, 5 * 512:5 * 512 + 512]
    sevac_tick = [0, 0]
    for oc in range(NCH):
        par = oc % 2
        wsem, wtk, wslot = wos.load(
            wobuf[:, par * DM:(par + 1) * DM].rearrange("p (c n) -> p c n", n=128),
            wo[:, 128 * oc:128 * (oc + 1)].rearrange("(c p) n -> p c n", p=128))
        sb0 = 2 * par
        b.wait("tensor", wsem, wtk)
        if oc == 0:
            b.wait("tensor", ss, Y_S)
        b.wait("tensor", ss, sevac_tick[par])
        for h in range(2):
            for dc in range(NCH):
                b.op("tensor", lambda e, sbk=sb0 + h, par=par, dc=dc, h=h: e.matmul(
                    pall[:, 512 * sbk:512 * (sbk + 1)],
                    wobuf[:, par * DM + 128 * dc:par * DM + 128 * (dc + 1)],
                    y_sb[:, dc * TOK + 512 * h:dc * TOK + 512 * (h + 1)],
                    start=(dc == 0), stop=(dc == NCH - 1)),
                    inc=(st, 1) if dc == NCH - 1 else None)
        s_t = b.tick(st)
        wos.consumed(wslot, "st", s_t)
        b.wait("scalar", st, s_t)
        for h in range(2):
            b.op("scalar", lambda e, sbk=sb0 + h, oc=oc, h=h: e.copy(
                s_sb[:, oc * TOK + 512 * h:oc * TOK + 512 * (h + 1)],
                pall[:, 512 * sbk:512 * (sbk + 1)]), inc=(ss, 1))
            b.op("scalar", lambda e, sbk=sb0 + h, h=h: e.activation(
                sq_bf[:, 512 * h:512 * (h + 1)],
                pall[:, 512 * sbk:512 * (sbk + 1)], AF.Square), inc=(ss, 1))
        sevac_tick[par] = b.tick(ss)
        b.wait("tensor", ss, sevac_tick[par])
        if oc == 0:
            b.wait("tensor", sv, CONSTS_V)
        b.op("tensor", lambda e, oc=oc: e.matmul(
            STAT0, ones_bf[:, :], sq_bf[:, 0:512],
            start=(oc == 0), stop=(oc == NCH - 1)))
        b.op("tensor", lambda e, oc=oc: e.matmul(
            STAT1, ones_bf[:, :], sq_bf[:, 512:1024],
            start=(oc == 0), stop=(oc == NCH - 1)),
            inc=(st, 1) if oc == NCH - 1 else None)
    STATS_T = b.tick(st)
    b.wait("scalar", st, STATS_T)
    b.op("scalar", lambda e: e.activation(
        srow["sqv"][:, 0:512], STAT0, AF.Sqrt, scale=1.0 / DM,
        bias=epst[0:1, 1:2]), inc=(ss, 1))
    b.op("scalar", lambda e: e.activation(
        srow["sqv"][:, 512:1024], STAT1, AF.Sqrt, scale=1.0 / DM,
        bias=epst[0:1, 1:2]), inc=(ss, 1))
    b.wait("vector", ss, b.tick(ss))
    b.op("vector", lambda e: e.reciprocal(srow["rstd"][:, :], srow["sqv"][:, :]),
         inc=(sv, 1))
    RST_V = b.tick(sv)
    b.wait("tensor", sv, RST_V)
    b.wait("tensor", ss, b.tick(ss))
    for h in range(2):
        b.op("tensor", lambda e, h=h: e.matmul(
            pall[:, 512 * h:512 * (h + 1)], ones_f[:, :],
            srow["rstd"][:, 512 * h:512 * (h + 1)], start=True, stop=True),
            inc=(st, 1) if h == 1 else None)
    BC_T = b.tick(st)
    b.wait("scalar", st, BC_T)
    b.op("scalar", lambda e: e.copy(rstd_bc[:, :], pall[:, 0:1024]), inc=(ss, 1))
    BCE_S = b.tick(ss)
    b.wait("vector", ss, BCE_S)
    for oc in range(NCH):
        par = oc % 2
        if oc >= 2:
            b.wait("vector", so[par], s1dma[par])
        b.op("vector", lambda e, oc=oc, par=par: e.tensor_mul(
            s1n_buf[:, par * TOK:(par + 1) * TOK],
            s_sb[:, oc * TOK:(oc + 1) * TOK], rstd_bc[:, :]), inc=(sv, 1))
        mul_v = b.tick(sv)
        b.wait("sync", sv, mul_v)
        b.wait("sync", so[par], b.tick(so[par]))
        if oc == 0:
            s1dma = [0, 0]
        b.op("sync", lambda e, oc=oc, par=par: e.dma_start(
            out=s1n_o[128 * oc:128 * (oc + 1), :],
            in_=s1n_buf[:, par * TOK:(par + 1) * TOK]), inc=(so[par], 16))
        s1dma[par] = b.tick(so[par])

    b.wait("sync", sx, b.tick(sx))
    b.wait("sync", so[0], b.tick(so[0]))
    b.wait("sync", so[1], b.tick(so[1]))
    b.wait("gpsimd", sg, b.tick(sg))
    b.realize(ctx)
    nc._ctx_keepalive = ctx
    return nc



def build_launch_F():
    nc = bass.Bass("TRN2", target_bir_lowering=False)
    xn1_i = nc.dram_tensor("xn1_i", [DM, TOK], BF16, kind="ExternalInput").ap()
    s1n_i = nc.dram_tensor("s1n_i", [DM, TOK], F32, kind="ExternalInput").ap()
    x1_i = nc.dram_tensor("x1_i", [DM, TOK], F32, kind="ExternalInput").ap()
    wgt = nc.dram_tensor("wgt", [DM, DH], F32, kind="ExternalInput").ap()
    wut = nc.dram_tensor("wut", [DM, DH], F32, kind="ExternalInput").ap()
    wdt = nc.dram_tensor("wdt", [DH, DM], F32, kind="ExternalInput").ap()
    y_o = nc.dram_tensor("y_o", [DM, TOK], F32, kind="ExternalOutput").ap()

    ctx = ExitStack()
    sb = lambda n, s, d: ctx.enter_context(nc.sbuf_tensor(n, s, d))
    xn = sb("xn", [128, NCH * TOK], BF16)
    big = sb("big", [128, NHC * 512], BF16)
    wgbuf = sb("wgbuf", [128, 2 * DM], BF16)
    wubuf = sb("wubuf", [128, 2 * DM], BF16)
    wdbuf = sb("wdbuf", [128, 2 * DH], BF16)
    gate_s = sb("gate_s", [128, 2 * 512], F32)
    g_buf = sb("g_buf", [128, 2 * 512], F32)
    s_buf = sb("s_buf", [128, 2 * 512], F32)
    x_buf = sb("x_buf", [128, 2 * 512], F32)
    tmp_f = sb("tmp_f", [128, 2 * 512], F32)
    y_buf = sb("y_buf", [128, 2 * 512], F32)
    pall = ctx.enter_context(nc.psum_tensor("pall", [128, 8 * 512], F32))

    def bank(i, width=512):
        return pall[:, 512 * i:512 * i + width]

    b = Bld(nc)
    sx = b.sem("sx")
    sg = b.sem("sg")
    st = b.sem("st")
    sv = b.sem("sv")
    ss = b.sem("ss")
    so = [b.sem("so0"), b.sem("so1")]

    for dc in range(NCH):
        b.dma("sync", xn[:, dc * TOK:(dc + 1) * TOK],
              xn1_i[128 * dc:128 * (dc + 1), :], sx)
    XLOAD = b.tick(sx)

    wgs = Stream(b, "wgs", "gpsimd", 2)
    wus = Stream(b, "wus", "gpsimd", 2)
    wds = Stream(b, "wds", "gpsimd", 2)
    s_in = Stream(b, "sin", "sync", 2)
    x_in = Stream(b, "xin", "sync", 2)

    silu_tick = [0, 0]
    hmul_tick = [0, 0]
    gsil_tick = [0, 0]
    res_tick = [0, 0]
    ydma_tick = [0, 0]
    down_last = 0

    for tb in range(2):
        for hc in range(NHC):
            par = hc % 2
            wgsem, wgtk, wgslot = wgs.load(
                wgbuf[:, par * DM:(par + 1) * DM].rearrange("p (c n) -> p c n", n=128),
                wgt[:, 128 * hc:128 * (hc + 1)].rearrange("(c p) n -> p c n", p=128))
            wusem, wutk, wuslot = wus.load(
                wubuf[:, par * DM:(par + 1) * DM].rearrange("p (c n) -> p c n", n=128),
                wut[:, 128 * hc:128 * (hc + 1)].rearrange("(c p) n -> p c n", p=128))
            gb, ub = 4 * par, 4 * par + 1
            b.wait("tensor", wgsem, wgtk)
            b.wait("tensor", wusem, wutk)
            if hc < 2 and tb == 0:
                b.wait("tensor", sx, XLOAD)
            b.wait("tensor", ss, silu_tick[par])
            b.wait("tensor", sv, hmul_tick[par])
            for dc in range(NCH):
                b.op("tensor", lambda e, gb=gb, par=par, dc=dc, tb=tb: e.matmul(
                    bank(gb), wgbuf[:, par * DM + 128 * dc:par * DM + 128 * (dc + 1)],
                    xn[:, dc * TOK + 512 * tb:dc * TOK + 512 * (tb + 1)],
                    start=(dc == 0), stop=(dc == NCH - 1)))
            for dc in range(NCH):
                b.op("tensor", lambda e, ub=ub, par=par, dc=dc, tb=tb: e.matmul(
                    bank(ub), wubuf[:, par * DM + 128 * dc:par * DM + 128 * (dc + 1)],
                    xn[:, dc * TOK + 512 * tb:dc * TOK + 512 * (tb + 1)],
                    start=(dc == 0), stop=(dc == NCH - 1)),
                    inc=(st, 1) if dc == NCH - 1 else None)
            up_t = b.tick(st)
            wgs.consumed(wgslot, "st", up_t)
            wus.consumed(wuslot, "st", up_t)
            b.wait("scalar", st, up_t)
            b.op("scalar", lambda e, gb=gb, par=par: e.activation(
                gate_s[:, 512 * par:512 * (par + 1)], bank(gb), AF.Silu),
                inc=(ss, 1))
            silu_tick[par] = b.tick(ss)
            b.wait("vector", ss, silu_tick[par])
            if hc < 2:
                b.wait("vector", st, down_last)
            b.op("vector", lambda e, ub=ub, par=par, hc=hc: e.tensor_mul(
                big[:, hc * 512:(hc + 1) * 512], bank(ub),
                gate_s[:, 512 * par:512 * (par + 1)]), inc=(sv, 1))
            hmul_tick[par] = b.tick(sv)
        HID_V = b.tick(sv)

        for dc in range(NCH):
            par = dc % 2
            wdsem, wdtk, wdslot = wds.load(
                wdbuf[:, par * DH:(par + 1) * DH].rearrange("p (c n) -> p c n", n=128),
                wdt[:, 128 * dc:128 * (dc + 1)].rearrange("(c p) n -> p c n", p=128))
            ssem, stk, sslot = s_in.load(s_buf[:, par * 512:(par + 1) * 512],
                                         s1n_i[128 * dc:128 * (dc + 1),
                                               512 * tb:512 * (tb + 1)])
            xsem, xtk, xslot = x_in.load(x_buf[:, par * 512:(par + 1) * 512],
                                         x1_i[128 * dc:128 * (dc + 1),
                                              512 * tb:512 * (tb + 1)])
            db = 4 * par + 3
            b.wait("tensor", wdsem, wdtk)
            if dc == 0:
                b.wait("tensor", sv, HID_V)
            b.wait("tensor", ss, gsil_tick[par])
            for hcc in range(NHC):
                b.op("tensor", lambda e, db=db, hcc=hcc, par=par: e.matmul(
                    bank(db), wdbuf[:, par * DH + hcc * 128:par * DH + hcc * 128 + 128],
                    big[:, hcc * 512:(hcc + 1) * 512],
                    start=(hcc == 0), stop=(hcc == NHC - 1)),
                    inc=(st, 1) if hcc == NHC - 1 else None)
            down_t = b.tick(st)
            down_last = down_t
            wds.consumed(wdslot, "st", down_t)
            b.wait("scalar", st, down_t)
            b.wait("scalar", sv, res_tick[par])
            b.op("scalar", lambda e, db=db, par=par: e.activation(
                g_buf[:, 512 * par:512 * (par + 1)], bank(db), AF.Silu),
                inc=(ss, 1))
            gsil_tick[par] = b.tick(ss)
            b.wait("vector", ss, gsil_tick[par])
            b.wait("vector", ssem, stk)
            b.wait("vector", xsem, xtk)
            b.wait("vector", so[par], ydma_tick[par])
            b.guard("vector", sv)
            b.op("vector", lambda e, par=par: e.tensor_mul(
                tmp_f[:, 512 * par:512 * (par + 1)],
                g_buf[:, 512 * par:512 * (par + 1)],
                s_buf[:, 512 * par:512 * (par + 1)]), inc=(sv, 1))
            b.guard("vector", sv)
            b.op("vector", lambda e, par=par: e.tensor_add(
                y_buf[:, 512 * par:512 * (par + 1)],
                tmp_f[:, 512 * par:512 * (par + 1)],
                x_buf[:, 512 * par:512 * (par + 1)]), inc=(sv, 1))
            res_v = b.tick(sv)
            res_tick[par] = res_v
            s_in.consumed(sslot, "sv", res_v)
            x_in.consumed(xslot, "sv", res_v)
            b.wait("sync", sv, res_v)
            b.wait("sync", so[par], b.tick(so[par]))
            b.op("sync", lambda e, dc=dc, tb=tb, par=par: e.dma_start(
                out=y_o[128 * dc:128 * (dc + 1), 512 * tb:512 * (tb + 1)],
                in_=y_buf[:, 512 * par:512 * (par + 1)]), inc=(so[par], 16))
            ydma_tick[par] = b.tick(so[par])

    b.wait("sync", sx, b.tick(sx))
    b.wait("sync", so[0], b.tick(so[0]))
    b.wait("sync", so[1], b.tick(so[1]))
    b.wait("gpsimd", sg, b.tick(sg))
    b.realize(ctx)
    nc._ctx_keepalive = ctx
    return nc




def _l0_sched(qb):
    lks = list(range(qb + 1))
    if qb < 8:
        lks += list(range(8, 16))
    return lks


def build_fused():
    nc = bass.Bass("TRN2", target_bir_lowering=False)
    xT = nc.dram_tensor("xT", [DM, SEQ], F32, kind="ExternalInput").ap()
    wq0 = nc.dram_tensor("wq0", [DM, DM], F32, kind="ExternalInput").ap()
    wk0 = nc.dram_tensor("wk0", [DM, DM], F32, kind="ExternalInput").ap()
    wv0 = nc.dram_tensor("wv0", [DM, DM], F32, kind="ExternalInput").ap()
    wo0 = nc.dram_tensor("wo0", [DM, DM], F32, kind="ExternalInput").ap()
    w1t = nc.dram_tensor("w1t", [DM, DH], F32, kind="ExternalInput").ap()
    w3t = nc.dram_tensor("w3t", [DM, DH], F32, kind="ExternalInput").ap()
    w2tp = nc.dram_tensor("w2tp", [DH, DM], F32, kind="ExternalInput").ap()
    wq1 = nc.dram_tensor("wq1", [DM, DM], F32, kind="ExternalInput").ap()
    wv1 = nc.dram_tensor("wv1", [DM, 512], F32, kind="ExternalInput").ap()
    wo1 = nc.dram_tensor("wo1", [DM, DM], F32, kind="ExternalInput").ap()
    wgt = nc.dram_tensor("wgt", [DM, DH], F32, kind="ExternalInput").ap()
    wut = nc.dram_tensor("wut", [DM, DH], F32, kind="ExternalInput").ap()
    wdt = nc.dram_tensor("wdt", [DH, DM], F32, kind="ExternalInput").ap()
    mask_i = nc.dram_tensor("mask_i", [128, 17 * 128], BF16, kind="ExternalInput").ap()
    ident_i = nc.dram_tensor("ident_i", [128, 128], BF16, kind="ExternalInput").ap()
    h128_i = nc.dram_tensor("h128_i", [128, 128], BF16, kind="ExternalInput").ap()
    h64_i = nc.dram_tensor("h64_i", [128, 64], BF16, kind="ExternalInput").ap()
    y_o = nc.dram_tensor("y_o", [DM, TOK], F32, kind="ExternalOutput").ap()

    fw_d = nc.dram_tensor("fw_d", [DH, TOK], BF16).ap()
    g_d = nc.dram_tensor("g_d", [DM, TOK], F32).ap()
    q0_d = nc.dram_tensor("q0_d", [DM, TOK], BF16).ap()
    k0_d = nc.dram_tensor("k0_d", [DM, SEQ], BF16).ap()
    v0hm_d = nc.dram_tensor("v0hm_d", [16 * SEQ, 128], BF16).ap()
    x1_d = nc.dram_tensor("x1_d", [DM, TOK], F32).ap()
    s1n_d = nc.dram_tensor("s1n_d", [DM, TOK], F32).ap()
    hd_d = nc.dram_tensor("hd_d", [DH, TOK], BF16).ap()
    v1pad_t = nc.dram_tensor("v1pad_d", [SEQ, 512], F32)
    v1pad_d = v1pad_t.ap()
    v1ag_t = nc.dram_tensor("v1ag_d", [SEQ, 512], F32)
    v1ag_d = v1ag_t.ap()
    msel_i = nc.dram_tensor("msel_i", [128, 2], F32, kind="ExternalInput").ap()

    ctx = ExitStack()
    sb = lambda n, s, d: ctx.enter_context(nc.sbuf_tensor(n, s, d))
    ident = sb("ident", [128, 128], BF16)
    h128_sb = sb("h128_sb", [128, 128], BF16)
    h64_sb = sb("h64_sb", [128, 64], BF16)
    mask_sb = sb("mask_sb", [128, 17 * 128], BF16)
    msel_sb = sb("msel_sb", [128, 2], F32)
    ones_bf = sb("ones_bf", [128, 1], BF16)
    ones_f = sb("ones_f", [1, 128], F32)
    onescol = sb("onescol", [128, 1], F32)
    epst = sb("epst", [128, 2], F32)
    mneg = sb("mneg", [128, 3], F32)
    l_col = sb("l_col", [128, 6], F32)
    linv_col = sb("linv_col", [128, 3], F32)
    pall = ctx.enter_context(nc.psum_tensor("pall", [128, 6 * 512], F32))
    ptp = ctx.enter_context(nc.psum_tensor("ptp", [128, 2 * 512], F32))

    def bank(i, width=512):
        return pall[:, 512 * i:512 * i + width]

    b = Bld(nc)
    sx = b.sem("sx")
    sg = b.sem("sg")
    st = b.sem("st")
    sv = b.sem("sv")
    ss = b.sem("ss")
    sid = b.sem("sid")
    so = [b.sem("so0"), b.sem("so1")]
    sf = [b.sem("sf0"), b.sem("sf1")]
    swv = b.sem("swv")
    swv1 = b.sem("swv1")
    shw = [b.sem("shw0"), b.sem("shw1")]
    sol = b.sem("sol")
    scc = b.sem("scc")

    ps_ones_bf[b] = ones_bf
    ps_ones_f[b] = ones_f
    eps_row[b] = epst
    b.op("vector", lambda e: e.memset(ones_bf[:, :], 1.0), inc=(sv, 1))
    b.op("vector", lambda e: e.memset(ones_f[:, :], 1.0), inc=(sv, 1))
    b.op("vector", lambda e: e.memset(epst[:, 0:1], LN_EPS), inc=(sv, 1))
    b.op("vector", lambda e: e.memset(epst[:, 1:2], RMS_EPS), inc=(sv, 1))
    b.op("vector", lambda e: e.memset(onescol[:, :], 1.0), inc=(sv, 1))
    consts = b.tick(sv)
    b.dma("sync", ident[:, :], ident_i, sid)
    b.dma("sync", h128_sb[:, :], h128_i, sid)
    b.dma("sync", h64_sb[:, :], h64_i, sid)
    b.dma("sync", mask_sb[:, :], mask_i, sid)
    b.dma("sync", msel_sb[:, :], msel_i, sid)
    CONST_LOAD = b.tick(sid)
    DIAG = mask_sb[:, 0:128]
    DIAG_A8 = mask_sb[:, 0:9 * 128]
    MASKB8 = mask_sb[:, 9 * 128:17 * 128]

    def sp_out(queue, par, dst, src):
        b.wait(queue, sf[par], b.tick(sf[par]))
        b.op(queue, lambda e, d=dst, s=src: e.dma_start(out=d, in_=s),
             inc=(sf[par], 16))
        return b.tick(sf[par])

    def fence(q):
        b.wait(q, sv, b.tick(sv))
        b.wait(q, ss, b.tick(ss))
        b.wait(q, st, b.tick(st))
        for sem in (sf[0], sf[1], so[0], so[1]):
            b.wait(q, sem, b.tick(sem))

    def fences():
        for q in ("gpsimd", "sync", "tensor", "vector", "scalar"):
            fence(q)

    xn0_cm = nc.sbuf_tensor("xn0", [128, NCH * SEQ], BF16)
    xn0 = xn0_cm.__enter__()
    wv_cm = nc.sbuf_tensor("wv_sb", [128, NCH * DM], BF16)
    wv_sb = wv_cm.__enter__()
    for dc in range(NCH):
        b.dma("gpsimd", wv_sb[:, dc * DM:(dc + 1) * DM],
              wv0[128 * dc:128 * (dc + 1), :], swv)
    WV_G = b.tick(swv)
    p0 = ExitStack()
    sbl = lambda n, s, d: p0.enter_context(nc.sbuf_tensor(n, s, d))
    rows = {k: sbl(k, [1, TOK], F32) for k in ("m_row", "ex2", "m2", "var", "sqv", "rstd")}
    rows["tmp32"] = sbl("tmp32", [128, 1024], F32)
    x_buf = sbl("x_buf", [128, 2 * TOK], F32)
    xbf_buf = sbl("xbf_buf", [128, 2 * TOK], BF16)
    sq_buf = sbl("sq_buf", [128, 2 * TOK], BF16)
    x_stream = Stream(b, "lx", "sync", 2)
    x_stream.buf = x_buf
    xbf_stream = Stream(b, "lb", "gpsimd", 2)
    xbf_stream.buf = xbf_buf
    for th in range(2):
        emit_ln_stats_and_xn(
            b, bank, consts, xT[:, TOK * th:TOK * (th + 1)], x_stream, xbf_stream,
            sq_buf, rows, xn0, TOK, sx, sg, st, sv, ss,
            xn_cols=lambda dc, th=th: xn0[:, dc * SEQ + TOK * th:dc * SEQ + TOK * (th + 1)])
    XN0_V = b.tick(sv)
    p0.close()

    fences()
    p1 = ExitStack()
    sbl = lambda n, s, d: p1.enter_context(nc.sbuf_tensor(n, s, d))
    v_sb = sbl("v_sb", [128, 2 * DM], BF16)
    qk_buf = sbl("qk_buf", [128, 2 * DM], BF16)
    q_stg = sbl("q_stg", [128, 2 * SEQ], BF16)
    b.wait("tensor", swv, WV_G)
    b.wait("tensor", sv, XN0_V)
    vidx = 0
    v_evac = {}
    vdma_tick = [0, 0]
    for tc in range(SEQ // 128):
        for fg in range(4):
            bk = vidx % 2
            if vidx >= 2:
                b.wait("tensor", ss, v_evac[vidx - 2])
            for dc in range(NCH):
                b.op("tensor", lambda e, bk=bk, tc=tc, fg=fg, dc=dc: e.matmul(
                    bank(bk), xn0[:, dc * SEQ + 128 * tc:dc * SEQ + 128 * (tc + 1)],
                    wv_sb[:, dc * DM + 512 * fg:dc * DM + 512 * (fg + 1)],
                    start=(dc == 0), stop=(dc == NCH - 1)),
                    inc=(st, 1) if dc == NCH - 1 else None)
            b.wait("scalar", st, b.tick(st))
            vslot = tc % 2
            if fg == 0:
                b.wait("scalar", sf[vslot], vdma_tick[vslot])
            b.op("scalar", lambda e, bk=bk, vslot=vslot, fg=fg: e.activation(
                v_sb[:, vslot * DM + 512 * fg:vslot * DM + 512 * (fg + 1)],
                bank(bk), AF.Copy), inc=(ss, 1))
            v_evac[vidx] = b.tick(ss)
            vidx += 1
        vslot = tc % 2
        b.wait("sync", ss, v_evac[vidx - 1])
        b.wait("sync", sf[vslot], b.tick(sf[vslot]))
        for hh in range(16):
            b.op("sync", lambda e, hh=hh, tc=tc, vslot=vslot: e.dma_start(
                out=v0hm_d[SEQ * hh + 128 * tc:SEQ * hh + 128 * (tc + 1), :],
                in_=v_sb[:, vslot * DM + 128 * hh:vslot * DM + 128 * (hh + 1)]),
                inc=(sf[vslot], 16))
        vdma_tick[vslot] = b.tick(sf[vslot])
    wstream = Stream(b, "wqk", "gpsimd", 2)
    qk_evac = {}
    qkbank_tick = [0, 0]
    qdma_tick = [0, 0]
    for ji, (which, oc) in enumerate([("q", oc) for oc in range(NCH)] +
                                     [("k", oc) for oc in range(NCH)]):
        wsrc = wq0 if which == "q" else wk0
        src = wsrc[:, 128 * oc:128 * (oc + 1)].rearrange("(c p) n -> p c n", p=128)
        slot = ji % 2
        dst = qk_buf[:, slot * DM:(slot + 1) * DM].rearrange("p (c n) -> p c n", n=128)
        wsem, wtk, ws = wstream.load(dst, src)
        b.wait("tensor", wsem, wtk)
        nh4 = 2 if which == "q" else 4
        for h4 in range(nh4):
            bk = 2 + h4 % 2
            b.wait("tensor", ss, qkbank_tick[h4 % 2])
            for dc in range(NCH):
                b.op("tensor", lambda e, bk=bk, slot=slot, dc=dc, h4=h4: e.matmul(
                    bank(bk),
                    qk_buf[:, slot * DM + 128 * dc:slot * DM + 128 * (dc + 1)],
                    xn0[:, dc * SEQ + 512 * h4:dc * SEQ + 512 * (h4 + 1)],
                    start=(dc == 0), stop=(dc == NCH - 1)),
                    inc=(st, 1) if dc == NCH - 1 else None)
            mm_t = b.tick(st)
            qslot = ji % 2
            scale = RS128 if which == "q" else 1.0
            b.wait("scalar", st, mm_t)
            if h4 == 0:
                b.wait("scalar", sf[qslot], qdma_tick[qslot])
            b.op("scalar", lambda e, qslot=qslot, h4=h4, scale=scale, bk=bk: e.activation(
                q_stg[:, qslot * SEQ + 512 * h4:qslot * SEQ + 512 * (h4 + 1)],
                bank(bk), AF.Copy, scale=scale), inc=(ss, 1))
            qk_evac[(ji, h4)] = b.tick(ss)
            qkbank_tick[h4 % 2] = b.tick(ss)
        wstream.consumed(ws, "st", b.tick(st))
        b.wait("sync", ss, qk_evac[(ji, nh4 - 1)])
        dstd = q0_d if which == "q" else k0_d
        wid = 512 * nh4
        qdma_tick[ji % 2] = sp_out("sync", ji % 2,
                                   dstd[128 * oc:128 * (oc + 1), :],
                                   q_stg[:, (ji % 2) * SEQ:(ji % 2) * SEQ + wid])
    p1.close()
    wv_cm.__exit__(None, None, None)

    fences()
    p2 = ExitStack()
    sbl = lambda n, s, d: p2.enter_context(nc.sbuf_tensor(n, s, d))
    w1buf = sbl("w1buf", [128, 2 * DM], BF16)
    w3buf = sbl("w3buf", [128, 2 * DM], BF16)
    gate_s = sbl("gate_s", [128, 2 * 512], F32)
    hid_bf = sbl("hid_bf", [128, 2 * 512], BF16)
    h1_bf = sbl("h1_bf", [128, 2 * 512], BF16)
    w1s = Stream(b, "w1s", "gpsimd", 2)
    w3s = Stream(b, "w3s", "gpsimd", 2)
    b.wait("tensor", sid, CONST_LOAD)
    silu_tick = [0, 0]
    mul_tick = [0, 0]
    h1evac_tick = [0, 0]
    fwdma_tick = [0, 0]
    h1pass_tick2 = [0, 0]
    p2_pending = [None]
    for hc in range(NHC):
        wpar = hc % 2
        w1sem, w1tk, w1slot = w1s.load(
            w1buf[:, wpar * DM:(wpar + 1) * DM].rearrange("p (c n) -> p c n", n=128),
            w1t[:, 128 * hc:128 * (hc + 1)].rearrange("(c p) n -> p c n", p=128))
        w3sem, w3tk, w3slot = w3s.load(
            w3buf[:, wpar * DM:(wpar + 1) * DM].rearrange("p (c n) -> p c n", n=128),
            w3t[:, 128 * hc:128 * (hc + 1)].rearrange("(c p) n -> p c n", p=128))
        b.wait("tensor", w1sem, w1tk)
        b.wait("tensor", w3sem, w3tk)
        last_mm = 0
        for tq in range(2):
            par = tq % 2
            gb, lb, hb = 3 * par, 3 * par + 1, 3 * par + 2
            b.wait("tensor", ss, silu_tick[par])
            b.wait("tensor", sv, mul_tick[par])
            for dc in range(NCH):
                b.op("tensor", lambda e, gb=gb, wpar=wpar, dc=dc, tq=tq: e.matmul(
                    bank(gb), w1buf[:, wpar * DM + 128 * dc:wpar * DM + 128 * (dc + 1)],
                    xn0[:, dc * SEQ + 512 * tq:dc * SEQ + 512 * (tq + 1)],
                    start=(dc == 0), stop=(dc == NCH - 1)))
            for dc in range(NCH):
                b.op("tensor", lambda e, lb=lb, wpar=wpar, dc=dc, tq=tq: e.matmul(
                    bank(lb), w3buf[:, wpar * DM + 128 * dc:wpar * DM + 128 * (dc + 1)],
                    xn0[:, dc * SEQ + 512 * tq:dc * SEQ + 512 * (tq + 1)],
                    start=(dc == 0), stop=(dc == NCH - 1)),
                    inc=(st, 1) if dc == NCH - 1 else None)
            up_t = b.tick(st)
            last_mm = up_t
            if p2_pending[0] is not None:
                p2_pending[0]()
                p2_pending[0] = None
            b.wait("scalar", st, up_t)
            b.op("scalar", lambda e, gb=gb, par=par: e.activation(
                gate_s[:, 512 * par:512 * (par + 1)], bank(gb), AF.Silu),
                inc=(ss, 1))
            silu_tick[par] = b.tick(ss)
            b.wait("vector", ss, silu_tick[par])
            b.wait("vector", st, h1pass_tick2[par])
            b.op("vector", lambda e, lb=lb, par=par: e.tensor_mul(
                hid_bf[:, 512 * par:512 * (par + 1)], bank(lb),
                gate_s[:, 512 * par:512 * (par + 1)]), inc=(sv, 1))
            mul_tick[par] = b.tick(sv)
            mt = mul_tick[par]

            def p2tail(hc=hc, tq=tq, par=par, hb=hb, mt=mt):
                b.wait("tensor", sv, mt)
                b.wait("tensor", ss, h1evac_tick[par])
                b.op("tensor", lambda e, hb=hb, par=par: e.matmul(
                    bank(hb), h128_sb[:, :], hid_bf[:, 512 * par:512 * (par + 1)],
                    start=True, stop=True), inc=(st, 1))
                h1pass_tick2[par] = b.tick(st)
                b.wait("scalar", st, h1pass_tick2[par])
                b.wait("scalar", sf[par], fwdma_tick[par])
                b.op("scalar", lambda e, hb=hb, par=par: e.copy(
                    h1_bf[:, 512 * par:512 * (par + 1)], bank(hb)), inc=(ss, 1))
                h1evac_tick[par] = b.tick(ss)
                b.wait("sync", ss, h1evac_tick[par])
                fwdma_tick[par] = sp_out(
                    "sync", par,
                    fw_d[128 * hc:128 * (hc + 1), 512 * tq:512 * (tq + 1)],
                    h1_bf[:, 512 * par:512 * (par + 1)])

            p2_pending[0] = p2tail
        w1s.consumed(w1slot, "st", last_mm)
        w3s.consumed(w3slot, "st", last_mm)
    p2_pending[0]()
    p2_pending[0] = None
    p2.close()
    xn0_cm.__exit__(None, None, None)

    fences()
    hw_cm = nc.sbuf_tensor("hw_sb", [128, NHC * 1024], BF16)
    hw_sb = hw_cm.__enter__()
    p3 = ExitStack()
    sbl = lambda n, s, d: p3.enter_context(nc.sbuf_tensor(n, s, d))
    yb_buf = sbl("yb_buf", [128, 4 * TOK], BF16)
    ybs0 = Stream(b, "ybs0", "sync", 4)
    ybs1 = Stream(b, "ybs1", "scalar", 4)
    b.wait("sync", sf[0], b.tick(sf[0]))
    b.wait("sync", sf[1], b.tick(sf[1]))
    b.wait("tensor", ss, b.tick(ss))
    b.wait("tensor", sv, b.tick(sv))
    b.wait("tensor", sid, CONST_LOAD)
    pairevac_tick = [0, 0, 0]
    for pp in range(64):
        par = pp % 4
        ysem, ytk, yslot = ybs0.load(
            yb_buf[0:64, par * TOK:(par + 1) * TOK], fw_d[2 * pp::128, :])
        ysem2, ytk2, yslot2 = ybs1.load(
            yb_buf[64:128, par * TOK:(par + 1) * TOK], fw_d[2 * pp + 1::128, :])
        r = pp % 3
        b.wait("tensor", ysem, ytk)
        b.wait("tensor", ysem2, ytk2)
        b.wait("tensor", ss, pairevac_tick[r])
        for j2 in range(2):
            for hq in range(2):
                b.op("tensor", lambda e, r=r, par=par, j2=j2, hq=hq: e.matmul(
                    pall[64 * j2:64 * (j2 + 1),
                         512 * (2 * r + hq):512 * (2 * r + hq + 1)],
                    h64_sb[64 * j2:64 * (j2 + 1), :],
                    yb_buf[64 * j2:64 * (j2 + 1),
                           par * TOK + 512 * hq:par * TOK + 512 * (hq + 1)],
                    start=True, stop=True),
                    inc=(st, 1) if (j2 == 1 and hq == 1) else None)
        p2_t = b.tick(st)
        b.wait("scalar", st, p2_t)
        b.op("scalar", lambda e, r=r, pp=pp: e.activation(
            hw_sb[:, pp * 1024:(pp + 1) * 1024],
            pall[:, 512 * 2 * r:512 * (2 * r + 2)], AF.Copy,
            scale=FW_SCALE), inc=(ss, 1))
        pairevac_tick[r] = b.tick(ss)
        ybs0.consumed(yslot, "st", p2_t)
        ybs1.consumed(yslot2, "st", p2_t)
    p3.close()
    fences()
    p4 = ExitStack()
    sbl = lambda n, s, d: p4.enter_context(nc.sbuf_tensor(n, s, d))
    w2buf = sbl("w2buf", [128, 2 * DH], BF16)
    g_stg = sbl("g_stg", [128, 2 * 1024], F32)
    w2s = Stream(b, "w2s", "gpsimd", 2)
    b.wait("sync", sf[0], b.tick(sf[0]))
    b.wait("sync", sf[1], b.tick(sf[1]))
    gsil_tick = [0, 0]
    gdma_tick = [0, 0]
    for dc in range(NCH):
        par = dc % 2
        w2sem, w2tk, w2slot = w2s.load(
            w2buf[:, par * DH:(par + 1) * DH].rearrange("p (c n) -> p c n", n=128),
            w2tp[:, 128 * dc:128 * (dc + 1)].rearrange("(c p) n -> p c n", p=128))
        b.wait("tensor", w2sem, w2tk)
        b.wait("tensor", ss, gsil_tick[par])
        for h in range(2):
            dbk = 2 * par + h
            for hcc in range(NHC):
                b.op("tensor", lambda e, dbk=dbk, hcc=hcc, par=par, h=h: e.matmul(
                    bank(dbk),
                    w2buf[:, par * DH + hcc * 128:par * DH + hcc * 128 + 128],
                    hw_sb[:, hcc * 1024 + 512 * h:hcc * 1024 + 512 * (h + 1)],
                    start=(hcc == 0), stop=(hcc == NHC - 1)),
                    inc=(st, 1) if hcc == NHC - 1 else None)
        down_t = b.tick(st)
        w2s.consumed(w2slot, "st", down_t)
        b.wait("scalar", st, down_t)
        b.wait("scalar", sf[par], gdma_tick[par])
        b.op("scalar", lambda e, par=par: e.activation(
            g_stg[:, par * 1024:(par + 1) * 1024],
            pall[:, 512 * 2 * par:512 * (2 * par + 2)], AF.Silu), inc=(ss, 1))
        gsil_tick[par] = b.tick(ss)
        b.wait("sync", ss, gsil_tick[par])
        gdma_tick[par] = sp_out("sync", par,
                                g_d[128 * dc:128 * (dc + 1), :],
                                g_stg[:, par * 1024:(par + 1) * 1024])
    p4.close()
    hw_cm.__exit__(None, None, None)

    fences()
    o_cm = nc.sbuf_tensor("o_sb", [128, NCH * TOK], BF16)
    o_sb = o_cm.__enter__()
    p5 = ExitStack()
    sbl = lambda n, s, d: p5.enter_context(nc.sbuf_tensor(n, s, d))
    q_hb = sbl("q_hb", [128, 2 * TOK], BF16)
    k_hb = sbl("k_hb", [128, 2 * SEQ], BF16)
    v_hb = sbl("v_hb", [128, 2 * SEQ], BF16)
    p_sb = sbl("p_sb", [128, 3 * SEQ], BF16)
    dg5 = sbl("dg5", [128, 3 * 128], BF16)
    pt_sb = sbl("pt_sb", [128, SEQ], BF16)
    qs5 = Stream(b, "qs5", "sync", 2)
    ks5 = Stream(b, "ks5", "sync", 2)
    vs5 = Stream(b, "vs5", "sync", 2)
    b.wait("sync", sf[0], b.tick(sf[0]))
    b.wait("sync", sf[1], b.tick(sf[1]))
    S5 = pall[:, 0:2048]
    SEQ_HLF5 = SEQ
    OT5 = lambda sl: pall[:, (4 + sl) * 512: (4 + sl) * 512 + 128]
    PTP5 = lambda j: ptp[:, 512 * (j % 2): 512 * (j % 2) + 128]
    pt_evac_tick = [0, 0]
    ot_evac_tick = [0, 0]
    exp_tick5 = [0, 0, 0]
    exp_half5 = [0, 0, 0]
    exp_full_prev = [0]
    recip_tick5 = [0, 0, 0]
    pn_tick5 = [0, 0, 0]
    trdone_tick5 = [0, 0, 0]
    tails5 = []
    head_res = {}
    units = [(h, qb) for h in range(16) for qb in range(8)]

    def flush_tail5():
        tail, meta = tails5.pop(0)
        tail()
        hh, qq = meta
        if qq == 7:
            qsl, ksl, vsl, s_lastt, av_lastt = head_res[hh]
            qs5.consumed(qsl, "st", s_lastt[0])
            ks5.consumed(ksl, "st", s_lastt[0])
            vs5.consumed(vsl, "st", av_lastt[0])

    for u, (h, qb) in enumerate(units):
        up = u % 3
        hp = h % 2
        if qb == 0:
            qsem, qtk, qslot = qs5.load(q_hb[:, hp * TOK:(hp + 1) * TOK],
                                        q0_d[128 * h:128 * (h + 1), :])
            ksem, ktk, kslot = ks5.load(k_hb[:, hp * SEQ:(hp + 1) * SEQ],
                                        k0_d[128 * h:128 * (h + 1), :])
            vsem, vtk, vslot = vs5.load(
                v_hb[:, hp * SEQ:(hp + 1) * SEQ].rearrange("p (c n) -> p c n", n=128),
                v0hm_d[SEQ * h:SEQ * (h + 1), :].rearrange("(c p) n -> p c n", p=128))
            s_lastt = [0]
            av_lastt = [0]
            head_res[h] = (qslot, kslot, vslot, s_lastt, av_lastt)
        lks = _l0_sched(qb)
        nblk = len(lks)
        b.wait("tensor", qsem, qtk)
        b.wait("tensor", ksem, ktk)
        if u == 0:
            b.wait("tensor", sid, CONST_LOAD)
        runs = []
        i0 = 0
        while i0 < nblk:
            i1 = i0
            while i1 + 1 < nblk and lks[i1 + 1] == lks[i1] + 1:
                i1 += 1
            runs.append((i0, i1))
            i0 = i1 + 1
        pieces = []
        for (i0, i1) in runs:
            base_lk = lks[i0]
            width = 128 * (i1 - i0 + 1)
            off = 128 * i0
            cur = 0
            while cur < width:
                w = min(512 - (off + cur) % 512, width - cur, 512)
                pieces.append((off + cur, 128 * base_lk + cur, w))
                cur += w
        waited2 = False
        b.wait("tensor", ss, exp_half5[(u - 1) % 3] if u >= 1 else 0)
        for (doff, soff, w) in pieces:
            if doff + w > 1024 and not waited2:
                b.wait("tensor", ss, exp_full_prev[0])
                waited2 = True
            b.op("tensor", lambda e, hp=hp, qb=qb, doff=doff, soff=soff,
                 w=w: e.matmul(
                S5[:, doff:doff + w],
                q_hb[:, hp * TOK + 128 * qb:hp * TOK + 128 * (qb + 1)],
                k_hb[:, hp * SEQ + soff:hp * SEQ + soff + w],
                start=True, stop=True), inc=(st, 1))
        s_mm = b.tick(st)
        s_lastt[0] = s_mm
        b.wait("vector", st, s_mm)
        if u == 0:
            b.wait("vector", sid, CONST_LOAD)
        b.op("vector", lambda e, qb=qb: e.tensor_tensor(
            S5[:, 128 * qb:128 * (qb + 9)], S5[:, 128 * qb:128 * (qb + 9)],
            DIAG_A8, ALU.add), inc=(sv, 1))
        b.guard("vector", sv)
        b.wait("vector", ss, exp_tick5[up])
        b.op("vector", lambda e, nblk=nblk, up=up: e.reduce_max(
            mneg[:, up:up + 1], S5[:, 0:128 * nblk], axis=AX.X, negate=True),
            inc=(sv, 1))
        mneg_v = b.tick(sv)
        b.wait("scalar", sv, mneg_v)
        b.wait("scalar", sv, recip_tick5[up])
        b.wait("scalar", st, trdone_tick5[up])
        w1_ = min(128 * nblk, 1024)
        w2_ = 128 * nblk - w1_
        b.op("scalar", lambda e, w1_=w1_, up=up: e.activation(
            p_sb[:, up * SEQ + 0:up * SEQ + w1_],
            S5[:, 0:w1_], AF.Exp, bias=mneg[:, up:up + 1],
            accum_out=l_col[:, 2 * up:2 * up + 1]), inc=(ss, 1))
        exp_half5[up] = b.tick(ss)
        if w2_ > 0:
            b.op("scalar", lambda e, w1_=w1_, w2_=w2_, up=up: e.activation(
                p_sb[:, up * SEQ + w1_:up * SEQ + w1_ + w2_],
                S5[:, w1_:w1_ + w2_], AF.Exp, bias=mneg[:, up:up + 1],
                accum_out=l_col[:, 2 * up + 1:2 * up + 2]), inc=(ss, 1))
        exp_tick5[up] = b.tick(ss)
        exp_full_prev[0] = b.tick(ss)
        b.wait("vector", ss, b.tick(ss))
        if w2_ > 0:
            b.op("vector", lambda e, up=up: e.tensor_add(
                l_col[:, 2 * up:2 * up + 1], l_col[:, 2 * up:2 * up + 1],
                l_col[:, 2 * up + 1:2 * up + 2]), inc=(sv, 1))
            b.guard("vector", sv)
        b.op("vector", lambda e, up=up: e.reciprocal(
            linv_col[:, up:up + 1], l_col[:, 2 * up:2 * up + 1]), inc=(sv, 1))
        recip_tick5[up] = b.tick(sv)
        b.guard("vector", sv)
        b.wait("vector", st, trdone_tick5[up])
        b.op("vector", lambda e, up=up: e.tensor_scalar_mul(
            dg5[:, up * 128:(up + 1) * 128], ident[:, :],
            linv_col[:, up:up + 1]), inc=(sv, 1))
        pn_tick5[up] = b.tick(sv)
        pn_v = b.tick(sv)
        exp_u5 = exp_tick5[up]

        def tail(h=h, qb=qb, hp=hp, lks=lks, nblk=nblk, up=up, pn_v=pn_v,
                 vsem=vsem, vtk=vtk, av_lastt=av_lastt, u=u, exp_u5=exp_u5):
            tp = u % 2
            b.wait("tensor", sv, pn_v)
            b.wait("tensor", ss, exp_u5)
            if qb == 0:
                b.wait("tensor", vsem, vtk)
            b.wait("tensor", ss, ot_evac_tick[tp])
            grp_tick = {}
            ngr = ceil_div(nblk, 4)

            def emit_av(g):
                gsem, gtk = grp_tick[g]
                b.wait("tensor", gsem, gtk)
                for p in range(4 * g, min(4 * g + 4, nblk)):
                    lk = lks[p]
                    b.op("tensor", lambda e, p=p, lk=lk, tp=tp, hp=hp,
                         nblk=nblk: e.matmul(
                        OT5(tp), v_hb[:, hp * SEQ + 128 * lk:hp * SEQ + 128 * (lk + 1)],
                        pt_sb[:, 128 * p:128 * (p + 1)],
                        start=(p == 0), stop=(p == nblk - 1)),
                        inc=(st, 1) if p == nblk - 1 else None)

            for g in range(ngr):
                blocks = list(range(4 * g, min(4 * g + 4, nblk)))
                bj = g % 2
                esem = ss if bj == 0 else sv
                eeng = "scalar" if bj == 0 else "vector"
                b.wait("tensor", esem, pt_evac_tick[bj])
                for ip in blocks:
                    b.op("tensor", lambda e, ip=ip, bj=bj, up=up: e.matmul(
                        ptp[:, 512 * bj + 128 * (ip % 4):
                            512 * bj + 128 * (ip % 4) + 128],
                        p_sb[:, up * SEQ + 128 * ip:up * SEQ + 128 * (ip + 1)],
                        dg5[:, up * 128:(up + 1) * 128],
                        start=True, stop=True),
                        inc=(st, 1) if ip == blocks[-1] else None)
                tr_t = b.tick(st)
                b.wait(eeng, st, tr_t)
                nb = len(blocks)
                if bj == 0:
                    b.op("scalar", lambda e, g=g, bj=bj, nb=nb: e.copy(
                        pt_sb[:, 512 * g:512 * g + 128 * nb],
                        ptp[:, 512 * bj:512 * bj + 128 * nb]), inc=(ss, 1))
                else:
                    b.guard("vector", sv)
                    b.op("vector", lambda e, g=g, bj=bj, nb=nb: e.tensor_scalar_mul(
                        pt_sb[:, 512 * g:512 * g + 128 * nb],
                        ptp[:, 512 * bj:512 * bj + 128 * nb], onescol[:, :]),
                        inc=(sv, 1))
                pt_evac_tick[bj] = b.tick(ss if bj == 0 else sv)
                grp_tick[g] = (esem, pt_evac_tick[bj])
                if g >= 1:
                    emit_av(g - 1)
            trdone_tick5[up] = b.tick(st)
            emit_av(ngr - 1)
            av_t = b.tick(st)
            av_lastt[0] = av_t
            b.wait("scalar", st, av_t)
            b.op("scalar", lambda e, h=h, qb=qb, tp=tp: e.copy(
                o_sb[:, h * TOK + 128 * qb:h * TOK + 128 * (qb + 1)],
                OT5(tp)), inc=(ss, 1))
            ot_evac_tick[tp] = b.tick(ss)

        tails5.append((tail, (h, qb)))
        if len(tails5) > 2:
            flush_tail5()
    while tails5:
        flush_tail5()
    p5.close()

    fences()
    p6 = ExitStack()
    sbl = lambda n, s, d: p6.enter_context(nc.sbuf_tensor(n, s, d))
    wobuf = sbl("wobuf", [128, 2 * DM], BF16)
    g_buf6 = sbl("g_buf6", [128, 2 * TOK], F32)
    x_buf6 = sbl("x_buf6", [128, 2 * TOK], F32)
    tmp6 = sbl("tmp6", [128, 512], F32)
    x1_stg = sbl("x1_stg", [128, 2 * TOK], F32)
    wos = Stream(b, "wos", "gpsimd", 2)
    gs6 = Stream(b, "gs6", "sync", 2)
    xs6 = Stream(b, "xs6", "sync", 2)
    b.wait("sync", sf[0], b.tick(sf[0]))
    b.wait("sync", sf[1], b.tick(sf[1]))
    OLOAD = 0
    sbank_tick = [0, 0]
    x1dma_tick = [0, 0]
    for oc in range(NCH):
        par = oc % 2
        wsem, wtk, wslot = wos.load(
            wobuf[:, par * DM:(par + 1) * DM].rearrange("p (c n) -> p c n", n=128),
            wo0[:, 128 * oc:128 * (oc + 1)].rearrange("(c p) n -> p c n", p=128))
        gsem, gtk, gslot = gs6.load(g_buf6[:, par * TOK:(par + 1) * TOK],
                                    g_d[128 * oc:128 * (oc + 1), :])
        xsem, xtk, xslot = xs6.load(x_buf6[:, par * TOK:(par + 1) * TOK],
                                    xT[128 * oc:128 * (oc + 1), 0:TOK])
        b.wait("tensor", wsem, wtk)
        res_v = 0
        for tq in range(2):
            sbk = 2 * par + tq % 2
            b.wait("tensor", sv, sbank_tick[par])
            for dc in range(NCH):
                b.op("tensor", lambda e, sbk=sbk, par=par, dc=dc, tq=tq: e.matmul(
                    bank(sbk), wobuf[:, par * DM + 128 * dc:par * DM + 128 * (dc + 1)],
                    o_sb[:, dc * TOK + 512 * tq:dc * TOK + 512 * (tq + 1)],
                    start=(dc == 0), stop=(dc == NCH - 1)),
                    inc=(st, 1) if dc == NCH - 1 else None)
            s_t = b.tick(st)
            b.wait("vector", st, s_t)
            b.wait("vector", gsem, gtk)
            b.wait("vector", xsem, xtk)
            if tq == 0:
                b.wait("vector", so[par], x1dma_tick[par])
            b.guard("vector", sv)
            b.op("vector", lambda e, sbk=sbk, par=par, tq=tq: e.tensor_mul(
                tmp6[:, :], bank(sbk),
                g_buf6[:, par * TOK + 512 * tq:par * TOK + 512 * (tq + 1)]),
                inc=(sv, 1))
            b.guard("vector", sv)
            b.op("vector", lambda e, par=par, tq=tq: e.tensor_add(
                x1_stg[:, par * TOK + 512 * tq:par * TOK + 512 * (tq + 1)],
                tmp6[:, :],
                x_buf6[:, par * TOK + 512 * tq:par * TOK + 512 * (tq + 1)]),
                inc=(sv, 1))
            res_v = b.tick(sv)
        sbank_tick[par] = res_v
        wos.consumed(wslot, "st", b.tick(st))
        gs6.consumed(gslot, "sv", res_v)
        xs6.consumed(xslot, "sv", res_v)
        b.wait("sync", sv, res_v)
        b.wait("sync", so[par], b.tick(so[par]))
        b.op("sync", lambda e, oc=oc, par=par: e.dma_start(
            out=x1_d[128 * oc:128 * (oc + 1), :],
            in_=x1_stg[:, par * TOK:(par + 1) * TOK]), inc=(so[par], 16))
        x1dma_tick[par] = b.tick(so[par])
    p6.close()
    o_cm.__exit__(None, None, None)

    fences()
    xn1_cm = nc.sbuf_tensor("xn1", [128, NCH * TOK], BF16)
    xn1 = xn1_cm.__enter__()
    p7 = ExitStack()
    sbl = lambda n, s, d: p7.enter_context(nc.sbuf_tensor(n, s, d))
    rows = {k: sbl("r7" + k, [1, TOK], F32) for k in ("m_row", "ex2", "m2", "var", "sqv", "rstd")}
    rows["tmp32"] = sbl("r7tmp32", [128, 1024], F32)
    x_buf7 = sbl("x_buf7", [128, 2 * TOK], F32)
    xbf_buf7 = sbl("xbf_buf7", [128, 2 * TOK], BF16)
    sq_buf7 = sbl("sq_buf7", [128, 2 * TOK], BF16)
    x_stream7 = Stream(b, "lx7", "sync", 2)
    x_stream7.buf = x_buf7
    xbf_stream7 = Stream(b, "lb7", "gpsimd", 2)
    xbf_stream7.buf = xbf_buf7
    b.wait("sync", so[0], b.tick(so[0]))
    b.wait("sync", so[1], b.tick(so[1]))
    b.wait("gpsimd", so[0], b.tick(so[0]))
    b.wait("gpsimd", so[1], b.tick(so[1]))
    emit_ln_stats_and_xn(
        b, bank, consts, x1_d[:, :], x_stream7, xbf_stream7,
        sq_buf7, rows, xn1, TOK, sx, sg, st, sv, ss,
        xn_cols=lambda dc: xn1[:, dc * TOK:(dc + 1) * TOK])
    XN1_V = b.tick(sv)
    p7.close()

    fences()
    y1_cm = nc.sbuf_tensor("y1sb", [128, NCH * TOK], BF16)
    y1_sb = y1_cm.__enter__()
    k1_cm = nc.sbuf_tensor("k1sb", [128, 4 * SEQ], BF16)
    k1_sb = k1_cm.__enter__()
    v1_cm = nc.sbuf_tensor("v1sb", [128, 4 * SEQ], BF16)
    v1_sb = v1_cm.__enter__()
    q1_cm = nc.sbuf_tensor("q1sb", [128, NCH * TOK], BF16)
    q1_sb = q1_cm.__enter__()
    p8 = ExitStack()
    sbl = lambda n, s, d: p8.enter_context(nc.sbuf_tensor(n, s, d))
    wv1_sb = sbl("wv1_sb", [128, NCH * 512], BF16)
    vz8 = sbl("vz8", [128, 2 * 1024], F32)
    for dc in range(NCH):
        b.dma("gpsimd", wv1_sb[:, dc * 512:(dc + 1) * 512],
              wv1[128 * dc:128 * (dc + 1), :], swv1)
    WV1_G = b.tick(swv1)
    b.wait("tensor", swv1, WV1_G)
    b.wait("tensor", sv, XN1_V)
    b.wait("tensor", sid, CONST_LOAD)
    b.wait("vector", sid, CONST_LOAD)
    vevac8 = [0, 0]
    vspill8 = [0, 0]
    for tc in range(TOK // 128):
        par = tc % 2
        b.wait("tensor", sv, vevac8[par])
        for dc in range(NCH):
            b.op("tensor", lambda e, par=par, tc=tc, dc=dc: e.matmul(
                bank(par), xn1[:, dc * TOK + 128 * tc:dc * TOK + 128 * (tc + 1)],
                wv1_sb[:, dc * 512:(dc + 1) * 512],
                start=(dc == 0), stop=(dc == NCH - 1)),
                inc=(st, 1) if dc == NCH - 1 else None)
        v_t = b.tick(st)
        b.wait("vector", st, v_t)
        b.wait("vector", sf[par], vspill8[par])
        b.op("vector", lambda e, par=par: e.tensor_scalar_mul(
            vz8[:, par * 1024:par * 1024 + 512], bank(par), msel_sb[:, 0:1]),
            inc=(sv, 1))
        b.op("vector", lambda e, par=par: e.tensor_scalar_mul(
            vz8[:, par * 1024 + 512:par * 1024 + 1024], bank(par),
            msel_sb[:, 1:2]), inc=(sv, 1))
        vevac8[par] = b.tick(sv)
        b.wait("sync", sv, vevac8[par])
        b.wait("sync", sf[par], b.tick(sf[par]))
        b.op("sync", lambda e, tc=tc, par=par: e.dma_start(
            out=v1pad_d[128 * tc:128 * (tc + 1), :],
            in_=vz8[:, par * 1024:par * 1024 + 512]), inc=(sf[par], 16))
        b.op("sync", lambda e, tc=tc, par=par: e.dma_start(
            out=v1pad_d[TOK + 128 * tc:TOK + 128 * (tc + 1), :],
            in_=vz8[:, par * 1024 + 512:par * 1024 + 1024]), inc=(sf[par], 16))
        vspill8[par] = b.tick(sf[par])
    b.wait("gpsimd", sf[0], b.tick(sf[0]))
    b.wait("gpsimd", sf[1], b.tick(sf[1]))
    b.op("gpsimd", lambda e: e.collective_compute(
        "AllReduce", ALU.add,
        replica_groups=[[0, 1], [2, 3], [4, 5], [6, 7]],
        ins=[v1pad_d], outs=[v1ag_d]), inc=(scc, 1))
    p8.close()
    p8 = ExitStack()
    sbl = lambda n, s, d: p8.enter_context(nc.sbuf_tensor(n, s, d))
    wq1_buf = sbl("wq1_buf", [128, 2 * DM], BF16)
    qsq8 = sbl("qsq8", [128, TOK], BF16)
    rstd_bc8 = sbl("rstd_bc8", [128, TOK], F32)
    qrow8 = {k: sbl("q8" + k, [1, TOK], F32) for k in ("sqv", "rstd")}
    fences()
    wq1s = Stream(b, "wq1s", "gpsimd", 2)
    qmul8 = 0
    for qh in range(16):
        par = qh % 2
        wsem, wtk, wslot = wq1s.load(
            wq1_buf[:, par * DM:(par + 1) * DM].rearrange("p (c n) -> p c n", n=128),
            wq1[:, 128 * qh:128 * (qh + 1)].rearrange("(c p) n -> p c n", p=128))
        b.wait("tensor", wsem, wtk)
        b.wait("tensor", sv, qmul8)
        b.wait("tensor", ss, b.tick(ss))
        for h in range(2):
            for dc in range(NCH):
                b.op("tensor", lambda e, h=h, par=par, dc=dc: e.matmul(
                    bank(h), wq1_buf[:, par * DM + 128 * dc:par * DM + 128 * (dc + 1)],
                    xn1[:, dc * TOK + 512 * h:dc * TOK + 512 * (h + 1)],
                    start=(dc == 0), stop=(dc == NCH - 1)),
                    inc=(st, 1) if dc == NCH - 1 else None)
        q_t = b.tick(st)
        wq1s.consumed(wslot, "st", q_t)
        b.wait("scalar", st, q_t)
        b.op("scalar", lambda e: e.activation(qsq8[:, :], pall[:, 0:1024], AF.Square),
             inc=(ss, 1))
        qsq_t = b.tick(ss)
        b.wait("tensor", ss, qsq_t)
        for h in range(2):
            b.op("tensor", lambda e, h=h: e.matmul(
                pall[0:1, 512 * (2 + h):512 * (3 + h)], ones_bf[:, :],
                qsq8[:, 512 * h:512 * (h + 1)], start=True, stop=True),
                inc=(st, 1) if h == 1 else None)
        qs_t = b.tick(st)
        b.wait("scalar", st, qs_t)
        b.op("scalar", lambda e: e.activation(
            qrow8["sqv"][:, :], pall[0:1, 1024:2048], AF.Sqrt, scale=1.0 / 128.0,
            bias=epst[0:1, 1:2]), inc=(ss, 1))
        b.wait("vector", ss, b.tick(ss))
        b.op("vector", lambda e: e.reciprocal(qrow8["rstd"][:, :], qrow8["sqv"][:, :]),
             inc=(sv, 1))
        qr_v = b.tick(sv)
        b.wait("tensor", sv, qr_v)
        for h in range(2):
            b.op("tensor", lambda e, h=h: e.matmul(
                pall[:, 512 * (4 + h):512 * (5 + h)], ones_f[:, :],
                qrow8["rstd"][:, 512 * h:512 * (h + 1)], start=True, stop=True),
                inc=(st, 1) if h == 1 else None)
        bc_t = b.tick(st)
        b.wait("scalar", st, bc_t)
        b.wait("scalar", sv, qmul8)
        b.op("scalar", lambda e: e.copy(rstd_bc8[:, :], pall[:, 2048:3072]),
             inc=(ss, 1))
        bce_t = b.tick(ss)
        b.wait("vector", ss, bce_t)
        b.op("vector", lambda e, qh=qh: e.tensor_mul(
            q1_sb[:, qh * TOK:(qh + 1) * TOK], pall[:, 0:1024], rstd_bc8[:, :]),
            inc=(sv, 1))
        qmul8 = b.tick(sv)
    p8.close()

    p8 = ExitStack()
    sbl = lambda n, s, d: p8.enter_context(nc.sbuf_tensor(n, s, d))
    v_f8 = sbl("v_f8d", [128, 2 * 512], F32)
    sqv_f8 = sbl("sqv_f8", [128, 128], F32)
    vsum8 = sbl("vsum8", [128, 8], F32)
    vsd8 = sbl("vsd8", [128, 8], F32)
    vrstd8 = sbl("vrstd8", [128, 8], F32)
    k_tmp8 = sbl("k_tmp8", [128, 2 * 512], BF16)
    fences()
    vag = Stream(b, "vag", "sync", 2)
    b.wait("sync", scc, 1)
    kmul8 = [0, 0]
    ktr8 = [0, 0]
    trevac8 = [0, 0]
    for tc in range(SEQ // 128):
        par = tc % 2
        vsem, vtk, vslot = vag.load(v_f8[:, 512 * par:512 * (par + 1)],
                                    v1ag_d[128 * tc:128 * (tc + 1), :])
        b.wait("scalar", vsem, vtk)
        b.wait("scalar", sv, kmul8[par])
        for kvh in range(4):
            b.op("scalar", lambda e, par=par, tc=tc, kvh=kvh: e.activation(
                v1_sb[:, kvh * SEQ + 128 * tc:kvh * SEQ + 128 * (tc + 1)],
                v_f8[:, 512 * par + 128 * kvh:512 * par + 128 * (kvh + 1)],
                AF.Copy), inc=(ss, 1))
        VE8 = b.tick(ss)
        for kvh in range(4):
            col = 512 * par + 128 * kvh
            b.op("scalar", lambda e, col=col: e.activation(
                sqv_f8[:, 0:128], v_f8[:, col:col + 128], AF.Square), inc=(ss, 1))
            sq_t = b.tick(ss)
            b.wait("vector", ss, sq_t)
            b.op("vector", lambda e, kvh=kvh: e.reduce_sum(
                vsum8[:, kvh:kvh + 1], sqv_f8[:, 0:128], axis=AX.X), inc=(sv, 1))
            sum_v = b.tick(sv)
            b.wait("scalar", sv, sum_v)
            b.op("scalar", lambda e, kvh=kvh: e.activation(
                vsd8[:, kvh:kvh + 1], vsum8[:, kvh:kvh + 1], AF.Sqrt,
                scale=1.0 / 128.0, bias=epst[:, 1:2]), inc=(ss, 1))
            sd_t = b.tick(ss)
            b.wait("vector", ss, sd_t)
            b.op("vector", lambda e, kvh=kvh: e.reciprocal(
                vrstd8[:, kvh:kvh + 1], vsd8[:, kvh:kvh + 1]), inc=(sv, 1))
            b.guard("vector", sv)
            if kvh == 0:
                b.wait("vector", st, ktr8[par])
            b.op("vector", lambda e, col=col, kvh=kvh: e.tensor_scalar_mul(
                k_tmp8[:, col:col + 128], v_f8[:, col:col + 128],
                vrstd8[:, kvh:kvh + 1]), inc=(sv, 1))
        krow_v = b.tick(sv)
        kmul8[par] = krow_v
        vag.consumed(vslot, "sv", krow_v)
        b.wait("tensor", sv, krow_v)
        for kvh in range(4):
            j = kvh % 2
            b.wait("tensor", ss, trevac8[j])
            b.op("tensor", lambda e, par=par, kvh=kvh, j=j: e.matmul(
                ptp[:, 512 * j:512 * j + 128],
                k_tmp8[:, 512 * par + 128 * kvh:512 * par + 128 * (kvh + 1)],
                ident[:, :], start=True, stop=True), inc=(st, 1))
            tr_t = b.tick(st)
            b.wait("scalar", st, tr_t)
            b.op("scalar", lambda e, kvh=kvh, tc=tc, j=j: e.copy(
                k1_sb[:, kvh * SEQ + 128 * tc:kvh * SEQ + 128 * (tc + 1)],
                ptp[:, 512 * j:512 * j + 128]), inc=(ss, 1))
            trevac8[j] = b.tick(ss)
        ktr8[par] = b.tick(st)
    p8.close()

    fences()
    p9 = ExitStack()
    sbl = lambda n, s, d: p9.enter_context(nc.sbuf_tensor(n, s, d))
    p_sb9 = sbl("p_sb9", [128, 3 * SEQ], BF16)
    dg9 = sbl("dg9", [128, 3 * 128], BF16)
    pt_sb9 = sbl("pt_sb9", [128, SEQ], BF16)
    pt_evac9 = [0, 0]
    ot_evac9 = [0, 0]
    exp_tick9 = [0, 0, 0]
    exp_half9 = [0, 0, 0]
    exp_full9 = [0]
    recip_tick9 = [0, 0, 0]
    pn_tick9 = [0, 0, 0]
    trdone_tick9 = [0, 0, 0]
    tails9 = []
    b.wait("tensor", sv, b.tick(sv))
    b.wait("tensor", ss, b.tick(ss))
    units9 = [(qh, qc) for qh in range(16) for qc in range(TOK // 128)]
    for u, (qh, qc) in enumerate(units9):
        up = u % 3
        kvh = qh // 4
        kbase = kvh * SEQ
        b.wait("tensor", ss, exp_half9[(u - 1) % 3] if u >= 1 else 0)
        for kg in range(4):
            if kg == 2:
                b.wait("tensor", ss, exp_full9[0])
            b.op("tensor", lambda e, qh=qh, qc=qc, kg=kg, kbase=kbase: e.matmul(
                pall[:, 512 * kg:512 * (kg + 1)],
                q1_sb[:, qh * TOK + 128 * qc:qh * TOK + 128 * (qc + 1)],
                k1_sb[:, kbase + 512 * kg:kbase + 512 * (kg + 1)],
                start=True, stop=True), inc=(st, 1))
        s_mm = b.tick(st)
        b.wait("vector", st, s_mm)
        b.wait("vector", ss, exp_tick9[up])
        b.op("vector", lambda e, up=up: e.reduce_max(
            mneg[:, up:up + 1], pall[:, 0:2048], axis=AX.X, negate=True),
            inc=(sv, 1))
        mneg_v = b.tick(sv)
        b.wait("scalar", sv, mneg_v)
        b.wait("scalar", sv, recip_tick9[up])
        b.wait("scalar", st, trdone_tick9[up])
        b.op("scalar", lambda e, up=up: e.activation(
            p_sb9[:, up * SEQ:up * SEQ + 1024], pall[:, 0:1024], AF.Exp,
            bias=mneg[:, up:up + 1], accum_out=l_col[:, 2 * up:2 * up + 1]),
            inc=(ss, 1))
        exp_half9[up] = b.tick(ss)
        b.op("scalar", lambda e, up=up: e.activation(
            p_sb9[:, up * SEQ + 1024:(up + 1) * SEQ], pall[:, 1024:2048],
            AF.Exp, bias=mneg[:, up:up + 1],
            accum_out=l_col[:, 2 * up + 1:2 * up + 2]), inc=(ss, 1))
        exp_tick9[up] = b.tick(ss)
        exp_full9[0] = b.tick(ss)
        b.wait("vector", ss, b.tick(ss))
        b.op("vector", lambda e, up=up: e.tensor_add(
            l_col[:, 2 * up:2 * up + 1], l_col[:, 2 * up:2 * up + 1],
            l_col[:, 2 * up + 1:2 * up + 2]), inc=(sv, 1))
        b.guard("vector", sv)
        b.op("vector", lambda e, up=up: e.reciprocal(
            linv_col[:, up:up + 1], l_col[:, 2 * up:2 * up + 1]), inc=(sv, 1))
        recip_tick9[up] = b.tick(sv)
        b.guard("vector", sv)
        b.wait("vector", st, trdone_tick9[up])
        b.op("vector", lambda e, up=up: e.tensor_scalar_mul(
            dg9[:, up * 128:(up + 1) * 128], ident[:, :],
            linv_col[:, up:up + 1]), inc=(sv, 1))
        pn_tick9[up] = b.tick(sv)
        pn_v = b.tick(sv)
        exp_u9 = exp_tick9[up]

        def tail9(qh=qh, qc=qc, kbase=kbase, up=up, pn_v=pn_v, u=u,
                  exp_u9=exp_u9):
            tp = u % 2
            b.wait("tensor", sv, pn_v)
            b.wait("tensor", ss, exp_u9)
            b.wait("tensor", ss, ot_evac9[tp])
            grp_tick = {}

            def emit_av(g):
                gsem, gtk = grp_tick[g]
                b.wait("tensor", gsem, gtk)
                for p in range(4 * g, 4 * g + 4):
                    b.op("tensor", lambda e, p=p, tp=tp, kbase=kbase: e.matmul(
                        pall[:, (4 + tp) * 512:(4 + tp) * 512 + 128],
                        v1_sb[:, kbase + 128 * p:kbase + 128 * (p + 1)],
                        pt_sb9[:, 128 * p:128 * (p + 1)],
                        start=(p == 0), stop=(p == 15)),
                        inc=(st, 1) if p == 15 else None)

            for g in range(4):
                bj = g % 2
                esem = ss if bj == 0 else sv
                eeng = "scalar" if bj == 0 else "vector"
                b.wait("tensor", esem, pt_evac9[bj])
                for ip in range(4 * g, 4 * g + 4):
                    b.op("tensor", lambda e, ip=ip, bj=bj, up=up: e.matmul(
                        ptp[:, 512 * bj + 128 * (ip % 4):
                            512 * bj + 128 * (ip % 4) + 128],
                        p_sb9[:, up * SEQ + 128 * ip:up * SEQ + 128 * (ip + 1)],
                        dg9[:, up * 128:(up + 1) * 128],
                        start=True, stop=True),
                        inc=(st, 1) if ip % 4 == 3 else None)
                tr_t = b.tick(st)
                b.wait(eeng, st, tr_t)
                if bj == 0:
                    b.op("scalar", lambda e, g=g, bj=bj: e.copy(
                        pt_sb9[:, 512 * g:512 * (g + 1)],
                        ptp[:, 512 * bj:512 * bj + 512]), inc=(ss, 1))
                else:
                    b.guard("vector", sv)
                    b.op("vector", lambda e, g=g, bj=bj: e.tensor_scalar_mul(
                        pt_sb9[:, 512 * g:512 * (g + 1)],
                        ptp[:, 512 * bj:512 * bj + 512], onescol[:, :]),
                        inc=(sv, 1))
                pt_evac9[bj] = b.tick(ss if bj == 0 else sv)
                grp_tick[g] = (esem, pt_evac9[bj])
                if g >= 1:
                    emit_av(g - 1)
            trdone_tick9[up] = b.tick(st)
            emit_av(3)
            av_t = b.tick(st)
            b.wait("scalar", st, av_t)
            b.op("scalar", lambda e, qh=qh, qc=qc, tp=tp: e.copy(
                y1_sb[:, qh * TOK + 128 * qc:qh * TOK + 128 * (qc + 1)],
                pall[:, (4 + tp) * 512:(4 + tp) * 512 + 128]), inc=(ss, 1))
            ot_evac9[tp] = b.tick(ss)

        tails9.append(tail9)
        if len(tails9) > 2:
            tails9.pop(0)()
    while tails9:
        tails9.pop(0)()
    p9.close()
    q1_cm.__exit__(None, None, None)
    v1_cm.__exit__(None, None, None)
    k1_cm.__exit__(None, None, None)

    fences()
    p10 = ExitStack()
    sbl = lambda n, s, d: p10.enter_context(nc.sbuf_tensor(n, s, d))
    s_sb10 = sbl("s_sb10", [128, NCH * TOK], F32)
    wobuf10 = sbl("wobuf10", [128, 2 * DM], BF16)
    sq_bf10 = sbl("sq_bf10", [128, TOK], BF16)
    s1n_stg = sbl("s1n_stg", [128, 2 * TOK], F32)
    rstd_bc10 = sbl("rstd_bc10", [128, TOK], F32)
    srow10 = {k: sbl("s10" + k, [1, TOK], F32) for k in ("sqv", "rstd")}
    wos10 = Stream(b, "wos10", "gpsimd", 2)
    sevac10 = [0, 0]
    for oc in range(NCH):
        par = oc % 2
        wsem, wtk, wslot = wos10.load(
            wobuf10[:, par * DM:(par + 1) * DM].rearrange("p (c n) -> p c n", n=128),
            wo1[:, 128 * oc:128 * (oc + 1)].rearrange("(c p) n -> p c n", p=128))
        sb0 = 2 * par
        b.wait("tensor", wsem, wtk)
        b.wait("tensor", ss, sevac10[par])
        for h in range(2):
            for dc in range(NCH):
                b.op("tensor", lambda e, sbk=sb0 + h, par=par, dc=dc, h=h: e.matmul(
                    pall[:, 512 * sbk:512 * (sbk + 1)],
                    wobuf10[:, par * DM + 128 * dc:par * DM + 128 * (dc + 1)],
                    y1_sb[:, dc * TOK + 512 * h:dc * TOK + 512 * (h + 1)],
                    start=(dc == 0), stop=(dc == NCH - 1)),
                    inc=(st, 1) if dc == NCH - 1 else None)
        s_t = b.tick(st)
        wos10.consumed(wslot, "st", s_t)
        b.wait("scalar", st, s_t)
        for h in range(2):
            b.op("scalar", lambda e, sbk=sb0 + h, oc=oc, h=h: e.copy(
                s_sb10[:, oc * TOK + 512 * h:oc * TOK + 512 * (h + 1)],
                pall[:, 512 * sbk:512 * (sbk + 1)]), inc=(ss, 1))
            b.op("scalar", lambda e, sbk=sb0 + h, h=h: e.activation(
                sq_bf10[:, 512 * h:512 * (h + 1)],
                pall[:, 512 * sbk:512 * (sbk + 1)], AF.Square), inc=(ss, 1))
        sevac10[par] = b.tick(ss)
        b.wait("tensor", ss, sevac10[par])
        b.op("tensor", lambda e, oc=oc: e.matmul(
            pall[0:1, 4 * 512:4 * 512 + 512], ones_bf[:, :], sq_bf10[:, 0:512],
            start=(oc == 0), stop=(oc == NCH - 1)))
        b.op("tensor", lambda e, oc=oc: e.matmul(
            pall[0:1, 5 * 512:5 * 512 + 512], ones_bf[:, :], sq_bf10[:, 512:1024],
            start=(oc == 0), stop=(oc == NCH - 1)),
            inc=(st, 1) if oc == NCH - 1 else None)
    STATS10 = b.tick(st)
    b.wait("scalar", st, STATS10)
    b.op("scalar", lambda e: e.activation(
        srow10["sqv"][:, 0:512], pall[0:1, 4 * 512:4 * 512 + 512], AF.Sqrt,
        scale=1.0 / DM, bias=epst[0:1, 1:2]), inc=(ss, 1))
    b.op("scalar", lambda e: e.activation(
        srow10["sqv"][:, 512:1024], pall[0:1, 5 * 512:5 * 512 + 512], AF.Sqrt,
        scale=1.0 / DM, bias=epst[0:1, 1:2]), inc=(ss, 1))
    b.wait("vector", ss, b.tick(ss))
    b.op("vector", lambda e: e.reciprocal(srow10["rstd"][:, :], srow10["sqv"][:, :]),
         inc=(sv, 1))
    RST10 = b.tick(sv)
    b.wait("tensor", sv, RST10)
    b.wait("tensor", ss, b.tick(ss))
    for h in range(2):
        b.op("tensor", lambda e, h=h: e.matmul(
            pall[:, 512 * h:512 * (h + 1)], ones_f[:, :],
            srow10["rstd"][:, 512 * h:512 * (h + 1)], start=True, stop=True),
            inc=(st, 1) if h == 1 else None)
    BC10 = b.tick(st)
    b.wait("scalar", st, BC10)
    b.op("scalar", lambda e: e.copy(rstd_bc10[:, :], pall[:, 0:1024]), inc=(ss, 1))
    BCE10 = b.tick(ss)
    b.wait("vector", ss, BCE10)
    s1dma10 = [0, 0]
    for oc in range(NCH):
        par = oc % 2
        b.wait("vector", sf[par], s1dma10[par])
        b.op("vector", lambda e, oc=oc, par=par: e.tensor_mul(
            s1n_stg[:, par * TOK:(par + 1) * TOK],
            s_sb10[:, oc * TOK:(oc + 1) * TOK], rstd_bc10[:, :]), inc=(sv, 1))
        mul_v = b.tick(sv)
        b.wait("sync", sv, mul_v)
        s1dma10[par] = sp_out("sync", par, s1n_d[128 * oc:128 * (oc + 1), :],
                              s1n_stg[:, par * TOK:(par + 1) * TOK])
    p10.close()
    y1_cm.__exit__(None, None, None)

    fences()
    p11 = ExitStack()
    sbl = lambda n, s, d: p11.enter_context(nc.sbuf_tensor(n, s, d))
    wg11 = sbl("wg11", [128, 2 * DM], BF16)
    wu11 = sbl("wu11", [128, 2 * DM], BF16)
    gate11 = sbl("gate11", [128, 2 * 512], F32)
    hd11 = sbl("hd11", [128, 2 * 512], BF16)
    wgs11 = Stream(b, "wgs11", "gpsimd", 2)
    wus11 = Stream(b, "wus11", "gpsimd", 2)
    b.wait("sync", sf[0], b.tick(sf[0]))
    b.wait("sync", sf[1], b.tick(sf[1]))
    silu11 = [0, 0]
    hmul11 = [0, 0]
    hddma11 = [0, 0]
    p11_pending = [None]
    for hc in range(NHC):
        wpar = hc % 2
        wgsem, wgtk, wgslot = wgs11.load(
            wg11[:, wpar * DM:(wpar + 1) * DM].rearrange("p (c n) -> p c n", n=128),
            wgt[:, 128 * hc:128 * (hc + 1)].rearrange("(c p) n -> p c n", p=128))
        wusem, wutk, wuslot = wus11.load(
            wu11[:, wpar * DM:(wpar + 1) * DM].rearrange("p (c n) -> p c n", n=128),
            wut[:, 128 * hc:128 * (hc + 1)].rearrange("(c p) n -> p c n", p=128))
        b.wait("tensor", wgsem, wgtk)
        b.wait("tensor", wusem, wutk)
        last_mm = 0
        for tq in range(2):
            par = tq % 2
            gb, ub = 3 * par, 3 * par + 1
            b.wait("tensor", ss, silu11[par])
            b.wait("tensor", sv, hmul11[par])
            for dc in range(NCH):
                b.op("tensor", lambda e, gb=gb, wpar=wpar, dc=dc, tq=tq: e.matmul(
                    bank(gb), wg11[:, wpar * DM + 128 * dc:wpar * DM + 128 * (dc + 1)],
                    xn1[:, dc * TOK + 512 * tq:dc * TOK + 512 * (tq + 1)],
                    start=(dc == 0), stop=(dc == NCH - 1)))
            for dc in range(NCH):
                b.op("tensor", lambda e, ub=ub, wpar=wpar, dc=dc, tq=tq: e.matmul(
                    bank(ub), wu11[:, wpar * DM + 128 * dc:wpar * DM + 128 * (dc + 1)],
                    xn1[:, dc * TOK + 512 * tq:dc * TOK + 512 * (tq + 1)],
                    start=(dc == 0), stop=(dc == NCH - 1)),
                    inc=(st, 1) if dc == NCH - 1 else None)
            up_t = b.tick(st)
            last_mm = up_t
            b.wait("scalar", st, up_t)
            b.op("scalar", lambda e, gb=gb, par=par: e.activation(
                gate11[:, 512 * par:512 * (par + 1)], bank(gb), AF.Silu),
                inc=(ss, 1))
            silu11[par] = b.tick(ss)
            b.wait("vector", ss, silu11[par])
            b.wait("vector", sf[par], hddma11[par])
            b.op("vector", lambda e, ub=ub, par=par: e.tensor_mul(
                hd11[:, 512 * par:512 * (par + 1)], bank(ub),
                gate11[:, 512 * par:512 * (par + 1)]), inc=(sv, 1))
            hmul11[par] = b.tick(sv)
            b.wait("sync", sv, hmul11[par])
            hddma11[par] = sp_out("sync", par,
                                  hd_d[128 * hc:128 * (hc + 1), 512 * tq:512 * (tq + 1)],
                                  hd11[:, 512 * par:512 * (par + 1)])
        wgs11.consumed(wgslot, "st", last_mm)
        wus11.consumed(wuslot, "st", last_mm)
    p11.close()
    xn1_cm.__exit__(None, None, None)

    fences()
    p11 = ExitStack()
    sbl = lambda n, s, d: p11.enter_context(nc.sbuf_tensor(n, s, d))
    hw11 = sbl("hw11", [128, NHC * 1024], BF16)
    wd11 = sbl("wd11", [128, 2 * DH], BF16)
    g11 = sbl("g11", [128, 2 * 512], F32)
    s11 = sbl("s11", [128, 4 * 512], F32)
    x11 = sbl("x11", [128, 4 * 512], F32)
    t11 = sbl("t11", [128, 2 * 512], F32)
    yb11 = sbl("yb11", [128, 2 * 512], F32)
    wds11 = Stream(b, "wds11", "gpsimd", 2)
    sin11 = Stream(b, "sin11", "sync", 2)
    xin11 = Stream(b, "xin11", "sync", 2)
    b.wait("sync", sf[0], b.tick(sf[0]))
    b.wait("sync", sf[1], b.tick(sf[1]))
    for hcc in range(NHC):
        b.dma("sync", hw11[:, hcc * 1024:(hcc + 1) * 1024],
              hd_d[128 * hcc:128 * (hcc + 1), :], shw[0])
    HDL = b.tick(shw[0])
    b.wait("tensor", shw[0], HDL)
    gsil11 = [0, 0]
    res11 = [0, 0]
    ydma11 = [0, 0]
    for dc in range(NCH):
        par = dc % 2
        wdsem, wdtk, wdslot = wds11.load(
            wd11[:, par * DH:(par + 1) * DH].rearrange("p (c n) -> p c n", n=128),
            wdt[:, 128 * dc:128 * (dc + 1)].rearrange("(c p) n -> p c n", p=128))
        rA = 2 * par
        rB = 2 * par + 1
        ssem, stk, sslot = sin11.load(s11[:, rA * 512:(rA + 1) * 512],
                                      s1n_d[128 * dc:128 * (dc + 1), 0:512])
        xsem, xtk, xslot = xin11.load(x11[:, rA * 512:(rA + 1) * 512],
                                      x1_d[128 * dc:128 * (dc + 1), 0:512])
        b.wait("tensor", wdsem, wdtk)
        b.wait("tensor", ss, gsil11[par])
        for tb in range(2):
            dbk = 2 * par + tb
            for hcc in range(NHC):
                b.op("tensor", lambda e, dbk=dbk, hcc=hcc, par=par, tb=tb: e.matmul(
                    bank(dbk), wd11[:, par * DH + hcc * 128:par * DH + hcc * 128 + 128],
                    hw11[:, hcc * 1024 + 512 * tb:hcc * 1024 + 512 * (tb + 1)],
                    start=(hcc == 0), stop=(hcc == NHC - 1)),
                    inc=(st, 1) if hcc == NHC - 1 else None)
        down_t = b.tick(st)
        wds11.consumed(wdslot, "st", down_t)
        b.wait("scalar", st, down_t)
        b.wait("scalar", sv, res11[par])
        b.op("scalar", lambda e, par=par: e.activation(
            g11[:, 512 * par:512 * (par + 1)],
            pall[:, 512 * 2 * par:512 * (2 * par + 1)], AF.Silu), inc=(ss, 1))
        gsil_half = b.tick(ss)
        b.wait("vector", ss, gsil_half)
        b.wait("vector", ssem, stk)
        b.wait("vector", xsem, xtk)
        b.wait("vector", so[par], ydma11[par])
        b.guard("vector", sv)
        b.op("vector", lambda e, par=par, rA=rA: e.tensor_mul(
            t11[:, 512 * par:512 * (par + 1)], g11[:, 512 * par:512 * (par + 1)],
            s11[:, rA * 512:(rA + 1) * 512]), inc=(sv, 1))
        b.guard("vector", sv)
        b.op("vector", lambda e, par=par, rA=rA: e.tensor_add(
            yb11[:, 512 * par:512 * (par + 1)], t11[:, 512 * par:512 * (par + 1)],
            x11[:, rA * 512:(rA + 1) * 512]), inc=(sv, 1))
        res_v = b.tick(sv)
        b.wait("sync", sv, res_v)
        b.wait("sync", so[par], b.tick(so[par]))
        b.op("sync", lambda e, dc=dc, par=par: e.dma_start(
            out=y_o[128 * dc:128 * (dc + 1), 0:512],
            in_=yb11[:, 512 * par:512 * (par + 1)]), inc=(so[par], 16))
        ydma11[par] = b.tick(so[par])
        ssemB, stkB, sslotB = sin11.load(s11[:, rB * 512:(rB + 1) * 512],
                                         s1n_d[128 * dc:128 * (dc + 1), 512:1024])
        xsemB, xtkB, xslotB = xin11.load(x11[:, rB * 512:(rB + 1) * 512],
                                         x1_d[128 * dc:128 * (dc + 1), 512:1024])
        b.wait("scalar", sv, res_v)
        b.op("scalar", lambda e, par=par: e.activation(
            g11[:, 512 * par:512 * (par + 1)],
            pall[:, 512 * (2 * par + 1):512 * (2 * par + 2)], AF.Silu), inc=(ss, 1))
        gsil11[par] = b.tick(ss)
        b.wait("vector", ss, gsil11[par])
        b.wait("vector", ssemB, stkB)
        b.wait("vector", xsemB, xtkB)
        b.wait("vector", so[par], ydma11[par])
        b.guard("vector", sv)
        b.op("vector", lambda e, par=par, rB=rB: e.tensor_mul(
            t11[:, 512 * par:512 * (par + 1)], g11[:, 512 * par:512 * (par + 1)],
            s11[:, rB * 512:(rB + 1) * 512]), inc=(sv, 1))
        b.guard("vector", sv)
        b.op("vector", lambda e, par=par, rB=rB: e.tensor_add(
            yb11[:, 512 * par:512 * (par + 1)], t11[:, 512 * par:512 * (par + 1)],
            x11[:, rB * 512:(rB + 1) * 512]), inc=(sv, 1))
        res_vB = b.tick(sv)
        res11[par] = res_vB
        sin11.consumed(sslot, "sv", res_v)
        sin11.consumed(sslotB, "sv", res_vB)
        xin11.consumed(xslot, "sv", res_v)
        xin11.consumed(xslotB, "sv", res_vB)
        b.wait("sync", sv, res_vB)
        b.wait("sync", so[par], b.tick(so[par]))
        b.op("sync", lambda e, dc=dc, par=par: e.dma_start(
            out=y_o[128 * dc:128 * (dc + 1), 512:1024],
            in_=yb11[:, 512 * par:512 * (par + 1)]), inc=(so[par], 16))
        ydma11[par] = b.tick(so[par])
    p11.close()

    p11.close()

    b.wait("sync", sx, b.tick(sx))
    b.wait("sync", so[0], b.tick(so[0]))
    b.wait("sync", so[1], b.tick(so[1]))
    b.wait("sync", sf[0], b.tick(sf[0]))
    b.wait("sync", sf[1], b.tick(sf[1]))
    b.wait("gpsimd", sg, b.tick(sg))
    b.realize(ctx)
    nc._ctx_keepalive = ctx
    return nc


def host_fused_masks():
    diag = np.where(np.arange(128)[:, None] >= np.arange(128)[None, :], 0.0, NEG)
    zero = np.zeros((128, 128), np.float32)
    neg = np.full((128, 128), NEG, np.float32)
    out = {}
    for par in (0, 1):
        a = neg if par == 0 else zero
        bb = zero if par == 0 else neg
        out[par] = np.concatenate([diag] + [a] * 8 + [bb] * 8, axis=1).astype(BF)
    return out


def kernel_fused(**inputs):
    x = np.asarray(inputs["L0_attn_x"], np.float32)
    masks = host_fused_masks()
    h128, h64, perm = host_fwht_consts()
    ident = np.eye(128, dtype=np.float32).astype(BF)
    f32c = lambda a: np.ascontiguousarray(np.asarray(a, np.float32))
    in_maps = []
    common = {
        "wq0": f32c(inputs["L0_attn_Wq"]), "wk0": f32c(inputs["L0_attn_Wk"]),
        "wv0": f32c(inputs["L0_attn_Wv"]), "wo0": f32c(inputs["L0_attn_Wo"]),
        "w1t": f32c(inputs["L0_ffn_W1"].T), "w3t": f32c(inputs["L0_ffn_W3"].T),
        "w2tp": f32c(np.asarray(inputs["L0_ffn_W2"], np.float32).T[perm]),
        "wq1": f32c(inputs["L1_attn_Wq"]), "wv1": f32c(inputs["L1_attn_Wv"]),
        "wo1": f32c(inputs["L1_attn_Wo"]),
        "wgt": f32c(inputs["L1_ffn_w_gate"].T), "wut": f32c(inputs["L1_ffn_w_up"].T),
        "wdt": f32c(inputs["L1_ffn_w_down"].T),
        "ident_i": ident, "h128_i": np.asarray(h128),
        "h64_i": np.concatenate([np.asarray(h64), np.asarray(h64)], axis=0),
    }
    for c in range(8):
        seq, half = c // 2, c % 2
        own = x[seq, half * TOK:(half + 1) * TOK, :]
        par = x[seq, (1 - half) * TOK:(2 - half) * TOK, :]
        xf = np.ascontiguousarray(np.concatenate([own, par], axis=0).T)
        m = dict(common)
        m["xT"] = xf
        m["mask_i"] = masks[half]
        m["msel_i"] = np.tile(np.array([[1.0 - half, float(half)]], np.float32),
                              (128, 1))
        in_maps.append(m)
    res = _run(build_fused(), in_maps)
    out = np.empty((BSZ, SEQ, DM), np.float32)
    for c in range(8):
        seq, half = c // 2, c % 2
        out[seq, half * TOK:(half + 1) * TOK, :] = np.asarray(
            res.results[c]["y_o"], np.float32).T
    return out


# version 17
# speedup vs baseline: 1.0092x; 1.0092x over previous
import numpy as np
import ml_dtypes
from contextlib import ExitStack

import concourse.bass as bass
import concourse.mybir as mybir
from concourse.bass_utils import run_bass_kernel_spmd

F32 = mybir.dt.float32
BF16 = mybir.dt.bfloat16
AF = mybir.ActivationFunctionType
ALU = mybir.AluOpType
AX = mybir.AxisListType

BSZ, SEQ, DM, DH = 4, 2048, 2048, 8192
TOK = 1024
NCH = DM // 128
NEG = -1.0e30
LN_EPS = 1e-5
RMS_EPS = 1.1920929e-07
BF = ml_dtypes.bfloat16
RS128 = float(1.0 / np.sqrt(128.0))

ENGINES = ("sync", "tensor", "vector", "scalar", "gpsimd")


class Bld:

    def __init__(self, nc):
        self.nc = nc
        self.prog = {e: [] for e in ENGINES}
        self.cnt = {}

    def sem(self, name):
        assert name not in self.cnt
        self.cnt[name] = 0
        return name

    def wait(self, eng, sem, val):
        if val > 0:
            self.prog[eng].append(("w", sem, val))

    def op(self, eng, fn, inc=None):
        incs = []
        if inc is not None:
            items = [inc] if isinstance(inc, tuple) else list(inc)
            for s, a in items:
                self.cnt[s] += a
                incs.append((s, a))
        self.prog[eng].append(("o", fn, incs))

    def dma(self, eng, out, in_, sem):
        self.op(eng, lambda e, o=out, i=in_: e.dma_start(out=o, in_=i), inc=(sem, 16))
        return self.cnt[sem]

    def tick(self, sem):
        return self.cnt[sem]

    def guard(self, eng, sem):
        self.wait(eng, sem, self.cnt[sem])

    def realize(self, ctx):
        sems = {n: ctx.enter_context(self.nc.semaphore(n)) for n in self.cnt}
        block = ctx.enter_context(self.nc.Block())
        for name in ENGINES:
            items = self.prog[name]

            def body(eng_obj, items=items):
                for it in items:
                    if it[0] == "w":
                        eng_obj.wait_ge(sems[it[1]], it[2])
                    else:
                        inst = it[1](eng_obj)
                        for (s, a) in it[2]:
                            inst.then_inc(sems[s], a)

            getattr(block, name)(body)


class Stream:

    def __init__(self, b, name, queue, nslots=2):
        self.b = b
        self.queue = queue
        self.n = nslots
        self.sems = [b.sem(f"{name}{k}") for k in range(nslots)]
        self.gate = [None] * nslots
        self.i = 0

    def load(self, out_ap, in_ap):
        s = self.i % self.n
        if self.gate[s] is not None:
            self.b.wait(self.queue, self.gate[s][0], self.gate[s][1])
            self.gate[s] = None
        t = self.b.dma(self.queue, out_ap, in_ap, self.sems[s])
        self.i += 1
        return self.sems[s], t, s

    def consumed(self, slot, sem, tick):
        self.gate[slot] = (sem, tick)


class OutStream:

    def __init__(self, b, name, queue, nslots=2):
        self.b = b
        self.queue = queue
        self.n = nslots
        self.sems = [b.sem(f"{name}{k}") for k in range(nslots)]
        self.i = 0

    def out(self, slot, dst, src):
        sem = self.sems[slot]
        self.b.wait(self.queue, sem, self.b.tick(sem))
        t = self.b.dma(self.queue, dst, src, sem)
        return sem, t

    def drain(self):
        for sem in self.sems:
            self.b.wait(self.queue, sem, self.b.tick(sem))


def ceil_div(a, b):
    return (a + b - 1) // b



def emit_ln_stats_and_xn(b, bank, consts, xsrc_dram, x_stream, xbf_stream, sqbuf,
                         rows, xn, tok, sx, sg, st, sv, ss, xn_out_dram=None,
                         x_keep=None, xn_cols=None):
    nh = tok // 512
    assert tok == 1024
    if xn_cols is None:
        xn_cols = lambda dc: xn[:, dc * tok:(dc + 1) * tok]
    b.wait("tensor", ss, b.tick(ss))
    b.wait("tensor", sv, b.tick(sv))
    ps_sum = bank(0, 1024)[0:1, :]
    ps_sq = bank(2, 1024)[0:1, :]
    sq_ticks = {}
    for dc in range(NCH):
        if x_keep is not None:
            xslot = x_keep[:, dc * tok:(dc + 1) * tok]
            xsem, xt, _ = x_stream.load(xslot, xsrc_dram[128 * dc:128 * (dc + 1), :])
        else:
            xsem, xt, xs = x_stream.load(
                x_stream.buf[:, (dc % x_stream.n) * tok:((dc % x_stream.n) + 1) * tok],
                xsrc_dram[128 * dc:128 * (dc + 1), :])
            xslot = x_stream.buf[:, (dc % x_stream.n) * tok:((dc % x_stream.n) + 1) * tok]
        bsem, bt, bslot = xbf_stream.load(
            xbf_stream.buf[:, (dc % xbf_stream.n) * tok:((dc % xbf_stream.n) + 1) * tok],
            xsrc_dram[128 * dc:128 * (dc + 1), :])
        xbfslot = xbf_stream.buf[:, (dc % xbf_stream.n) * tok:((dc % xbf_stream.n) + 1) * tok]
        b.wait("scalar", xsem, xt)
        sqslot = sqbuf[:, (dc % 2) * tok:((dc % 2) + 1) * tok]
        if dc >= 2:
            b.wait("scalar", st, sq_ticks[dc - 2][1])
        b.op("scalar", lambda e, o=sqslot, i=xslot: e.activation(o, i, AF.Square),
             inc=(ss, 1))
        sq_s = b.tick(ss)
        b.wait("tensor", bsem, bt)
        b.wait("tensor", ss, sq_s)
        if dc == 0:
            b.wait("tensor", sv, consts)
        for h in range(nh):
            b.op("tensor", lambda e, i=xbfslot, h=h, dc=dc: e.matmul(
                ps_sum[:, 512 * h:512 * (h + 1)], ps_ones_bf[b][:, :],
                i[:, 512 * h:512 * (h + 1)],
                start=(dc == 0), stop=(dc == NCH - 1)))
            b.op("tensor", lambda e, i=sqslot, h=h, dc=dc: e.matmul(
                ps_sq[:, 512 * h:512 * (h + 1)], ps_ones_bf[b][:, :],
                i[:, 512 * h:512 * (h + 1)],
                start=(dc == 0), stop=(dc == NCH - 1)),
                inc=(st, 1) if h == nh - 1 else None)
        mm_t = b.tick(st)
        sq_ticks[dc] = (sq_s, mm_t)
        if x_keep is None:
            x_stream.consumed(dc % x_stream.n, "ss", sq_s)
        xbf_stream.consumed(bslot, "st", mm_t)
    stats_t = b.tick(st)

    m_row, ex2, m2, var, sqv, rstd = (rows[k] for k in
                                      ("m_row", "ex2", "m2", "var", "sqv", "rstd"))
    b.wait("scalar", st, stats_t)
    b.op("scalar", lambda e: e.activation(m_row[:, :], ps_sum, AF.Copy,
                                          scale=1.0 / DM), inc=(ss, 1))
    b.op("scalar", lambda e: e.activation(ex2[:, :], ps_sq, AF.Copy,
                                          scale=1.0 / DM), inc=(ss, 1))
    b.guard("scalar", ss)
    b.op("scalar", lambda e: e.activation(m2[:, :], m_row[:, :], AF.Square),
         inc=(ss, 1))
    b.wait("vector", ss, b.tick(ss))
    b.op("vector", lambda e: e.tensor_sub(var[:, :], ex2[:, :], m2[:, :]),
         inc=(sv, 1))
    b.wait("scalar", sv, b.tick(sv))
    b.op("scalar", lambda e: e.activation(sqv[:, :], var[:, :], AF.Sqrt,
                                          bias=eps_row[b][0:1, 0:1]), inc=(ss, 1))
    b.wait("vector", ss, b.tick(ss))
    b.op("vector", lambda e: e.reciprocal(rstd[:, :], sqv[:, :]), inc=(sv, 1))
    rstd_v = b.tick(sv)

    ps_M = bank(4)
    ps_R = bank(5)
    tmp32 = rows["tmp32"]
    b.wait("tensor", sv, rstd_v)
    xn_half_done = {}
    for h in range(nh):
        if h > 0:
            b.wait("tensor", sv, b.tick(sv))
        b.op("tensor", lambda e, h=h: e.matmul(
            ps_M, ps_ones_f[b][:, :],
            m_row[:, 512 * h:512 * (h + 1)], start=True, stop=True))
        b.op("tensor", lambda e, h=h: e.matmul(
            ps_R, ps_ones_f[b][:, :],
            rstd[:, 512 * h:512 * (h + 1)], start=True, stop=True), inc=(st, 1))
        bcast_t = b.tick(st)
        for dc in range(NCH):
            if x_keep is not None:
                xsl = x_keep[:, dc * tok + 512 * h:dc * tok + 512 * (h + 1)]
            else:
                slot = x_stream.i % x_stream.n
                dst = x_stream.buf[:, slot * tok:slot * tok + 512]
                xsem, xt, xs = x_stream.load(
                    dst, xsrc_dram[128 * dc:128 * (dc + 1), 512 * h:512 * (h + 1)])
                xsl = dst
                b.wait("vector", xsem, xt)
            if dc == 0:
                b.wait("vector", st, bcast_t)
            b.guard("vector", sv)
            b.op("vector", lambda e, x_=xsl: e.tensor_sub(
                tmp32[:, 0:512], x_, ps_M), inc=(sv, 1))
            b.guard("vector", sv)
            b.op("vector", lambda e, o=xn_cols(dc), h=h: e.tensor_mul(
                o[:, 512 * h:512 * (h + 1)], tmp32[:, 0:512], ps_R), inc=(sv, 1))
            dc_v = b.tick(sv)
            if x_keep is None:
                x_stream.consumed(xs, "sv", dc_v)
            if xn_out_dram is not None and h == nh - 1:
                b.wait("sync", sv, dc_v)
                b.dma("sync", xn_out_dram[128 * dc:128 * (dc + 1), :],
                      xn_cols(dc), sx)
    return b.tick(sv)


ps_ones_bf = {}
ps_ones_f = {}
eps_row = {}



def build_launch_A():
    nc = bass.Bass("TRN2", target_bir_lowering=False)
    xT = nc.dram_tensor("xT", [DM, TOK], F32, kind="ExternalInput").ap()
    wq = nc.dram_tensor("wq", [DM, DM], F32, kind="ExternalInput").ap()
    wk = nc.dram_tensor("wk", [DM, DM], F32, kind="ExternalInput").ap()
    wv = nc.dram_tensor("wv", [DM, DM], F32, kind="ExternalInput").ap()
    xn_o = nc.dram_tensor("xn_o", [DM, TOK], BF16, kind="ExternalOutput").ap()
    q_o = nc.dram_tensor("q_o", [DM, TOK], BF16, kind="ExternalOutput").ap()
    k_o = nc.dram_tensor("k_o", [DM, TOK], BF16, kind="ExternalOutput").ap()
    v_o = nc.dram_tensor("v_o", [TOK, DM], BF16, kind="ExternalOutput").ap()

    ctx = ExitStack()
    sb = lambda n, s, d: ctx.enter_context(nc.sbuf_tensor(n, s, d))
    xn = sb("xn", [128, NCH * TOK], BF16)
    wv_sb = sb("wv_sb", [128, NCH * DM], BF16)
    x_buf = sb("x_buf", [128, 2 * TOK], F32)
    xbf_buf = sb("xbf_buf", [128, 2 * TOK], BF16)
    sq_buf = sb("sq_buf", [128, 2 * TOK], BF16)
    wqk_buf = sb("wqk_buf", [128, 2 * DM], BF16)
    v_sb = sb("v_sb", [128, 2 * DM], BF16)
    q_sb = sb("q_sb", [128, 2 * TOK], BF16)
    ones_bf = sb("ones_bf", [128, 1], BF16)
    ones_f = sb("ones_f", [1, 128], F32)
    rows = {k: sb(k, [1, TOK], F32) for k in ("m_row", "ex2", "m2", "var", "sqv", "rstd")}
    epst = sb("epst", [128, 2], F32)
    rows["tmp32"] = sb("tmp32", [128, 1024], F32)
    pall = ctx.enter_context(nc.psum_tensor("pall", [128, 8 * 512], F32))

    def bank(i, width=512):
        return pall[:, 512 * i:512 * i + width]

    b = Bld(nc)
    sx = b.sem("sx")
    sg = b.sem("sg")
    st = b.sem("st")
    sv = b.sem("sv")
    ss = b.sem("ss")
    so0 = b.sem("so0")
    so1 = b.sem("so1")
    so = [so0, so1]

    ps_ones_bf[b] = ones_bf
    ps_ones_f[b] = ones_f
    eps_row[b] = epst
    b.op("vector", lambda e: e.memset(ones_bf[:, :], 1.0), inc=(sv, 1))
    b.op("vector", lambda e: e.memset(ones_f[:, :], 1.0), inc=(sv, 1))
    b.op("vector", lambda e: e.memset(epst[:, 0:1], LN_EPS), inc=(sv, 1))
    b.op("vector", lambda e: e.memset(epst[:, 1:2], RMS_EPS), inc=(sv, 1))
    consts = b.tick(sv)

    x_stream = Stream(b, "lx", "sync", 2)
    x_stream.buf = x_buf
    xbf_stream = Stream(b, "lb", "gpsimd", 2)
    xbf_stream.buf = xbf_buf

    XN_V = emit_ln_stats_and_xn(b, bank, consts, xT, x_stream, xbf_stream, sq_buf,
                                rows, xn, TOK, sx, sg, st, sv, ss, xn_out_dram=xn_o)

    wv_sem_t = []
    for dc in range(NCH):
        b.dma("gpsimd", wv_sb[:, dc * DM:(dc + 1) * DM],
              wv[128 * dc:128 * (dc + 1), :], sg)
    WV_G = b.tick(sg)
    b.wait("tensor", sg, WV_G)
    b.wait("tensor", sv, XN_V)
    vidx = 0
    v_evac = {}
    vdma = {}
    for tc in range(TOK // 128):
        for fg in range(4):
            bk = vidx % 2
            if vidx >= 2:
                b.wait("tensor", ss, v_evac[vidx - 2])
            for dc in range(NCH):
                b.op("tensor", lambda e, bk=bk, tc=tc, fg=fg, dc=dc: e.matmul(
                    bank(bk), xn[:, dc * TOK + 128 * tc:dc * TOK + 128 * (tc + 1)],
                    wv_sb[:, dc * DM + 512 * fg:dc * DM + 512 * (fg + 1)],
                    start=(dc == 0), stop=(dc == NCH - 1)),
                    inc=(st, 1) if dc == NCH - 1 else None)
            b.wait("scalar", st, b.tick(st))
            vslot = tc % 2
            if tc >= 2 and fg == 0:
                b.wait("scalar", so[vslot], vdma[tc - 2])
            b.op("scalar", lambda e, bk=bk, vslot=vslot, fg=fg: e.activation(
                v_sb[:, vslot * DM + 512 * fg:vslot * DM + 512 * (fg + 1)],
                bank(bk), AF.Copy), inc=(ss, 1))
            v_evac[vidx] = b.tick(ss)
            vidx += 1
        b.wait("sync", ss, v_evac[vidx - 1])
        b.wait("sync", so[tc % 2], b.tick(so[tc % 2]))
        vdma[tc] = b.dma("sync", v_o[128 * tc:128 * (tc + 1), :],
                         v_sb[:, (tc % 2) * DM:(tc % 2 + 1) * DM], so[tc % 2])

    wstream = Stream(b, "wqk", "gpsimd", 2)
    qk_evac = {}
    qdma = {}
    jobs = [("q", oc) for oc in range(NCH)] + [("k", oc) for oc in range(NCH)]
    for ji, (which, oc) in enumerate(jobs):
        wsrc = wq if which == "q" else wk
        src = wsrc[:, 128 * oc:128 * (oc + 1)].rearrange("(c p) n -> p c n", p=128)
        slot = ji % 2
        dst = wqk_buf[:, slot * DM:(slot + 1) * DM].rearrange("p (c n) -> p c n", n=128)
        wsem, wt, ws = wstream.load(dst, src)
        b.wait("tensor", wsem, wt)
        for h in range(2):
            bk = 2 + h
            if ji >= 1:
                b.wait("tensor", ss, qk_evac[(ji - 1, h)])
            for dc in range(NCH):
                b.op("tensor", lambda e, bk=bk, slot=slot, dc=dc, h=h: e.matmul(
                    bank(bk),
                    wqk_buf[:, slot * DM + 128 * dc:slot * DM + 128 * (dc + 1)],
                    xn[:, dc * TOK + 512 * h:dc * TOK + 512 * (h + 1)],
                    start=(dc == 0), stop=(dc == NCH - 1)),
                    inc=(st, 1) if dc == NCH - 1 else None)
        wstream.consumed(ws, "st", b.tick(st))
        qslot = ji % 2
        b.wait("scalar", st, b.tick(st))
        if ji >= 2:
            b.wait("scalar", so[qslot], qdma[ji - 2])
        scale = RS128 if which == "q" else 1.0
        for h in range(2):
            b.op("scalar", lambda e, qslot=qslot, h=h, scale=scale, bk=2 + h: e.activation(
                q_sb[:, qslot * TOK + 512 * h:qslot * TOK + 512 * (h + 1)],
                bank(bk), AF.Copy, scale=scale), inc=(ss, 1))
            qk_evac[(ji, h)] = b.tick(ss)
        b.wait("sync", ss, qk_evac[(ji, 1)])
        b.wait("sync", so[qslot], b.tick(so[qslot]))
        dst = q_o if which == "q" else k_o
        qdma[ji] = b.dma("sync", dst[128 * oc:128 * (oc + 1), :],
                         q_sb[:, qslot * TOK:(qslot + 1) * TOK], so[qslot])

    b.wait("sync", sx, b.tick(sx))
    b.wait("sync", so0, b.tick(so0))
    b.wait("sync", so1, b.tick(so1))
    b.wait("gpsimd", sg, b.tick(sg))

    b.realize(ctx)
    nc._ctx_keepalive = ctx
    return nc



def shard_tokens_fm(x):
    outs = []
    for c in range(8):
        seq, half = c // 2, c % 2
        sl = x[seq, half * TOK:(half + 1) * TOK, :]
        outs.append(np.ascontiguousarray(np.asarray(sl, dtype=np.float32).T))
    return outs


def _run(nc, in_maps):
    return run_bass_kernel_spmd(nc, in_maps, core_ids=list(range(8)))


def kernel(**inputs):
    return kernel_fused(**inputs)


def kernel_multilaunch(**inputs):
    x = np.asarray(inputs["L0_attn_x"], np.float32)
    xts = shard_tokens_fm(x)
    masks = host_l0_masks()
    h128, h64, perm = host_fwht_consts()
    ident = np.eye(128, dtype=np.float32).astype(BF)
    f32c = lambda a: np.ascontiguousarray(np.asarray(a, np.float32))

    wq0, wk0, wv0, wo0 = (f32c(inputs[k]) for k in
                          ("L0_attn_Wq", "L0_attn_Wk", "L0_attn_Wv", "L0_attn_Wo"))
    w1t = f32c(inputs["L0_ffn_W1"].T)
    w3t = f32c(inputs["L0_ffn_W3"].T)
    w2tp = f32c(np.asarray(inputs["L0_ffn_W2"], np.float32).T[perm])
    wq1 = f32c(inputs["L1_attn_Wq"])
    wv1 = f32c(inputs["L1_attn_Wv"])
    wo1 = f32c(inputs["L1_attn_Wo"])
    wgt = f32c(inputs["L1_ffn_w_gate"].T)
    wut = f32c(inputs["L1_ffn_w_up"].T)
    wdt = f32c(inputs["L1_ffn_w_down"].T)

    resA = _run(build_launch_A(),
                [{"xT": xts[c], "wq": wq0, "wk": wk0, "wv": wv0} for c in range(8)])
    ra = resA.results

    in_b = []
    for c in range(8):
        p = c ^ 1
        half = c % 2
        kc, kp = ra[c]["k_o"], ra[p]["k_o"]
        kfull = np.concatenate([kc, kp] if half == 0 else [kp, kc], axis=1)
        vfull = np.concatenate(
            [ra[c]["v_o"], ra[p]["v_o"]] if half == 0 else
            [ra[p]["v_o"], ra[c]["v_o"]], axis=0)
        vhm = np.ascontiguousarray(
            vfull.reshape(SEQ, 16, 128).transpose(1, 0, 2)).reshape(16 * SEQ, 128)
        in_b.append({"q_i": ra[c]["q_o"], "k_i": np.ascontiguousarray(kfull),
                     "vhm_i": vhm, "mask_i": masks[half], "ident_i": ident})
    resB = _run(build_launch_B(), in_b)

    in_c = [{"xT": xts[c], "xn_i": ra[c]["xn_o"], "o_i": resB.results[c]["o_o"],
             "wo": wo0, "w1t": w1t, "w3t": w3t, "w2tp": w2tp,
             "h128_i": np.asarray(h128), "h64_i": np.asarray(h64)}
            for c in range(8)]
    resC = _run(build_launch_C(), in_c)
    x1 = [resC.results[c]["x1_o"] for c in range(8)]

    in_d = []
    for c in range(8):
        p = c ^ 1
        x1f = np.ascontiguousarray(np.concatenate([x1[c], x1[p]], axis=1))
        in_d.append({"x1f": x1f, "wq": wq1, "wv": wv1, "ident_i": ident})
    resD = _run(build_launch_D(), in_d)
    rd = resD.results

    in_e = [{"q_i": rd[c]["q1_o"], "k_i": rd[c]["k1_o"], "vhm_i": rd[c]["v1hm_o"],
             "ident_i": ident, "wo": wo1} for c in range(8)]
    resE = _run(build_launch_E(), in_e)

    in_f = [{"xn1_i": rd[c]["xn1_o"], "s1n_i": resE.results[c]["s1n_o"],
             "x1_i": x1[c], "wgt": wgt, "wut": wut, "wdt": wdt}
            for c in range(8)]
    resF = _run(build_launch_F(), in_f)

    out = np.empty((BSZ, SEQ, DM), np.float32)
    for c in range(8):
        seq, half = c // 2, c % 2
        out[seq, half * TOK:(half + 1) * TOK, :] = np.asarray(
            resF.results[c]["y_o"], np.float32).T
    return out



def build_launch_B():
    nc = bass.Bass("TRN2", target_bir_lowering=False)
    q_i = nc.dram_tensor("q_i", [DM, TOK], BF16, kind="ExternalInput").ap()
    k_i = nc.dram_tensor("k_i", [DM, SEQ], BF16, kind="ExternalInput").ap()
    vhm_i = nc.dram_tensor("vhm_i", [16 * SEQ, 128], BF16, kind="ExternalInput").ap()
    mask_i = nc.dram_tensor("mask_i", [128, 8 * 9 * 128], BF16, kind="ExternalInput").ap()
    ident_i = nc.dram_tensor("ident_i", [128, 128], BF16, kind="ExternalInput").ap()
    o_o = nc.dram_tensor("o_o", [DM, TOK], BF16, kind="ExternalOutput").ap()

    NQC = TOK // 128
    NKB = SEQ // 128

    ctx = ExitStack()
    sb = lambda n, s, d: ctx.enter_context(nc.sbuf_tensor(n, s, d))
    q_sb = sb("q_sb", [128, NCH * TOK], BF16)
    k_sb = sb("k_sb", [128, 2 * SEQ], BF16)
    v_sb = sb("v_sb", [128, 2 * SEQ], BF16)
    p_sb = sb("p_sb", [128, SEQ], BF16)
    pt_sb = sb("pt_sb", [128, NKB * 128], BF16)
    mask_sb = sb("mask_sb", [128, 8 * 9 * 128], BF16)
    ident = sb("ident", [128, 128], BF16)
    ones_bf = sb("ones_bf", [128, 1], BF16)
    ones_f = sb("ones_f", [1, 128], F32)
    mneg = sb("mneg", [128, 3], F32)
    l_col = sb("l_col", [128, 6], F32)
    linv_col = sb("linv_col", [128, 3], F32)
    pn_sb = sb("pn_sb", [128, SEQ], BF16)
    o_sb = sb("o_sb", [128, 2 * TOK], BF16)
    pall = ctx.enter_context(nc.psum_tensor("pall", [128, 6 * 512], F32))
    ptp = ctx.enter_context(nc.psum_tensor("ptp", [128, 2 * 1024], BF16))

    b = Bld(nc)
    sx = b.sem("sx")
    sg = b.sem("sg")
    st = b.sem("st")
    sv = b.sem("sv")
    ss = b.sem("ss")
    so0 = b.sem("so0")
    so1 = b.sem("so1")
    so = [so0, so1]

    b.op("vector", lambda e: e.memset(ones_bf[:, :], 1.0), inc=(sv, 1))
    b.op("vector", lambda e: e.memset(ones_f[:, :], 1.0), inc=(sv, 1))
    CONSTS_V = b.tick(sv)
    b.dma("sync", mask_sb[:, :], mask_i, sx)
    b.dma("sync", ident[:, :], ident_i, sx)
    for dc in range(NCH):
        b.dma("sync", q_sb[:, dc * TOK:(dc + 1) * TOK],
              q_i[128 * dc:128 * (dc + 1), :], sx)
    QLOAD_X = b.tick(sx)

    kstream = Stream(b, "kst", "sync", 2)
    vstream = Stream(b, "vst", "sync", 2)
    pt_evac_tick = [0, 0]
    ot_evac_tick = [0, 0]
    l_recip_tick = 0
    tr_last = 0
    exp_tick = [0]

    S = pall[:, 0:2048]
    OT = lambda s: pall[:, (4 + s) * 512 + 0: (4 + s) * 512 + 128]
    PTP = lambda j: ptp[:, 1024 * (j % 2): 1024 * (j % 2) + 128]

    for h in range(16):
        ksem, kt, kslot = kstream.load(k_sb[:, (h % 2) * SEQ:(h % 2 + 1) * SEQ],
                                       k_i[128 * h:128 * (h + 1), :])
        vsem, vt, vslot = vstream.load(
            v_sb[:, (h % 2) * SEQ:(h % 2 + 1) * SEQ].rearrange("p (c n) -> p c n", n=128),
            vhm_i[SEQ * h:SEQ * (h + 1), :].rearrange("(c p) n -> p c n", p=128))
        kbase = (h % 2) * SEQ
        vbase = (h % 2) * SEQ
        for qc in range(NQC):
            nkb = qc + 9
            nk = 128 * nkb
            b.wait("tensor", ksem, kt)
            if h == 0 and qc == 0:
                b.wait("tensor", sx, QLOAD_X)
            b.wait("tensor", ss, exp_tick[0])
            for kg in range(ceil_div(nk, 512)):
                w = min(512, nk - 512 * kg)
                b.op("tensor", lambda e, h=h, qc=qc, kg=kg, w=w, kbase=kbase: e.matmul(
                    S[:, 512 * kg:512 * kg + w],
                    q_sb[:, h * TOK + 128 * qc:h * TOK + 128 * (qc + 1)],
                    k_sb[:, kbase + 512 * kg:kbase + 512 * kg + w],
                    start=True, stop=True), inc=(st, 1))
            s_mm = b.tick(st)
            b.wait("vector", st, s_mm)
            if h == 0 and qc == 0:
                b.wait("vector", sx, QLOAD_X)
            for j in range(9):
                kb = qc + j
                b.op("vector", lambda e, qc=qc, j=j, kb=kb: e.tensor_tensor(
                    S[:, 128 * kb:128 * (kb + 1)], S[:, 128 * kb:128 * (kb + 1)],
                    mask_sb[:, (qc * 9 + j) * 128:(qc * 9 + j + 1) * 128], ALU.add),
                    inc=(sv, 1))
            b.guard("vector", sv)
            b.op("vector", lambda e, nk=nk: e.reduce_max(
                mneg[:, :], S[:, 0:nk], axis=AX.X, negate=True), inc=(sv, 1))
            mneg_v = b.tick(sv)
            b.wait("scalar", sv, mneg_v)
            b.wait("scalar", sv, l_recip_tick)
            b.op("scalar", lambda e, nk=nk: e.activation(
                p_sb[:, 0:nk], S[:, 0:nk], AF.Exp, bias=mneg[:, :],
                accum_out=l_col[:, :]), inc=(ss, 1))
            exp_tick[0] = b.tick(ss)
            b.wait("vector", ss, exp_tick[0])
            b.op("vector", lambda e: e.reciprocal(linv_col[:, :], l_col[:, :]),
                 inc=(sv, 1))
            l_recip_tick = b.tick(sv)
            b.guard("vector", sv)
            b.wait("vector", st, tr_last)
            b.op("vector", lambda e, nk=nk: e.tensor_scalar_mul(
                pn_sb[:, 0:nk], p_sb[:, 0:nk], linv_col[:, :]), inc=(sv, 1))
            pn_v = b.tick(sv)
            b.wait("tensor", sv, pn_v)
            for kb in range(nkb):
                j = kb % 2
                b.wait("tensor", ss, pt_evac_tick[j])
                b.op("tensor", lambda e, kb=kb, j=j: e.transpose(
                    PTP(j), pn_sb[:, 128 * kb:128 * (kb + 1)], ident[:, :]),
                    inc=(st, 1))
                tr_t = b.tick(st)
                b.wait("scalar", st, tr_t)
                b.op("scalar", lambda e, kb=kb, j=j: e.copy(
                    pt_sb[:, 128 * kb:128 * (kb + 1)], PTP(j)), inc=(ss, 1))
                pt_evac_tick[j] = b.tick(ss)
            tr_last = b.tick(st)
            ots = qc % 2
            b.wait("tensor", ss, b.tick(ss))
            b.wait("tensor", vsem, vt)
            b.wait("tensor", ss, ot_evac_tick[ots])
            for kb in range(nkb):
                b.op("tensor", lambda e, kb=kb, ots=ots, vbase=vbase, nkb=nkb: e.matmul(
                    OT(ots), v_sb[:, vbase + 128 * kb:vbase + 128 * (kb + 1)],
                    pt_sb[:, 128 * kb:128 * (kb + 1)],
                    start=(kb == 0), stop=(kb == nkb - 1)),
                    inc=(st, 1) if kb == nkb - 1 else None)
            av_t = b.tick(st)
            hs = h % 2
            b.wait("scalar", st, av_t)
            if qc == 0 and h >= 2:
                b.wait("scalar", so[h % 2], odma_tick.get(h - 2, 0))
            b.op("scalar", lambda e, hs=hs, qc=qc, ots=ots: e.copy(
                o_sb[:, hs * TOK + 128 * qc:hs * TOK + 128 * (qc + 1)], OT(ots)),
                inc=(ss, 1))
            ot_evac_tick[ots] = b.tick(ss)
        if h == 0:
            odma_tick = {}
        b.wait("sync", ss, b.tick(ss))
        b.wait("sync", so[h % 2], b.tick(so[h % 2]))
        odma_tick[h] = b.dma("sync", o_o[128 * h:128 * (h + 1), :],
                             o_sb[:, (h % 2) * TOK:(h % 2 + 1) * TOK], so[h % 2])
        kstream.consumed(kslot, "st", s_mm)
        vstream.consumed(vslot, "st", av_t)

    b.wait("sync", sx, b.tick(sx))
    b.wait("sync", so0, b.tick(so0))
    b.wait("sync", so1, b.tick(so1))
    b.realize(ctx)
    nc._ctx_keepalive = ctx
    return nc


def host_l0_masks():
    diag = np.where(np.arange(128)[:, None] >= np.arange(128)[None, :], 0.0, NEG)
    full = np.zeros((128, 128), np.float32)
    neg = np.full((128, 128), NEG, np.float32)
    out = {}
    for par in (0, 1):
        tiles = []
        for qc in range(8):
            for j in range(9):
                if par == 0:
                    tiles.append(diag if j == 0 else neg)
                else:
                    tiles.append(diag if j == 8 else full)
        out[par] = np.concatenate(tiles, axis=1).astype(BF)
    return out



FW_SCALE = float(1.0 / np.sqrt(8192.0))
NHC = DH // 128


def build_launch_C():
    nc = bass.Bass("TRN2", target_bir_lowering=False)
    xT = nc.dram_tensor("xT", [DM, TOK], F32, kind="ExternalInput").ap()
    xn_i = nc.dram_tensor("xn_i", [DM, TOK], BF16, kind="ExternalInput").ap()
    o_i = nc.dram_tensor("o_i", [DM, TOK], BF16, kind="ExternalInput").ap()
    wo = nc.dram_tensor("wo", [DM, DM], F32, kind="ExternalInput").ap()
    w1t = nc.dram_tensor("w1t", [DM, DH], F32, kind="ExternalInput").ap()
    w3t = nc.dram_tensor("w3t", [DM, DH], F32, kind="ExternalInput").ap()
    w2tp = nc.dram_tensor("w2tp", [DH, DM], F32, kind="ExternalInput").ap()
    h128_i = nc.dram_tensor("h128_i", [128, 128], BF16, kind="ExternalInput").ap()
    h64_i = nc.dram_tensor("h64_i", [128, 64], BF16, kind="ExternalInput").ap()
    x1_o = nc.dram_tensor("x1_o", [DM, TOK], F32, kind="ExternalOutput").ap()
    fw_d = nc.dram_tensor("fw_d", [DH, 512], BF16).ap()
    g_d = nc.dram_tensor("g_d", [DM, TOK], F32).ap()

    ctx = ExitStack()
    sb = lambda n, s, d: ctx.enter_context(nc.sbuf_tensor(n, s, d))
    xn = sb("xn", [128, NCH * TOK], BF16)
    big = sb("big", [128, NHC * 512], BF16)
    w1buf = sb("w1buf", [128, 2 * DM], BF16)
    w3buf = sb("w3buf", [128, 2 * DM], BF16)
    w2buf = sb("w2buf", [128, 2 * DH], BF16)
    wobuf = sb("wobuf", [128, 2 * DM], BF16)
    gate_s = sb("gate_s", [128, 2 * 512], F32)
    hid_bf = sb("hid_bf", [128, 2 * 512], BF16)
    h1_bf = sb("h1_bf", [128, 2 * 512], BF16)
    yb_buf = sb("yb_buf", [64, 2 * 512], BF16)
    h128_sb = sb("h128_sb", [128, 128], BF16)
    h64_sb = sb("h64_sb", [128, 64], BF16)
    g_buf = sb("g_buf", [128, 2 * TOK], F32)
    x_buf2 = sb("x_buf2", [128, 2 * TOK], F32)
    tmp_f = sb("tmp_f", [128, TOK], F32)
    x1_buf = sb("x1_buf", [128, 2 * TOK], F32)
    pall = ctx.enter_context(nc.psum_tensor("pall", [128, 8 * 512], F32))

    def bank(i, width=512):
        return pall[:, 512 * i:512 * i + width]

    b = Bld(nc)
    sx = b.sem("sx")
    sg = b.sem("sg")
    st = b.sem("st")
    sv = b.sem("sv")
    ss = b.sem("ss")
    so = [b.sem("so0"), b.sem("so1")]
    sf = [b.sem("sf0"), b.sem("sf1")]

    b.dma("sync", h128_sb[:, :], h128_i, sx)
    b.dma("sync", h64_sb[:, :], h64_i, sx)
    for dc in range(NCH):
        b.dma("sync", xn[:, dc * TOK:(dc + 1) * TOK],
              xn_i[128 * dc:128 * (dc + 1), :], sx)
    XLOAD = b.tick(sx)

    w1s = Stream(b, "w1s", "gpsimd", 2)
    w3s = Stream(b, "w3s", "gpsimd", 2)
    w2s = Stream(b, "w2s", "gpsimd", 2)
    wos = Stream(b, "wos", "gpsimd", 2)
    ybs = Stream(b, "ybs", "sync", 2)
    gs_in = Stream(b, "gsin", "sync", 2)
    xs_in = Stream(b, "xsin", "sync", 2)

    silu_tick = [0, 0]
    mul_tick = [0, 0]
    h1evac_tick = [0, 0]
    fwdma_tick = [0, 0]
    gdma_tick = [0, 0]
    gsil_tick = [0, 0]
    h1_pass_tick = 0

    for tb in range(2):
        if tb == 1:
            b.wait("sync", st, b.tick(st))
        for hc in range(NHC):
            par = hc % 2
            w1sem, w1tk, w1slot = w1s.load(
                w1buf[:, par * DM:(par + 1) * DM].rearrange("p (c n) -> p c n", n=128),
                w1t[:, 128 * hc:128 * (hc + 1)].rearrange("(c p) n -> p c n", p=128))
            w3sem, w3tk, w3slot = w3s.load(
                w3buf[:, par * DM:(par + 1) * DM].rearrange("p (c n) -> p c n", n=128),
                w3t[:, 128 * hc:128 * (hc + 1)].rearrange("(c p) n -> p c n", p=128))
            gb, lb, hb = 4 * par, 4 * par + 1, 4 * par + 2
            b.wait("tensor", w1sem, w1tk)
            b.wait("tensor", w3sem, w3tk)
            if hc < 2 and tb == 0:
                b.wait("tensor", sx, XLOAD)
            b.wait("tensor", ss, silu_tick[par])
            b.wait("tensor", sv, mul_tick[par])
            for dc in range(NCH):
                b.op("tensor", lambda e, gb=gb, par=par, dc=dc, tb=tb: e.matmul(
                    bank(gb), w1buf[:, par * DM + 128 * dc:par * DM + 128 * (dc + 1)],
                    xn[:, dc * TOK + 512 * tb:dc * TOK + 512 * (tb + 1)],
                    start=(dc == 0), stop=(dc == NCH - 1)))
            for dc in range(NCH):
                b.op("tensor", lambda e, lb=lb, par=par, dc=dc, tb=tb: e.matmul(
                    bank(lb), w3buf[:, par * DM + 128 * dc:par * DM + 128 * (dc + 1)],
                    xn[:, dc * TOK + 512 * tb:dc * TOK + 512 * (tb + 1)],
                    start=(dc == 0), stop=(dc == NCH - 1)),
                    inc=(st, 1) if dc == NCH - 1 else None)
            up_t = b.tick(st)
            w1s.consumed(w1slot, "st", up_t)
            w3s.consumed(w3slot, "st", up_t)
            b.wait("scalar", st, up_t)
            b.op("scalar", lambda e, gb=gb, par=par: e.activation(
                gate_s[:, 512 * par:512 * (par + 1)], bank(gb), AF.Silu),
                inc=(ss, 1))
            silu_tick[par] = b.tick(ss)
            b.wait("vector", ss, silu_tick[par])
            b.wait("vector", st, h1_pass_tick)
            b.op("vector", lambda e, lb=lb, par=par: e.tensor_mul(
                hid_bf[:, 512 * par:512 * (par + 1)], bank(lb),
                gate_s[:, 512 * par:512 * (par + 1)]), inc=(sv, 1))
            mul_tick[par] = b.tick(sv)
            b.wait("tensor", sv, mul_tick[par])
            b.wait("tensor", ss, h1evac_tick[par])
            b.op("tensor", lambda e, hb=hb, par=par: e.matmul(
                bank(hb), h128_sb[:, :], hid_bf[:, 512 * par:512 * (par + 1)],
                start=True, stop=True), inc=(st, 1))
            h1_pass_tick = b.tick(st)
            b.wait("scalar", st, h1_pass_tick)
            b.wait("scalar", sf[par], fwdma_tick[par])
            b.op("scalar", lambda e, hb=hb, par=par: e.copy(
                h1_bf[:, 512 * par:512 * (par + 1)], bank(hb)), inc=(ss, 1))
            h1evac_tick[par] = b.tick(ss)
            b.wait("sync", ss, h1evac_tick[par])
            b.wait("sync", sf[par], b.tick(sf[par]))
            b.op("sync", lambda e, hc=hc, par=par: e.dma_start(
                out=fw_d[128 * hc:128 * (hc + 1), :],
                in_=h1_bf[:, 512 * par:512 * (par + 1)]), inc=(sf[par], 16))
            fwdma_tick[par] = b.tick(sf[par])

        b.wait("sync", sf[0], b.tick(sf[0]))
        b.wait("sync", sf[1], b.tick(sf[1]))
        fw3 = fw_d.rearrange("(c p) t -> c p t", p=128)
        p2evac_tick = [0, 0]
        b.wait("tensor", ss, b.tick(ss))
        b.wait("tensor", sv, b.tick(sv))
        for bp in range(128):
            par = bp % 2
            ysem, yt, yslot = ybs.load(
                yb_buf[0:64, par * 512:(par + 1) * 512],
                fw_d[bp::128, :])
            b.wait("tensor", ysem, yt)
            b.wait("tensor", ss, p2evac_tick[par])
            b.op("tensor", lambda e, par=par: e.matmul(
                pall[0:64, 512 * par:512 * (par + 1)], h64_sb[:, :],
                yb_buf[0:64, par * 512:(par + 1) * 512],
                start=True, stop=True), inc=(st, 1))
            p2_t = b.tick(st)
            ybs.consumed(yslot, "st", p2_t)
            cpp = bp // 2
            poff = 64 * (bp % 2)
            b.wait("scalar", st, p2_t)
            b.op("scalar", lambda e, par=par, cpp=cpp, poff=poff: e.activation(
                big[poff:poff + 64, cpp * 512:(cpp + 1) * 512],
                pall[0:64, 512 * par:512 * (par + 1)], AF.Copy, scale=FW_SCALE),
                inc=(ss, 1))
            p2evac_tick[par] = b.tick(ss)
        HW_S = b.tick(ss)

        for dc in range(NCH):
            par = dc % 2
            w2sem, w2tk, w2slot = w2s.load(
                w2buf[:, par * DH:(par + 1) * DH].rearrange("p (c n) -> p c n", n=128),
                w2tp[:, 128 * dc:128 * (dc + 1)].rearrange("(c p) n -> p c n", p=128))
            db = 4 * par + 3
            b.wait("tensor", w2sem, w2tk)
            if dc == 0:
                b.wait("tensor", ss, HW_S)
            b.wait("tensor", ss, gsil_tick[par])
            for hcc in range(NHC):
                b.op("tensor", lambda e, db=db, hcc=hcc, par=par: e.matmul(
                    bank(db),
                    w2buf[:, par * DH + hcc * 128:par * DH + hcc * 128 + 128],
                    big[:, hcc * 512:(hcc + 1) * 512],
                    start=(hcc == 0), stop=(hcc == NHC - 1)),
                    inc=(st, 1) if hcc == NHC - 1 else None)
            down_t = b.tick(st)
            w2s.consumed(w2slot, "st", down_t)
            b.wait("scalar", st, down_t)
            b.wait("scalar", sf[par], gdma_tick[par])
            b.op("scalar", lambda e, db=db, par=par: e.activation(
                g_buf[:, par * TOK:par * TOK + 512], bank(db), AF.Silu),
                inc=(ss, 1))
            gsil_tick[par] = b.tick(ss)
            b.wait("sync", ss, gsil_tick[par])
            b.wait("sync", sf[par], b.tick(sf[par]))
            b.op("sync", lambda e, dc=dc, tb=tb, par=par: e.dma_start(
                out=g_d[128 * dc:128 * (dc + 1), 512 * tb:512 * (tb + 1)],
                in_=g_buf[:, par * TOK:par * TOK + 512]), inc=(sf[par], 16))
            gdma_tick[par] = b.tick(sf[par])

    b.wait("sync", sf[0], b.tick(sf[0]))
    b.wait("sync", sf[1], b.tick(sf[1]))
    o_sb = big
    b.wait("sync", st, b.tick(st))
    for dc in range(NCH):
        b.dma("sync", o_sb[:, dc * TOK:(dc + 1) * TOK],
              o_i[128 * dc:128 * (dc + 1), :], sx)
    OLOAD = b.tick(sx)
    x1dma_tick = [0, 0]
    sres_tick = [0, 0]
    for oc in range(NCH):
        par = oc % 2
        wsem, wtk, wslot = wos.load(
            wobuf[:, par * DM:(par + 1) * DM].rearrange("p (c n) -> p c n", n=128),
            wo[:, 128 * oc:128 * (oc + 1)].rearrange("(c p) n -> p c n", p=128))
        gsem, gt, gslot = gs_in.load(g_buf[:, par * TOK:(par + 1) * TOK],
                                     g_d[128 * oc:128 * (oc + 1), :])
        xsem, xt, xslot = xs_in.load(x_buf2[:, par * TOK:(par + 1) * TOK],
                                     xT[128 * oc:128 * (oc + 1), :])
        sb0 = 4 * par
        b.wait("tensor", wsem, wtk)
        if oc < 2:
            b.wait("tensor", sx, OLOAD)
        b.wait("tensor", sv, sres_tick[par])
        for h in range(2):
            for dc in range(NCH):
                b.op("tensor", lambda e, sbk=sb0 + h, par=par, dc=dc, h=h: e.matmul(
                    bank(sbk), wobuf[:, par * DM + 128 * dc:par * DM + 128 * (dc + 1)],
                    o_sb[:, dc * TOK + 512 * h:dc * TOK + 512 * (h + 1)],
                    start=(dc == 0), stop=(dc == NCH - 1)),
                    inc=(st, 1) if dc == NCH - 1 else None)
        s_t = b.tick(st)
        wos.consumed(wslot, "st", s_t)
        b.wait("vector", st, s_t)
        b.wait("vector", gsem, gt)
        b.wait("vector", xsem, xt)
        if oc >= 2:
            b.wait("vector", so[par], x1dma_tick[par])
        for h in range(2):
            b.op("vector", lambda e, sbk=sb0 + h, h=h, par=par: e.tensor_mul(
                tmp_f[:, 512 * h:512 * (h + 1)], bank(sbk),
                g_buf[:, par * TOK + 512 * h:par * TOK + 512 * (h + 1)]), inc=(sv, 1))
            b.guard("vector", sv)
            b.op("vector", lambda e, h=h, par=par: e.tensor_add(
                x1_buf[:, par * TOK + 512 * h:par * TOK + 512 * (h + 1)],
                tmp_f[:, 512 * h:512 * (h + 1)],
                x_buf2[:, par * TOK + 512 * h:par * TOK + 512 * (h + 1)]), inc=(sv, 1))
        res_v = b.tick(sv)
        sres_tick[par] = res_v
        gs_in.consumed(gslot, "sv", res_v)
        xs_in.consumed(xslot, "sv", res_v)
        b.wait("sync", sv, res_v)
        b.wait("sync", so[par], b.tick(so[par]))
        b.op("sync", lambda e, oc=oc, par=par: e.dma_start(
            out=x1_o[128 * oc:128 * (oc + 1), :],
            in_=x1_buf[:, par * TOK:(par + 1) * TOK]), inc=(so[par], 16))
        x1dma_tick[par] = b.tick(so[par])

    b.wait("sync", sx, b.tick(sx))
    b.wait("sync", so[0], b.tick(so[0]))
    b.wait("sync", so[1], b.tick(so[1]))
    b.realize(ctx)
    nc._ctx_keepalive = ctx
    return nc


def host_fwht_consts():
    def hmat(n):
        h = np.array([[1.0]])
        while h.shape[0] < n:
            h = np.block([[h, h], [h, -h]])
        return h
    h128 = hmat(128).astype(BF)
    h64 = hmat(64).astype(BF)
    perm = np.empty(DH, np.int64)
    for cpp in range(64):
        for rr in range(128):
            bprime = 2 * cpp + (1 if rr >= 64 else 0)
            aprime = rr % 64
            perm[cpp * 128 + rr] = aprime * 128 + bprime
    return h128, h64, perm



def build_launch_D():
    nc = bass.Bass("TRN2", target_bir_lowering=False)
    x1f = nc.dram_tensor("x1f", [DM, SEQ], F32, kind="ExternalInput").ap()
    wq = nc.dram_tensor("wq", [DM, DM], F32, kind="ExternalInput").ap()
    wv = nc.dram_tensor("wv", [DM, 512], F32, kind="ExternalInput").ap()
    ident_i = nc.dram_tensor("ident_i", [128, 128], BF16, kind="ExternalInput").ap()
    xn1_o = nc.dram_tensor("xn1_o", [DM, TOK], BF16, kind="ExternalOutput").ap()
    q1_o = nc.dram_tensor("q1_o", [DM, TOK], BF16, kind="ExternalOutput").ap()
    k1_o = nc.dram_tensor("k1_o", [512, SEQ], BF16, kind="ExternalOutput").ap()
    v1hm_o = nc.dram_tensor("v1hm_o", [4 * SEQ, 128], BF16, kind="ExternalOutput").ap()

    ctx = ExitStack()
    sb = lambda n, s, d: ctx.enter_context(nc.sbuf_tensor(n, s, d))
    xn = sb("xn", [128, NCH * SEQ], BF16)
    wv_sb = sb("wv_sb", [128, NCH * 512], BF16)
    wq_buf = sb("wq_buf", [128, 2 * DM], BF16)
    x_buf = sb("x_buf", [128, 2 * TOK], F32)
    xbf_buf = sb("xbf_buf", [128, 2 * TOK], BF16)
    sq_buf = sb("sq_buf", [128, 2 * TOK], BF16)
    v_f = sb("v_f", [128, 2 * 512], F32)
    v_bf = sb("v_bf", [128, 2 * 512], BF16)
    k_bf = sb("k_bf", [128, 2 * 512], BF16)
    kt_bf = sb("kt_bf", [128, 2 * 512], BF16)
    sqv_f = sb("sqv_f", [128, 128], F32)
    vsum = sb("vsum", [128, 8], F32)
    vsd = sb("vsd", [128, 8], F32)
    vrstd = sb("vrstd", [128, 8], F32)
    qsq = sb("qsq", [128, TOK], BF16)
    q_bf = sb("q_bf", [128, 2 * TOK], BF16)
    rstd_bc = sb("rstd_bc", [128, TOK], F32)
    ident = sb("ident", [128, 128], BF16)
    ones_bf = sb("ones_bf", [128, 1], BF16)
    ones_f = sb("ones_f", [1, 128], F32)
    epst = sb("epst", [128, 2], F32)
    rows = {k: sb(k, [1, TOK], F32) for k in ("m_row", "ex2", "m2", "var", "sqv", "rstd")}
    rows["tmp32"] = sb("tmp32", [128, 1024], F32)
    qrow = {k: sb("q" + k, [1, TOK], F32) for k in ("sqv", "rstd")}
    pall = ctx.enter_context(nc.psum_tensor("pall", [128, 6 * 512], F32))
    ptp = ctx.enter_context(nc.psum_tensor("ptp", [128, 2 * 1024], BF16))

    def bank(i, width=512):
        return pall[:, 512 * i:512 * i + width]

    b = Bld(nc)
    sx = b.sem("sx")
    sg = b.sem("sg")
    st = b.sem("st")
    sv = b.sem("sv")
    ss = b.sem("ss")
    so = [b.sem("so0"), b.sem("so1")]
    sid = b.sem("sid")

    ps_ones_bf[b] = ones_bf
    ps_ones_f[b] = ones_f
    eps_row[b] = epst
    b.op("vector", lambda e: e.memset(ones_bf[:, :], 1.0), inc=(sv, 1))
    b.op("vector", lambda e: e.memset(ones_f[:, :], 1.0), inc=(sv, 1))
    b.op("vector", lambda e: e.memset(epst[:, 0:1], LN_EPS), inc=(sv, 1))
    b.op("vector", lambda e: e.memset(epst[:, 1:2], RMS_EPS), inc=(sv, 1))
    consts = b.tick(sv)
    b.dma("sync", ident[:, :], ident_i, sid)
    IDLOAD = b.tick(sid)

    x_stream = Stream(b, "lx", "sync", 2)
    x_stream.buf = x_buf
    xbf_stream = Stream(b, "lb", "gpsimd", 2)
    xbf_stream.buf = xbf_buf

    for th in range(2):
        xt_v = emit_ln_stats_and_xn(
            b, bank, consts, x1f[:, TOK * th:TOK * (th + 1)], x_stream, xbf_stream,
            sq_buf, rows, xn, TOK, sx, sg, st, sv, ss,
            xn_out_dram=(xn1_o if th == 0 else None),
            xn_cols=lambda dc, th=th: xn[:, dc * SEQ + TOK * th: dc * SEQ + TOK * (th + 1)])
    XN_V = b.tick(sv)

    for dc in range(NCH):
        b.dma("gpsimd", wv_sb[:, dc * 512:(dc + 1) * 512],
              wv[128 * dc:128 * (dc + 1), :], sg)
    WVL = b.tick(sg)
    b.wait("tensor", sg, WVL)
    b.wait("tensor", sv, XN_V)
    vevac_tick = [0, 0]
    kdma_tick = [0, 0]
    vdma_tick = [0, 0]
    trevac_tick = [0, 0]
    kmul_tick = [0, 0]
    ktr_tick = [0, 0]
    for tc in range(SEQ // 128):
        par = tc % 2
        vb = par
        b.wait("tensor", ss, vevac_tick[par])
        for dc in range(NCH):
            b.op("tensor", lambda e, vb=vb, tc=tc, dc=dc: e.matmul(
                bank(vb), xn[:, dc * SEQ + 128 * tc:dc * SEQ + 128 * (tc + 1)],
                wv_sb[:, dc * 512:(dc + 1) * 512],
                start=(dc == 0), stop=(dc == NCH - 1)),
                inc=(st, 1) if dc == NCH - 1 else None)
        v_t = b.tick(st)
        b.wait("scalar", st, v_t)
        b.wait("scalar", so[0], vdma_tick[par])
        b.wait("scalar", sv, kmul_tick[par])
        b.op("scalar", lambda e, vb=vb, par=par: e.copy(
            v_f[:, 512 * par:512 * (par + 1)], bank(vb)), inc=(ss, 1))
        b.guard("scalar", ss)
        b.op("scalar", lambda e, vb=vb, par=par: e.copy(
            v_bf[:, 512 * par:512 * (par + 1)], bank(vb)), inc=(ss, 1))
        vevac_tick[par] = b.tick(ss)
        VE = b.tick(ss)
        b.wait("scalar", ss, VE)
        for kvh in range(4):
            col = 512 * par + 128 * kvh
            b.op("scalar", lambda e, col=col: e.activation(
                sqv_f[:, 0:128], v_f[:, col:col + 128], AF.Square), inc=(ss, 1))
            sq_t = b.tick(ss)
            b.wait("vector", ss, sq_t)
            b.op("vector", lambda e, kvh=kvh: e.reduce_sum(
                vsum[:, kvh:kvh + 1], sqv_f[:, 0:128], axis=AX.X), inc=(sv, 1))
            sum_v = b.tick(sv)
            b.wait("scalar", sv, sum_v)
            b.op("scalar", lambda e, kvh=kvh: e.activation(
                vsd[:, kvh:kvh + 1], vsum[:, kvh:kvh + 1], AF.Sqrt,
                scale=1.0 / 128.0, bias=epst[:, 1:2]), inc=(ss, 1))
            sd_t = b.tick(ss)
            b.wait("vector", ss, sd_t)
            b.op("vector", lambda e, kvh=kvh: e.reciprocal(
                vrstd[:, kvh:kvh + 1], vsd[:, kvh:kvh + 1]), inc=(sv, 1))
            b.guard("vector", sv)
            if kvh == 0:
                b.wait("vector", st, ktr_tick[par])
            b.op("vector", lambda e, col=col, kvh=kvh: e.tensor_scalar_mul(
                k_bf[:, col:col + 128], v_f[:, col:col + 128],
                vrstd[:, kvh:kvh + 1]), inc=(sv, 1))
        krow_v = b.tick(sv)
        kmul_tick[par] = krow_v
        b.wait("tensor", sv, krow_v)
        if tc == 0:
            b.wait("tensor", sid, IDLOAD)
        for kvh in range(4):
            j = kvh % 2
            b.wait("tensor", ss, trevac_tick[j])
            b.op("tensor", lambda e, par=par, kvh=kvh, j=j: e.transpose(
                ptp[:, 1024 * j:1024 * j + 128],
                k_bf[:, 512 * par + 128 * kvh:512 * par + 128 * (kvh + 1)],
                ident[:, :]), inc=(st, 1))
            tr_t = b.tick(st)
            b.wait("scalar", st, tr_t)
            b.wait("scalar", so[1], kdma_tick[j])
            b.op("scalar", lambda e, j=j: e.copy(
                kt_bf[:, 512 * j:512 * j + 128], ptp[:, 1024 * j:1024 * j + 128]),
                inc=(ss, 1))
            trevac_tick[j] = b.tick(ss)
            b.wait("sync", ss, trevac_tick[j])
            b.wait("sync", so[1], b.tick(so[1]))
            b.op("sync", lambda e, kvh=kvh, tc=tc, j=j: e.dma_start(
                out=k1_o[128 * kvh:128 * (kvh + 1), 128 * tc:128 * (tc + 1)],
                in_=kt_bf[:, 512 * j:512 * j + 128]), inc=(so[1], 16))
            kdma_tick[j] = b.tick(so[1])
        ktr_tick[par] = b.tick(st)
        b.wait("sync", ss, VE)
        b.wait("sync", so[0], b.tick(so[0]))
        for kvh in range(4):
            b.op("sync", lambda e, kvh=kvh, tc=tc, par=par: e.dma_start(
                out=v1hm_o[SEQ * kvh + 128 * tc:SEQ * kvh + 128 * (tc + 1), :],
                in_=v_bf[:, 512 * par + 128 * kvh:512 * par + 128 * (kvh + 1)]),
                inc=(so[0], 16))
        vdma_tick[par] = b.tick(so[0])

    wqs = Stream(b, "wqs", "gpsimd", 2)
    qdma_tick = [0, 0]
    qmul_tick = 0
    for qh in range(16):
        par = qh % 2
        wsem, wtk, wslot = wqs.load(
            wq_buf[:, par * DM:(par + 1) * DM].rearrange("p (c n) -> p c n", n=128),
            wq[:, 128 * qh:128 * (qh + 1)].rearrange("(c p) n -> p c n", p=128))
        b.wait("tensor", wsem, wtk)
        b.wait("tensor", sv, qmul_tick)
        b.wait("tensor", ss, b.tick(ss))
        for h in range(2):
            for dc in range(NCH):
                b.op("tensor", lambda e, h=h, par=par, dc=dc: e.matmul(
                    bank(h), wq_buf[:, par * DM + 128 * dc:par * DM + 128 * (dc + 1)],
                    xn[:, dc * SEQ + 512 * h:dc * SEQ + 512 * (h + 1)],
                    start=(dc == 0), stop=(dc == NCH - 1)),
                    inc=(st, 1) if dc == NCH - 1 else None)
        q_t = b.tick(st)
        wqs.consumed(wslot, "st", q_t)
        b.wait("scalar", st, q_t)
        b.op("scalar", lambda e: e.activation(qsq[:, :], pall[:, 0:1024], AF.Square),
             inc=(ss, 1))
        qsq_t = b.tick(ss)
        b.wait("tensor", ss, qsq_t)
        for h in range(2):
            b.op("tensor", lambda e, h=h: e.matmul(
                pall[0:1, 512 * (2 + h):512 * (3 + h)], ones_bf[:, :],
                qsq[:, 512 * h:512 * (h + 1)], start=True, stop=True),
                inc=(st, 1) if h == 1 else None)
        qs_t = b.tick(st)
        b.wait("scalar", st, qs_t)
        b.op("scalar", lambda e: e.activation(
            qrow["sqv"][:, :], pall[0:1, 1024:2048], AF.Sqrt, scale=1.0 / 128.0,
            bias=epst[0:1, 1:2]), inc=(ss, 1))
        b.wait("vector", ss, b.tick(ss))
        b.op("vector", lambda e: e.reciprocal(qrow["rstd"][:, :], qrow["sqv"][:, :]),
             inc=(sv, 1))
        qr_v = b.tick(sv)
        b.wait("tensor", sv, qr_v)
        b.wait("tensor", sv, qmul_tick)
        for h in range(2):
            b.op("tensor", lambda e, h=h: e.matmul(
                pall[:, 512 * (4 + h):512 * (5 + h)], ones_f[:, :],
                qrow["rstd"][:, 512 * h:512 * (h + 1)], start=True, stop=True),
                inc=(st, 1) if h == 1 else None)
        bc_t = b.tick(st)
        b.wait("scalar", st, bc_t)
        b.wait("scalar", sv, qmul_tick)
        b.op("scalar", lambda e: e.copy(rstd_bc[:, :], pall[:, 2048:3072]),
             inc=(ss, 1))
        bce_t = b.tick(ss)
        b.wait("vector", ss, bce_t)
        if qh >= 2:
            b.wait("vector", so[par], qdma_tick[par])
        b.op("vector", lambda e, par=par: e.tensor_mul(
            q_bf[:, par * TOK:(par + 1) * TOK], pall[:, 0:1024], rstd_bc[:, :]),
            inc=(sv, 1))
        qmul_tick = b.tick(sv)
        b.wait("sync", sv, qmul_tick)
        b.wait("sync", so[par], b.tick(so[par]))
        b.op("sync", lambda e, qh=qh, par=par: e.dma_start(
            out=q1_o[128 * qh:128 * (qh + 1), :],
            in_=q_bf[:, par * TOK:(par + 1) * TOK]), inc=(so[par], 16))
        qdma_tick[par] = b.tick(so[par])

    b.wait("sync", sx, b.tick(sx))
    b.wait("sync", so[0], b.tick(so[0]))
    b.wait("sync", so[1], b.tick(so[1]))
    b.wait("gpsimd", sg, b.tick(sg))
    b.realize(ctx)
    nc._ctx_keepalive = ctx
    return nc



def build_launch_E():
    nc = bass.Bass("TRN2", target_bir_lowering=False)
    q_i = nc.dram_tensor("q_i", [DM, TOK], BF16, kind="ExternalInput").ap()
    k_i = nc.dram_tensor("k_i", [512, SEQ], BF16, kind="ExternalInput").ap()
    vhm_i = nc.dram_tensor("vhm_i", [4 * SEQ, 128], BF16, kind="ExternalInput").ap()
    ident_i = nc.dram_tensor("ident_i", [128, 128], BF16, kind="ExternalInput").ap()
    wo = nc.dram_tensor("wo", [DM, DM], F32, kind="ExternalInput").ap()
    s1n_o = nc.dram_tensor("s1n_o", [DM, TOK], F32, kind="ExternalOutput").ap()

    NQC = TOK // 128
    NKB = SEQ // 128

    ctx = ExitStack()
    sb = lambda n, s, d: ctx.enter_context(nc.sbuf_tensor(n, s, d))
    q_sb = sb("q_sb", [128, NCH * TOK], BF16)
    k_sb = sb("k_sb", [128, 4 * SEQ], BF16)
    v_sb = sb("v_sb", [128, 4 * SEQ], BF16)
    y_sb = sb("y_sb", [128, NCH * TOK], BF16)
    s_sb = sb("s_sb", [128, NCH * TOK], F32)
    p_sb = sb("p_sb", [128, SEQ], BF16)
    pn_sb = sb("pn_sb", [128, SEQ], BF16)
    pt_sb = sb("pt_sb", [128, NKB * 128], BF16)
    wobuf = sb("wobuf", [128, 2 * DM], BF16)
    sq_bf = sb("sq_bf", [128, TOK], BF16)
    s1n_buf = sb("s1n_buf", [128, 2 * TOK], F32)
    rstd_bc = sb("rstd_bc", [128, TOK], F32)
    ident = sb("ident", [128, 128], BF16)
    ones_bf = sb("ones_bf", [128, 1], BF16)
    ones_f = sb("ones_f", [1, 128], F32)
    epst = sb("epst", [128, 2], F32)
    mneg = sb("mneg", [128, 3], F32)
    l_col = sb("l_col", [128, 6], F32)
    linv_col = sb("linv_col", [128, 3], F32)
    srow = {k: sb("s" + k, [1, TOK], F32) for k in ("sqv", "rstd")}
    pall = ctx.enter_context(nc.psum_tensor("pall", [128, 6 * 512], F32))
    ptp = ctx.enter_context(nc.psum_tensor("ptp", [128, 2 * 1024], BF16))

    b = Bld(nc)
    sx = b.sem("sx")
    sg = b.sem("sg")
    st = b.sem("st")
    sv = b.sem("sv")
    ss = b.sem("ss")
    so = [b.sem("so0"), b.sem("so1")]

    b.op("vector", lambda e: e.memset(ones_bf[:, :], 1.0), inc=(sv, 1))
    b.op("vector", lambda e: e.memset(ones_f[:, :], 1.0), inc=(sv, 1))
    b.op("vector", lambda e: e.memset(epst[:, 1:2], RMS_EPS), inc=(sv, 1))
    CONSTS_V = b.tick(sv)
    b.dma("sync", ident[:, :], ident_i, sx)
    for dc in range(NCH):
        b.dma("sync", q_sb[:, dc * TOK:(dc + 1) * TOK],
              q_i[128 * dc:128 * (dc + 1), :], sx)
    for kvh in range(4):
        b.dma("sync", k_sb[:, kvh * SEQ:(kvh + 1) * SEQ],
              k_i[128 * kvh:128 * (kvh + 1), :], sx)
        b.dma("sync",
              v_sb[:, kvh * SEQ:(kvh + 1) * SEQ].rearrange("p (c n) -> p c n", n=128),
              vhm_i[kvh * SEQ:(kvh + 1) * SEQ, :].rearrange("(c p) n -> p c n", p=128),
              sx)
    LOAD_X = b.tick(sx)

    S = pall[:, 0:2048]
    OT = lambda s: pall[:, (4 + s) * 512: (4 + s) * 512 + 128]
    PTP = lambda j: ptp[:, 1024 * (j % 2): 1024 * (j % 2) + 128]

    pt_evac_tick = [0, 0]
    ot_evac_tick = [0, 0]
    l_recip_tick = 0
    tr_last = 0
    exp_tick = [0]

    for qh in range(16):
        kvh = qh // 4
        kbase = kvh * SEQ
        for qc in range(NQC):
            b.wait("tensor", ss, exp_tick[0])
            if qh == 0 and qc == 0:
                b.wait("tensor", sx, LOAD_X)
            for kg in range(4):
                b.op("tensor", lambda e, qh=qh, qc=qc, kg=kg, kbase=kbase: e.matmul(
                    S[:, 512 * kg:512 * (kg + 1)],
                    q_sb[:, qh * TOK + 128 * qc:qh * TOK + 128 * (qc + 1)],
                    k_sb[:, kbase + 512 * kg:kbase + 512 * (kg + 1)],
                    start=True, stop=True), inc=(st, 1))
            s_mm = b.tick(st)
            b.wait("vector", st, s_mm)
            b.guard("vector", sv)
            b.op("vector", lambda e: e.reduce_max(
                mneg[:, :], S[:, :], axis=AX.X, negate=True), inc=(sv, 1))
            mneg_v = b.tick(sv)
            b.wait("scalar", sv, mneg_v)
            b.wait("scalar", sv, l_recip_tick)
            b.op("scalar", lambda e: e.activation(
                p_sb[:, :], S[:, :], AF.Exp, bias=mneg[:, :],
                accum_out=l_col[:, :]), inc=(ss, 1))
            exp_tick[0] = b.tick(ss)
            b.wait("vector", ss, exp_tick[0])
            b.op("vector", lambda e: e.reciprocal(linv_col[:, :], l_col[:, :]),
                 inc=(sv, 1))
            l_recip_tick = b.tick(sv)
            b.guard("vector", sv)
            b.wait("vector", st, tr_last)
            b.op("vector", lambda e: e.tensor_scalar_mul(
                pn_sb[:, :], p_sb[:, :], linv_col[:, :]), inc=(sv, 1))
            pn_v = b.tick(sv)
            b.wait("tensor", sv, pn_v)
            for kb in range(NKB):
                j = kb % 2
                b.wait("tensor", ss, pt_evac_tick[j])
                b.op("tensor", lambda e, kb=kb, j=j: e.transpose(
                    PTP(j), pn_sb[:, 128 * kb:128 * (kb + 1)], ident[:, :]),
                    inc=(st, 1))
                tr_t = b.tick(st)
                b.wait("scalar", st, tr_t)
                b.op("scalar", lambda e, kb=kb, j=j: e.copy(
                    pt_sb[:, 128 * kb:128 * (kb + 1)], PTP(j)), inc=(ss, 1))
                pt_evac_tick[j] = b.tick(ss)
            tr_last = b.tick(st)
            ots = qc % 2
            b.wait("tensor", ss, b.tick(ss))
            b.wait("tensor", ss, ot_evac_tick[ots])
            for kb in range(NKB):
                b.op("tensor", lambda e, kb=kb, ots=ots, kbase=kbase: e.matmul(
                    OT(ots), v_sb[:, kbase + 128 * kb:kbase + 128 * (kb + 1)],
                    pt_sb[:, 128 * kb:128 * (kb + 1)],
                    start=(kb == 0), stop=(kb == NKB - 1)),
                    inc=(st, 1) if kb == NKB - 1 else None)
            av_t = b.tick(st)
            b.wait("scalar", st, av_t)
            b.op("scalar", lambda e, qh=qh, qc=qc, ots=ots: e.copy(
                y_sb[:, qh * TOK + 128 * qc:qh * TOK + 128 * (qc + 1)], OT(ots)),
                inc=(ss, 1))
            ot_evac_tick[ots] = b.tick(ss)
    Y_S = b.tick(ss)

    wos = Stream(b, "wos", "gpsimd", 2)
    STAT0 = pall[0:1, 4 * 512:4 * 512 + 512]
    STAT1 = pall[0:1, 5 * 512:5 * 512 + 512]
    sevac_tick = [0, 0]
    for oc in range(NCH):
        par = oc % 2
        wsem, wtk, wslot = wos.load(
            wobuf[:, par * DM:(par + 1) * DM].rearrange("p (c n) -> p c n", n=128),
            wo[:, 128 * oc:128 * (oc + 1)].rearrange("(c p) n -> p c n", p=128))
        sb0 = 2 * par
        b.wait("tensor", wsem, wtk)
        if oc == 0:
            b.wait("tensor", ss, Y_S)
        b.wait("tensor", ss, sevac_tick[par])
        for h in range(2):
            for dc in range(NCH):
                b.op("tensor", lambda e, sbk=sb0 + h, par=par, dc=dc, h=h: e.matmul(
                    pall[:, 512 * sbk:512 * (sbk + 1)],
                    wobuf[:, par * DM + 128 * dc:par * DM + 128 * (dc + 1)],
                    y_sb[:, dc * TOK + 512 * h:dc * TOK + 512 * (h + 1)],
                    start=(dc == 0), stop=(dc == NCH - 1)),
                    inc=(st, 1) if dc == NCH - 1 else None)
        s_t = b.tick(st)
        wos.consumed(wslot, "st", s_t)
        b.wait("scalar", st, s_t)
        for h in range(2):
            b.op("scalar", lambda e, sbk=sb0 + h, oc=oc, h=h: e.copy(
                s_sb[:, oc * TOK + 512 * h:oc * TOK + 512 * (h + 1)],
                pall[:, 512 * sbk:512 * (sbk + 1)]), inc=(ss, 1))
            b.op("scalar", lambda e, sbk=sb0 + h, h=h: e.activation(
                sq_bf[:, 512 * h:512 * (h + 1)],
                pall[:, 512 * sbk:512 * (sbk + 1)], AF.Square), inc=(ss, 1))
        sevac_tick[par] = b.tick(ss)
        b.wait("tensor", ss, sevac_tick[par])
        if oc == 0:
            b.wait("tensor", sv, CONSTS_V)
        b.op("tensor", lambda e, oc=oc: e.matmul(
            STAT0, ones_bf[:, :], sq_bf[:, 0:512],
            start=(oc == 0), stop=(oc == NCH - 1)))
        b.op("tensor", lambda e, oc=oc: e.matmul(
            STAT1, ones_bf[:, :], sq_bf[:, 512:1024],
            start=(oc == 0), stop=(oc == NCH - 1)),
            inc=(st, 1) if oc == NCH - 1 else None)
    STATS_T = b.tick(st)
    b.wait("scalar", st, STATS_T)
    b.op("scalar", lambda e: e.activation(
        srow["sqv"][:, 0:512], STAT0, AF.Sqrt, scale=1.0 / DM,
        bias=epst[0:1, 1:2]), inc=(ss, 1))
    b.op("scalar", lambda e: e.activation(
        srow["sqv"][:, 512:1024], STAT1, AF.Sqrt, scale=1.0 / DM,
        bias=epst[0:1, 1:2]), inc=(ss, 1))
    b.wait("vector", ss, b.tick(ss))
    b.op("vector", lambda e: e.reciprocal(srow["rstd"][:, :], srow["sqv"][:, :]),
         inc=(sv, 1))
    RST_V = b.tick(sv)
    b.wait("tensor", sv, RST_V)
    b.wait("tensor", ss, b.tick(ss))
    for h in range(2):
        b.op("tensor", lambda e, h=h: e.matmul(
            pall[:, 512 * h:512 * (h + 1)], ones_f[:, :],
            srow["rstd"][:, 512 * h:512 * (h + 1)], start=True, stop=True),
            inc=(st, 1) if h == 1 else None)
    BC_T = b.tick(st)
    b.wait("scalar", st, BC_T)
    b.op("scalar", lambda e: e.copy(rstd_bc[:, :], pall[:, 0:1024]), inc=(ss, 1))
    BCE_S = b.tick(ss)
    b.wait("vector", ss, BCE_S)
    for oc in range(NCH):
        par = oc % 2
        if oc >= 2:
            b.wait("vector", so[par], s1dma[par])
        b.op("vector", lambda e, oc=oc, par=par: e.tensor_mul(
            s1n_buf[:, par * TOK:(par + 1) * TOK],
            s_sb[:, oc * TOK:(oc + 1) * TOK], rstd_bc[:, :]), inc=(sv, 1))
        mul_v = b.tick(sv)
        b.wait("sync", sv, mul_v)
        b.wait("sync", so[par], b.tick(so[par]))
        if oc == 0:
            s1dma = [0, 0]
        b.op("sync", lambda e, oc=oc, par=par: e.dma_start(
            out=s1n_o[128 * oc:128 * (oc + 1), :],
            in_=s1n_buf[:, par * TOK:(par + 1) * TOK]), inc=(so[par], 16))
        s1dma[par] = b.tick(so[par])

    b.wait("sync", sx, b.tick(sx))
    b.wait("sync", so[0], b.tick(so[0]))
    b.wait("sync", so[1], b.tick(so[1]))
    b.wait("gpsimd", sg, b.tick(sg))
    b.realize(ctx)
    nc._ctx_keepalive = ctx
    return nc



def build_launch_F():
    nc = bass.Bass("TRN2", target_bir_lowering=False)
    xn1_i = nc.dram_tensor("xn1_i", [DM, TOK], BF16, kind="ExternalInput").ap()
    s1n_i = nc.dram_tensor("s1n_i", [DM, TOK], F32, kind="ExternalInput").ap()
    x1_i = nc.dram_tensor("x1_i", [DM, TOK], F32, kind="ExternalInput").ap()
    wgt = nc.dram_tensor("wgt", [DM, DH], F32, kind="ExternalInput").ap()
    wut = nc.dram_tensor("wut", [DM, DH], F32, kind="ExternalInput").ap()
    wdt = nc.dram_tensor("wdt", [DH, DM], F32, kind="ExternalInput").ap()
    y_o = nc.dram_tensor("y_o", [DM, TOK], F32, kind="ExternalOutput").ap()

    ctx = ExitStack()
    sb = lambda n, s, d: ctx.enter_context(nc.sbuf_tensor(n, s, d))
    xn = sb("xn", [128, NCH * TOK], BF16)
    big = sb("big", [128, NHC * 512], BF16)
    wgbuf = sb("wgbuf", [128, 2 * DM], BF16)
    wubuf = sb("wubuf", [128, 2 * DM], BF16)
    wdbuf = sb("wdbuf", [128, 2 * DH], BF16)
    gate_s = sb("gate_s", [128, 2 * 512], F32)
    g_buf = sb("g_buf", [128, 2 * 512], F32)
    s_buf = sb("s_buf", [128, 2 * 512], F32)
    x_buf = sb("x_buf", [128, 2 * 512], F32)
    tmp_f = sb("tmp_f", [128, 2 * 512], F32)
    y_buf = sb("y_buf", [128, 2 * 512], F32)
    pall = ctx.enter_context(nc.psum_tensor("pall", [128, 8 * 512], F32))

    def bank(i, width=512):
        return pall[:, 512 * i:512 * i + width]

    b = Bld(nc)
    sx = b.sem("sx")
    sg = b.sem("sg")
    st = b.sem("st")
    sv = b.sem("sv")
    ss = b.sem("ss")
    so = [b.sem("so0"), b.sem("so1")]

    for dc in range(NCH):
        b.dma("sync", xn[:, dc * TOK:(dc + 1) * TOK],
              xn1_i[128 * dc:128 * (dc + 1), :], sx)
    XLOAD = b.tick(sx)

    wgs = Stream(b, "wgs", "gpsimd", 2)
    wus = Stream(b, "wus", "gpsimd", 2)
    wds = Stream(b, "wds", "gpsimd", 2)
    s_in = Stream(b, "sin", "sync", 2)
    x_in = Stream(b, "xin", "sync", 2)

    silu_tick = [0, 0]
    hmul_tick = [0, 0]
    gsil_tick = [0, 0]
    res_tick = [0, 0]
    ydma_tick = [0, 0]
    down_last = 0

    for tb in range(2):
        for hc in range(NHC):
            par = hc % 2
            wgsem, wgtk, wgslot = wgs.load(
                wgbuf[:, par * DM:(par + 1) * DM].rearrange("p (c n) -> p c n", n=128),
                wgt[:, 128 * hc:128 * (hc + 1)].rearrange("(c p) n -> p c n", p=128))
            wusem, wutk, wuslot = wus.load(
                wubuf[:, par * DM:(par + 1) * DM].rearrange("p (c n) -> p c n", n=128),
                wut[:, 128 * hc:128 * (hc + 1)].rearrange("(c p) n -> p c n", p=128))
            gb, ub = 4 * par, 4 * par + 1
            b.wait("tensor", wgsem, wgtk)
            b.wait("tensor", wusem, wutk)
            if hc < 2 and tb == 0:
                b.wait("tensor", sx, XLOAD)
            b.wait("tensor", ss, silu_tick[par])
            b.wait("tensor", sv, hmul_tick[par])
            for dc in range(NCH):
                b.op("tensor", lambda e, gb=gb, par=par, dc=dc, tb=tb: e.matmul(
                    bank(gb), wgbuf[:, par * DM + 128 * dc:par * DM + 128 * (dc + 1)],
                    xn[:, dc * TOK + 512 * tb:dc * TOK + 512 * (tb + 1)],
                    start=(dc == 0), stop=(dc == NCH - 1)))
            for dc in range(NCH):
                b.op("tensor", lambda e, ub=ub, par=par, dc=dc, tb=tb: e.matmul(
                    bank(ub), wubuf[:, par * DM + 128 * dc:par * DM + 128 * (dc + 1)],
                    xn[:, dc * TOK + 512 * tb:dc * TOK + 512 * (tb + 1)],
                    start=(dc == 0), stop=(dc == NCH - 1)),
                    inc=(st, 1) if dc == NCH - 1 else None)
            up_t = b.tick(st)
            wgs.consumed(wgslot, "st", up_t)
            wus.consumed(wuslot, "st", up_t)
            b.wait("scalar", st, up_t)
            b.op("scalar", lambda e, gb=gb, par=par: e.activation(
                gate_s[:, 512 * par:512 * (par + 1)], bank(gb), AF.Silu),
                inc=(ss, 1))
            silu_tick[par] = b.tick(ss)
            b.wait("vector", ss, silu_tick[par])
            if hc < 2:
                b.wait("vector", st, down_last)
            b.op("vector", lambda e, ub=ub, par=par, hc=hc: e.tensor_mul(
                big[:, hc * 512:(hc + 1) * 512], bank(ub),
                gate_s[:, 512 * par:512 * (par + 1)]), inc=(sv, 1))
            hmul_tick[par] = b.tick(sv)
        HID_V = b.tick(sv)

        for dc in range(NCH):
            par = dc % 2
            wdsem, wdtk, wdslot = wds.load(
                wdbuf[:, par * DH:(par + 1) * DH].rearrange("p (c n) -> p c n", n=128),
                wdt[:, 128 * dc:128 * (dc + 1)].rearrange("(c p) n -> p c n", p=128))
            ssem, stk, sslot = s_in.load(s_buf[:, par * 512:(par + 1) * 512],
                                         s1n_i[128 * dc:128 * (dc + 1),
                                               512 * tb:512 * (tb + 1)])
            xsem, xtk, xslot = x_in.load(x_buf[:, par * 512:(par + 1) * 512],
                                         x1_i[128 * dc:128 * (dc + 1),
                                              512 * tb:512 * (tb + 1)])
            db = 4 * par + 3
            b.wait("tensor", wdsem, wdtk)
            if dc == 0:
                b.wait("tensor", sv, HID_V)
            b.wait("tensor", ss, gsil_tick[par])
            for hcc in range(NHC):
                b.op("tensor", lambda e, db=db, hcc=hcc, par=par: e.matmul(
                    bank(db), wdbuf[:, par * DH + hcc * 128:par * DH + hcc * 128 + 128],
                    big[:, hcc * 512:(hcc + 1) * 512],
                    start=(hcc == 0), stop=(hcc == NHC - 1)),
                    inc=(st, 1) if hcc == NHC - 1 else None)
            down_t = b.tick(st)
            down_last = down_t
            wds.consumed(wdslot, "st", down_t)
            b.wait("scalar", st, down_t)
            b.wait("scalar", sv, res_tick[par])
            b.op("scalar", lambda e, db=db, par=par: e.activation(
                g_buf[:, 512 * par:512 * (par + 1)], bank(db), AF.Silu),
                inc=(ss, 1))
            gsil_tick[par] = b.tick(ss)
            b.wait("vector", ss, gsil_tick[par])
            b.wait("vector", ssem, stk)
            b.wait("vector", xsem, xtk)
            b.wait("vector", so[par], ydma_tick[par])
            b.guard("vector", sv)
            b.op("vector", lambda e, par=par: e.tensor_mul(
                tmp_f[:, 512 * par:512 * (par + 1)],
                g_buf[:, 512 * par:512 * (par + 1)],
                s_buf[:, 512 * par:512 * (par + 1)]), inc=(sv, 1))
            b.guard("vector", sv)
            b.op("vector", lambda e, par=par: e.tensor_add(
                y_buf[:, 512 * par:512 * (par + 1)],
                tmp_f[:, 512 * par:512 * (par + 1)],
                x_buf[:, 512 * par:512 * (par + 1)]), inc=(sv, 1))
            res_v = b.tick(sv)
            res_tick[par] = res_v
            s_in.consumed(sslot, "sv", res_v)
            x_in.consumed(xslot, "sv", res_v)
            b.wait("sync", sv, res_v)
            b.wait("sync", so[par], b.tick(so[par]))
            b.op("sync", lambda e, dc=dc, tb=tb, par=par: e.dma_start(
                out=y_o[128 * dc:128 * (dc + 1), 512 * tb:512 * (tb + 1)],
                in_=y_buf[:, 512 * par:512 * (par + 1)]), inc=(so[par], 16))
            ydma_tick[par] = b.tick(so[par])

    b.wait("sync", sx, b.tick(sx))
    b.wait("sync", so[0], b.tick(so[0]))
    b.wait("sync", so[1], b.tick(so[1]))
    b.wait("gpsimd", sg, b.tick(sg))
    b.realize(ctx)
    nc._ctx_keepalive = ctx
    return nc




def _l0_sched(qb):
    lks = list(range(qb + 1))
    if qb < 8:
        lks += list(range(8, 16))
    return lks


def build_fused():
    nc = bass.Bass("TRN2", target_bir_lowering=False)
    xT = nc.dram_tensor("xT", [DM, SEQ], F32, kind="ExternalInput").ap()
    wq0 = nc.dram_tensor("wq0", [DM, DM], F32, kind="ExternalInput").ap()
    wk0 = nc.dram_tensor("wk0", [DM, DM], F32, kind="ExternalInput").ap()
    wv0 = nc.dram_tensor("wv0", [DM, DM], F32, kind="ExternalInput").ap()
    wo0 = nc.dram_tensor("wo0", [DM, DM], F32, kind="ExternalInput").ap()
    w1t = nc.dram_tensor("w1t", [DM, DH], F32, kind="ExternalInput").ap()
    w3t = nc.dram_tensor("w3t", [DM, DH], F32, kind="ExternalInput").ap()
    w2tp = nc.dram_tensor("w2tp", [DH, DM], F32, kind="ExternalInput").ap()
    wq1 = nc.dram_tensor("wq1", [DM, DM], F32, kind="ExternalInput").ap()
    wv1 = nc.dram_tensor("wv1", [DM, 512], F32, kind="ExternalInput").ap()
    wo1 = nc.dram_tensor("wo1", [DM, DM], F32, kind="ExternalInput").ap()
    wgt = nc.dram_tensor("wgt", [DM, DH], F32, kind="ExternalInput").ap()
    wut = nc.dram_tensor("wut", [DM, DH], F32, kind="ExternalInput").ap()
    wdt = nc.dram_tensor("wdt", [DH, DM], F32, kind="ExternalInput").ap()
    mask_i = nc.dram_tensor("mask_i", [128, 17 * 128], BF16, kind="ExternalInput").ap()
    ident_i = nc.dram_tensor("ident_i", [128, 128], BF16, kind="ExternalInput").ap()
    h128_i = nc.dram_tensor("h128_i", [128, 128], BF16, kind="ExternalInput").ap()
    h64_i = nc.dram_tensor("h64_i", [128, 64], BF16, kind="ExternalInput").ap()
    y_o = nc.dram_tensor("y_o", [DM, TOK], F32, kind="ExternalOutput").ap()

    fw_d = nc.dram_tensor("fw_d", [DH, TOK], BF16).ap()
    g_d = nc.dram_tensor("g_d", [DM, TOK], F32).ap()
    q0_d = nc.dram_tensor("q0_d", [DM, TOK], BF16).ap()
    k0_d = nc.dram_tensor("k0_d", [DM, SEQ], BF16).ap()
    v0hm_d = nc.dram_tensor("v0hm_d", [16 * SEQ, 128], BF16).ap()
    x1_d = nc.dram_tensor("x1_d", [DM, TOK], F32).ap()
    s1n_d = nc.dram_tensor("s1n_d", [DM, TOK], F32).ap()
    hd_d = nc.dram_tensor("hd_d", [DH, TOK], BF16).ap()
    v1pad_t = nc.dram_tensor("v1pad_d", [SEQ, 512], F32)
    v1pad_d = v1pad_t.ap()
    v1ag_t = nc.dram_tensor("v1ag_d", [SEQ, 512], F32)
    v1ag_d = v1ag_t.ap()
    msel_i = nc.dram_tensor("msel_i", [128, 2], F32, kind="ExternalInput").ap()

    ctx = ExitStack()
    sb = lambda n, s, d: ctx.enter_context(nc.sbuf_tensor(n, s, d))
    ident = sb("ident", [128, 128], BF16)
    h128_sb = sb("h128_sb", [128, 128], BF16)
    h64_sb = sb("h64_sb", [128, 64], BF16)
    mask_sb = sb("mask_sb", [128, 17 * 128], BF16)
    msel_sb = sb("msel_sb", [128, 2], F32)
    ones_bf = sb("ones_bf", [128, 1], BF16)
    ones_f = sb("ones_f", [1, 128], F32)
    onescol = sb("onescol", [128, 1], F32)
    epst = sb("epst", [128, 2], F32)
    mneg = sb("mneg", [128, 3], F32)
    pmax = sb("pmax", [128, 8], F32)
    l_col = sb("l_col", [128, 6], F32)
    linv_col = sb("linv_col", [128, 3], F32)
    pall = ctx.enter_context(nc.psum_tensor("pall", [128, 6 * 512], F32))
    ptp = ctx.enter_context(nc.psum_tensor("ptp", [128, 2 * 512], F32))

    def bank(i, width=512):
        return pall[:, 512 * i:512 * i + width]

    b = Bld(nc)
    sx = b.sem("sx")
    sg = b.sem("sg")
    st = b.sem("st")
    sv = b.sem("sv")
    ss = b.sem("ss")
    sid = b.sem("sid")
    so = [b.sem("so0"), b.sem("so1")]
    sf = [b.sem("sf0"), b.sem("sf1")]
    swv = b.sem("swv")
    swv1 = b.sem("swv1")
    shw = [b.sem("shw0"), b.sem("shw1")]
    sol = b.sem("sol")
    scc = b.sem("scc")

    ps_ones_bf[b] = ones_bf
    ps_ones_f[b] = ones_f
    eps_row[b] = epst
    b.op("vector", lambda e: e.memset(ones_bf[:, :], 1.0), inc=(sv, 1))
    b.op("vector", lambda e: e.memset(ones_f[:, :], 1.0), inc=(sv, 1))
    b.op("vector", lambda e: e.memset(epst[:, 0:1], LN_EPS), inc=(sv, 1))
    b.op("vector", lambda e: e.memset(epst[:, 1:2], RMS_EPS), inc=(sv, 1))
    b.op("vector", lambda e: e.memset(onescol[:, :], 1.0), inc=(sv, 1))
    consts = b.tick(sv)
    b.dma("sync", ident[:, :], ident_i, sid)
    b.dma("sync", h128_sb[:, :], h128_i, sid)
    b.dma("sync", h64_sb[:, :], h64_i, sid)
    b.dma("sync", mask_sb[:, :], mask_i, sid)
    b.dma("sync", msel_sb[:, :], msel_i, sid)
    CONST_LOAD = b.tick(sid)
    DIAG = mask_sb[:, 0:128]
    DIAG_A8 = mask_sb[:, 0:9 * 128]
    MASKB8 = mask_sb[:, 9 * 128:17 * 128]

    def sp_out(queue, par, dst, src):
        b.wait(queue, sf[par], b.tick(sf[par]))
        b.op(queue, lambda e, d=dst, s=src: e.dma_start(out=d, in_=s),
             inc=(sf[par], 16))
        return b.tick(sf[par])

    def fence(q):
        b.wait(q, sv, b.tick(sv))
        b.wait(q, ss, b.tick(ss))
        b.wait(q, st, b.tick(st))
        for sem in (sf[0], sf[1], so[0], so[1]):
            b.wait(q, sem, b.tick(sem))

    def fences():
        for q in ("gpsimd", "sync", "tensor", "vector", "scalar"):
            fence(q)

    xn0_cm = nc.sbuf_tensor("xn0", [128, NCH * SEQ], BF16)
    xn0 = xn0_cm.__enter__()
    wv_cm = nc.sbuf_tensor("wv_sb", [128, NCH * DM], BF16)
    wv_sb = wv_cm.__enter__()
    for dc in range(NCH):
        b.dma("gpsimd", wv_sb[:, dc * DM:(dc + 1) * DM],
              wv0[128 * dc:128 * (dc + 1), :], swv)
    WV_G = b.tick(swv)
    p0 = ExitStack()
    sbl = lambda n, s, d: p0.enter_context(nc.sbuf_tensor(n, s, d))
    rows = {k: sbl(k, [1, TOK], F32) for k in ("m_row", "ex2", "m2", "var", "sqv", "rstd")}
    rows["tmp32"] = sbl("tmp32", [128, 1024], F32)
    x_buf = sbl("x_buf", [128, 2 * TOK], F32)
    xbf_buf = sbl("xbf_buf", [128, 2 * TOK], BF16)
    sq_buf = sbl("sq_buf", [128, 2 * TOK], BF16)
    x_stream = Stream(b, "lx", "sync", 2)
    x_stream.buf = x_buf
    xbf_stream = Stream(b, "lb", "gpsimd", 2)
    xbf_stream.buf = xbf_buf
    for th in range(2):
        emit_ln_stats_and_xn(
            b, bank, consts, xT[:, TOK * th:TOK * (th + 1)], x_stream, xbf_stream,
            sq_buf, rows, xn0, TOK, sx, sg, st, sv, ss,
            xn_cols=lambda dc, th=th: xn0[:, dc * SEQ + TOK * th:dc * SEQ + TOK * (th + 1)])
    XN0_V = b.tick(sv)
    p0.close()

    fences()
    p1 = ExitStack()
    sbl = lambda n, s, d: p1.enter_context(nc.sbuf_tensor(n, s, d))
    v_sb = sbl("v_sb", [128, 2 * DM], BF16)
    qk_buf = sbl("qk_buf", [128, 2 * DM], BF16)
    q_stg = sbl("q_stg", [128, 2 * SEQ], BF16)
    b.wait("tensor", swv, WV_G)
    b.wait("tensor", sv, XN0_V)
    vidx = 0
    v_evac = {}
    vdma_tick = [0, 0]
    for tc in range(SEQ // 128):
        for fg in range(4):
            bk = vidx % 2
            if vidx >= 2:
                b.wait("tensor", ss, v_evac[vidx - 2])
            for dc in range(NCH):
                b.op("tensor", lambda e, bk=bk, tc=tc, fg=fg, dc=dc: e.matmul(
                    bank(bk), xn0[:, dc * SEQ + 128 * tc:dc * SEQ + 128 * (tc + 1)],
                    wv_sb[:, dc * DM + 512 * fg:dc * DM + 512 * (fg + 1)],
                    start=(dc == 0), stop=(dc == NCH - 1)),
                    inc=(st, 1) if dc == NCH - 1 else None)
            b.wait("scalar", st, b.tick(st))
            vslot = tc % 2
            if fg == 0:
                b.wait("scalar", sf[vslot], vdma_tick[vslot])
            b.op("scalar", lambda e, bk=bk, vslot=vslot, fg=fg: e.activation(
                v_sb[:, vslot * DM + 512 * fg:vslot * DM + 512 * (fg + 1)],
                bank(bk), AF.Copy), inc=(ss, 1))
            v_evac[vidx] = b.tick(ss)
            vidx += 1
        vslot = tc % 2
        b.wait("sync", ss, v_evac[vidx - 1])
        b.wait("sync", sf[vslot], b.tick(sf[vslot]))
        for hh in range(16):
            b.op("sync", lambda e, hh=hh, tc=tc, vslot=vslot: e.dma_start(
                out=v0hm_d[SEQ * hh + 128 * tc:SEQ * hh + 128 * (tc + 1), :],
                in_=v_sb[:, vslot * DM + 128 * hh:vslot * DM + 128 * (hh + 1)]),
                inc=(sf[vslot], 16))
        vdma_tick[vslot] = b.tick(sf[vslot])
    wstream = Stream(b, "wqk", "gpsimd", 2)
    qk_evac = {}
    qkbank_tick = [0, 0]
    qdma_tick = [0, 0]
    for ji, (which, oc) in enumerate([("q", oc) for oc in range(NCH)] +
                                     [("k", oc) for oc in range(NCH)]):
        wsrc = wq0 if which == "q" else wk0
        src = wsrc[:, 128 * oc:128 * (oc + 1)].rearrange("(c p) n -> p c n", p=128)
        slot = ji % 2
        dst = qk_buf[:, slot * DM:(slot + 1) * DM].rearrange("p (c n) -> p c n", n=128)
        wsem, wtk, ws = wstream.load(dst, src)
        b.wait("tensor", wsem, wtk)
        nh4 = 2 if which == "q" else 4
        for h4 in range(nh4):
            bk = 2 + h4 % 2
            b.wait("tensor", ss, qkbank_tick[h4 % 2])
            for dc in range(NCH):
                b.op("tensor", lambda e, bk=bk, slot=slot, dc=dc, h4=h4: e.matmul(
                    bank(bk),
                    qk_buf[:, slot * DM + 128 * dc:slot * DM + 128 * (dc + 1)],
                    xn0[:, dc * SEQ + 512 * h4:dc * SEQ + 512 * (h4 + 1)],
                    start=(dc == 0), stop=(dc == NCH - 1)),
                    inc=(st, 1) if dc == NCH - 1 else None)
            mm_t = b.tick(st)
            qslot = ji % 2
            scale = RS128 if which == "q" else 1.0
            b.wait("scalar", st, mm_t)
            if h4 == 0:
                b.wait("scalar", sf[qslot], qdma_tick[qslot])
            b.op("scalar", lambda e, qslot=qslot, h4=h4, scale=scale, bk=bk: e.activation(
                q_stg[:, qslot * SEQ + 512 * h4:qslot * SEQ + 512 * (h4 + 1)],
                bank(bk), AF.Copy, scale=scale), inc=(ss, 1))
            qk_evac[(ji, h4)] = b.tick(ss)
            qkbank_tick[h4 % 2] = b.tick(ss)
        wstream.consumed(ws, "st", b.tick(st))
        b.wait("sync", ss, qk_evac[(ji, nh4 - 1)])
        dstd = q0_d if which == "q" else k0_d
        wid = 512 * nh4
        qdma_tick[ji % 2] = sp_out("sync", ji % 2,
                                   dstd[128 * oc:128 * (oc + 1), :],
                                   q_stg[:, (ji % 2) * SEQ:(ji % 2) * SEQ + wid])
    p1.close()
    wv_cm.__exit__(None, None, None)

    fences()
    p2 = ExitStack()
    sbl = lambda n, s, d: p2.enter_context(nc.sbuf_tensor(n, s, d))
    w1buf = sbl("w1buf", [128, 2 * DM], BF16)
    w3buf = sbl("w3buf", [128, 2 * DM], BF16)
    gate_s = sbl("gate_s", [128, 2 * 512], F32)
    hid_bf = sbl("hid_bf", [128, 2 * 512], BF16)
    h1_bf = sbl("h1_bf", [128, 2 * 512], BF16)
    w1s = Stream(b, "w1s", "gpsimd", 2)
    w3s = Stream(b, "w3s", "gpsimd", 2)
    b.wait("tensor", sid, CONST_LOAD)
    silu_tick = [0, 0]
    mul_tick = [0, 0]
    h1evac_tick = [0, 0]
    fwdma_tick = [0, 0]
    h1pass_tick2 = [0, 0]
    p2_pending = [None]
    for hc in range(NHC):
        wpar = hc % 2
        w1sem, w1tk, w1slot = w1s.load(
            w1buf[:, wpar * DM:(wpar + 1) * DM].rearrange("p (c n) -> p c n", n=128),
            w1t[:, 128 * hc:128 * (hc + 1)].rearrange("(c p) n -> p c n", p=128))
        w3sem, w3tk, w3slot = w3s.load(
            w3buf[:, wpar * DM:(wpar + 1) * DM].rearrange("p (c n) -> p c n", n=128),
            w3t[:, 128 * hc:128 * (hc + 1)].rearrange("(c p) n -> p c n", p=128))
        b.wait("tensor", w1sem, w1tk)
        b.wait("tensor", w3sem, w3tk)
        last_mm = 0
        for tq in range(2):
            par = tq % 2
            gb, lb, hb = 3 * par, 3 * par + 1, 3 * par + 2
            b.wait("tensor", ss, silu_tick[par])
            b.wait("tensor", sv, mul_tick[par])
            for dc in range(NCH):
                b.op("tensor", lambda e, gb=gb, wpar=wpar, dc=dc, tq=tq: e.matmul(
                    bank(gb), w1buf[:, wpar * DM + 128 * dc:wpar * DM + 128 * (dc + 1)],
                    xn0[:, dc * SEQ + 512 * tq:dc * SEQ + 512 * (tq + 1)],
                    start=(dc == 0), stop=(dc == NCH - 1)))
            for dc in range(NCH):
                b.op("tensor", lambda e, lb=lb, wpar=wpar, dc=dc, tq=tq: e.matmul(
                    bank(lb), w3buf[:, wpar * DM + 128 * dc:wpar * DM + 128 * (dc + 1)],
                    xn0[:, dc * SEQ + 512 * tq:dc * SEQ + 512 * (tq + 1)],
                    start=(dc == 0), stop=(dc == NCH - 1)),
                    inc=(st, 1) if dc == NCH - 1 else None)
            up_t = b.tick(st)
            last_mm = up_t
            if p2_pending[0] is not None:
                p2_pending[0]()
                p2_pending[0] = None
            b.wait("scalar", st, up_t)
            b.op("scalar", lambda e, gb=gb, par=par: e.activation(
                gate_s[:, 512 * par:512 * (par + 1)], bank(gb), AF.Silu),
                inc=(ss, 1))
            silu_tick[par] = b.tick(ss)
            b.wait("vector", ss, silu_tick[par])
            b.wait("vector", st, h1pass_tick2[par])
            b.op("vector", lambda e, lb=lb, par=par: e.tensor_mul(
                hid_bf[:, 512 * par:512 * (par + 1)], bank(lb),
                gate_s[:, 512 * par:512 * (par + 1)]), inc=(sv, 1))
            mul_tick[par] = b.tick(sv)
            mt = mul_tick[par]

            def p2tail(hc=hc, tq=tq, par=par, hb=hb, mt=mt):
                b.wait("tensor", sv, mt)
                b.wait("tensor", ss, h1evac_tick[par])
                b.op("tensor", lambda e, hb=hb, par=par: e.matmul(
                    bank(hb), h128_sb[:, :], hid_bf[:, 512 * par:512 * (par + 1)],
                    start=True, stop=True), inc=(st, 1))
                h1pass_tick2[par] = b.tick(st)
                b.wait("scalar", st, h1pass_tick2[par])
                b.wait("scalar", sf[par], fwdma_tick[par])
                b.op("scalar", lambda e, hb=hb, par=par: e.copy(
                    h1_bf[:, 512 * par:512 * (par + 1)], bank(hb)), inc=(ss, 1))
                h1evac_tick[par] = b.tick(ss)
                b.wait("sync", ss, h1evac_tick[par])
                fwdma_tick[par] = sp_out(
                    "sync", par,
                    fw_d[128 * hc:128 * (hc + 1), 512 * tq:512 * (tq + 1)],
                    h1_bf[:, 512 * par:512 * (par + 1)])

            p2_pending[0] = p2tail
        w1s.consumed(w1slot, "st", last_mm)
        w3s.consumed(w3slot, "st", last_mm)
    p2_pending[0]()
    p2_pending[0] = None
    p2.close()
    xn0_cm.__exit__(None, None, None)

    fences()
    hw_cm = nc.sbuf_tensor("hw_sb", [128, NHC * 1024], BF16)
    hw_sb = hw_cm.__enter__()
    p3 = ExitStack()
    sbl = lambda n, s, d: p3.enter_context(nc.sbuf_tensor(n, s, d))
    yb_buf = sbl("yb_buf", [128, 4 * TOK], BF16)
    ybs0 = Stream(b, "ybs0", "sync", 4)
    ybs1 = Stream(b, "ybs1", "scalar", 4)
    b.wait("sync", sf[0], b.tick(sf[0]))
    b.wait("sync", sf[1], b.tick(sf[1]))
    b.wait("tensor", ss, b.tick(ss))
    b.wait("tensor", sv, b.tick(sv))
    b.wait("tensor", sid, CONST_LOAD)
    pairevac_tick = [0, 0, 0]
    for pp in range(64):
        par = pp % 4
        ysem, ytk, yslot = ybs0.load(
            yb_buf[0:64, par * TOK:(par + 1) * TOK], fw_d[2 * pp::128, :])
        ysem2, ytk2, yslot2 = ybs1.load(
            yb_buf[64:128, par * TOK:(par + 1) * TOK], fw_d[2 * pp + 1::128, :])
        r = pp % 3
        b.wait("tensor", ysem, ytk)
        b.wait("tensor", ysem2, ytk2)
        b.wait("tensor", ss, pairevac_tick[r])
        for j2 in range(2):
            for hq in range(2):
                b.op("tensor", lambda e, r=r, par=par, j2=j2, hq=hq: e.matmul(
                    pall[64 * j2:64 * (j2 + 1),
                         512 * (2 * r + hq):512 * (2 * r + hq + 1)],
                    h64_sb[64 * j2:64 * (j2 + 1), :],
                    yb_buf[64 * j2:64 * (j2 + 1),
                           par * TOK + 512 * hq:par * TOK + 512 * (hq + 1)],
                    start=True, stop=True),
                    inc=(st, 1) if (j2 == 1 and hq == 1) else None)
        p2_t = b.tick(st)
        b.wait("scalar", st, p2_t)
        b.op("scalar", lambda e, r=r, pp=pp: e.activation(
            hw_sb[:, pp * 1024:(pp + 1) * 1024],
            pall[:, 512 * 2 * r:512 * (2 * r + 2)], AF.Copy,
            scale=FW_SCALE), inc=(ss, 1))
        pairevac_tick[r] = b.tick(ss)
        ybs0.consumed(yslot, "st", p2_t)
        ybs1.consumed(yslot2, "st", p2_t)
    p3.close()
    fences()
    p4 = ExitStack()
    sbl = lambda n, s, d: p4.enter_context(nc.sbuf_tensor(n, s, d))
    w2buf = sbl("w2buf", [128, 2 * DH], BF16)
    g_stg = sbl("g_stg", [128, 2 * 1024], F32)
    w2s = Stream(b, "w2s", "gpsimd", 2)
    b.wait("sync", sf[0], b.tick(sf[0]))
    b.wait("sync", sf[1], b.tick(sf[1]))
    gsil_tick = [0, 0]
    gdma_tick = [0, 0]
    for dc in range(NCH):
        par = dc % 2
        w2sem, w2tk, w2slot = w2s.load(
            w2buf[:, par * DH:(par + 1) * DH].rearrange("p (c n) -> p c n", n=128),
            w2tp[:, 128 * dc:128 * (dc + 1)].rearrange("(c p) n -> p c n", p=128))
        b.wait("tensor", w2sem, w2tk)
        b.wait("tensor", ss, gsil_tick[par])
        for h in range(2):
            dbk = 2 * par + h
            for hcc in range(NHC):
                b.op("tensor", lambda e, dbk=dbk, hcc=hcc, par=par, h=h: e.matmul(
                    bank(dbk),
                    w2buf[:, par * DH + hcc * 128:par * DH + hcc * 128 + 128],
                    hw_sb[:, hcc * 1024 + 512 * h:hcc * 1024 + 512 * (h + 1)],
                    start=(hcc == 0), stop=(hcc == NHC - 1)),
                    inc=(st, 1) if hcc == NHC - 1 else None)
        down_t = b.tick(st)
        w2s.consumed(w2slot, "st", down_t)
        b.wait("scalar", st, down_t)
        b.wait("scalar", sf[par], gdma_tick[par])
        b.op("scalar", lambda e, par=par: e.activation(
            g_stg[:, par * 1024:(par + 1) * 1024],
            pall[:, 512 * 2 * par:512 * (2 * par + 2)], AF.Silu), inc=(ss, 1))
        gsil_tick[par] = b.tick(ss)
        b.wait("sync", ss, gsil_tick[par])
        gdma_tick[par] = sp_out("sync", par,
                                g_d[128 * dc:128 * (dc + 1), :],
                                g_stg[:, par * 1024:(par + 1) * 1024])
    p4.close()
    hw_cm.__exit__(None, None, None)

    fences()
    o_cm = nc.sbuf_tensor("o_sb", [128, NCH * TOK], BF16)
    o_sb = o_cm.__enter__()
    p5 = ExitStack()
    sbl = lambda n, s, d: p5.enter_context(nc.sbuf_tensor(n, s, d))
    q_hb = sbl("q_hb", [128, 2 * TOK], BF16)
    k_hb = sbl("k_hb", [128, 2 * SEQ], BF16)
    v_hb = sbl("v_hb", [128, 2 * SEQ], BF16)
    p_sb = sbl("p_sb", [128, 3 * SEQ], BF16)
    dg5 = sbl("dg5", [128, 3 * 128], BF16)
    pt_sb = sbl("pt_sb", [128, SEQ], BF16)
    qs5 = Stream(b, "qs5", "sync", 2)
    ks5 = Stream(b, "ks5", "sync", 2)
    vs5 = Stream(b, "vs5", "sync", 2)
    b.wait("sync", sf[0], b.tick(sf[0]))
    b.wait("sync", sf[1], b.tick(sf[1]))
    S5 = pall[:, 0:2048]
    SEQ_HLF5 = SEQ
    OT5 = lambda sl: pall[:, (4 + sl) * 512: (4 + sl) * 512 + 128]
    PTP5 = lambda j: ptp[:, 512 * (j % 2): 512 * (j % 2) + 128]
    pt_evac_tick = [0, 0]
    ot_evac_tick = [0, 0]
    exp_tick5 = [0, 0, 0]
    exp_half5 = [0, 0, 0]
    exp_full_prev = [0]
    recip_tick5 = [0, 0, 0]
    pn_tick5 = [0, 0, 0]
    trdone_tick5 = [0, 0, 0]
    tails5 = []
    head_res = {}
    units = [(h, qb) for h in range(16) for qb in range(8)]

    def flush_tail5():
        tail, meta = tails5.pop(0)
        tail()
        hh, qq = meta
        if qq == 7:
            qsl, ksl, vsl, s_lastt, av_lastt = head_res[hh]
            qs5.consumed(qsl, "st", s_lastt[0])
            ks5.consumed(ksl, "st", s_lastt[0])
            vs5.consumed(vsl, "st", av_lastt[0])

    for u, (h, qb) in enumerate(units):
        up = u % 3
        hp = h % 2
        if qb == 0:
            qsem, qtk, qslot = qs5.load(q_hb[:, hp * TOK:(hp + 1) * TOK],
                                        q0_d[128 * h:128 * (h + 1), :])
            ksem, ktk, kslot = ks5.load(k_hb[:, hp * SEQ:(hp + 1) * SEQ],
                                        k0_d[128 * h:128 * (h + 1), :])
            vsem, vtk, vslot = vs5.load(
                v_hb[:, hp * SEQ:(hp + 1) * SEQ].rearrange("p (c n) -> p c n", n=128),
                v0hm_d[SEQ * h:SEQ * (h + 1), :].rearrange("(c p) n -> p c n", p=128))
            s_lastt = [0]
            av_lastt = [0]
            head_res[h] = (qslot, kslot, vslot, s_lastt, av_lastt)
        lks = _l0_sched(qb)
        nblk = len(lks)
        b.wait("tensor", qsem, qtk)
        b.wait("tensor", ksem, ktk)
        if u == 0:
            b.wait("tensor", sid, CONST_LOAD)
        runs = []
        i0 = 0
        while i0 < nblk:
            i1 = i0
            while i1 + 1 < nblk and lks[i1 + 1] == lks[i1] + 1:
                i1 += 1
            runs.append((i0, i1))
            i0 = i1 + 1
        pieces = []
        for (i0, i1) in runs:
            base_lk = lks[i0]
            width = 128 * (i1 - i0 + 1)
            off = 128 * i0
            cur = 0
            while cur < width:
                w = min(512 - (off + cur) % 512, width - cur, 512)
                pieces.append((off + cur, 128 * base_lk + cur, w))
                cur += w
        waited2 = False
        b.wait("tensor", ss, exp_half5[(u - 1) % 3] if u >= 1 else 0)
        piece_ticks = []
        for (doff, soff, w) in pieces:
            if doff + w > 1024 and not waited2:
                b.wait("tensor", ss, exp_full_prev[0])
                waited2 = True
            b.op("tensor", lambda e, hp=hp, qb=qb, doff=doff, soff=soff,
                 w=w: e.matmul(
                S5[:, doff:doff + w],
                q_hb[:, hp * TOK + 128 * qb:hp * TOK + 128 * (qb + 1)],
                k_hb[:, hp * SEQ + soff:hp * SEQ + soff + w],
                start=True, stop=True), inc=(st, 1))
            piece_ticks.append(b.tick(st))
        s_mm = b.tick(st)
        s_lastt[0] = s_mm
        b.wait("vector", st, s_mm)
        if u == 0:
            b.wait("vector", sid, CONST_LOAD)
        b.op("vector", lambda e, qb=qb: e.tensor_tensor(
            S5[:, 128 * qb:128 * (qb + 9)], S5[:, 128 * qb:128 * (qb + 9)],
            DIAG_A8, ALU.add), inc=(sv, 1))
        b.guard("vector", sv)
        b.wait("vector", ss, exp_tick5[up])
        b.op("vector", lambda e, nblk=nblk, up=up: e.reduce_max(
            mneg[:, up:up + 1], S5[:, 0:128 * nblk], axis=AX.X, negate=True),
            inc=(sv, 1))
        mneg_v = b.tick(sv)
        b.wait("scalar", sv, mneg_v)
        b.wait("scalar", sv, recip_tick5[up])
        b.wait("scalar", st, trdone_tick5[up])
        w1_ = min(128 * nblk, 1024)
        w2_ = 128 * nblk - w1_
        b.op("scalar", lambda e, w1_=w1_, up=up: e.activation(
            p_sb[:, up * SEQ + 0:up * SEQ + w1_],
            S5[:, 0:w1_], AF.Exp, bias=mneg[:, up:up + 1],
            accum_out=l_col[:, 2 * up:2 * up + 1]), inc=(ss, 1))
        exp_half5[up] = b.tick(ss)
        if w2_ > 0:
            b.op("scalar", lambda e, w1_=w1_, w2_=w2_, up=up: e.activation(
                p_sb[:, up * SEQ + w1_:up * SEQ + w1_ + w2_],
                S5[:, w1_:w1_ + w2_], AF.Exp, bias=mneg[:, up:up + 1],
                accum_out=l_col[:, 2 * up + 1:2 * up + 2]), inc=(ss, 1))
        exp_tick5[up] = b.tick(ss)
        exp_full_prev[0] = b.tick(ss)
        b.wait("vector", ss, b.tick(ss))
        if w2_ > 0:
            b.op("vector", lambda e, up=up: e.tensor_add(
                l_col[:, 2 * up:2 * up + 1], l_col[:, 2 * up:2 * up + 1],
                l_col[:, 2 * up + 1:2 * up + 2]), inc=(sv, 1))
            b.guard("vector", sv)
        b.op("vector", lambda e, up=up: e.reciprocal(
            linv_col[:, up:up + 1], l_col[:, 2 * up:2 * up + 1]), inc=(sv, 1))
        recip_tick5[up] = b.tick(sv)
        b.guard("vector", sv)
        b.wait("vector", st, trdone_tick5[up])
        b.op("vector", lambda e, up=up: e.tensor_scalar_mul(
            dg5[:, up * 128:(up + 1) * 128], ident[:, :],
            linv_col[:, up:up + 1]), inc=(sv, 1))
        pn_tick5[up] = b.tick(sv)
        pn_v = b.tick(sv)
        exp_u5 = exp_tick5[up]

        def tail(h=h, qb=qb, hp=hp, lks=lks, nblk=nblk, up=up, pn_v=pn_v,
                 vsem=vsem, vtk=vtk, av_lastt=av_lastt, u=u, exp_u5=exp_u5):
            tp = u % 2
            b.wait("tensor", sv, pn_v)
            b.wait("tensor", ss, exp_u5)
            if qb == 0:
                b.wait("tensor", vsem, vtk)
            b.wait("tensor", ss, ot_evac_tick[tp])
            grp_tick = {}
            ngr = ceil_div(nblk, 4)

            def emit_av(g):
                gsem, gtk = grp_tick[g]
                b.wait("tensor", gsem, gtk)
                for p in range(4 * g, min(4 * g + 4, nblk)):
                    lk = lks[p]
                    b.op("tensor", lambda e, p=p, lk=lk, tp=tp, hp=hp,
                         nblk=nblk: e.matmul(
                        OT5(tp), v_hb[:, hp * SEQ + 128 * lk:hp * SEQ + 128 * (lk + 1)],
                        pt_sb[:, 128 * p:128 * (p + 1)],
                        start=(p == 0), stop=(p == nblk - 1)),
                        inc=(st, 1) if p == nblk - 1 else None)

            for g in range(ngr):
                blocks = list(range(4 * g, min(4 * g + 4, nblk)))
                bj = g % 2
                b.wait("tensor", ss, pt_evac_tick[bj])
                for ip in blocks:
                    b.op("tensor", lambda e, ip=ip, bj=bj, up=up: e.matmul(
                        ptp[:, 512 * bj + 128 * (ip % 4):
                            512 * bj + 128 * (ip % 4) + 128],
                        p_sb[:, up * SEQ + 128 * ip:up * SEQ + 128 * (ip + 1)],
                        dg5[:, up * 128:(up + 1) * 128],
                        start=True, stop=True),
                        inc=(st, 1) if ip == blocks[-1] else None)
                tr_t = b.tick(st)
                b.wait("scalar", st, tr_t)
                nb = len(blocks)
                b.op("scalar", lambda e, g=g, bj=bj, nb=nb: e.copy(
                    pt_sb[:, 512 * g:512 * g + 128 * nb],
                    ptp[:, 512 * bj:512 * bj + 128 * nb]), inc=(ss, 1))
                pt_evac_tick[bj] = b.tick(ss)
                grp_tick[g] = (ss, pt_evac_tick[bj])
                if g >= 1:
                    emit_av(g - 1)
            trdone_tick5[up] = b.tick(st)
            emit_av(ngr - 1)
            av_t = b.tick(st)
            av_lastt[0] = av_t
            b.wait("scalar", st, av_t)
            b.op("scalar", lambda e, h=h, qb=qb, tp=tp: e.copy(
                o_sb[:, h * TOK + 128 * qb:h * TOK + 128 * (qb + 1)],
                OT5(tp)), inc=(ss, 1))
            ot_evac_tick[tp] = b.tick(ss)

        tails5.append((tail, (h, qb)))
        if len(tails5) > 2:
            flush_tail5()
    while tails5:
        flush_tail5()
    p5.close()

    fences()
    p6 = ExitStack()
    sbl = lambda n, s, d: p6.enter_context(nc.sbuf_tensor(n, s, d))
    wobuf = sbl("wobuf", [128, 2 * DM], BF16)
    g_buf6 = sbl("g_buf6", [128, 2 * TOK], F32)
    x_buf6 = sbl("x_buf6", [128, 2 * TOK], F32)
    tmp6 = sbl("tmp6", [128, 512], F32)
    x1_stg = sbl("x1_stg", [128, 2 * TOK], F32)
    wos = Stream(b, "wos", "gpsimd", 2)
    gs6 = Stream(b, "gs6", "sync", 2)
    xs6 = Stream(b, "xs6", "sync", 2)
    b.wait("sync", sf[0], b.tick(sf[0]))
    b.wait("sync", sf[1], b.tick(sf[1]))
    OLOAD = 0
    sbank_tick = [0, 0]
    x1dma_tick = [0, 0]
    for oc in range(NCH):
        par = oc % 2
        wsem, wtk, wslot = wos.load(
            wobuf[:, par * DM:(par + 1) * DM].rearrange("p (c n) -> p c n", n=128),
            wo0[:, 128 * oc:128 * (oc + 1)].rearrange("(c p) n -> p c n", p=128))
        gsem, gtk, gslot = gs6.load(g_buf6[:, par * TOK:(par + 1) * TOK],
                                    g_d[128 * oc:128 * (oc + 1), :])
        xsem, xtk, xslot = xs6.load(x_buf6[:, par * TOK:(par + 1) * TOK],
                                    xT[128 * oc:128 * (oc + 1), 0:TOK])
        b.wait("tensor", wsem, wtk)
        res_v = 0
        for tq in range(2):
            sbk = 2 * par + tq % 2
            b.wait("tensor", sv, sbank_tick[par])
            for dc in range(NCH):
                b.op("tensor", lambda e, sbk=sbk, par=par, dc=dc, tq=tq: e.matmul(
                    bank(sbk), wobuf[:, par * DM + 128 * dc:par * DM + 128 * (dc + 1)],
                    o_sb[:, dc * TOK + 512 * tq:dc * TOK + 512 * (tq + 1)],
                    start=(dc == 0), stop=(dc == NCH - 1)),
                    inc=(st, 1) if dc == NCH - 1 else None)
            s_t = b.tick(st)
            b.wait("vector", st, s_t)
            b.wait("vector", gsem, gtk)
            b.wait("vector", xsem, xtk)
            if tq == 0:
                b.wait("vector", so[par], x1dma_tick[par])
            b.guard("vector", sv)
            b.op("vector", lambda e, sbk=sbk, par=par, tq=tq: e.tensor_mul(
                tmp6[:, :], bank(sbk),
                g_buf6[:, par * TOK + 512 * tq:par * TOK + 512 * (tq + 1)]),
                inc=(sv, 1))
            b.guard("vector", sv)
            b.op("vector", lambda e, par=par, tq=tq: e.tensor_add(
                x1_stg[:, par * TOK + 512 * tq:par * TOK + 512 * (tq + 1)],
                tmp6[:, :],
                x_buf6[:, par * TOK + 512 * tq:par * TOK + 512 * (tq + 1)]),
                inc=(sv, 1))
            res_v = b.tick(sv)
        sbank_tick[par] = res_v
        wos.consumed(wslot, "st", b.tick(st))
        gs6.consumed(gslot, "sv", res_v)
        xs6.consumed(xslot, "sv", res_v)
        b.wait("sync", sv, res_v)
        b.wait("sync", so[par], b.tick(so[par]))
        b.op("sync", lambda e, oc=oc, par=par: e.dma_start(
            out=x1_d[128 * oc:128 * (oc + 1), :],
            in_=x1_stg[:, par * TOK:(par + 1) * TOK]), inc=(so[par], 16))
        x1dma_tick[par] = b.tick(so[par])
    p6.close()
    o_cm.__exit__(None, None, None)

    fences()
    xn1_cm = nc.sbuf_tensor("xn1", [128, NCH * TOK], BF16)
    xn1 = xn1_cm.__enter__()
    p7 = ExitStack()
    sbl = lambda n, s, d: p7.enter_context(nc.sbuf_tensor(n, s, d))
    rows = {k: sbl("r7" + k, [1, TOK], F32) for k in ("m_row", "ex2", "m2", "var", "sqv", "rstd")}
    rows["tmp32"] = sbl("r7tmp32", [128, 1024], F32)
    x_buf7 = sbl("x_buf7", [128, 2 * TOK], F32)
    xbf_buf7 = sbl("xbf_buf7", [128, 2 * TOK], BF16)
    sq_buf7 = sbl("sq_buf7", [128, 2 * TOK], BF16)
    x_stream7 = Stream(b, "lx7", "sync", 2)
    x_stream7.buf = x_buf7
    xbf_stream7 = Stream(b, "lb7", "gpsimd", 2)
    xbf_stream7.buf = xbf_buf7
    b.wait("sync", so[0], b.tick(so[0]))
    b.wait("sync", so[1], b.tick(so[1]))
    b.wait("gpsimd", so[0], b.tick(so[0]))
    b.wait("gpsimd", so[1], b.tick(so[1]))
    emit_ln_stats_and_xn(
        b, bank, consts, x1_d[:, :], x_stream7, xbf_stream7,
        sq_buf7, rows, xn1, TOK, sx, sg, st, sv, ss,
        xn_cols=lambda dc: xn1[:, dc * TOK:(dc + 1) * TOK])
    XN1_V = b.tick(sv)
    p7.close()

    fences()
    y1_cm = nc.sbuf_tensor("y1sb", [128, NCH * TOK], BF16)
    y1_sb = y1_cm.__enter__()
    k1_cm = nc.sbuf_tensor("k1sb", [128, 4 * SEQ], BF16)
    k1_sb = k1_cm.__enter__()
    v1_cm = nc.sbuf_tensor("v1sb", [128, 4 * SEQ], BF16)
    v1_sb = v1_cm.__enter__()
    q1_cm = nc.sbuf_tensor("q1sb", [128, NCH * TOK], BF16)
    q1_sb = q1_cm.__enter__()
    p8 = ExitStack()
    sbl = lambda n, s, d: p8.enter_context(nc.sbuf_tensor(n, s, d))
    wv1_sb = sbl("wv1_sb", [128, NCH * 512], BF16)
    vz8 = sbl("vz8", [128, 2 * 1024], F32)
    for dc in range(NCH):
        b.dma("gpsimd", wv1_sb[:, dc * 512:(dc + 1) * 512],
              wv1[128 * dc:128 * (dc + 1), :], swv1)
    WV1_G = b.tick(swv1)
    b.wait("tensor", swv1, WV1_G)
    b.wait("tensor", sv, XN1_V)
    b.wait("tensor", sid, CONST_LOAD)
    b.wait("vector", sid, CONST_LOAD)
    vevac8 = [0, 0]
    vspill8 = [0, 0]
    for tc in range(TOK // 128):
        par = tc % 2
        b.wait("tensor", sv, vevac8[par])
        for dc in range(NCH):
            b.op("tensor", lambda e, par=par, tc=tc, dc=dc: e.matmul(
                bank(par), xn1[:, dc * TOK + 128 * tc:dc * TOK + 128 * (tc + 1)],
                wv1_sb[:, dc * 512:(dc + 1) * 512],
                start=(dc == 0), stop=(dc == NCH - 1)),
                inc=(st, 1) if dc == NCH - 1 else None)
        v_t = b.tick(st)
        b.wait("vector", st, v_t)
        b.wait("vector", sf[par], vspill8[par])
        b.op("vector", lambda e, par=par: e.tensor_scalar_mul(
            vz8[:, par * 1024:par * 1024 + 512], bank(par), msel_sb[:, 0:1]),
            inc=(sv, 1))
        b.op("vector", lambda e, par=par: e.tensor_scalar_mul(
            vz8[:, par * 1024 + 512:par * 1024 + 1024], bank(par),
            msel_sb[:, 1:2]), inc=(sv, 1))
        vevac8[par] = b.tick(sv)
        b.wait("sync", sv, vevac8[par])
        b.wait("sync", sf[par], b.tick(sf[par]))
        b.op("sync", lambda e, tc=tc, par=par: e.dma_start(
            out=v1pad_d[128 * tc:128 * (tc + 1), :],
            in_=vz8[:, par * 1024:par * 1024 + 512]), inc=(sf[par], 16))
        b.op("sync", lambda e, tc=tc, par=par: e.dma_start(
            out=v1pad_d[TOK + 128 * tc:TOK + 128 * (tc + 1), :],
            in_=vz8[:, par * 1024 + 512:par * 1024 + 1024]), inc=(sf[par], 16))
        vspill8[par] = b.tick(sf[par])
    b.wait("gpsimd", sf[0], b.tick(sf[0]))
    b.wait("gpsimd", sf[1], b.tick(sf[1]))
    b.op("gpsimd", lambda e: e.collective_compute(
        "AllReduce", ALU.add,
        replica_groups=[[0, 1], [2, 3], [4, 5], [6, 7]],
        ins=[v1pad_d], outs=[v1ag_d]), inc=(scc, 1))
    p8.close()
    p8 = ExitStack()
    sbl = lambda n, s, d: p8.enter_context(nc.sbuf_tensor(n, s, d))
    wq1_buf = sbl("wq1_buf", [128, 2 * DM], BF16)
    qsq8 = sbl("qsq8", [128, TOK], BF16)
    rstd_bc8 = sbl("rstd_bc8", [128, TOK], F32)
    qrow8 = {k: sbl("q8" + k, [1, TOK], F32) for k in ("sqv", "rstd")}
    fences()
    wq1s = Stream(b, "wq1s", "gpsimd", 2)
    qmul8 = 0
    for qh in range(16):
        par = qh % 2
        wsem, wtk, wslot = wq1s.load(
            wq1_buf[:, par * DM:(par + 1) * DM].rearrange("p (c n) -> p c n", n=128),
            wq1[:, 128 * qh:128 * (qh + 1)].rearrange("(c p) n -> p c n", p=128))
        b.wait("tensor", wsem, wtk)
        b.wait("tensor", sv, qmul8)
        b.wait("tensor", ss, b.tick(ss))
        for h in range(2):
            for dc in range(NCH):
                b.op("tensor", lambda e, h=h, par=par, dc=dc: e.matmul(
                    bank(h), wq1_buf[:, par * DM + 128 * dc:par * DM + 128 * (dc + 1)],
                    xn1[:, dc * TOK + 512 * h:dc * TOK + 512 * (h + 1)],
                    start=(dc == 0), stop=(dc == NCH - 1)),
                    inc=(st, 1) if dc == NCH - 1 else None)
        q_t = b.tick(st)
        wq1s.consumed(wslot, "st", q_t)
        b.wait("scalar", st, q_t)
        b.op("scalar", lambda e: e.activation(qsq8[:, :], pall[:, 0:1024], AF.Square),
             inc=(ss, 1))
        qsq_t = b.tick(ss)
        b.wait("tensor", ss, qsq_t)
        for h in range(2):
            b.op("tensor", lambda e, h=h: e.matmul(
                pall[0:1, 512 * (2 + h):512 * (3 + h)], ones_bf[:, :],
                qsq8[:, 512 * h:512 * (h + 1)], start=True, stop=True),
                inc=(st, 1) if h == 1 else None)
        qs_t = b.tick(st)
        b.wait("scalar", st, qs_t)
        b.op("scalar", lambda e: e.activation(
            qrow8["sqv"][:, :], pall[0:1, 1024:2048], AF.Sqrt, scale=1.0 / 128.0,
            bias=epst[0:1, 1:2]), inc=(ss, 1))
        b.wait("vector", ss, b.tick(ss))
        b.op("vector", lambda e: e.reciprocal(qrow8["rstd"][:, :], qrow8["sqv"][:, :]),
             inc=(sv, 1))
        qr_v = b.tick(sv)
        b.wait("tensor", sv, qr_v)
        for h in range(2):
            b.op("tensor", lambda e, h=h: e.matmul(
                pall[:, 512 * (4 + h):512 * (5 + h)], ones_f[:, :],
                qrow8["rstd"][:, 512 * h:512 * (h + 1)], start=True, stop=True),
                inc=(st, 1) if h == 1 else None)
        bc_t = b.tick(st)
        b.wait("scalar", st, bc_t)
        b.wait("scalar", sv, qmul8)
        b.op("scalar", lambda e: e.copy(rstd_bc8[:, :], pall[:, 2048:3072]),
             inc=(ss, 1))
        bce_t = b.tick(ss)
        b.wait("vector", ss, bce_t)
        b.op("vector", lambda e, qh=qh: e.tensor_mul(
            q1_sb[:, qh * TOK:(qh + 1) * TOK], pall[:, 0:1024], rstd_bc8[:, :]),
            inc=(sv, 1))
        qmul8 = b.tick(sv)
    p8.close()

    p8 = ExitStack()
    sbl = lambda n, s, d: p8.enter_context(nc.sbuf_tensor(n, s, d))
    v_f8 = sbl("v_f8d", [128, 2 * 512], F32)
    sqv_f8 = sbl("sqv_f8", [128, 128], F32)
    vsum8 = sbl("vsum8", [128, 8], F32)
    vsd8 = sbl("vsd8", [128, 8], F32)
    vrstd8 = sbl("vrstd8", [128, 8], F32)
    k_tmp8 = sbl("k_tmp8", [128, 2 * 512], BF16)
    fences()
    vag = Stream(b, "vag", "sync", 2)
    b.wait("sync", scc, 1)
    kmul8 = [0, 0]
    ktr8 = [0, 0]
    trevac8 = [0, 0]
    for tc in range(SEQ // 128):
        par = tc % 2
        vsem, vtk, vslot = vag.load(v_f8[:, 512 * par:512 * (par + 1)],
                                    v1ag_d[128 * tc:128 * (tc + 1), :])
        b.wait("scalar", vsem, vtk)
        b.wait("scalar", sv, kmul8[par])
        for kvh in range(4):
            b.op("scalar", lambda e, par=par, tc=tc, kvh=kvh: e.activation(
                v1_sb[:, kvh * SEQ + 128 * tc:kvh * SEQ + 128 * (tc + 1)],
                v_f8[:, 512 * par + 128 * kvh:512 * par + 128 * (kvh + 1)],
                AF.Copy), inc=(ss, 1))
        VE8 = b.tick(ss)
        for kvh in range(4):
            col = 512 * par + 128 * kvh
            b.op("scalar", lambda e, col=col: e.activation(
                sqv_f8[:, 0:128], v_f8[:, col:col + 128], AF.Square), inc=(ss, 1))
            sq_t = b.tick(ss)
            b.wait("vector", ss, sq_t)
            b.op("vector", lambda e, kvh=kvh: e.reduce_sum(
                vsum8[:, kvh:kvh + 1], sqv_f8[:, 0:128], axis=AX.X), inc=(sv, 1))
            sum_v = b.tick(sv)
            b.wait("scalar", sv, sum_v)
            b.op("scalar", lambda e, kvh=kvh: e.activation(
                vsd8[:, kvh:kvh + 1], vsum8[:, kvh:kvh + 1], AF.Sqrt,
                scale=1.0 / 128.0, bias=epst[:, 1:2]), inc=(ss, 1))
            sd_t = b.tick(ss)
            b.wait("vector", ss, sd_t)
            b.op("vector", lambda e, kvh=kvh: e.reciprocal(
                vrstd8[:, kvh:kvh + 1], vsd8[:, kvh:kvh + 1]), inc=(sv, 1))
            b.guard("vector", sv)
            if kvh == 0:
                b.wait("vector", st, ktr8[par])
            b.op("vector", lambda e, col=col, kvh=kvh: e.tensor_scalar_mul(
                k_tmp8[:, col:col + 128], v_f8[:, col:col + 128],
                vrstd8[:, kvh:kvh + 1]), inc=(sv, 1))
        krow_v = b.tick(sv)
        kmul8[par] = krow_v
        vag.consumed(vslot, "sv", krow_v)
        b.wait("tensor", sv, krow_v)
        for kvh in range(4):
            j = kvh % 2
            b.wait("tensor", ss, trevac8[j])
            b.op("tensor", lambda e, par=par, kvh=kvh, j=j: e.matmul(
                ptp[:, 512 * j:512 * j + 128],
                k_tmp8[:, 512 * par + 128 * kvh:512 * par + 128 * (kvh + 1)],
                ident[:, :], start=True, stop=True), inc=(st, 1))
            tr_t = b.tick(st)
            b.wait("scalar", st, tr_t)
            b.op("scalar", lambda e, kvh=kvh, tc=tc, j=j: e.copy(
                k1_sb[:, kvh * SEQ + 128 * tc:kvh * SEQ + 128 * (tc + 1)],
                ptp[:, 512 * j:512 * j + 128]), inc=(ss, 1))
            trevac8[j] = b.tick(ss)
        ktr8[par] = b.tick(st)
    p8.close()

    fences()
    p9 = ExitStack()
    sbl = lambda n, s, d: p9.enter_context(nc.sbuf_tensor(n, s, d))
    p_sb9 = sbl("p_sb9", [128, 3 * SEQ], BF16)
    dg9 = sbl("dg9", [128, 3 * 128], BF16)
    pt_sb9 = sbl("pt_sb9", [128, SEQ], BF16)
    pt_evac9 = [0, 0]
    ot_evac9 = [0, 0]
    exp_tick9 = [0, 0, 0]
    exp_half9 = [0, 0, 0]
    exp_full9 = [0]
    recip_tick9 = [0, 0, 0]
    pn_tick9 = [0, 0, 0]
    trdone_tick9 = [0, 0, 0]
    tails9 = []
    b.wait("tensor", sv, b.tick(sv))
    b.wait("tensor", ss, b.tick(ss))
    units9 = [(qh, qc) for qh in range(16) for qc in range(TOK // 128)]
    for u, (qh, qc) in enumerate(units9):
        up = u % 3
        kvh = qh // 4
        kbase = kvh * SEQ
        b.wait("tensor", ss, exp_half9[(u - 1) % 3] if u >= 1 else 0)
        kg_ticks = []
        for kg in range(4):
            if kg == 2:
                b.wait("tensor", ss, exp_full9[0])
            b.op("tensor", lambda e, qh=qh, qc=qc, kg=kg, kbase=kbase: e.matmul(
                pall[:, 512 * kg:512 * (kg + 1)],
                q1_sb[:, qh * TOK + 128 * qc:qh * TOK + 128 * (qc + 1)],
                k1_sb[:, kbase + 512 * kg:kbase + 512 * (kg + 1)],
                start=True, stop=True), inc=(st, 1))
            kg_ticks.append(b.tick(st))
        s_mm = b.tick(st)
        for kg in range(4):
            b.wait("vector", st, kg_ticks[kg])
            b.op("vector", lambda e, kg=kg: e.reduce_max(
                pmax[:, kg:kg + 1], pall[:, 512 * kg:512 * (kg + 1)],
                axis=AX.X), inc=(sv, 1))
        b.wait("vector", ss, exp_tick9[up])
        b.op("vector", lambda e, up=up: e.reduce_max(
            mneg[:, up:up + 1], pmax[:, 0:4], axis=AX.X, negate=True),
            inc=(sv, 1))
        mneg_v = b.tick(sv)
        b.wait("scalar", sv, mneg_v)
        b.wait("scalar", sv, recip_tick9[up])
        b.wait("scalar", st, trdone_tick9[up])
        b.op("scalar", lambda e, up=up: e.activation(
            p_sb9[:, up * SEQ:up * SEQ + 1024], pall[:, 0:1024], AF.Exp,
            bias=mneg[:, up:up + 1], accum_out=l_col[:, 2 * up:2 * up + 1]),
            inc=(ss, 1))
        exp_half9[up] = b.tick(ss)
        b.op("scalar", lambda e, up=up: e.activation(
            p_sb9[:, up * SEQ + 1024:(up + 1) * SEQ], pall[:, 1024:2048],
            AF.Exp, bias=mneg[:, up:up + 1],
            accum_out=l_col[:, 2 * up + 1:2 * up + 2]), inc=(ss, 1))
        exp_tick9[up] = b.tick(ss)
        exp_full9[0] = b.tick(ss)
        b.wait("vector", ss, b.tick(ss))
        b.op("vector", lambda e, up=up: e.tensor_add(
            l_col[:, 2 * up:2 * up + 1], l_col[:, 2 * up:2 * up + 1],
            l_col[:, 2 * up + 1:2 * up + 2]), inc=(sv, 1))
        b.guard("vector", sv)
        b.op("vector", lambda e, up=up: e.reciprocal(
            linv_col[:, up:up + 1], l_col[:, 2 * up:2 * up + 1]), inc=(sv, 1))
        recip_tick9[up] = b.tick(sv)
        b.guard("vector", sv)
        b.wait("vector", st, trdone_tick9[up])
        b.op("vector", lambda e, up=up: e.tensor_scalar_mul(
            dg9[:, up * 128:(up + 1) * 128], ident[:, :],
            linv_col[:, up:up + 1]), inc=(sv, 1))
        pn_tick9[up] = b.tick(sv)
        pn_v = b.tick(sv)
        exp_u9 = exp_tick9[up]

        def tail9(qh=qh, qc=qc, kbase=kbase, up=up, pn_v=pn_v, u=u,
                  exp_u9=exp_u9):
            tp = u % 2
            b.wait("tensor", sv, pn_v)
            b.wait("tensor", ss, exp_u9)
            b.wait("tensor", ss, ot_evac9[tp])
            grp_tick = {}

            def emit_av(g):
                gsem, gtk = grp_tick[g]
                b.wait("tensor", gsem, gtk)
                for p in range(4 * g, 4 * g + 4):
                    b.op("tensor", lambda e, p=p, tp=tp, kbase=kbase: e.matmul(
                        pall[:, (4 + tp) * 512:(4 + tp) * 512 + 128],
                        v1_sb[:, kbase + 128 * p:kbase + 128 * (p + 1)],
                        pt_sb9[:, 128 * p:128 * (p + 1)],
                        start=(p == 0), stop=(p == 15)),
                        inc=(st, 1) if p == 15 else None)

            for g in range(4):
                bj = g % 2
                esem = ss if bj == 0 else sv
                eeng = "scalar" if bj == 0 else "vector"
                b.wait("tensor", esem, pt_evac9[bj])
                for ip in range(4 * g, 4 * g + 4):
                    b.op("tensor", lambda e, ip=ip, bj=bj, up=up: e.matmul(
                        ptp[:, 512 * bj + 128 * (ip % 4):
                            512 * bj + 128 * (ip % 4) + 128],
                        p_sb9[:, up * SEQ + 128 * ip:up * SEQ + 128 * (ip + 1)],
                        dg9[:, up * 128:(up + 1) * 128],
                        start=True, stop=True),
                        inc=(st, 1) if ip % 4 == 3 else None)
                tr_t = b.tick(st)
                b.wait(eeng, st, tr_t)
                if bj == 0:
                    b.op("scalar", lambda e, g=g, bj=bj: e.copy(
                        pt_sb9[:, 512 * g:512 * (g + 1)],
                        ptp[:, 512 * bj:512 * bj + 512]), inc=(ss, 1))
                else:
                    b.guard("vector", sv)
                    b.op("vector", lambda e, g=g, bj=bj: e.tensor_scalar_mul(
                        pt_sb9[:, 512 * g:512 * (g + 1)],
                        ptp[:, 512 * bj:512 * bj + 512], onescol[:, :]),
                        inc=(sv, 1))
                pt_evac9[bj] = b.tick(ss if bj == 0 else sv)
                grp_tick[g] = (esem, pt_evac9[bj])
                if g >= 1:
                    emit_av(g - 1)
            trdone_tick9[up] = b.tick(st)
            emit_av(3)
            av_t = b.tick(st)
            b.wait("scalar", st, av_t)
            b.op("scalar", lambda e, qh=qh, qc=qc, tp=tp: e.copy(
                y1_sb[:, qh * TOK + 128 * qc:qh * TOK + 128 * (qc + 1)],
                pall[:, (4 + tp) * 512:(4 + tp) * 512 + 128]), inc=(ss, 1))
            ot_evac9[tp] = b.tick(ss)

        tails9.append(tail9)
        if len(tails9) > 2:
            tails9.pop(0)()
    while tails9:
        tails9.pop(0)()
    p9.close()
    q1_cm.__exit__(None, None, None)
    v1_cm.__exit__(None, None, None)
    k1_cm.__exit__(None, None, None)

    fences()
    p10 = ExitStack()
    sbl = lambda n, s, d: p10.enter_context(nc.sbuf_tensor(n, s, d))
    s_sb10 = sbl("s_sb10", [128, NCH * TOK], F32)
    wobuf10 = sbl("wobuf10", [128, 2 * DM], BF16)
    sq_bf10 = sbl("sq_bf10", [128, TOK], BF16)
    s1n_stg = sbl("s1n_stg", [128, 2 * TOK], F32)
    rstd_bc10 = sbl("rstd_bc10", [128, TOK], F32)
    srow10 = {k: sbl("s10" + k, [1, TOK], F32) for k in ("sqv", "rstd")}
    wos10 = Stream(b, "wos10", "gpsimd", 2)
    sevac10 = [0, 0]
    for oc in range(NCH):
        par = oc % 2
        wsem, wtk, wslot = wos10.load(
            wobuf10[:, par * DM:(par + 1) * DM].rearrange("p (c n) -> p c n", n=128),
            wo1[:, 128 * oc:128 * (oc + 1)].rearrange("(c p) n -> p c n", p=128))
        sb0 = 2 * par
        b.wait("tensor", wsem, wtk)
        b.wait("tensor", ss, sevac10[par])
        for h in range(2):
            for dc in range(NCH):
                b.op("tensor", lambda e, sbk=sb0 + h, par=par, dc=dc, h=h: e.matmul(
                    pall[:, 512 * sbk:512 * (sbk + 1)],
                    wobuf10[:, par * DM + 128 * dc:par * DM + 128 * (dc + 1)],
                    y1_sb[:, dc * TOK + 512 * h:dc * TOK + 512 * (h + 1)],
                    start=(dc == 0), stop=(dc == NCH - 1)),
                    inc=(st, 1) if dc == NCH - 1 else None)
        s_t = b.tick(st)
        wos10.consumed(wslot, "st", s_t)
        b.wait("scalar", st, s_t)
        for h in range(2):
            b.op("scalar", lambda e, sbk=sb0 + h, oc=oc, h=h: e.copy(
                s_sb10[:, oc * TOK + 512 * h:oc * TOK + 512 * (h + 1)],
                pall[:, 512 * sbk:512 * (sbk + 1)]), inc=(ss, 1))
            b.op("scalar", lambda e, sbk=sb0 + h, h=h: e.activation(
                sq_bf10[:, 512 * h:512 * (h + 1)],
                pall[:, 512 * sbk:512 * (sbk + 1)], AF.Square), inc=(ss, 1))
        sevac10[par] = b.tick(ss)
        b.wait("tensor", ss, sevac10[par])
        b.op("tensor", lambda e, oc=oc: e.matmul(
            pall[0:1, 4 * 512:4 * 512 + 512], ones_bf[:, :], sq_bf10[:, 0:512],
            start=(oc == 0), stop=(oc == NCH - 1)))
        b.op("tensor", lambda e, oc=oc: e.matmul(
            pall[0:1, 5 * 512:5 * 512 + 512], ones_bf[:, :], sq_bf10[:, 512:1024],
            start=(oc == 0), stop=(oc == NCH - 1)),
            inc=(st, 1) if oc == NCH - 1 else None)
    STATS10 = b.tick(st)
    b.wait("scalar", st, STATS10)
    b.op("scalar", lambda e: e.activation(
        srow10["sqv"][:, 0:512], pall[0:1, 4 * 512:4 * 512 + 512], AF.Sqrt,
        scale=1.0 / DM, bias=epst[0:1, 1:2]), inc=(ss, 1))
    b.op("scalar", lambda e: e.activation(
        srow10["sqv"][:, 512:1024], pall[0:1, 5 * 512:5 * 512 + 512], AF.Sqrt,
        scale=1.0 / DM, bias=epst[0:1, 1:2]), inc=(ss, 1))
    b.wait("vector", ss, b.tick(ss))
    b.op("vector", lambda e: e.reciprocal(srow10["rstd"][:, :], srow10["sqv"][:, :]),
         inc=(sv, 1))
    RST10 = b.tick(sv)
    b.wait("tensor", sv, RST10)
    b.wait("tensor", ss, b.tick(ss))
    for h in range(2):
        b.op("tensor", lambda e, h=h: e.matmul(
            pall[:, 512 * h:512 * (h + 1)], ones_f[:, :],
            srow10["rstd"][:, 512 * h:512 * (h + 1)], start=True, stop=True),
            inc=(st, 1) if h == 1 else None)
    BC10 = b.tick(st)
    b.wait("scalar", st, BC10)
    b.op("scalar", lambda e: e.copy(rstd_bc10[:, :], pall[:, 0:1024]), inc=(ss, 1))
    BCE10 = b.tick(ss)
    b.wait("vector", ss, BCE10)
    s1dma10 = [0, 0]
    for oc in range(NCH):
        par = oc % 2
        b.wait("vector", sf[par], s1dma10[par])
        b.op("vector", lambda e, oc=oc, par=par: e.tensor_mul(
            s1n_stg[:, par * TOK:(par + 1) * TOK],
            s_sb10[:, oc * TOK:(oc + 1) * TOK], rstd_bc10[:, :]), inc=(sv, 1))
        mul_v = b.tick(sv)
        b.wait("sync", sv, mul_v)
        s1dma10[par] = sp_out("sync", par, s1n_d[128 * oc:128 * (oc + 1), :],
                              s1n_stg[:, par * TOK:(par + 1) * TOK])
    p10.close()
    y1_cm.__exit__(None, None, None)

    fences()
    p11 = ExitStack()
    sbl = lambda n, s, d: p11.enter_context(nc.sbuf_tensor(n, s, d))
    wg11 = sbl("wg11", [128, 2 * DM], BF16)
    wu11 = sbl("wu11", [128, 2 * DM], BF16)
    gate11 = sbl("gate11", [128, 2 * 512], F32)
    hd11 = sbl("hd11", [128, 2 * 512], BF16)
    wgs11 = Stream(b, "wgs11", "gpsimd", 2)
    wus11 = Stream(b, "wus11", "gpsimd", 2)
    b.wait("sync", sf[0], b.tick(sf[0]))
    b.wait("sync", sf[1], b.tick(sf[1]))
    silu11 = [0, 0]
    hmul11 = [0, 0]
    hddma11 = [0, 0]
    p11_pending = [None]
    for hc in range(NHC):
        wpar = hc % 2
        wgsem, wgtk, wgslot = wgs11.load(
            wg11[:, wpar * DM:(wpar + 1) * DM].rearrange("p (c n) -> p c n", n=128),
            wgt[:, 128 * hc:128 * (hc + 1)].rearrange("(c p) n -> p c n", p=128))
        wusem, wutk, wuslot = wus11.load(
            wu11[:, wpar * DM:(wpar + 1) * DM].rearrange("p (c n) -> p c n", n=128),
            wut[:, 128 * hc:128 * (hc + 1)].rearrange("(c p) n -> p c n", p=128))
        b.wait("tensor", wgsem, wgtk)
        b.wait("tensor", wusem, wutk)
        last_mm = 0
        for tq in range(2):
            par = tq % 2
            gb, ub = 3 * par, 3 * par + 1
            b.wait("tensor", ss, silu11[par])
            b.wait("tensor", sv, hmul11[par])
            for dc in range(NCH):
                b.op("tensor", lambda e, gb=gb, wpar=wpar, dc=dc, tq=tq: e.matmul(
                    bank(gb), wg11[:, wpar * DM + 128 * dc:wpar * DM + 128 * (dc + 1)],
                    xn1[:, dc * TOK + 512 * tq:dc * TOK + 512 * (tq + 1)],
                    start=(dc == 0), stop=(dc == NCH - 1)))
            for dc in range(NCH):
                b.op("tensor", lambda e, ub=ub, wpar=wpar, dc=dc, tq=tq: e.matmul(
                    bank(ub), wu11[:, wpar * DM + 128 * dc:wpar * DM + 128 * (dc + 1)],
                    xn1[:, dc * TOK + 512 * tq:dc * TOK + 512 * (tq + 1)],
                    start=(dc == 0), stop=(dc == NCH - 1)),
                    inc=(st, 1) if dc == NCH - 1 else None)
            up_t = b.tick(st)
            last_mm = up_t
            b.wait("scalar", st, up_t)
            b.op("scalar", lambda e, gb=gb, par=par: e.activation(
                gate11[:, 512 * par:512 * (par + 1)], bank(gb), AF.Silu),
                inc=(ss, 1))
            silu11[par] = b.tick(ss)
            b.wait("vector", ss, silu11[par])
            b.wait("vector", sf[par], hddma11[par])
            b.op("vector", lambda e, ub=ub, par=par: e.tensor_mul(
                hd11[:, 512 * par:512 * (par + 1)], bank(ub),
                gate11[:, 512 * par:512 * (par + 1)]), inc=(sv, 1))
            hmul11[par] = b.tick(sv)
            b.wait("sync", sv, hmul11[par])
            hddma11[par] = sp_out("sync", par,
                                  hd_d[128 * hc:128 * (hc + 1), 512 * tq:512 * (tq + 1)],
                                  hd11[:, 512 * par:512 * (par + 1)])
        wgs11.consumed(wgslot, "st", last_mm)
        wus11.consumed(wuslot, "st", last_mm)
    p11.close()
    xn1_cm.__exit__(None, None, None)

    fences()
    p11 = ExitStack()
    sbl = lambda n, s, d: p11.enter_context(nc.sbuf_tensor(n, s, d))
    hw11 = sbl("hw11", [128, NHC * 1024], BF16)
    wd11 = sbl("wd11", [128, 2 * DH], BF16)
    g11 = sbl("g11", [128, 2 * 512], F32)
    s11 = sbl("s11", [128, 4 * 512], F32)
    x11 = sbl("x11", [128, 4 * 512], F32)
    t11 = sbl("t11", [128, 2 * 512], F32)
    yb11 = sbl("yb11", [128, 2 * 512], F32)
    wds11 = Stream(b, "wds11", "gpsimd", 2)
    sin11 = Stream(b, "sin11", "sync", 2)
    xin11 = Stream(b, "xin11", "sync", 2)
    b.wait("sync", sf[0], b.tick(sf[0]))
    b.wait("sync", sf[1], b.tick(sf[1]))
    for hcc in range(NHC):
        b.dma("sync", hw11[:, hcc * 1024:(hcc + 1) * 1024],
              hd_d[128 * hcc:128 * (hcc + 1), :], shw[0])
    HDL = b.tick(shw[0])
    b.wait("tensor", shw[0], HDL)
    gsil11 = [0, 0]
    res11 = [0, 0]
    ydma11 = [0, 0]
    for dc in range(NCH):
        par = dc % 2
        wdsem, wdtk, wdslot = wds11.load(
            wd11[:, par * DH:(par + 1) * DH].rearrange("p (c n) -> p c n", n=128),
            wdt[:, 128 * dc:128 * (dc + 1)].rearrange("(c p) n -> p c n", p=128))
        rA = 2 * par
        rB = 2 * par + 1
        ssem, stk, sslot = sin11.load(s11[:, rA * 512:(rA + 1) * 512],
                                      s1n_d[128 * dc:128 * (dc + 1), 0:512])
        xsem, xtk, xslot = xin11.load(x11[:, rA * 512:(rA + 1) * 512],
                                      x1_d[128 * dc:128 * (dc + 1), 0:512])
        b.wait("tensor", wdsem, wdtk)
        b.wait("tensor", ss, gsil11[par])
        for tb in range(2):
            dbk = 2 * par + tb
            for hcc in range(NHC):
                b.op("tensor", lambda e, dbk=dbk, hcc=hcc, par=par, tb=tb: e.matmul(
                    bank(dbk), wd11[:, par * DH + hcc * 128:par * DH + hcc * 128 + 128],
                    hw11[:, hcc * 1024 + 512 * tb:hcc * 1024 + 512 * (tb + 1)],
                    start=(hcc == 0), stop=(hcc == NHC - 1)),
                    inc=(st, 1) if hcc == NHC - 1 else None)
        down_t = b.tick(st)
        wds11.consumed(wdslot, "st", down_t)
        b.wait("scalar", st, down_t)
        b.wait("scalar", sv, res11[par])
        b.op("scalar", lambda e, par=par: e.activation(
            g11[:, 512 * par:512 * (par + 1)],
            pall[:, 512 * 2 * par:512 * (2 * par + 1)], AF.Silu), inc=(ss, 1))
        gsil_half = b.tick(ss)
        b.wait("vector", ss, gsil_half)
        b.wait("vector", ssem, stk)
        b.wait("vector", xsem, xtk)
        b.wait("vector", so[par], ydma11[par])
        b.guard("vector", sv)
        b.op("vector", lambda e, par=par, rA=rA: e.tensor_mul(
            t11[:, 512 * par:512 * (par + 1)], g11[:, 512 * par:512 * (par + 1)],
            s11[:, rA * 512:(rA + 1) * 512]), inc=(sv, 1))
        b.guard("vector", sv)
        b.op("vector", lambda e, par=par, rA=rA: e.tensor_add(
            yb11[:, 512 * par:512 * (par + 1)], t11[:, 512 * par:512 * (par + 1)],
            x11[:, rA * 512:(rA + 1) * 512]), inc=(sv, 1))
        res_v = b.tick(sv)
        b.wait("sync", sv, res_v)
        b.wait("sync", so[par], b.tick(so[par]))
        b.op("sync", lambda e, dc=dc, par=par: e.dma_start(
            out=y_o[128 * dc:128 * (dc + 1), 0:512],
            in_=yb11[:, 512 * par:512 * (par + 1)]), inc=(so[par], 16))
        ydma11[par] = b.tick(so[par])
        ssemB, stkB, sslotB = sin11.load(s11[:, rB * 512:(rB + 1) * 512],
                                         s1n_d[128 * dc:128 * (dc + 1), 512:1024])
        xsemB, xtkB, xslotB = xin11.load(x11[:, rB * 512:(rB + 1) * 512],
                                         x1_d[128 * dc:128 * (dc + 1), 512:1024])
        b.wait("scalar", sv, res_v)
        b.op("scalar", lambda e, par=par: e.activation(
            g11[:, 512 * par:512 * (par + 1)],
            pall[:, 512 * (2 * par + 1):512 * (2 * par + 2)], AF.Silu), inc=(ss, 1))
        gsil11[par] = b.tick(ss)
        b.wait("vector", ss, gsil11[par])
        b.wait("vector", ssemB, stkB)
        b.wait("vector", xsemB, xtkB)
        b.wait("vector", so[par], ydma11[par])
        b.guard("vector", sv)
        b.op("vector", lambda e, par=par, rB=rB: e.tensor_mul(
            t11[:, 512 * par:512 * (par + 1)], g11[:, 512 * par:512 * (par + 1)],
            s11[:, rB * 512:(rB + 1) * 512]), inc=(sv, 1))
        b.guard("vector", sv)
        b.op("vector", lambda e, par=par, rB=rB: e.tensor_add(
            yb11[:, 512 * par:512 * (par + 1)], t11[:, 512 * par:512 * (par + 1)],
            x11[:, rB * 512:(rB + 1) * 512]), inc=(sv, 1))
        res_vB = b.tick(sv)
        res11[par] = res_vB
        sin11.consumed(sslot, "sv", res_v)
        sin11.consumed(sslotB, "sv", res_vB)
        xin11.consumed(xslot, "sv", res_v)
        xin11.consumed(xslotB, "sv", res_vB)
        b.wait("sync", sv, res_vB)
        b.wait("sync", so[par], b.tick(so[par]))
        b.op("sync", lambda e, dc=dc, par=par: e.dma_start(
            out=y_o[128 * dc:128 * (dc + 1), 512:1024],
            in_=yb11[:, 512 * par:512 * (par + 1)]), inc=(so[par], 16))
        ydma11[par] = b.tick(so[par])
    p11.close()

    p11.close()

    b.wait("sync", sx, b.tick(sx))
    b.wait("sync", so[0], b.tick(so[0]))
    b.wait("sync", so[1], b.tick(so[1]))
    b.wait("sync", sf[0], b.tick(sf[0]))
    b.wait("sync", sf[1], b.tick(sf[1]))
    b.wait("gpsimd", sg, b.tick(sg))
    b.realize(ctx)
    nc._ctx_keepalive = ctx
    return nc


def host_fused_masks():
    diag = np.where(np.arange(128)[:, None] >= np.arange(128)[None, :], 0.0, NEG)
    zero = np.zeros((128, 128), np.float32)
    neg = np.full((128, 128), NEG, np.float32)
    out = {}
    for par in (0, 1):
        a = neg if par == 0 else zero
        bb = zero if par == 0 else neg
        out[par] = np.concatenate([diag] + [a] * 8 + [bb] * 8, axis=1).astype(BF)
    return out


def kernel_fused(**inputs):
    x = np.asarray(inputs["L0_attn_x"], np.float32)
    masks = host_fused_masks()
    h128, h64, perm = host_fwht_consts()
    ident = np.eye(128, dtype=np.float32).astype(BF)
    f32c = lambda a: np.ascontiguousarray(np.asarray(a, np.float32))
    in_maps = []
    common = {
        "wq0": f32c(inputs["L0_attn_Wq"]), "wk0": f32c(inputs["L0_attn_Wk"]),
        "wv0": f32c(inputs["L0_attn_Wv"]), "wo0": f32c(inputs["L0_attn_Wo"]),
        "w1t": f32c(inputs["L0_ffn_W1"].T), "w3t": f32c(inputs["L0_ffn_W3"].T),
        "w2tp": f32c(np.asarray(inputs["L0_ffn_W2"], np.float32).T[perm]),
        "wq1": f32c(inputs["L1_attn_Wq"]), "wv1": f32c(inputs["L1_attn_Wv"]),
        "wo1": f32c(inputs["L1_attn_Wo"]),
        "wgt": f32c(inputs["L1_ffn_w_gate"].T), "wut": f32c(inputs["L1_ffn_w_up"].T),
        "wdt": f32c(inputs["L1_ffn_w_down"].T),
        "ident_i": ident, "h128_i": np.asarray(h128),
        "h64_i": np.concatenate([np.asarray(h64), np.asarray(h64)], axis=0),
    }
    for c in range(8):
        seq, half = c // 2, c % 2
        own = x[seq, half * TOK:(half + 1) * TOK, :]
        par = x[seq, (1 - half) * TOK:(2 - half) * TOK, :]
        xf = np.ascontiguousarray(np.concatenate([own, par], axis=0).T)
        m = dict(common)
        m["xT"] = xf
        m["mask_i"] = masks[half]
        m["msel_i"] = np.tile(np.array([[1.0 - half, float(half)]], np.float32),
                              (128, 1))
        in_maps.append(m)
    res = _run(build_fused(), in_maps)
    out = np.empty((BSZ, SEQ, DM), np.float32)
    for c in range(8):
        seq, half = c // 2, c % 2
        out[seq, half * TOK:(half + 1) * TOK, :] = np.asarray(
            res.results[c]["y_o"], np.float32).T
    return out


# version 19 (final)
# speedup vs baseline: 1.0408x; 1.0313x over previous
import numpy as np
import ml_dtypes
from contextlib import ExitStack

import concourse.bass as bass
import concourse.mybir as mybir
from concourse.bass_utils import run_bass_kernel_spmd

F32 = mybir.dt.float32
BF16 = mybir.dt.bfloat16
AF = mybir.ActivationFunctionType
ALU = mybir.AluOpType
AX = mybir.AxisListType

BSZ, SEQ, DM, DH = 4, 2048, 2048, 8192
TOK = 1024
NCH = DM // 128
NEG = -1.0e30
LN_EPS = 1e-5
RMS_EPS = 1.1920929e-07
BF = ml_dtypes.bfloat16
RS128 = float(1.0 / np.sqrt(128.0))

ENGINES = ("sync", "tensor", "vector", "scalar", "gpsimd")


class Bld:

    def __init__(self, nc):
        self.nc = nc
        self.prog = {e: [] for e in ENGINES}
        self.cnt = {}

    def sem(self, name):
        assert name not in self.cnt
        self.cnt[name] = 0
        return name

    def wait(self, eng, sem, val):
        if val > 0:
            self.prog[eng].append(("w", sem, val))

    def op(self, eng, fn, inc=None):
        incs = []
        if inc is not None:
            items = [inc] if isinstance(inc, tuple) else list(inc)
            for s, a in items:
                self.cnt[s] += a
                incs.append((s, a))
        self.prog[eng].append(("o", fn, incs))

    def dma(self, eng, out, in_, sem):
        self.op(eng, lambda e, o=out, i=in_: e.dma_start(out=o, in_=i), inc=(sem, 16))
        return self.cnt[sem]

    def tick(self, sem):
        return self.cnt[sem]

    def guard(self, eng, sem):
        self.wait(eng, sem, self.cnt[sem])

    def realize(self, ctx):
        sems = {n: ctx.enter_context(self.nc.semaphore(n)) for n in self.cnt}
        block = ctx.enter_context(self.nc.Block())
        for name in ENGINES:
            items = self.prog[name]

            def body(eng_obj, items=items):
                for it in items:
                    if it[0] == "w":
                        eng_obj.wait_ge(sems[it[1]], it[2])
                    else:
                        inst = it[1](eng_obj)
                        for (s, a) in it[2]:
                            inst.then_inc(sems[s], a)

            getattr(block, name)(body)


class Stream:

    def __init__(self, b, name, queue, nslots=2):
        self.b = b
        self.queue = queue
        self.n = nslots
        self.sems = [b.sem(f"{name}{k}") for k in range(nslots)]
        self.gate = [None] * nslots
        self.i = 0

    def load(self, out_ap, in_ap):
        s = self.i % self.n
        if self.gate[s] is not None:
            self.b.wait(self.queue, self.gate[s][0], self.gate[s][1])
            self.gate[s] = None
        t = self.b.dma(self.queue, out_ap, in_ap, self.sems[s])
        self.i += 1
        return self.sems[s], t, s

    def consumed(self, slot, sem, tick):
        self.gate[slot] = (sem, tick)


class OutStream:

    def __init__(self, b, name, queue, nslots=2):
        self.b = b
        self.queue = queue
        self.n = nslots
        self.sems = [b.sem(f"{name}{k}") for k in range(nslots)]
        self.i = 0

    def out(self, slot, dst, src):
        sem = self.sems[slot]
        self.b.wait(self.queue, sem, self.b.tick(sem))
        t = self.b.dma(self.queue, dst, src, sem)
        return sem, t

    def drain(self):
        for sem in self.sems:
            self.b.wait(self.queue, sem, self.b.tick(sem))


def ceil_div(a, b):
    return (a + b - 1) // b



def emit_ln_stats_and_xn(b, bank, consts, xsrc_dram, x_stream, xbf_stream, sqbuf,
                         rows, xn, tok, sx, sg, st, sv, ss, xn_out_dram=None,
                         x_keep=None, xn_cols=None):
    nh = tok // 512
    assert tok == 1024
    if xn_cols is None:
        xn_cols = lambda dc: xn[:, dc * tok:(dc + 1) * tok]
    b.wait("tensor", ss, b.tick(ss))
    b.wait("tensor", sv, b.tick(sv))
    ps_sum = bank(0, 1024)[0:1, :]
    ps_sq = bank(2, 1024)[0:1, :]
    sq_ticks = {}
    for dc in range(NCH):
        if x_keep is not None:
            xslot = x_keep[:, dc * tok:(dc + 1) * tok]
            xsem, xt, _ = x_stream.load(xslot, xsrc_dram[128 * dc:128 * (dc + 1), :])
        else:
            xsem, xt, xs = x_stream.load(
                x_stream.buf[:, (dc % x_stream.n) * tok:((dc % x_stream.n) + 1) * tok],
                xsrc_dram[128 * dc:128 * (dc + 1), :])
            xslot = x_stream.buf[:, (dc % x_stream.n) * tok:((dc % x_stream.n) + 1) * tok]
        bsem, bt, bslot = xbf_stream.load(
            xbf_stream.buf[:, (dc % xbf_stream.n) * tok:((dc % xbf_stream.n) + 1) * tok],
            xsrc_dram[128 * dc:128 * (dc + 1), :])
        xbfslot = xbf_stream.buf[:, (dc % xbf_stream.n) * tok:((dc % xbf_stream.n) + 1) * tok]
        b.wait("scalar", xsem, xt)
        sqslot = sqbuf[:, (dc % 2) * tok:((dc % 2) + 1) * tok]
        if dc >= 2:
            b.wait("scalar", st, sq_ticks[dc - 2][1])
        b.op("scalar", lambda e, o=sqslot, i=xslot: e.activation(o, i, AF.Square),
             inc=(ss, 1))
        sq_s = b.tick(ss)
        b.wait("tensor", bsem, bt)
        b.wait("tensor", ss, sq_s)
        if dc == 0:
            b.wait("tensor", sv, consts)
        for h in range(nh):
            b.op("tensor", lambda e, i=xbfslot, h=h, dc=dc: e.matmul(
                ps_sum[:, 512 * h:512 * (h + 1)], ps_ones_bf[b][:, :],
                i[:, 512 * h:512 * (h + 1)],
                start=(dc == 0), stop=(dc == NCH - 1)))
            b.op("tensor", lambda e, i=sqslot, h=h, dc=dc: e.matmul(
                ps_sq[:, 512 * h:512 * (h + 1)], ps_ones_bf[b][:, :],
                i[:, 512 * h:512 * (h + 1)],
                start=(dc == 0), stop=(dc == NCH - 1)),
                inc=(st, 1) if h == nh - 1 else None)
        mm_t = b.tick(st)
        sq_ticks[dc] = (sq_s, mm_t)
        if x_keep is None:
            x_stream.consumed(dc % x_stream.n, "ss", sq_s)
        xbf_stream.consumed(bslot, "st", mm_t)
    stats_t = b.tick(st)

    m_row, ex2, m2, var, sqv, rstd = (rows[k] for k in
                                      ("m_row", "ex2", "m2", "var", "sqv", "rstd"))
    b.wait("scalar", st, stats_t)
    b.op("scalar", lambda e: e.activation(m_row[:, :], ps_sum, AF.Copy,
                                          scale=1.0 / DM), inc=(ss, 1))
    b.op("scalar", lambda e: e.activation(ex2[:, :], ps_sq, AF.Copy,
                                          scale=1.0 / DM), inc=(ss, 1))
    b.guard("scalar", ss)
    b.op("scalar", lambda e: e.activation(m2[:, :], m_row[:, :], AF.Square),
         inc=(ss, 1))
    b.wait("vector", ss, b.tick(ss))
    b.op("vector", lambda e: e.tensor_sub(var[:, :], ex2[:, :], m2[:, :]),
         inc=(sv, 1))
    b.wait("scalar", sv, b.tick(sv))
    b.op("scalar", lambda e: e.activation(sqv[:, :], var[:, :], AF.Sqrt,
                                          bias=eps_row[b][0:1, 0:1]), inc=(ss, 1))
    b.wait("vector", ss, b.tick(ss))
    b.op("vector", lambda e: e.reciprocal(rstd[:, :], sqv[:, :]), inc=(sv, 1))
    rstd_v = b.tick(sv)

    ps_M = bank(4)
    ps_R = bank(5)
    tmp32 = rows["tmp32"]
    b.wait("tensor", sv, rstd_v)
    xn_half_done = {}
    for h in range(nh):
        if h > 0:
            b.wait("tensor", sv, b.tick(sv))
        b.op("tensor", lambda e, h=h: e.matmul(
            ps_M, ps_ones_f[b][:, :],
            m_row[:, 512 * h:512 * (h + 1)], start=True, stop=True))
        b.op("tensor", lambda e, h=h: e.matmul(
            ps_R, ps_ones_f[b][:, :],
            rstd[:, 512 * h:512 * (h + 1)], start=True, stop=True), inc=(st, 1))
        bcast_t = b.tick(st)
        for dc in range(NCH):
            if x_keep is not None:
                xsl = x_keep[:, dc * tok + 512 * h:dc * tok + 512 * (h + 1)]
            else:
                slot = x_stream.i % x_stream.n
                dst = x_stream.buf[:, slot * tok:slot * tok + 512]
                xsem, xt, xs = x_stream.load(
                    dst, xsrc_dram[128 * dc:128 * (dc + 1), 512 * h:512 * (h + 1)])
                xsl = dst
                b.wait("vector", xsem, xt)
            if dc == 0:
                b.wait("vector", st, bcast_t)
            b.guard("vector", sv)
            b.op("vector", lambda e, x_=xsl: e.tensor_sub(
                tmp32[:, 0:512], x_, ps_M), inc=(sv, 1))
            b.guard("vector", sv)
            b.op("vector", lambda e, o=xn_cols(dc), h=h: e.tensor_mul(
                o[:, 512 * h:512 * (h + 1)], tmp32[:, 0:512], ps_R), inc=(sv, 1))
            dc_v = b.tick(sv)
            if x_keep is None:
                x_stream.consumed(xs, "sv", dc_v)
            if xn_out_dram is not None and h == nh - 1:
                b.wait("sync", sv, dc_v)
                b.dma("sync", xn_out_dram[128 * dc:128 * (dc + 1), :],
                      xn_cols(dc), sx)
    return b.tick(sv)


ps_ones_bf = {}
ps_ones_f = {}
eps_row = {}



def build_launch_A():
    nc = bass.Bass("TRN2", target_bir_lowering=False)
    xT = nc.dram_tensor("xT", [DM, TOK], F32, kind="ExternalInput").ap()
    wq = nc.dram_tensor("wq", [DM, DM], F32, kind="ExternalInput").ap()
    wk = nc.dram_tensor("wk", [DM, DM], F32, kind="ExternalInput").ap()
    wv = nc.dram_tensor("wv", [DM, DM], F32, kind="ExternalInput").ap()
    xn_o = nc.dram_tensor("xn_o", [DM, TOK], BF16, kind="ExternalOutput").ap()
    q_o = nc.dram_tensor("q_o", [DM, TOK], BF16, kind="ExternalOutput").ap()
    k_o = nc.dram_tensor("k_o", [DM, TOK], BF16, kind="ExternalOutput").ap()
    v_o = nc.dram_tensor("v_o", [TOK, DM], BF16, kind="ExternalOutput").ap()

    ctx = ExitStack()
    sb = lambda n, s, d: ctx.enter_context(nc.sbuf_tensor(n, s, d))
    xn = sb("xn", [128, NCH * TOK], BF16)
    wv_sb = sb("wv_sb", [128, NCH * DM], BF16)
    x_buf = sb("x_buf", [128, 2 * TOK], F32)
    xbf_buf = sb("xbf_buf", [128, 2 * TOK], BF16)
    sq_buf = sb("sq_buf", [128, 2 * TOK], BF16)
    wqk_buf = sb("wqk_buf", [128, 2 * DM], BF16)
    v_sb = sb("v_sb", [128, 2 * DM], BF16)
    q_sb = sb("q_sb", [128, 2 * TOK], BF16)
    ones_bf = sb("ones_bf", [128, 1], BF16)
    ones_f = sb("ones_f", [1, 128], F32)
    rows = {k: sb(k, [1, TOK], F32) for k in ("m_row", "ex2", "m2", "var", "sqv", "rstd")}
    epst = sb("epst", [128, 2], F32)
    rows["tmp32"] = sb("tmp32", [128, 1024], F32)
    pall = ctx.enter_context(nc.psum_tensor("pall", [128, 8 * 512], F32))

    def bank(i, width=512):
        return pall[:, 512 * i:512 * i + width]

    b = Bld(nc)
    sx = b.sem("sx")
    sg = b.sem("sg")
    st = b.sem("st")
    sv = b.sem("sv")
    ss = b.sem("ss")
    so0 = b.sem("so0")
    so1 = b.sem("so1")
    so = [so0, so1]

    ps_ones_bf[b] = ones_bf
    ps_ones_f[b] = ones_f
    eps_row[b] = epst
    b.op("vector", lambda e: e.memset(ones_bf[:, :], 1.0), inc=(sv, 1))
    b.op("vector", lambda e: e.memset(ones_f[:, :], 1.0), inc=(sv, 1))
    b.op("vector", lambda e: e.memset(epst[:, 0:1], LN_EPS), inc=(sv, 1))
    b.op("vector", lambda e: e.memset(epst[:, 1:2], RMS_EPS), inc=(sv, 1))
    consts = b.tick(sv)

    x_stream = Stream(b, "lx", "sync", 2)
    x_stream.buf = x_buf
    xbf_stream = Stream(b, "lb", "gpsimd", 2)
    xbf_stream.buf = xbf_buf

    XN_V = emit_ln_stats_and_xn(b, bank, consts, xT, x_stream, xbf_stream, sq_buf,
                                rows, xn, TOK, sx, sg, st, sv, ss, xn_out_dram=xn_o)

    wv_sem_t = []
    for dc in range(NCH):
        b.dma("gpsimd", wv_sb[:, dc * DM:(dc + 1) * DM],
              wv[128 * dc:128 * (dc + 1), :], sg)
    WV_G = b.tick(sg)
    b.wait("tensor", sg, WV_G)
    b.wait("tensor", sv, XN_V)
    vidx = 0
    v_evac = {}
    vdma = {}
    for tc in range(TOK // 128):
        for fg in range(4):
            bk = vidx % 2
            if vidx >= 2:
                b.wait("tensor", ss, v_evac[vidx - 2])
            for dc in range(NCH):
                b.op("tensor", lambda e, bk=bk, tc=tc, fg=fg, dc=dc: e.matmul(
                    bank(bk), xn[:, dc * TOK + 128 * tc:dc * TOK + 128 * (tc + 1)],
                    wv_sb[:, dc * DM + 512 * fg:dc * DM + 512 * (fg + 1)],
                    start=(dc == 0), stop=(dc == NCH - 1)),
                    inc=(st, 1) if dc == NCH - 1 else None)
            b.wait("scalar", st, b.tick(st))
            vslot = tc % 2
            if tc >= 2 and fg == 0:
                b.wait("scalar", so[vslot], vdma[tc - 2])
            b.op("scalar", lambda e, bk=bk, vslot=vslot, fg=fg: e.activation(
                v_sb[:, vslot * DM + 512 * fg:vslot * DM + 512 * (fg + 1)],
                bank(bk), AF.Copy), inc=(ss, 1))
            v_evac[vidx] = b.tick(ss)
            vidx += 1
        b.wait("sync", ss, v_evac[vidx - 1])
        b.wait("sync", so[tc % 2], b.tick(so[tc % 2]))
        vdma[tc] = b.dma("sync", v_o[128 * tc:128 * (tc + 1), :],
                         v_sb[:, (tc % 2) * DM:(tc % 2 + 1) * DM], so[tc % 2])

    wstream = Stream(b, "wqk", "gpsimd", 2)
    qk_evac = {}
    qdma = {}
    jobs = [("q", oc) for oc in range(NCH)] + [("k", oc) for oc in range(NCH)]
    for ji, (which, oc) in enumerate(jobs):
        wsrc = wq if which == "q" else wk
        src = wsrc[:, 128 * oc:128 * (oc + 1)].rearrange("(c p) n -> p c n", p=128)
        slot = ji % 2
        dst = wqk_buf[:, slot * DM:(slot + 1) * DM].rearrange("p (c n) -> p c n", n=128)
        wsem, wt, ws = wstream.load(dst, src)
        b.wait("tensor", wsem, wt)
        for h in range(2):
            bk = 2 + h
            if ji >= 1:
                b.wait("tensor", ss, qk_evac[(ji - 1, h)])
            for dc in range(NCH):
                b.op("tensor", lambda e, bk=bk, slot=slot, dc=dc, h=h: e.matmul(
                    bank(bk),
                    wqk_buf[:, slot * DM + 128 * dc:slot * DM + 128 * (dc + 1)],
                    xn[:, dc * TOK + 512 * h:dc * TOK + 512 * (h + 1)],
                    start=(dc == 0), stop=(dc == NCH - 1)),
                    inc=(st, 1) if dc == NCH - 1 else None)
        wstream.consumed(ws, "st", b.tick(st))
        qslot = ji % 2
        b.wait("scalar", st, b.tick(st))
        if ji >= 2:
            b.wait("scalar", so[qslot], qdma[ji - 2])
        scale = RS128 if which == "q" else 1.0
        for h in range(2):
            b.op("scalar", lambda e, qslot=qslot, h=h, scale=scale, bk=2 + h: e.activation(
                q_sb[:, qslot * TOK + 512 * h:qslot * TOK + 512 * (h + 1)],
                bank(bk), AF.Copy, scale=scale), inc=(ss, 1))
            qk_evac[(ji, h)] = b.tick(ss)
        b.wait("sync", ss, qk_evac[(ji, 1)])
        b.wait("sync", so[qslot], b.tick(so[qslot]))
        dst = q_o if which == "q" else k_o
        qdma[ji] = b.dma("sync", dst[128 * oc:128 * (oc + 1), :],
                         q_sb[:, qslot * TOK:(qslot + 1) * TOK], so[qslot])

    b.wait("sync", sx, b.tick(sx))
    b.wait("sync", so0, b.tick(so0))
    b.wait("sync", so1, b.tick(so1))
    b.wait("gpsimd", sg, b.tick(sg))

    b.realize(ctx)
    nc._ctx_keepalive = ctx
    return nc



def shard_tokens_fm(x):
    outs = []
    for c in range(8):
        seq, half = c // 2, c % 2
        sl = x[seq, half * TOK:(half + 1) * TOK, :]
        outs.append(np.ascontiguousarray(np.asarray(sl, dtype=np.float32).T))
    return outs


def _run(nc, in_maps):
    return run_bass_kernel_spmd(nc, in_maps, core_ids=list(range(8)))


def kernel(**inputs):
    return kernel_fused(**inputs)


def kernel_multilaunch(**inputs):
    x = np.asarray(inputs["L0_attn_x"], np.float32)
    xts = shard_tokens_fm(x)
    masks = host_l0_masks()
    h128, h64, perm = host_fwht_consts()
    ident = np.eye(128, dtype=np.float32).astype(BF)
    f32c = lambda a: np.ascontiguousarray(np.asarray(a, np.float32))

    wq0, wk0, wv0, wo0 = (f32c(inputs[k]) for k in
                          ("L0_attn_Wq", "L0_attn_Wk", "L0_attn_Wv", "L0_attn_Wo"))
    w1t = f32c(inputs["L0_ffn_W1"].T)
    w3t = f32c(inputs["L0_ffn_W3"].T)
    w2tp = f32c(np.asarray(inputs["L0_ffn_W2"], np.float32).T[perm])
    wq1 = f32c(inputs["L1_attn_Wq"])
    wv1 = f32c(inputs["L1_attn_Wv"])
    wo1 = f32c(inputs["L1_attn_Wo"])
    wgt = f32c(inputs["L1_ffn_w_gate"].T)
    wut = f32c(inputs["L1_ffn_w_up"].T)
    wdt = f32c(inputs["L1_ffn_w_down"].T)

    resA = _run(build_launch_A(),
                [{"xT": xts[c], "wq": wq0, "wk": wk0, "wv": wv0} for c in range(8)])
    ra = resA.results

    in_b = []
    for c in range(8):
        p = c ^ 1
        half = c % 2
        kc, kp = ra[c]["k_o"], ra[p]["k_o"]
        kfull = np.concatenate([kc, kp] if half == 0 else [kp, kc], axis=1)
        vfull = np.concatenate(
            [ra[c]["v_o"], ra[p]["v_o"]] if half == 0 else
            [ra[p]["v_o"], ra[c]["v_o"]], axis=0)
        vhm = np.ascontiguousarray(
            vfull.reshape(SEQ, 16, 128).transpose(1, 0, 2)).reshape(16 * SEQ, 128)
        in_b.append({"q_i": ra[c]["q_o"], "k_i": np.ascontiguousarray(kfull),
                     "vhm_i": vhm, "mask_i": masks[half], "ident_i": ident})
    resB = _run(build_launch_B(), in_b)

    in_c = [{"xT": xts[c], "xn_i": ra[c]["xn_o"], "o_i": resB.results[c]["o_o"],
             "wo": wo0, "w1t": w1t, "w3t": w3t, "w2tp": w2tp,
             "h128_i": np.asarray(h128), "h64_i": np.asarray(h64)}
            for c in range(8)]
    resC = _run(build_launch_C(), in_c)
    x1 = [resC.results[c]["x1_o"] for c in range(8)]

    in_d = []
    for c in range(8):
        p = c ^ 1
        x1f = np.ascontiguousarray(np.concatenate([x1[c], x1[p]], axis=1))
        in_d.append({"x1f": x1f, "wq": wq1, "wv": wv1, "ident_i": ident})
    resD = _run(build_launch_D(), in_d)
    rd = resD.results

    in_e = [{"q_i": rd[c]["q1_o"], "k_i": rd[c]["k1_o"], "vhm_i": rd[c]["v1hm_o"],
             "ident_i": ident, "wo": wo1} for c in range(8)]
    resE = _run(build_launch_E(), in_e)

    in_f = [{"xn1_i": rd[c]["xn1_o"], "s1n_i": resE.results[c]["s1n_o"],
             "x1_i": x1[c], "wgt": wgt, "wut": wut, "wdt": wdt}
            for c in range(8)]
    resF = _run(build_launch_F(), in_f)

    out = np.empty((BSZ, SEQ, DM), np.float32)
    for c in range(8):
        seq, half = c // 2, c % 2
        out[seq, half * TOK:(half + 1) * TOK, :] = np.asarray(
            resF.results[c]["y_o"], np.float32).T
    return out



def build_launch_B():
    nc = bass.Bass("TRN2", target_bir_lowering=False)
    q_i = nc.dram_tensor("q_i", [DM, TOK], BF16, kind="ExternalInput").ap()
    k_i = nc.dram_tensor("k_i", [DM, SEQ], BF16, kind="ExternalInput").ap()
    vhm_i = nc.dram_tensor("vhm_i", [16 * SEQ, 128], BF16, kind="ExternalInput").ap()
    mask_i = nc.dram_tensor("mask_i", [128, 8 * 9 * 128], BF16, kind="ExternalInput").ap()
    ident_i = nc.dram_tensor("ident_i", [128, 128], BF16, kind="ExternalInput").ap()
    o_o = nc.dram_tensor("o_o", [DM, TOK], BF16, kind="ExternalOutput").ap()

    NQC = TOK // 128
    NKB = SEQ // 128

    ctx = ExitStack()
    sb = lambda n, s, d: ctx.enter_context(nc.sbuf_tensor(n, s, d))
    q_sb = sb("q_sb", [128, NCH * TOK], BF16)
    k_sb = sb("k_sb", [128, 2 * SEQ], BF16)
    v_sb = sb("v_sb", [128, 2 * SEQ], BF16)
    p_sb = sb("p_sb", [128, SEQ], BF16)
    pt_sb = sb("pt_sb", [128, NKB * 128], BF16)
    mask_sb = sb("mask_sb", [128, 8 * 9 * 128], BF16)
    ident = sb("ident", [128, 128], BF16)
    ones_bf = sb("ones_bf", [128, 1], BF16)
    ones_f = sb("ones_f", [1, 128], F32)
    mneg = sb("mneg", [128, 3], F32)
    l_col = sb("l_col", [128, 6], F32)
    linv_col = sb("linv_col", [128, 3], F32)
    pn_sb = sb("pn_sb", [128, SEQ], BF16)
    o_sb = sb("o_sb", [128, 2 * TOK], BF16)
    pall = ctx.enter_context(nc.psum_tensor("pall", [128, 6 * 512], F32))
    ptp = ctx.enter_context(nc.psum_tensor("ptp", [128, 2 * 1024], BF16))

    b = Bld(nc)
    sx = b.sem("sx")
    sg = b.sem("sg")
    st = b.sem("st")
    sv = b.sem("sv")
    ss = b.sem("ss")
    so0 = b.sem("so0")
    so1 = b.sem("so1")
    so = [so0, so1]

    b.op("vector", lambda e: e.memset(ones_bf[:, :], 1.0), inc=(sv, 1))
    b.op("vector", lambda e: e.memset(ones_f[:, :], 1.0), inc=(sv, 1))
    CONSTS_V = b.tick(sv)
    b.dma("sync", mask_sb[:, :], mask_i, sx)
    b.dma("sync", ident[:, :], ident_i, sx)
    for dc in range(NCH):
        b.dma("sync", q_sb[:, dc * TOK:(dc + 1) * TOK],
              q_i[128 * dc:128 * (dc + 1), :], sx)
    QLOAD_X = b.tick(sx)

    kstream = Stream(b, "kst", "sync", 2)
    vstream = Stream(b, "vst", "sync", 2)
    pt_evac_tick = [0, 0]
    ot_evac_tick = [0, 0]
    l_recip_tick = 0
    tr_last = 0
    exp_tick = [0]

    S = pall[:, 0:2048]
    OT = lambda s: pall[:, (4 + s) * 512 + 0: (4 + s) * 512 + 128]
    PTP = lambda j: ptp[:, 1024 * (j % 2): 1024 * (j % 2) + 128]

    for h in range(16):
        ksem, kt, kslot = kstream.load(k_sb[:, (h % 2) * SEQ:(h % 2 + 1) * SEQ],
                                       k_i[128 * h:128 * (h + 1), :])
        vsem, vt, vslot = vstream.load(
            v_sb[:, (h % 2) * SEQ:(h % 2 + 1) * SEQ].rearrange("p (c n) -> p c n", n=128),
            vhm_i[SEQ * h:SEQ * (h + 1), :].rearrange("(c p) n -> p c n", p=128))
        kbase = (h % 2) * SEQ
        vbase = (h % 2) * SEQ
        for qc in range(NQC):
            nkb = qc + 9
            nk = 128 * nkb
            b.wait("tensor", ksem, kt)
            if h == 0 and qc == 0:
                b.wait("tensor", sx, QLOAD_X)
            b.wait("tensor", ss, exp_tick[0])
            for kg in range(ceil_div(nk, 512)):
                w = min(512, nk - 512 * kg)
                b.op("tensor", lambda e, h=h, qc=qc, kg=kg, w=w, kbase=kbase: e.matmul(
                    S[:, 512 * kg:512 * kg + w],
                    q_sb[:, h * TOK + 128 * qc:h * TOK + 128 * (qc + 1)],
                    k_sb[:, kbase + 512 * kg:kbase + 512 * kg + w],
                    start=True, stop=True), inc=(st, 1))
            s_mm = b.tick(st)
            b.wait("vector", st, s_mm)
            if h == 0 and qc == 0:
                b.wait("vector", sx, QLOAD_X)
            for j in range(9):
                kb = qc + j
                b.op("vector", lambda e, qc=qc, j=j, kb=kb: e.tensor_tensor(
                    S[:, 128 * kb:128 * (kb + 1)], S[:, 128 * kb:128 * (kb + 1)],
                    mask_sb[:, (qc * 9 + j) * 128:(qc * 9 + j + 1) * 128], ALU.add),
                    inc=(sv, 1))
            b.guard("vector", sv)
            b.op("vector", lambda e, nk=nk: e.reduce_max(
                mneg[:, :], S[:, 0:nk], axis=AX.X, negate=True), inc=(sv, 1))
            mneg_v = b.tick(sv)
            b.wait("scalar", sv, mneg_v)
            b.wait("scalar", sv, l_recip_tick)
            b.op("scalar", lambda e, nk=nk: e.activation(
                p_sb[:, 0:nk], S[:, 0:nk], AF.Exp, bias=mneg[:, :],
                accum_out=l_col[:, :]), inc=(ss, 1))
            exp_tick[0] = b.tick(ss)
            b.wait("vector", ss, exp_tick[0])
            b.op("vector", lambda e: e.reciprocal(linv_col[:, :], l_col[:, :]),
                 inc=(sv, 1))
            l_recip_tick = b.tick(sv)
            b.guard("vector", sv)
            b.wait("vector", st, tr_last)
            b.op("vector", lambda e, nk=nk: e.tensor_scalar_mul(
                pn_sb[:, 0:nk], p_sb[:, 0:nk], linv_col[:, :]), inc=(sv, 1))
            pn_v = b.tick(sv)
            b.wait("tensor", sv, pn_v)
            for kb in range(nkb):
                j = kb % 2
                b.wait("tensor", ss, pt_evac_tick[j])
                b.op("tensor", lambda e, kb=kb, j=j: e.transpose(
                    PTP(j), pn_sb[:, 128 * kb:128 * (kb + 1)], ident[:, :]),
                    inc=(st, 1))
                tr_t = b.tick(st)
                b.wait("scalar", st, tr_t)
                b.op("scalar", lambda e, kb=kb, j=j: e.copy(
                    pt_sb[:, 128 * kb:128 * (kb + 1)], PTP(j)), inc=(ss, 1))
                pt_evac_tick[j] = b.tick(ss)
            tr_last = b.tick(st)
            ots = qc % 2
            b.wait("tensor", ss, b.tick(ss))
            b.wait("tensor", vsem, vt)
            b.wait("tensor", ss, ot_evac_tick[ots])
            for kb in range(nkb):
                b.op("tensor", lambda e, kb=kb, ots=ots, vbase=vbase, nkb=nkb: e.matmul(
                    OT(ots), v_sb[:, vbase + 128 * kb:vbase + 128 * (kb + 1)],
                    pt_sb[:, 128 * kb:128 * (kb + 1)],
                    start=(kb == 0), stop=(kb == nkb - 1)),
                    inc=(st, 1) if kb == nkb - 1 else None)
            av_t = b.tick(st)
            hs = h % 2
            b.wait("scalar", st, av_t)
            if qc == 0 and h >= 2:
                b.wait("scalar", so[h % 2], odma_tick.get(h - 2, 0))
            b.op("scalar", lambda e, hs=hs, qc=qc, ots=ots: e.copy(
                o_sb[:, hs * TOK + 128 * qc:hs * TOK + 128 * (qc + 1)], OT(ots)),
                inc=(ss, 1))
            ot_evac_tick[ots] = b.tick(ss)
        if h == 0:
            odma_tick = {}
        b.wait("sync", ss, b.tick(ss))
        b.wait("sync", so[h % 2], b.tick(so[h % 2]))
        odma_tick[h] = b.dma("sync", o_o[128 * h:128 * (h + 1), :],
                             o_sb[:, (h % 2) * TOK:(h % 2 + 1) * TOK], so[h % 2])
        kstream.consumed(kslot, "st", s_mm)
        vstream.consumed(vslot, "st", av_t)

    b.wait("sync", sx, b.tick(sx))
    b.wait("sync", so0, b.tick(so0))
    b.wait("sync", so1, b.tick(so1))
    b.realize(ctx)
    nc._ctx_keepalive = ctx
    return nc


def host_l0_masks():
    diag = np.where(np.arange(128)[:, None] >= np.arange(128)[None, :], 0.0, NEG)
    full = np.zeros((128, 128), np.float32)
    neg = np.full((128, 128), NEG, np.float32)
    out = {}
    for par in (0, 1):
        tiles = []
        for qc in range(8):
            for j in range(9):
                if par == 0:
                    tiles.append(diag if j == 0 else neg)
                else:
                    tiles.append(diag if j == 8 else full)
        out[par] = np.concatenate(tiles, axis=1).astype(BF)
    return out



FW_SCALE = float(1.0 / np.sqrt(8192.0))
NHC = DH // 128


def build_launch_C():
    nc = bass.Bass("TRN2", target_bir_lowering=False)
    xT = nc.dram_tensor("xT", [DM, TOK], F32, kind="ExternalInput").ap()
    xn_i = nc.dram_tensor("xn_i", [DM, TOK], BF16, kind="ExternalInput").ap()
    o_i = nc.dram_tensor("o_i", [DM, TOK], BF16, kind="ExternalInput").ap()
    wo = nc.dram_tensor("wo", [DM, DM], F32, kind="ExternalInput").ap()
    w1t = nc.dram_tensor("w1t", [DM, DH], F32, kind="ExternalInput").ap()
    w3t = nc.dram_tensor("w3t", [DM, DH], F32, kind="ExternalInput").ap()
    w2tp = nc.dram_tensor("w2tp", [DH, DM], F32, kind="ExternalInput").ap()
    h128_i = nc.dram_tensor("h128_i", [128, 128], BF16, kind="ExternalInput").ap()
    h64_i = nc.dram_tensor("h64_i", [128, 64], BF16, kind="ExternalInput").ap()
    x1_o = nc.dram_tensor("x1_o", [DM, TOK], F32, kind="ExternalOutput").ap()
    fw_d = nc.dram_tensor("fw_d", [DH, 512], BF16).ap()
    g_d = nc.dram_tensor("g_d", [DM, TOK], F32).ap()

    ctx = ExitStack()
    sb = lambda n, s, d: ctx.enter_context(nc.sbuf_tensor(n, s, d))
    xn = sb("xn", [128, NCH * TOK], BF16)
    big = sb("big", [128, NHC * 512], BF16)
    w1buf = sb("w1buf", [128, 2 * DM], BF16)
    w3buf = sb("w3buf", [128, 2 * DM], BF16)
    w2buf = sb("w2buf", [128, 2 * DH], BF16)
    wobuf = sb("wobuf", [128, 2 * DM], BF16)
    gate_s = sb("gate_s", [128, 2 * 512], F32)
    hid_bf = sb("hid_bf", [128, 2 * 512], BF16)
    h1_bf = sb("h1_bf", [128, 2 * 512], BF16)
    yb_buf = sb("yb_buf", [64, 2 * 512], BF16)
    h128_sb = sb("h128_sb", [128, 128], BF16)
    h64_sb = sb("h64_sb", [128, 64], BF16)
    g_buf = sb("g_buf", [128, 2 * TOK], F32)
    x_buf2 = sb("x_buf2", [128, 2 * TOK], F32)
    tmp_f = sb("tmp_f", [128, TOK], F32)
    x1_buf = sb("x1_buf", [128, 2 * TOK], F32)
    pall = ctx.enter_context(nc.psum_tensor("pall", [128, 8 * 512], F32))

    def bank(i, width=512):
        return pall[:, 512 * i:512 * i + width]

    b = Bld(nc)
    sx = b.sem("sx")
    sg = b.sem("sg")
    st = b.sem("st")
    sv = b.sem("sv")
    ss = b.sem("ss")
    so = [b.sem("so0"), b.sem("so1")]
    sf = [b.sem("sf0"), b.sem("sf1")]

    b.dma("sync", h128_sb[:, :], h128_i, sx)
    b.dma("sync", h64_sb[:, :], h64_i, sx)
    for dc in range(NCH):
        b.dma("sync", xn[:, dc * TOK:(dc + 1) * TOK],
              xn_i[128 * dc:128 * (dc + 1), :], sx)
    XLOAD = b.tick(sx)

    w1s = Stream(b, "w1s", "gpsimd", 2)
    w3s = Stream(b, "w3s", "gpsimd", 2)
    w2s = Stream(b, "w2s", "gpsimd", 2)
    wos = Stream(b, "wos", "gpsimd", 2)
    ybs = Stream(b, "ybs", "sync", 2)
    gs_in = Stream(b, "gsin", "sync", 2)
    xs_in = Stream(b, "xsin", "sync", 2)

    silu_tick = [0, 0]
    mul_tick = [0, 0]
    h1evac_tick = [0, 0]
    fwdma_tick = [0, 0]
    gdma_tick = [0, 0]
    gsil_tick = [0, 0]
    h1_pass_tick = 0

    for tb in range(2):
        if tb == 1:
            b.wait("sync", st, b.tick(st))
        for hc in range(NHC):
            par = hc % 2
            w1sem, w1tk, w1slot = w1s.load(
                w1buf[:, par * DM:(par + 1) * DM].rearrange("p (c n) -> p c n", n=128),
                w1t[:, 128 * hc:128 * (hc + 1)].rearrange("(c p) n -> p c n", p=128))
            w3sem, w3tk, w3slot = w3s.load(
                w3buf[:, par * DM:(par + 1) * DM].rearrange("p (c n) -> p c n", n=128),
                w3t[:, 128 * hc:128 * (hc + 1)].rearrange("(c p) n -> p c n", p=128))
            gb, lb, hb = 4 * par, 4 * par + 1, 4 * par + 2
            b.wait("tensor", w1sem, w1tk)
            b.wait("tensor", w3sem, w3tk)
            if hc < 2 and tb == 0:
                b.wait("tensor", sx, XLOAD)
            b.wait("tensor", ss, silu_tick[par])
            b.wait("tensor", sv, mul_tick[par])
            for dc in range(NCH):
                b.op("tensor", lambda e, gb=gb, par=par, dc=dc, tb=tb: e.matmul(
                    bank(gb), w1buf[:, par * DM + 128 * dc:par * DM + 128 * (dc + 1)],
                    xn[:, dc * TOK + 512 * tb:dc * TOK + 512 * (tb + 1)],
                    start=(dc == 0), stop=(dc == NCH - 1)))
            for dc in range(NCH):
                b.op("tensor", lambda e, lb=lb, par=par, dc=dc, tb=tb: e.matmul(
                    bank(lb), w3buf[:, par * DM + 128 * dc:par * DM + 128 * (dc + 1)],
                    xn[:, dc * TOK + 512 * tb:dc * TOK + 512 * (tb + 1)],
                    start=(dc == 0), stop=(dc == NCH - 1)),
                    inc=(st, 1) if dc == NCH - 1 else None)
            up_t = b.tick(st)
            w1s.consumed(w1slot, "st", up_t)
            w3s.consumed(w3slot, "st", up_t)
            b.wait("scalar", st, up_t)
            b.op("scalar", lambda e, gb=gb, par=par: e.activation(
                gate_s[:, 512 * par:512 * (par + 1)], bank(gb), AF.Silu),
                inc=(ss, 1))
            silu_tick[par] = b.tick(ss)
            b.wait("vector", ss, silu_tick[par])
            b.wait("vector", st, h1_pass_tick)
            b.op("vector", lambda e, lb=lb, par=par: e.tensor_mul(
                hid_bf[:, 512 * par:512 * (par + 1)], bank(lb),
                gate_s[:, 512 * par:512 * (par + 1)]), inc=(sv, 1))
            mul_tick[par] = b.tick(sv)
            b.wait("tensor", sv, mul_tick[par])
            b.wait("tensor", ss, h1evac_tick[par])
            b.op("tensor", lambda e, hb=hb, par=par: e.matmul(
                bank(hb), h128_sb[:, :], hid_bf[:, 512 * par:512 * (par + 1)],
                start=True, stop=True), inc=(st, 1))
            h1_pass_tick = b.tick(st)
            b.wait("scalar", st, h1_pass_tick)
            b.wait("scalar", sf[par], fwdma_tick[par])
            b.op("scalar", lambda e, hb=hb, par=par: e.copy(
                h1_bf[:, 512 * par:512 * (par + 1)], bank(hb)), inc=(ss, 1))
            h1evac_tick[par] = b.tick(ss)
            b.wait("sync", ss, h1evac_tick[par])
            b.wait("sync", sf[par], b.tick(sf[par]))
            b.op("sync", lambda e, hc=hc, par=par: e.dma_start(
                out=fw_d[128 * hc:128 * (hc + 1), :],
                in_=h1_bf[:, 512 * par:512 * (par + 1)]), inc=(sf[par], 16))
            fwdma_tick[par] = b.tick(sf[par])

        b.wait("sync", sf[0], b.tick(sf[0]))
        b.wait("sync", sf[1], b.tick(sf[1]))
        fw3 = fw_d.rearrange("(c p) t -> c p t", p=128)
        p2evac_tick = [0, 0]
        b.wait("tensor", ss, b.tick(ss))
        b.wait("tensor", sv, b.tick(sv))
        for bp in range(128):
            par = bp % 2
            ysem, yt, yslot = ybs.load(
                yb_buf[0:64, par * 512:(par + 1) * 512],
                fw_d[bp::128, :])
            b.wait("tensor", ysem, yt)
            b.wait("tensor", ss, p2evac_tick[par])
            b.op("tensor", lambda e, par=par: e.matmul(
                pall[0:64, 512 * par:512 * (par + 1)], h64_sb[:, :],
                yb_buf[0:64, par * 512:(par + 1) * 512],
                start=True, stop=True), inc=(st, 1))
            p2_t = b.tick(st)
            ybs.consumed(yslot, "st", p2_t)
            cpp = bp // 2
            poff = 64 * (bp % 2)
            b.wait("scalar", st, p2_t)
            b.op("scalar", lambda e, par=par, cpp=cpp, poff=poff: e.activation(
                big[poff:poff + 64, cpp * 512:(cpp + 1) * 512],
                pall[0:64, 512 * par:512 * (par + 1)], AF.Copy, scale=FW_SCALE),
                inc=(ss, 1))
            p2evac_tick[par] = b.tick(ss)
        HW_S = b.tick(ss)

        for dc in range(NCH):
            par = dc % 2
            w2sem, w2tk, w2slot = w2s.load(
                w2buf[:, par * DH:(par + 1) * DH].rearrange("p (c n) -> p c n", n=128),
                w2tp[:, 128 * dc:128 * (dc + 1)].rearrange("(c p) n -> p c n", p=128))
            db = 4 * par + 3
            b.wait("tensor", w2sem, w2tk)
            if dc == 0:
                b.wait("tensor", ss, HW_S)
            b.wait("tensor", ss, gsil_tick[par])
            for hcc in range(NHC):
                b.op("tensor", lambda e, db=db, hcc=hcc, par=par: e.matmul(
                    bank(db),
                    w2buf[:, par * DH + hcc * 128:par * DH + hcc * 128 + 128],
                    big[:, hcc * 512:(hcc + 1) * 512],
                    start=(hcc == 0), stop=(hcc == NHC - 1)),
                    inc=(st, 1) if hcc == NHC - 1 else None)
            down_t = b.tick(st)
            w2s.consumed(w2slot, "st", down_t)
            b.wait("scalar", st, down_t)
            b.wait("scalar", sf[par], gdma_tick[par])
            b.op("scalar", lambda e, db=db, par=par: e.activation(
                g_buf[:, par * TOK:par * TOK + 512], bank(db), AF.Silu),
                inc=(ss, 1))
            gsil_tick[par] = b.tick(ss)
            b.wait("sync", ss, gsil_tick[par])
            b.wait("sync", sf[par], b.tick(sf[par]))
            b.op("sync", lambda e, dc=dc, tb=tb, par=par: e.dma_start(
                out=g_d[128 * dc:128 * (dc + 1), 512 * tb:512 * (tb + 1)],
                in_=g_buf[:, par * TOK:par * TOK + 512]), inc=(sf[par], 16))
            gdma_tick[par] = b.tick(sf[par])

    b.wait("sync", sf[0], b.tick(sf[0]))
    b.wait("sync", sf[1], b.tick(sf[1]))
    o_sb = big
    b.wait("sync", st, b.tick(st))
    for dc in range(NCH):
        b.dma("sync", o_sb[:, dc * TOK:(dc + 1) * TOK],
              o_i[128 * dc:128 * (dc + 1), :], sx)
    OLOAD = b.tick(sx)
    x1dma_tick = [0, 0]
    sres_tick = [0, 0]
    for oc in range(NCH):
        par = oc % 2
        wsem, wtk, wslot = wos.load(
            wobuf[:, par * DM:(par + 1) * DM].rearrange("p (c n) -> p c n", n=128),
            wo[:, 128 * oc:128 * (oc + 1)].rearrange("(c p) n -> p c n", p=128))
        gsem, gt, gslot = gs_in.load(g_buf[:, par * TOK:(par + 1) * TOK],
                                     g_d[128 * oc:128 * (oc + 1), :])
        xsem, xt, xslot = xs_in.load(x_buf2[:, par * TOK:(par + 1) * TOK],
                                     xT[128 * oc:128 * (oc + 1), :])
        sb0 = 4 * par
        b.wait("tensor", wsem, wtk)
        if oc < 2:
            b.wait("tensor", sx, OLOAD)
        b.wait("tensor", sv, sres_tick[par])
        for h in range(2):
            for dc in range(NCH):
                b.op("tensor", lambda e, sbk=sb0 + h, par=par, dc=dc, h=h: e.matmul(
                    bank(sbk), wobuf[:, par * DM + 128 * dc:par * DM + 128 * (dc + 1)],
                    o_sb[:, dc * TOK + 512 * h:dc * TOK + 512 * (h + 1)],
                    start=(dc == 0), stop=(dc == NCH - 1)),
                    inc=(st, 1) if dc == NCH - 1 else None)
        s_t = b.tick(st)
        wos.consumed(wslot, "st", s_t)
        b.wait("vector", st, s_t)
        b.wait("vector", gsem, gt)
        b.wait("vector", xsem, xt)
        if oc >= 2:
            b.wait("vector", so[par], x1dma_tick[par])
        for h in range(2):
            b.op("vector", lambda e, sbk=sb0 + h, h=h, par=par: e.tensor_mul(
                tmp_f[:, 512 * h:512 * (h + 1)], bank(sbk),
                g_buf[:, par * TOK + 512 * h:par * TOK + 512 * (h + 1)]), inc=(sv, 1))
            b.guard("vector", sv)
            b.op("vector", lambda e, h=h, par=par: e.tensor_add(
                x1_buf[:, par * TOK + 512 * h:par * TOK + 512 * (h + 1)],
                tmp_f[:, 512 * h:512 * (h + 1)],
                x_buf2[:, par * TOK + 512 * h:par * TOK + 512 * (h + 1)]), inc=(sv, 1))
        res_v = b.tick(sv)
        sres_tick[par] = res_v
        gs_in.consumed(gslot, "sv", res_v)
        xs_in.consumed(xslot, "sv", res_v)
        b.wait("sync", sv, res_v)
        b.wait("sync", so[par], b.tick(so[par]))
        b.op("sync", lambda e, oc=oc, par=par: e.dma_start(
            out=x1_o[128 * oc:128 * (oc + 1), :],
            in_=x1_buf[:, par * TOK:(par + 1) * TOK]), inc=(so[par], 16))
        x1dma_tick[par] = b.tick(so[par])

    b.wait("sync", sx, b.tick(sx))
    b.wait("sync", so[0], b.tick(so[0]))
    b.wait("sync", so[1], b.tick(so[1]))
    b.realize(ctx)
    nc._ctx_keepalive = ctx
    return nc


def host_fwht_consts():
    def hmat(n):
        h = np.array([[1.0]])
        while h.shape[0] < n:
            h = np.block([[h, h], [h, -h]])
        return h
    h128 = hmat(128).astype(BF)
    h64 = hmat(64).astype(BF)
    perm = np.empty(DH, np.int64)
    for cpp in range(64):
        for rr in range(128):
            bprime = 2 * cpp + (1 if rr >= 64 else 0)
            aprime = rr % 64
            perm[cpp * 128 + rr] = aprime * 128 + bprime
    return h128, h64, perm



def build_launch_D():
    nc = bass.Bass("TRN2", target_bir_lowering=False)
    x1f = nc.dram_tensor("x1f", [DM, SEQ], F32, kind="ExternalInput").ap()
    wq = nc.dram_tensor("wq", [DM, DM], F32, kind="ExternalInput").ap()
    wv = nc.dram_tensor("wv", [DM, 512], F32, kind="ExternalInput").ap()
    ident_i = nc.dram_tensor("ident_i", [128, 128], BF16, kind="ExternalInput").ap()
    xn1_o = nc.dram_tensor("xn1_o", [DM, TOK], BF16, kind="ExternalOutput").ap()
    q1_o = nc.dram_tensor("q1_o", [DM, TOK], BF16, kind="ExternalOutput").ap()
    k1_o = nc.dram_tensor("k1_o", [512, SEQ], BF16, kind="ExternalOutput").ap()
    v1hm_o = nc.dram_tensor("v1hm_o", [4 * SEQ, 128], BF16, kind="ExternalOutput").ap()

    ctx = ExitStack()
    sb = lambda n, s, d: ctx.enter_context(nc.sbuf_tensor(n, s, d))
    xn = sb("xn", [128, NCH * SEQ], BF16)
    wv_sb = sb("wv_sb", [128, NCH * 512], BF16)
    wq_buf = sb("wq_buf", [128, 2 * DM], BF16)
    x_buf = sb("x_buf", [128, 2 * TOK], F32)
    xbf_buf = sb("xbf_buf", [128, 2 * TOK], BF16)
    sq_buf = sb("sq_buf", [128, 2 * TOK], BF16)
    v_f = sb("v_f", [128, 2 * 512], F32)
    v_bf = sb("v_bf", [128, 2 * 512], BF16)
    k_bf = sb("k_bf", [128, 2 * 512], BF16)
    kt_bf = sb("kt_bf", [128, 2 * 512], BF16)
    sqv_f = sb("sqv_f", [128, 128], F32)
    vsum = sb("vsum", [128, 8], F32)
    vsd = sb("vsd", [128, 8], F32)
    vrstd = sb("vrstd", [128, 8], F32)
    qsq = sb("qsq", [128, TOK], BF16)
    q_bf = sb("q_bf", [128, 2 * TOK], BF16)
    rstd_bc = sb("rstd_bc", [128, TOK], F32)
    ident = sb("ident", [128, 128], BF16)
    ones_bf = sb("ones_bf", [128, 1], BF16)
    ones_f = sb("ones_f", [1, 128], F32)
    epst = sb("epst", [128, 2], F32)
    rows = {k: sb(k, [1, TOK], F32) for k in ("m_row", "ex2", "m2", "var", "sqv", "rstd")}
    rows["tmp32"] = sb("tmp32", [128, 1024], F32)
    qrow = {k: sb("q" + k, [1, TOK], F32) for k in ("sqv", "rstd")}
    pall = ctx.enter_context(nc.psum_tensor("pall", [128, 6 * 512], F32))
    ptp = ctx.enter_context(nc.psum_tensor("ptp", [128, 2 * 1024], BF16))

    def bank(i, width=512):
        return pall[:, 512 * i:512 * i + width]

    b = Bld(nc)
    sx = b.sem("sx")
    sg = b.sem("sg")
    st = b.sem("st")
    sv = b.sem("sv")
    ss = b.sem("ss")
    so = [b.sem("so0"), b.sem("so1")]
    sid = b.sem("sid")

    ps_ones_bf[b] = ones_bf
    ps_ones_f[b] = ones_f
    eps_row[b] = epst
    b.op("vector", lambda e: e.memset(ones_bf[:, :], 1.0), inc=(sv, 1))
    b.op("vector", lambda e: e.memset(ones_f[:, :], 1.0), inc=(sv, 1))
    b.op("vector", lambda e: e.memset(epst[:, 0:1], LN_EPS), inc=(sv, 1))
    b.op("vector", lambda e: e.memset(epst[:, 1:2], RMS_EPS), inc=(sv, 1))
    consts = b.tick(sv)
    b.dma("sync", ident[:, :], ident_i, sid)
    IDLOAD = b.tick(sid)

    x_stream = Stream(b, "lx", "sync", 2)
    x_stream.buf = x_buf
    xbf_stream = Stream(b, "lb", "gpsimd", 2)
    xbf_stream.buf = xbf_buf

    for th in range(2):
        xt_v = emit_ln_stats_and_xn(
            b, bank, consts, x1f[:, TOK * th:TOK * (th + 1)], x_stream, xbf_stream,
            sq_buf, rows, xn, TOK, sx, sg, st, sv, ss,
            xn_out_dram=(xn1_o if th == 0 else None),
            xn_cols=lambda dc, th=th: xn[:, dc * SEQ + TOK * th: dc * SEQ + TOK * (th + 1)])
    XN_V = b.tick(sv)

    for dc in range(NCH):
        b.dma("gpsimd", wv_sb[:, dc * 512:(dc + 1) * 512],
              wv[128 * dc:128 * (dc + 1), :], sg)
    WVL = b.tick(sg)
    b.wait("tensor", sg, WVL)
    b.wait("tensor", sv, XN_V)
    vevac_tick = [0, 0]
    kdma_tick = [0, 0]
    vdma_tick = [0, 0]
    trevac_tick = [0, 0]
    kmul_tick = [0, 0]
    ktr_tick = [0, 0]
    for tc in range(SEQ // 128):
        par = tc % 2
        vb = par
        b.wait("tensor", ss, vevac_tick[par])
        for dc in range(NCH):
            b.op("tensor", lambda e, vb=vb, tc=tc, dc=dc: e.matmul(
                bank(vb), xn[:, dc * SEQ + 128 * tc:dc * SEQ + 128 * (tc + 1)],
                wv_sb[:, dc * 512:(dc + 1) * 512],
                start=(dc == 0), stop=(dc == NCH - 1)),
                inc=(st, 1) if dc == NCH - 1 else None)
        v_t = b.tick(st)
        b.wait("scalar", st, v_t)
        b.wait("scalar", so[0], vdma_tick[par])
        b.wait("scalar", sv, kmul_tick[par])
        b.op("scalar", lambda e, vb=vb, par=par: e.copy(
            v_f[:, 512 * par:512 * (par + 1)], bank(vb)), inc=(ss, 1))
        b.guard("scalar", ss)
        b.op("scalar", lambda e, vb=vb, par=par: e.copy(
            v_bf[:, 512 * par:512 * (par + 1)], bank(vb)), inc=(ss, 1))
        vevac_tick[par] = b.tick(ss)
        VE = b.tick(ss)
        b.wait("scalar", ss, VE)
        for kvh in range(4):
            col = 512 * par + 128 * kvh
            b.op("scalar", lambda e, col=col: e.activation(
                sqv_f[:, 0:128], v_f[:, col:col + 128], AF.Square), inc=(ss, 1))
            sq_t = b.tick(ss)
            b.wait("vector", ss, sq_t)
            b.op("vector", lambda e, kvh=kvh: e.reduce_sum(
                vsum[:, kvh:kvh + 1], sqv_f[:, 0:128], axis=AX.X), inc=(sv, 1))
            sum_v = b.tick(sv)
            b.wait("scalar", sv, sum_v)
            b.op("scalar", lambda e, kvh=kvh: e.activation(
                vsd[:, kvh:kvh + 1], vsum[:, kvh:kvh + 1], AF.Sqrt,
                scale=1.0 / 128.0, bias=epst[:, 1:2]), inc=(ss, 1))
            sd_t = b.tick(ss)
            b.wait("vector", ss, sd_t)
            b.op("vector", lambda e, kvh=kvh: e.reciprocal(
                vrstd[:, kvh:kvh + 1], vsd[:, kvh:kvh + 1]), inc=(sv, 1))
            b.guard("vector", sv)
            if kvh == 0:
                b.wait("vector", st, ktr_tick[par])
            b.op("vector", lambda e, col=col, kvh=kvh: e.tensor_scalar_mul(
                k_bf[:, col:col + 128], v_f[:, col:col + 128],
                vrstd[:, kvh:kvh + 1]), inc=(sv, 1))
        krow_v = b.tick(sv)
        kmul_tick[par] = krow_v
        b.wait("tensor", sv, krow_v)
        if tc == 0:
            b.wait("tensor", sid, IDLOAD)
        for kvh in range(4):
            j = kvh % 2
            b.wait("tensor", ss, trevac_tick[j])
            b.op("tensor", lambda e, par=par, kvh=kvh, j=j: e.transpose(
                ptp[:, 1024 * j:1024 * j + 128],
                k_bf[:, 512 * par + 128 * kvh:512 * par + 128 * (kvh + 1)],
                ident[:, :]), inc=(st, 1))
            tr_t = b.tick(st)
            b.wait("scalar", st, tr_t)
            b.wait("scalar", so[1], kdma_tick[j])
            b.op("scalar", lambda e, j=j: e.copy(
                kt_bf[:, 512 * j:512 * j + 128], ptp[:, 1024 * j:1024 * j + 128]),
                inc=(ss, 1))
            trevac_tick[j] = b.tick(ss)
            b.wait("sync", ss, trevac_tick[j])
            b.wait("sync", so[1], b.tick(so[1]))
            b.op("sync", lambda e, kvh=kvh, tc=tc, j=j: e.dma_start(
                out=k1_o[128 * kvh:128 * (kvh + 1), 128 * tc:128 * (tc + 1)],
                in_=kt_bf[:, 512 * j:512 * j + 128]), inc=(so[1], 16))
            kdma_tick[j] = b.tick(so[1])
        ktr_tick[par] = b.tick(st)
        b.wait("sync", ss, VE)
        b.wait("sync", so[0], b.tick(so[0]))
        for kvh in range(4):
            b.op("sync", lambda e, kvh=kvh, tc=tc, par=par: e.dma_start(
                out=v1hm_o[SEQ * kvh + 128 * tc:SEQ * kvh + 128 * (tc + 1), :],
                in_=v_bf[:, 512 * par + 128 * kvh:512 * par + 128 * (kvh + 1)]),
                inc=(so[0], 16))
        vdma_tick[par] = b.tick(so[0])

    wqs = Stream(b, "wqs", "gpsimd", 2)
    qdma_tick = [0, 0]
    qmul_tick = 0
    for qh in range(16):
        par = qh % 2
        wsem, wtk, wslot = wqs.load(
            wq_buf[:, par * DM:(par + 1) * DM].rearrange("p (c n) -> p c n", n=128),
            wq[:, 128 * qh:128 * (qh + 1)].rearrange("(c p) n -> p c n", p=128))
        b.wait("tensor", wsem, wtk)
        b.wait("tensor", sv, qmul_tick)
        b.wait("tensor", ss, b.tick(ss))
        for h in range(2):
            for dc in range(NCH):
                b.op("tensor", lambda e, h=h, par=par, dc=dc: e.matmul(
                    bank(h), wq_buf[:, par * DM + 128 * dc:par * DM + 128 * (dc + 1)],
                    xn[:, dc * SEQ + 512 * h:dc * SEQ + 512 * (h + 1)],
                    start=(dc == 0), stop=(dc == NCH - 1)),
                    inc=(st, 1) if dc == NCH - 1 else None)
        q_t = b.tick(st)
        wqs.consumed(wslot, "st", q_t)
        b.wait("scalar", st, q_t)
        b.op("scalar", lambda e: e.activation(qsq[:, :], pall[:, 0:1024], AF.Square),
             inc=(ss, 1))
        qsq_t = b.tick(ss)
        b.wait("tensor", ss, qsq_t)
        for h in range(2):
            b.op("tensor", lambda e, h=h: e.matmul(
                pall[0:1, 512 * (2 + h):512 * (3 + h)], ones_bf[:, :],
                qsq[:, 512 * h:512 * (h + 1)], start=True, stop=True),
                inc=(st, 1) if h == 1 else None)
        qs_t = b.tick(st)
        b.wait("scalar", st, qs_t)
        b.op("scalar", lambda e: e.activation(
            qrow["sqv"][:, :], pall[0:1, 1024:2048], AF.Sqrt, scale=1.0 / 128.0,
            bias=epst[0:1, 1:2]), inc=(ss, 1))
        b.wait("vector", ss, b.tick(ss))
        b.op("vector", lambda e: e.reciprocal(qrow["rstd"][:, :], qrow["sqv"][:, :]),
             inc=(sv, 1))
        qr_v = b.tick(sv)
        b.wait("tensor", sv, qr_v)
        b.wait("tensor", sv, qmul_tick)
        for h in range(2):
            b.op("tensor", lambda e, h=h: e.matmul(
                pall[:, 512 * (4 + h):512 * (5 + h)], ones_f[:, :],
                qrow["rstd"][:, 512 * h:512 * (h + 1)], start=True, stop=True),
                inc=(st, 1) if h == 1 else None)
        bc_t = b.tick(st)
        b.wait("scalar", st, bc_t)
        b.wait("scalar", sv, qmul_tick)
        b.op("scalar", lambda e: e.copy(rstd_bc[:, :], pall[:, 2048:3072]),
             inc=(ss, 1))
        bce_t = b.tick(ss)
        b.wait("vector", ss, bce_t)
        if qh >= 2:
            b.wait("vector", so[par], qdma_tick[par])
        b.op("vector", lambda e, par=par: e.tensor_mul(
            q_bf[:, par * TOK:(par + 1) * TOK], pall[:, 0:1024], rstd_bc[:, :]),
            inc=(sv, 1))
        qmul_tick = b.tick(sv)
        b.wait("sync", sv, qmul_tick)
        b.wait("sync", so[par], b.tick(so[par]))
        b.op("sync", lambda e, qh=qh, par=par: e.dma_start(
            out=q1_o[128 * qh:128 * (qh + 1), :],
            in_=q_bf[:, par * TOK:(par + 1) * TOK]), inc=(so[par], 16))
        qdma_tick[par] = b.tick(so[par])

    b.wait("sync", sx, b.tick(sx))
    b.wait("sync", so[0], b.tick(so[0]))
    b.wait("sync", so[1], b.tick(so[1]))
    b.wait("gpsimd", sg, b.tick(sg))
    b.realize(ctx)
    nc._ctx_keepalive = ctx
    return nc



def build_launch_E():
    nc = bass.Bass("TRN2", target_bir_lowering=False)
    q_i = nc.dram_tensor("q_i", [DM, TOK], BF16, kind="ExternalInput").ap()
    k_i = nc.dram_tensor("k_i", [512, SEQ], BF16, kind="ExternalInput").ap()
    vhm_i = nc.dram_tensor("vhm_i", [4 * SEQ, 128], BF16, kind="ExternalInput").ap()
    ident_i = nc.dram_tensor("ident_i", [128, 128], BF16, kind="ExternalInput").ap()
    wo = nc.dram_tensor("wo", [DM, DM], F32, kind="ExternalInput").ap()
    s1n_o = nc.dram_tensor("s1n_o", [DM, TOK], F32, kind="ExternalOutput").ap()

    NQC = TOK // 128
    NKB = SEQ // 128

    ctx = ExitStack()
    sb = lambda n, s, d: ctx.enter_context(nc.sbuf_tensor(n, s, d))
    q_sb = sb("q_sb", [128, NCH * TOK], BF16)
    k_sb = sb("k_sb", [128, 4 * SEQ], BF16)
    v_sb = sb("v_sb", [128, 4 * SEQ], BF16)
    y_sb = sb("y_sb", [128, NCH * TOK], BF16)
    s_sb = sb("s_sb", [128, NCH * TOK], F32)
    p_sb = sb("p_sb", [128, SEQ], BF16)
    pn_sb = sb("pn_sb", [128, SEQ], BF16)
    pt_sb = sb("pt_sb", [128, NKB * 128], BF16)
    wobuf = sb("wobuf", [128, 2 * DM], BF16)
    sq_bf = sb("sq_bf", [128, TOK], BF16)
    s1n_buf = sb("s1n_buf", [128, 2 * TOK], F32)
    rstd_bc = sb("rstd_bc", [128, TOK], F32)
    ident = sb("ident", [128, 128], BF16)
    ones_bf = sb("ones_bf", [128, 1], BF16)
    ones_f = sb("ones_f", [1, 128], F32)
    epst = sb("epst", [128, 2], F32)
    mneg = sb("mneg", [128, 3], F32)
    l_col = sb("l_col", [128, 6], F32)
    linv_col = sb("linv_col", [128, 3], F32)
    srow = {k: sb("s" + k, [1, TOK], F32) for k in ("sqv", "rstd")}
    pall = ctx.enter_context(nc.psum_tensor("pall", [128, 6 * 512], F32))
    ptp = ctx.enter_context(nc.psum_tensor("ptp", [128, 2 * 1024], BF16))

    b = Bld(nc)
    sx = b.sem("sx")
    sg = b.sem("sg")
    st = b.sem("st")
    sv = b.sem("sv")
    ss = b.sem("ss")
    so = [b.sem("so0"), b.sem("so1")]

    b.op("vector", lambda e: e.memset(ones_bf[:, :], 1.0), inc=(sv, 1))
    b.op("vector", lambda e: e.memset(ones_f[:, :], 1.0), inc=(sv, 1))
    b.op("vector", lambda e: e.memset(epst[:, 1:2], RMS_EPS), inc=(sv, 1))
    CONSTS_V = b.tick(sv)
    b.dma("sync", ident[:, :], ident_i, sx)
    for dc in range(NCH):
        b.dma("sync", q_sb[:, dc * TOK:(dc + 1) * TOK],
              q_i[128 * dc:128 * (dc + 1), :], sx)
    for kvh in range(4):
        b.dma("sync", k_sb[:, kvh * SEQ:(kvh + 1) * SEQ],
              k_i[128 * kvh:128 * (kvh + 1), :], sx)
        b.dma("sync",
              v_sb[:, kvh * SEQ:(kvh + 1) * SEQ].rearrange("p (c n) -> p c n", n=128),
              vhm_i[kvh * SEQ:(kvh + 1) * SEQ, :].rearrange("(c p) n -> p c n", p=128),
              sx)
    LOAD_X = b.tick(sx)

    S = pall[:, 0:2048]
    OT = lambda s: pall[:, (4 + s) * 512: (4 + s) * 512 + 128]
    PTP = lambda j: ptp[:, 1024 * (j % 2): 1024 * (j % 2) + 128]

    pt_evac_tick = [0, 0]
    ot_evac_tick = [0, 0]
    l_recip_tick = 0
    tr_last = 0
    exp_tick = [0]

    for qh in range(16):
        kvh = qh // 4
        kbase = kvh * SEQ
        for qc in range(NQC):
            b.wait("tensor", ss, exp_tick[0])
            if qh == 0 and qc == 0:
                b.wait("tensor", sx, LOAD_X)
            for kg in range(4):
                b.op("tensor", lambda e, qh=qh, qc=qc, kg=kg, kbase=kbase: e.matmul(
                    S[:, 512 * kg:512 * (kg + 1)],
                    q_sb[:, qh * TOK + 128 * qc:qh * TOK + 128 * (qc + 1)],
                    k_sb[:, kbase + 512 * kg:kbase + 512 * (kg + 1)],
                    start=True, stop=True), inc=(st, 1))
            s_mm = b.tick(st)
            b.wait("vector", st, s_mm)
            b.guard("vector", sv)
            b.op("vector", lambda e: e.reduce_max(
                mneg[:, :], S[:, :], axis=AX.X, negate=True), inc=(sv, 1))
            mneg_v = b.tick(sv)
            b.wait("scalar", sv, mneg_v)
            b.wait("scalar", sv, l_recip_tick)
            b.op("scalar", lambda e: e.activation(
                p_sb[:, :], S[:, :], AF.Exp, bias=mneg[:, :],
                accum_out=l_col[:, :]), inc=(ss, 1))
            exp_tick[0] = b.tick(ss)
            b.wait("vector", ss, exp_tick[0])
            b.op("vector", lambda e: e.reciprocal(linv_col[:, :], l_col[:, :]),
                 inc=(sv, 1))
            l_recip_tick = b.tick(sv)
            b.guard("vector", sv)
            b.wait("vector", st, tr_last)
            b.op("vector", lambda e: e.tensor_scalar_mul(
                pn_sb[:, :], p_sb[:, :], linv_col[:, :]), inc=(sv, 1))
            pn_v = b.tick(sv)
            b.wait("tensor", sv, pn_v)
            for kb in range(NKB):
                j = kb % 2
                b.wait("tensor", ss, pt_evac_tick[j])
                b.op("tensor", lambda e, kb=kb, j=j: e.transpose(
                    PTP(j), pn_sb[:, 128 * kb:128 * (kb + 1)], ident[:, :]),
                    inc=(st, 1))
                tr_t = b.tick(st)
                b.wait("scalar", st, tr_t)
                b.op("scalar", lambda e, kb=kb, j=j: e.copy(
                    pt_sb[:, 128 * kb:128 * (kb + 1)], PTP(j)), inc=(ss, 1))
                pt_evac_tick[j] = b.tick(ss)
            tr_last = b.tick(st)
            ots = qc % 2
            b.wait("tensor", ss, b.tick(ss))
            b.wait("tensor", ss, ot_evac_tick[ots])
            for kb in range(NKB):
                b.op("tensor", lambda e, kb=kb, ots=ots, kbase=kbase: e.matmul(
                    OT(ots), v_sb[:, kbase + 128 * kb:kbase + 128 * (kb + 1)],
                    pt_sb[:, 128 * kb:128 * (kb + 1)],
                    start=(kb == 0), stop=(kb == NKB - 1)),
                    inc=(st, 1) if kb == NKB - 1 else None)
            av_t = b.tick(st)
            b.wait("scalar", st, av_t)
            b.op("scalar", lambda e, qh=qh, qc=qc, ots=ots: e.copy(
                y_sb[:, qh * TOK + 128 * qc:qh * TOK + 128 * (qc + 1)], OT(ots)),
                inc=(ss, 1))
            ot_evac_tick[ots] = b.tick(ss)
    Y_S = b.tick(ss)

    wos = Stream(b, "wos", "gpsimd", 2)
    STAT0 = pall[0:1, 4 * 512:4 * 512 + 512]
    STAT1 = pall[0:1, 5 * 512:5 * 512 + 512]
    sevac_tick = [0, 0]
    for oc in range(NCH):
        par = oc % 2
        wsem, wtk, wslot = wos.load(
            wobuf[:, par * DM:(par + 1) * DM].rearrange("p (c n) -> p c n", n=128),
            wo[:, 128 * oc:128 * (oc + 1)].rearrange("(c p) n -> p c n", p=128))
        sb0 = 2 * par
        b.wait("tensor", wsem, wtk)
        if oc == 0:
            b.wait("tensor", ss, Y_S)
        b.wait("tensor", ss, sevac_tick[par])
        for h in range(2):
            for dc in range(NCH):
                b.op("tensor", lambda e, sbk=sb0 + h, par=par, dc=dc, h=h: e.matmul(
                    pall[:, 512 * sbk:512 * (sbk + 1)],
                    wobuf[:, par * DM + 128 * dc:par * DM + 128 * (dc + 1)],
                    y_sb[:, dc * TOK + 512 * h:dc * TOK + 512 * (h + 1)],
                    start=(dc == 0), stop=(dc == NCH - 1)),
                    inc=(st, 1) if dc == NCH - 1 else None)
        s_t = b.tick(st)
        wos.consumed(wslot, "st", s_t)
        b.wait("scalar", st, s_t)
        for h in range(2):
            b.op("scalar", lambda e, sbk=sb0 + h, oc=oc, h=h: e.copy(
                s_sb[:, oc * TOK + 512 * h:oc * TOK + 512 * (h + 1)],
                pall[:, 512 * sbk:512 * (sbk + 1)]), inc=(ss, 1))
            b.op("scalar", lambda e, sbk=sb0 + h, h=h: e.activation(
                sq_bf[:, 512 * h:512 * (h + 1)],
                pall[:, 512 * sbk:512 * (sbk + 1)], AF.Square), inc=(ss, 1))
        sevac_tick[par] = b.tick(ss)
        b.wait("tensor", ss, sevac_tick[par])
        if oc == 0:
            b.wait("tensor", sv, CONSTS_V)
        b.op("tensor", lambda e, oc=oc: e.matmul(
            STAT0, ones_bf[:, :], sq_bf[:, 0:512],
            start=(oc == 0), stop=(oc == NCH - 1)))
        b.op("tensor", lambda e, oc=oc: e.matmul(
            STAT1, ones_bf[:, :], sq_bf[:, 512:1024],
            start=(oc == 0), stop=(oc == NCH - 1)),
            inc=(st, 1) if oc == NCH - 1 else None)
    STATS_T = b.tick(st)
    b.wait("scalar", st, STATS_T)
    b.op("scalar", lambda e: e.activation(
        srow["sqv"][:, 0:512], STAT0, AF.Sqrt, scale=1.0 / DM,
        bias=epst[0:1, 1:2]), inc=(ss, 1))
    b.op("scalar", lambda e: e.activation(
        srow["sqv"][:, 512:1024], STAT1, AF.Sqrt, scale=1.0 / DM,
        bias=epst[0:1, 1:2]), inc=(ss, 1))
    b.wait("vector", ss, b.tick(ss))
    b.op("vector", lambda e: e.reciprocal(srow["rstd"][:, :], srow["sqv"][:, :]),
         inc=(sv, 1))
    RST_V = b.tick(sv)
    b.wait("tensor", sv, RST_V)
    b.wait("tensor", ss, b.tick(ss))
    for h in range(2):
        b.op("tensor", lambda e, h=h: e.matmul(
            pall[:, 512 * h:512 * (h + 1)], ones_f[:, :],
            srow["rstd"][:, 512 * h:512 * (h + 1)], start=True, stop=True),
            inc=(st, 1) if h == 1 else None)
    BC_T = b.tick(st)
    b.wait("scalar", st, BC_T)
    b.op("scalar", lambda e: e.copy(rstd_bc[:, :], pall[:, 0:1024]), inc=(ss, 1))
    BCE_S = b.tick(ss)
    b.wait("vector", ss, BCE_S)
    for oc in range(NCH):
        par = oc % 2
        if oc >= 2:
            b.wait("vector", so[par], s1dma[par])
        b.op("vector", lambda e, oc=oc, par=par: e.tensor_mul(
            s1n_buf[:, par * TOK:(par + 1) * TOK],
            s_sb[:, oc * TOK:(oc + 1) * TOK], rstd_bc[:, :]), inc=(sv, 1))
        mul_v = b.tick(sv)
        b.wait("sync", sv, mul_v)
        b.wait("sync", so[par], b.tick(so[par]))
        if oc == 0:
            s1dma = [0, 0]
        b.op("sync", lambda e, oc=oc, par=par: e.dma_start(
            out=s1n_o[128 * oc:128 * (oc + 1), :],
            in_=s1n_buf[:, par * TOK:(par + 1) * TOK]), inc=(so[par], 16))
        s1dma[par] = b.tick(so[par])

    b.wait("sync", sx, b.tick(sx))
    b.wait("sync", so[0], b.tick(so[0]))
    b.wait("sync", so[1], b.tick(so[1]))
    b.wait("gpsimd", sg, b.tick(sg))
    b.realize(ctx)
    nc._ctx_keepalive = ctx
    return nc



def build_launch_F():
    nc = bass.Bass("TRN2", target_bir_lowering=False)
    xn1_i = nc.dram_tensor("xn1_i", [DM, TOK], BF16, kind="ExternalInput").ap()
    s1n_i = nc.dram_tensor("s1n_i", [DM, TOK], F32, kind="ExternalInput").ap()
    x1_i = nc.dram_tensor("x1_i", [DM, TOK], F32, kind="ExternalInput").ap()
    wgt = nc.dram_tensor("wgt", [DM, DH], F32, kind="ExternalInput").ap()
    wut = nc.dram_tensor("wut", [DM, DH], F32, kind="ExternalInput").ap()
    wdt = nc.dram_tensor("wdt", [DH, DM], F32, kind="ExternalInput").ap()
    y_o = nc.dram_tensor("y_o", [DM, TOK], F32, kind="ExternalOutput").ap()

    ctx = ExitStack()
    sb = lambda n, s, d: ctx.enter_context(nc.sbuf_tensor(n, s, d))
    xn = sb("xn", [128, NCH * TOK], BF16)
    big = sb("big", [128, NHC * 512], BF16)
    wgbuf = sb("wgbuf", [128, 2 * DM], BF16)
    wubuf = sb("wubuf", [128, 2 * DM], BF16)
    wdbuf = sb("wdbuf", [128, 2 * DH], BF16)
    gate_s = sb("gate_s", [128, 2 * 512], F32)
    g_buf = sb("g_buf", [128, 2 * 512], F32)
    s_buf = sb("s_buf", [128, 2 * 512], F32)
    x_buf = sb("x_buf", [128, 2 * 512], F32)
    tmp_f = sb("tmp_f", [128, 2 * 512], F32)
    y_buf = sb("y_buf", [128, 2 * 512], F32)
    pall = ctx.enter_context(nc.psum_tensor("pall", [128, 8 * 512], F32))

    def bank(i, width=512):
        return pall[:, 512 * i:512 * i + width]

    b = Bld(nc)
    sx = b.sem("sx")
    sg = b.sem("sg")
    st = b.sem("st")
    sv = b.sem("sv")
    ss = b.sem("ss")
    so = [b.sem("so0"), b.sem("so1")]

    for dc in range(NCH):
        b.dma("sync", xn[:, dc * TOK:(dc + 1) * TOK],
              xn1_i[128 * dc:128 * (dc + 1), :], sx)
    XLOAD = b.tick(sx)

    wgs = Stream(b, "wgs", "gpsimd", 2)
    wus = Stream(b, "wus", "gpsimd", 2)
    wds = Stream(b, "wds", "gpsimd", 2)
    s_in = Stream(b, "sin", "sync", 2)
    x_in = Stream(b, "xin", "sync", 2)

    silu_tick = [0, 0]
    hmul_tick = [0, 0]
    gsil_tick = [0, 0]
    res_tick = [0, 0]
    ydma_tick = [0, 0]
    down_last = 0

    for tb in range(2):
        for hc in range(NHC):
            par = hc % 2
            wgsem, wgtk, wgslot = wgs.load(
                wgbuf[:, par * DM:(par + 1) * DM].rearrange("p (c n) -> p c n", n=128),
                wgt[:, 128 * hc:128 * (hc + 1)].rearrange("(c p) n -> p c n", p=128))
            wusem, wutk, wuslot = wus.load(
                wubuf[:, par * DM:(par + 1) * DM].rearrange("p (c n) -> p c n", n=128),
                wut[:, 128 * hc:128 * (hc + 1)].rearrange("(c p) n -> p c n", p=128))
            gb, ub = 4 * par, 4 * par + 1
            b.wait("tensor", wgsem, wgtk)
            b.wait("tensor", wusem, wutk)
            if hc < 2 and tb == 0:
                b.wait("tensor", sx, XLOAD)
            b.wait("tensor", ss, silu_tick[par])
            b.wait("tensor", sv, hmul_tick[par])
            for dc in range(NCH):
                b.op("tensor", lambda e, gb=gb, par=par, dc=dc, tb=tb: e.matmul(
                    bank(gb), wgbuf[:, par * DM + 128 * dc:par * DM + 128 * (dc + 1)],
                    xn[:, dc * TOK + 512 * tb:dc * TOK + 512 * (tb + 1)],
                    start=(dc == 0), stop=(dc == NCH - 1)))
            for dc in range(NCH):
                b.op("tensor", lambda e, ub=ub, par=par, dc=dc, tb=tb: e.matmul(
                    bank(ub), wubuf[:, par * DM + 128 * dc:par * DM + 128 * (dc + 1)],
                    xn[:, dc * TOK + 512 * tb:dc * TOK + 512 * (tb + 1)],
                    start=(dc == 0), stop=(dc == NCH - 1)),
                    inc=(st, 1) if dc == NCH - 1 else None)
            up_t = b.tick(st)
            wgs.consumed(wgslot, "st", up_t)
            wus.consumed(wuslot, "st", up_t)
            b.wait("scalar", st, up_t)
            b.op("scalar", lambda e, gb=gb, par=par: e.activation(
                gate_s[:, 512 * par:512 * (par + 1)], bank(gb), AF.Silu),
                inc=(ss, 1))
            silu_tick[par] = b.tick(ss)
            b.wait("vector", ss, silu_tick[par])
            if hc < 2:
                b.wait("vector", st, down_last)
            b.op("vector", lambda e, ub=ub, par=par, hc=hc: e.tensor_mul(
                big[:, hc * 512:(hc + 1) * 512], bank(ub),
                gate_s[:, 512 * par:512 * (par + 1)]), inc=(sv, 1))
            hmul_tick[par] = b.tick(sv)
        HID_V = b.tick(sv)

        for dc in range(NCH):
            par = dc % 2
            wdsem, wdtk, wdslot = wds.load(
                wdbuf[:, par * DH:(par + 1) * DH].rearrange("p (c n) -> p c n", n=128),
                wdt[:, 128 * dc:128 * (dc + 1)].rearrange("(c p) n -> p c n", p=128))
            ssem, stk, sslot = s_in.load(s_buf[:, par * 512:(par + 1) * 512],
                                         s1n_i[128 * dc:128 * (dc + 1),
                                               512 * tb:512 * (tb + 1)])
            xsem, xtk, xslot = x_in.load(x_buf[:, par * 512:(par + 1) * 512],
                                         x1_i[128 * dc:128 * (dc + 1),
                                              512 * tb:512 * (tb + 1)])
            db = 4 * par + 3
            b.wait("tensor", wdsem, wdtk)
            if dc == 0:
                b.wait("tensor", sv, HID_V)
            b.wait("tensor", ss, gsil_tick[par])
            for hcc in range(NHC):
                b.op("tensor", lambda e, db=db, hcc=hcc, par=par: e.matmul(
                    bank(db), wdbuf[:, par * DH + hcc * 128:par * DH + hcc * 128 + 128],
                    big[:, hcc * 512:(hcc + 1) * 512],
                    start=(hcc == 0), stop=(hcc == NHC - 1)),
                    inc=(st, 1) if hcc == NHC - 1 else None)
            down_t = b.tick(st)
            down_last = down_t
            wds.consumed(wdslot, "st", down_t)
            b.wait("scalar", st, down_t)
            b.wait("scalar", sv, res_tick[par])
            b.op("scalar", lambda e, db=db, par=par: e.activation(
                g_buf[:, 512 * par:512 * (par + 1)], bank(db), AF.Silu),
                inc=(ss, 1))
            gsil_tick[par] = b.tick(ss)
            b.wait("vector", ss, gsil_tick[par])
            b.wait("vector", ssem, stk)
            b.wait("vector", xsem, xtk)
            b.wait("vector", so[par], ydma_tick[par])
            b.guard("vector", sv)
            b.op("vector", lambda e, par=par: e.tensor_mul(
                tmp_f[:, 512 * par:512 * (par + 1)],
                g_buf[:, 512 * par:512 * (par + 1)],
                s_buf[:, 512 * par:512 * (par + 1)]), inc=(sv, 1))
            b.guard("vector", sv)
            b.op("vector", lambda e, par=par: e.tensor_add(
                y_buf[:, 512 * par:512 * (par + 1)],
                tmp_f[:, 512 * par:512 * (par + 1)],
                x_buf[:, 512 * par:512 * (par + 1)]), inc=(sv, 1))
            res_v = b.tick(sv)
            res_tick[par] = res_v
            s_in.consumed(sslot, "sv", res_v)
            x_in.consumed(xslot, "sv", res_v)
            b.wait("sync", sv, res_v)
            b.wait("sync", so[par], b.tick(so[par]))
            b.op("sync", lambda e, dc=dc, tb=tb, par=par: e.dma_start(
                out=y_o[128 * dc:128 * (dc + 1), 512 * tb:512 * (tb + 1)],
                in_=y_buf[:, 512 * par:512 * (par + 1)]), inc=(so[par], 16))
            ydma_tick[par] = b.tick(so[par])

    b.wait("sync", sx, b.tick(sx))
    b.wait("sync", so[0], b.tick(so[0]))
    b.wait("sync", so[1], b.tick(so[1]))
    b.wait("gpsimd", sg, b.tick(sg))
    b.realize(ctx)
    nc._ctx_keepalive = ctx
    return nc




def _l0_sched(qb):
    lks = list(range(qb + 1))
    if qb < 8:
        lks += list(range(8, 16))
    return lks


def build_fused():
    nc = bass.Bass("TRN2", target_bir_lowering=False)
    xT = nc.dram_tensor("xT", [DM, SEQ], F32, kind="ExternalInput").ap()
    wq0 = nc.dram_tensor("wq0", [DM, DM], F32, kind="ExternalInput").ap()
    wk0 = nc.dram_tensor("wk0", [DM, DM], F32, kind="ExternalInput").ap()
    wv0 = nc.dram_tensor("wv0", [DM, DM], F32, kind="ExternalInput").ap()
    wo0 = nc.dram_tensor("wo0", [DM, DM], F32, kind="ExternalInput").ap()
    w1t = nc.dram_tensor("w1t", [DM, DH], F32, kind="ExternalInput").ap()
    w3t = nc.dram_tensor("w3t", [DM, DH], F32, kind="ExternalInput").ap()
    w2tp = nc.dram_tensor("w2tp", [DH, DM], F32, kind="ExternalInput").ap()
    wq1 = nc.dram_tensor("wq1", [DM, DM], F32, kind="ExternalInput").ap()
    wv1 = nc.dram_tensor("wv1", [DM, 512], F32, kind="ExternalInput").ap()
    wo1 = nc.dram_tensor("wo1", [DM, DM], F32, kind="ExternalInput").ap()
    wgt = nc.dram_tensor("wgt", [DM, DH], F32, kind="ExternalInput").ap()
    wut = nc.dram_tensor("wut", [DM, DH], F32, kind="ExternalInput").ap()
    wdt = nc.dram_tensor("wdt", [DH, DM], F32, kind="ExternalInput").ap()
    mask_i = nc.dram_tensor("mask_i", [128, 17 * 128], BF16, kind="ExternalInput").ap()
    mrow_i = nc.dram_tensor("mrow_i", [1, 128], BF16, kind="ExternalInput").ap()
    ident_i = nc.dram_tensor("ident_i", [128, 128], BF16, kind="ExternalInput").ap()
    h128_i = nc.dram_tensor("h128_i", [128, 128], BF16, kind="ExternalInput").ap()
    h64_i = nc.dram_tensor("h64_i", [128, 64], BF16, kind="ExternalInput").ap()
    y_o = nc.dram_tensor("y_o", [DM, TOK], F32, kind="ExternalOutput").ap()

    fw_d = nc.dram_tensor("fw_d", [DH, TOK], BF16).ap()
    g_d = nc.dram_tensor("g_d", [DM, TOK], F32).ap()
    q0_d = nc.dram_tensor("q0_d", [DM, TOK], BF16).ap()
    k0_d = nc.dram_tensor("k0_d", [DM, SEQ], BF16).ap()
    v0hm_d = nc.dram_tensor("v0hm_d", [16 * SEQ, 128], BF16).ap()
    x1_d = nc.dram_tensor("x1_d", [DM, TOK], F32).ap()
    s1n_d = nc.dram_tensor("s1n_d", [DM, TOK], F32).ap()
    hd_d = nc.dram_tensor("hd_d", [DH, TOK], BF16).ap()
    v1pad_t = nc.dram_tensor("v1pad_d", [SEQ, 512], F32)
    v1pad_d = v1pad_t.ap()
    v1ag_t = nc.dram_tensor("v1ag_d", [SEQ, 512], F32)
    v1ag_d = v1ag_t.ap()
    msel_i = nc.dram_tensor("msel_i", [128, 2], F32, kind="ExternalInput").ap()

    ctx = ExitStack()
    sb = lambda n, s, d: ctx.enter_context(nc.sbuf_tensor(n, s, d))
    ident = sb("ident", [128, 128], BF16)
    h128_sb = sb("h128_sb", [128, 128], BF16)
    h64_sb = sb("h64_sb", [128, 64], BF16)
    mask_sb = sb("mask_sb", [128, 17 * 128], BF16)
    msel_sb = sb("msel_sb", [128, 2], F32)
    mrow_sb = sb("mrow_sb", [1, 128], BF16)
    ones_row = sb("ones_row", [1, 512], BF16)
    ones_bf = sb("ones_bf", [128, 1], BF16)
    ones_f = sb("ones_f", [1, 128], F32)
    epst = sb("epst", [128, 2], F32)
    mneg = sb("mneg", [128, 3], F32)
    pmax = sb("pmax", [128, 8], F32)
    l_col = sb("l_col", [128, 6], F32)
    linv_col = sb("linv_col", [128, 3], F32)
    pall = ctx.enter_context(nc.psum_tensor("pall", [128, 6 * 512], F32))
    ptp = ctx.enter_context(nc.psum_tensor("ptp", [128, 2 * 1024], BF16))

    def bank(i, width=512):
        return pall[:, 512 * i:512 * i + width]

    b = Bld(nc)
    sx = b.sem("sx")
    sg = b.sem("sg")
    st = b.sem("st")
    sv = b.sem("sv")
    ss = b.sem("ss")
    sid = b.sem("sid")
    so = [b.sem("so0"), b.sem("so1")]
    sf = [b.sem("sf0"), b.sem("sf1")]
    swv = b.sem("swv")
    swv1 = b.sem("swv1")
    shw = [b.sem("shw0"), b.sem("shw1")]
    sol = b.sem("sol")
    scc = b.sem("scc")

    ps_ones_bf[b] = ones_bf
    ps_ones_f[b] = ones_f
    eps_row[b] = epst
    b.op("vector", lambda e: e.memset(ones_bf[:, :], 1.0), inc=(sv, 1))
    b.op("vector", lambda e: e.memset(ones_f[:, :], 1.0), inc=(sv, 1))
    b.op("vector", lambda e: e.memset(epst[:, 0:1], LN_EPS), inc=(sv, 1))
    b.op("vector", lambda e: e.memset(epst[:, 1:2], RMS_EPS), inc=(sv, 1))
    b.op("vector", lambda e: e.memset(ones_row[:, :], 1.0), inc=(sv, 1))
    consts = b.tick(sv)
    b.dma("sync", ident[:, :], ident_i, sid)
    b.dma("sync", h128_sb[:, :], h128_i, sid)
    b.dma("sync", h64_sb[:, :], h64_i, sid)
    b.dma("sync", mask_sb[:, :], mask_i, sid)
    b.dma("sync", msel_sb[:, :], msel_i, sid)
    b.dma("sync", mrow_sb[:, :], mrow_i, sid)
    CONST_LOAD = b.tick(sid)
    DIAG = mask_sb[:, 0:128]
    DIAG_A8 = mask_sb[:, 0:9 * 128]
    MASKB8 = mask_sb[:, 9 * 128:17 * 128]

    def sp_out(queue, par, dst, src):
        b.wait(queue, sf[par], b.tick(sf[par]))
        b.op(queue, lambda e, d=dst, s=src: e.dma_start(out=d, in_=s),
             inc=(sf[par], 16))
        return b.tick(sf[par])

    def fence(q):
        b.wait(q, sv, b.tick(sv))
        b.wait(q, ss, b.tick(ss))
        b.wait(q, st, b.tick(st))
        for sem in (sf[0], sf[1], so[0], so[1]):
            b.wait(q, sem, b.tick(sem))

    def fences():
        for q in ("gpsimd", "sync", "tensor", "vector", "scalar"):
            fence(q)

    xn0_cm = nc.sbuf_tensor("xn0", [128, NCH * SEQ], BF16)
    xn0 = xn0_cm.__enter__()
    wv_cm = nc.sbuf_tensor("wv_sb", [128, NCH * DM], BF16)
    wv_sb = wv_cm.__enter__()
    for dc in range(NCH):
        b.dma("gpsimd", wv_sb[:, dc * DM:(dc + 1) * DM],
              wv0[128 * dc:128 * (dc + 1), :], swv)
    WV_G = b.tick(swv)
    p0 = ExitStack()
    sbl = lambda n, s, d: p0.enter_context(nc.sbuf_tensor(n, s, d))
    rows = {k: sbl(k, [1, TOK], F32) for k in ("m_row", "ex2", "m2", "var", "sqv", "rstd")}
    rows["tmp32"] = sbl("tmp32", [128, 1024], F32)
    x_buf = sbl("x_buf", [128, 2 * TOK], F32)
    xbf_buf = sbl("xbf_buf", [128, 2 * TOK], BF16)
    sq_buf = sbl("sq_buf", [128, 2 * TOK], BF16)
    x_stream = Stream(b, "lx", "sync", 2)
    x_stream.buf = x_buf
    xbf_stream = Stream(b, "lb", "gpsimd", 2)
    xbf_stream.buf = xbf_buf
    for th in range(2):
        emit_ln_stats_and_xn(
            b, bank, consts, xT[:, TOK * th:TOK * (th + 1)], x_stream, xbf_stream,
            sq_buf, rows, xn0, TOK, sx, sg, st, sv, ss,
            xn_cols=lambda dc, th=th: xn0[:, dc * SEQ + TOK * th:dc * SEQ + TOK * (th + 1)])
    XN0_V = b.tick(sv)
    p0.close()

    fences()
    p1 = ExitStack()
    sbl = lambda n, s, d: p1.enter_context(nc.sbuf_tensor(n, s, d))
    v_sb = sbl("v_sb", [128, 2 * DM], BF16)
    qk_buf = sbl("qk_buf", [128, 2 * DM], BF16)
    q_stg = sbl("q_stg", [128, 2 * SEQ], BF16)
    b.wait("tensor", swv, WV_G)
    b.wait("tensor", sv, XN0_V)
    vidx = 0
    v_evac = {}
    vdma_tick = [0, 0]
    for tc in range(SEQ // 128):
        for fg in range(4):
            bk = vidx % 2
            if vidx >= 2:
                b.wait("tensor", ss, v_evac[vidx - 2])
            for dc in range(NCH):
                b.op("tensor", lambda e, bk=bk, tc=tc, fg=fg, dc=dc: e.matmul(
                    bank(bk), xn0[:, dc * SEQ + 128 * tc:dc * SEQ + 128 * (tc + 1)],
                    wv_sb[:, dc * DM + 512 * fg:dc * DM + 512 * (fg + 1)],
                    start=(dc == 0), stop=(dc == NCH - 1)),
                    inc=(st, 1) if dc == NCH - 1 else None)
            b.wait("scalar", st, b.tick(st))
            vslot = tc % 2
            if fg == 0:
                b.wait("scalar", sf[vslot], vdma_tick[vslot])
            b.op("scalar", lambda e, bk=bk, vslot=vslot, fg=fg: e.activation(
                v_sb[:, vslot * DM + 512 * fg:vslot * DM + 512 * (fg + 1)],
                bank(bk), AF.Copy), inc=(ss, 1))
            v_evac[vidx] = b.tick(ss)
            vidx += 1
        vslot = tc % 2
        b.wait("sync", ss, v_evac[vidx - 1])
        b.wait("sync", sf[vslot], b.tick(sf[vslot]))
        for hh in range(16):
            b.op("sync", lambda e, hh=hh, tc=tc, vslot=vslot: e.dma_start(
                out=v0hm_d[SEQ * hh + 128 * tc:SEQ * hh + 128 * (tc + 1), :],
                in_=v_sb[:, vslot * DM + 128 * hh:vslot * DM + 128 * (hh + 1)]),
                inc=(sf[vslot], 16))
        vdma_tick[vslot] = b.tick(sf[vslot])
    wstream = Stream(b, "wqk", "gpsimd", 2)
    qk_evac = {}
    qkbank_tick = [0, 0]
    qdma_tick = [0, 0]
    for ji, (which, oc) in enumerate([("q", oc) for oc in range(NCH)] +
                                     [("k", oc) for oc in range(NCH)]):
        wsrc = wq0 if which == "q" else wk0
        src = wsrc[:, 128 * oc:128 * (oc + 1)].rearrange("(c p) n -> p c n", p=128)
        slot = ji % 2
        dst = qk_buf[:, slot * DM:(slot + 1) * DM].rearrange("p (c n) -> p c n", n=128)
        wsem, wtk, ws = wstream.load(dst, src)
        b.wait("tensor", wsem, wtk)
        nh4 = 2 if which == "q" else 4
        for h4 in range(nh4):
            bk = 2 + h4 % 2
            b.wait("tensor", ss, qkbank_tick[h4 % 2])
            for dc in range(NCH):
                b.op("tensor", lambda e, bk=bk, slot=slot, dc=dc, h4=h4: e.matmul(
                    bank(bk),
                    qk_buf[:, slot * DM + 128 * dc:slot * DM + 128 * (dc + 1)],
                    xn0[:, dc * SEQ + 512 * h4:dc * SEQ + 512 * (h4 + 1)],
                    start=(dc == 0), stop=(dc == NCH - 1)),
                    inc=(st, 1) if dc == NCH - 1 else None)
            mm_t = b.tick(st)
            qslot = ji % 2
            scale = RS128 if which == "q" else 1.0
            b.wait("scalar", st, mm_t)
            if h4 == 0:
                b.wait("scalar", sf[qslot], qdma_tick[qslot])
            b.op("scalar", lambda e, qslot=qslot, h4=h4, scale=scale, bk=bk: e.activation(
                q_stg[:, qslot * SEQ + 512 * h4:qslot * SEQ + 512 * (h4 + 1)],
                bank(bk), AF.Copy, scale=scale), inc=(ss, 1))
            qk_evac[(ji, h4)] = b.tick(ss)
            qkbank_tick[h4 % 2] = b.tick(ss)
        wstream.consumed(ws, "st", b.tick(st))
        b.wait("sync", ss, qk_evac[(ji, nh4 - 1)])
        dstd = q0_d if which == "q" else k0_d
        wid = 512 * nh4
        qdma_tick[ji % 2] = sp_out("sync", ji % 2,
                                   dstd[128 * oc:128 * (oc + 1), :],
                                   q_stg[:, (ji % 2) * SEQ:(ji % 2) * SEQ + wid])
    p1.close()
    wv_cm.__exit__(None, None, None)

    fences()
    p2 = ExitStack()
    sbl = lambda n, s, d: p2.enter_context(nc.sbuf_tensor(n, s, d))
    w1buf = sbl("w1buf", [128, 2 * DM], BF16)
    w3buf = sbl("w3buf", [128, 2 * DM], BF16)
    gate_s = sbl("gate_s", [128, 2 * 512], F32)
    hid_bf = sbl("hid_bf", [128, 2 * 512], BF16)
    h1_bf = sbl("h1_bf", [128, 2 * 512], BF16)
    w1s = Stream(b, "w1s", "gpsimd", 2)
    w3s = Stream(b, "w3s", "gpsimd", 2)
    b.wait("tensor", sid, CONST_LOAD)
    silu_tick = [0, 0]
    mul_tick = [0, 0]
    h1evac_tick = [0, 0]
    fwdma_tick = [0, 0]
    h1pass_tick2 = [0, 0]
    p2_pending = [None]
    for hc in range(NHC):
        wpar = hc % 2
        w1sem, w1tk, w1slot = w1s.load(
            w1buf[:, wpar * DM:(wpar + 1) * DM].rearrange("p (c n) -> p c n", n=128),
            w1t[:, 128 * hc:128 * (hc + 1)].rearrange("(c p) n -> p c n", p=128))
        w3sem, w3tk, w3slot = w3s.load(
            w3buf[:, wpar * DM:(wpar + 1) * DM].rearrange("p (c n) -> p c n", n=128),
            w3t[:, 128 * hc:128 * (hc + 1)].rearrange("(c p) n -> p c n", p=128))
        b.wait("tensor", w1sem, w1tk)
        b.wait("tensor", w3sem, w3tk)
        last_mm = 0
        for tq in range(2):
            par = tq % 2
            gb, lb, hb = 3 * par, 3 * par + 1, 3 * par + 2
            b.wait("tensor", ss, silu_tick[par])
            b.wait("tensor", sv, mul_tick[par])
            for dc in range(NCH):
                b.op("tensor", lambda e, gb=gb, wpar=wpar, dc=dc, tq=tq: e.matmul(
                    bank(gb), w1buf[:, wpar * DM + 128 * dc:wpar * DM + 128 * (dc + 1)],
                    xn0[:, dc * SEQ + 512 * tq:dc * SEQ + 512 * (tq + 1)],
                    start=(dc == 0), stop=(dc == NCH - 1)))
            for dc in range(NCH):
                b.op("tensor", lambda e, lb=lb, wpar=wpar, dc=dc, tq=tq: e.matmul(
                    bank(lb), w3buf[:, wpar * DM + 128 * dc:wpar * DM + 128 * (dc + 1)],
                    xn0[:, dc * SEQ + 512 * tq:dc * SEQ + 512 * (tq + 1)],
                    start=(dc == 0), stop=(dc == NCH - 1)),
                    inc=(st, 1) if dc == NCH - 1 else None)
            up_t = b.tick(st)
            last_mm = up_t
            if p2_pending[0] is not None:
                p2_pending[0]()
                p2_pending[0] = None
            b.wait("scalar", st, up_t)
            b.op("scalar", lambda e, gb=gb, par=par: e.activation(
                gate_s[:, 512 * par:512 * (par + 1)], bank(gb), AF.Silu),
                inc=(ss, 1))
            silu_tick[par] = b.tick(ss)
            b.wait("vector", ss, silu_tick[par])
            b.wait("vector", st, h1pass_tick2[par])
            b.op("vector", lambda e, lb=lb, par=par: e.tensor_mul(
                hid_bf[:, 512 * par:512 * (par + 1)], bank(lb),
                gate_s[:, 512 * par:512 * (par + 1)]), inc=(sv, 1))
            mul_tick[par] = b.tick(sv)
            mt = mul_tick[par]

            def p2tail(hc=hc, tq=tq, par=par, hb=hb, mt=mt):
                b.wait("tensor", sv, mt)
                b.wait("tensor", ss, h1evac_tick[par])
                b.op("tensor", lambda e, hb=hb, par=par: e.matmul(
                    bank(hb), h128_sb[:, :], hid_bf[:, 512 * par:512 * (par + 1)],
                    start=True, stop=True), inc=(st, 1))
                h1pass_tick2[par] = b.tick(st)
                b.wait("scalar", st, h1pass_tick2[par])
                b.wait("scalar", sf[par], fwdma_tick[par])
                b.op("scalar", lambda e, hb=hb, par=par: e.copy(
                    h1_bf[:, 512 * par:512 * (par + 1)], bank(hb)), inc=(ss, 1))
                h1evac_tick[par] = b.tick(ss)
                b.wait("sync", ss, h1evac_tick[par])
                fwdma_tick[par] = sp_out(
                    "sync", par,
                    fw_d[128 * hc:128 * (hc + 1), 512 * tq:512 * (tq + 1)],
                    h1_bf[:, 512 * par:512 * (par + 1)])

            p2_pending[0] = p2tail
        w1s.consumed(w1slot, "st", last_mm)
        w3s.consumed(w3slot, "st", last_mm)
    p2_pending[0]()
    p2_pending[0] = None
    p2.close()
    xn0_cm.__exit__(None, None, None)

    fences()
    hw_cm = nc.sbuf_tensor("hw_sb", [128, NHC * 1024], BF16)
    hw_sb = hw_cm.__enter__()
    p3 = ExitStack()
    sbl = lambda n, s, d: p3.enter_context(nc.sbuf_tensor(n, s, d))
    yb_buf = sbl("yb_buf", [128, 4 * TOK], BF16)
    ybs0 = Stream(b, "ybs0", "sync", 4)
    ybs1 = Stream(b, "ybs1", "scalar", 4)
    b.wait("sync", sf[0], b.tick(sf[0]))
    b.wait("sync", sf[1], b.tick(sf[1]))
    b.wait("tensor", ss, b.tick(ss))
    b.wait("tensor", sv, b.tick(sv))
    b.wait("tensor", sid, CONST_LOAD)
    pairevac_tick = [0, 0, 0]
    for pp in range(64):
        par = pp % 4
        ysem, ytk, yslot = ybs0.load(
            yb_buf[0:64, par * TOK:(par + 1) * TOK], fw_d[2 * pp::128, :])
        ysem2, ytk2, yslot2 = ybs1.load(
            yb_buf[64:128, par * TOK:(par + 1) * TOK], fw_d[2 * pp + 1::128, :])
        r = pp % 3
        b.wait("tensor", ysem, ytk)
        b.wait("tensor", ysem2, ytk2)
        b.wait("tensor", ss, pairevac_tick[r])
        for j2 in range(2):
            for hq in range(2):
                b.op("tensor", lambda e, r=r, par=par, j2=j2, hq=hq: e.matmul(
                    pall[64 * j2:64 * (j2 + 1),
                         512 * (2 * r + hq):512 * (2 * r + hq + 1)],
                    h64_sb[64 * j2:64 * (j2 + 1), :],
                    yb_buf[64 * j2:64 * (j2 + 1),
                           par * TOK + 512 * hq:par * TOK + 512 * (hq + 1)],
                    start=True, stop=True),
                    inc=(st, 1) if (j2 == 1 and hq == 1) else None)
        p2_t = b.tick(st)
        b.wait("scalar", st, p2_t)
        b.op("scalar", lambda e, r=r, pp=pp: e.activation(
            hw_sb[:, pp * 1024:(pp + 1) * 1024],
            pall[:, 512 * 2 * r:512 * (2 * r + 2)], AF.Copy,
            scale=FW_SCALE), inc=(ss, 1))
        pairevac_tick[r] = b.tick(ss)
        ybs0.consumed(yslot, "st", p2_t)
        ybs1.consumed(yslot2, "st", p2_t)
    p3.close()
    fences()
    p4 = ExitStack()
    sbl = lambda n, s, d: p4.enter_context(nc.sbuf_tensor(n, s, d))
    w2buf = sbl("w2buf", [128, 2 * DH], BF16)
    g_stg = sbl("g_stg", [128, 2 * 1024], F32)
    w2s = Stream(b, "w2s", "gpsimd", 2)
    b.wait("sync", sf[0], b.tick(sf[0]))
    b.wait("sync", sf[1], b.tick(sf[1]))
    gsil_tick = [0, 0]
    gdma_tick = [0, 0]
    for dc in range(NCH):
        par = dc % 2
        w2sem, w2tk, w2slot = w2s.load(
            w2buf[:, par * DH:(par + 1) * DH].rearrange("p (c n) -> p c n", n=128),
            w2tp[:, 128 * dc:128 * (dc + 1)].rearrange("(c p) n -> p c n", p=128))
        b.wait("tensor", w2sem, w2tk)
        b.wait("tensor", ss, gsil_tick[par])
        for h in range(2):
            dbk = 2 * par + h
            for hcc in range(NHC):
                b.op("tensor", lambda e, dbk=dbk, hcc=hcc, par=par, h=h: e.matmul(
                    bank(dbk),
                    w2buf[:, par * DH + hcc * 128:par * DH + hcc * 128 + 128],
                    hw_sb[:, hcc * 1024 + 512 * h:hcc * 1024 + 512 * (h + 1)],
                    start=(hcc == 0), stop=(hcc == NHC - 1)),
                    inc=(st, 1) if hcc == NHC - 1 else None)
        down_t = b.tick(st)
        w2s.consumed(w2slot, "st", down_t)
        b.wait("scalar", st, down_t)
        b.wait("scalar", sf[par], gdma_tick[par])
        b.op("scalar", lambda e, par=par: e.activation(
            g_stg[:, par * 1024:(par + 1) * 1024],
            pall[:, 512 * 2 * par:512 * (2 * par + 2)], AF.Silu), inc=(ss, 1))
        gsil_tick[par] = b.tick(ss)
        b.wait("sync", ss, gsil_tick[par])
        gdma_tick[par] = sp_out("sync", par,
                                g_d[128 * dc:128 * (dc + 1), :],
                                g_stg[:, par * 1024:(par + 1) * 1024])
    p4.close()
    hw_cm.__exit__(None, None, None)

    fences()
    o_cm = nc.sbuf_tensor("o_sb", [128, NCH * TOK], BF16)
    o_sb = o_cm.__enter__()
    p5 = ExitStack()
    sbl = lambda n, s, d: p5.enter_context(nc.sbuf_tensor(n, s, d))
    q_hb = sbl("q_hb", [128, 2 * TOK], BF16)
    k_hb = sbl("k_hb", [128, 2 * SEQ], BF16)
    v_hb = sbl("v_hb", [128, 2 * SEQ], BF16)
    p_sb = sbl("p_sb", [128, 3 * SEQ], BF16)
    pn_sb = sbl("pn_sb", [128, 3 * SEQ], BF16)
    pt_sb = sbl("pt_sb", [128, SEQ], BF16)
    qs5 = Stream(b, "qs5", "sync", 2)
    ks5 = Stream(b, "ks5", "sync", 2)
    vs5 = Stream(b, "vs5", "sync", 2)
    b.wait("sync", sf[0], b.tick(sf[0]))
    b.wait("sync", sf[1], b.tick(sf[1]))
    S5 = pall[:, 0:2048]
    SEQ_HLF5 = SEQ
    OT5 = lambda sl: pall[:, (4 + sl) * 512: (4 + sl) * 512 + 128]
    PTP5 = lambda j: ptp[:, 1024 * (j % 2): 1024 * (j % 2) + 128]
    pt_evac_tick = [0, 0]
    ot_evac_tick = [0, 0]
    exp_tick5 = [0, 0, 0]
    exp_half5 = [0, 0, 0]
    exp_full_prev = [0]
    recip_tick5 = [0, 0, 0]
    pn_tick5 = [0, 0, 0]
    trdone_tick5 = [0, 0, 0]
    tails5 = []
    head_res = {}
    units = [(h, qb) for h in range(16) for qb in range(8)]

    def flush_tail5():
        tail, meta = tails5.pop(0)
        tail()
        hh, qq = meta
        if qq == 7:
            qsl, ksl, vsl, s_lastt, av_lastt = head_res[hh]
            qs5.consumed(qsl, "st", s_lastt[0])
            ks5.consumed(ksl, "st", s_lastt[0])
            vs5.consumed(vsl, "st", av_lastt[0])

    for u, (h, qb) in enumerate(units):
        up = u % 3
        hp = h % 2
        if qb == 0:
            qsem, qtk, qslot = qs5.load(q_hb[:, hp * TOK:(hp + 1) * TOK],
                                        q0_d[128 * h:128 * (h + 1), :])
            ksem, ktk, kslot = ks5.load(k_hb[:, hp * SEQ:(hp + 1) * SEQ],
                                        k0_d[128 * h:128 * (h + 1), :])
            vsem, vtk, vslot = vs5.load(
                v_hb[:, hp * SEQ:(hp + 1) * SEQ].rearrange("p (c n) -> p c n", n=128),
                v0hm_d[SEQ * h:SEQ * (h + 1), :].rearrange("(c p) n -> p c n", p=128))
            s_lastt = [0]
            av_lastt = [0]
            head_res[h] = (qslot, kslot, vslot, s_lastt, av_lastt)
        lks = _l0_sched(qb)
        nblk = len(lks)
        b.wait("tensor", qsem, qtk)
        b.wait("tensor", ksem, ktk)
        if u == 0:
            b.wait("tensor", sid, CONST_LOAD)
        runs = []
        i0 = 0
        while i0 < nblk:
            i1 = i0
            while i1 + 1 < nblk and lks[i1 + 1] == lks[i1] + 1:
                i1 += 1
            runs.append((i0, i1))
            i0 = i1 + 1
        pieces = []
        for (i0, i1) in runs:
            base_lk = lks[i0]
            width = 128 * (i1 - i0 + 1)
            off = 128 * i0
            cur = 0
            while cur < width:
                w = min(512 - (off + cur) % 512, width - cur, 512)
                pieces.append((off + cur, 128 * base_lk + cur, w))
                cur += w
        waited2 = False
        b.wait("tensor", ss, exp_half5[(u - 1) % 3] if u >= 1 else 0)
        if u == 0:
            b.wait("tensor", sv, consts)
        pb = 128 * (qb + 1)
        diag0 = 128 * qb
        piece_ticks = []
        for (doff, soff, w) in pieces:
            if doff + w > 1024 and not waited2:
                b.wait("tensor", ss, exp_full_prev[0])
                waited2 = True
            part = doff >= pb
            b.op("tensor", lambda e, hp=hp, qb=qb, doff=doff, soff=soff,
                 w=w, part=part: e.matmul(
                S5[:, doff:doff + w],
                q_hb[:, hp * TOK + 128 * qb:hp * TOK + 128 * (qb + 1)],
                k_hb[:, hp * SEQ + soff:hp * SEQ + soff + w],
                start=True, stop=not part),
                inc=None if part else (st, 1))
            if part:
                b.op("tensor", lambda e, doff=doff, w=w: e.matmul(
                    S5[:, doff:doff + w], mrow_sb[0:1, :],
                    ones_row[0:1, 0:w], start=False, stop=True), inc=(st, 1))
            piece_ticks.append(b.tick(st))
        s_mm = b.tick(st)
        s_lastt[0] = s_mm
        if u == 0:
            b.wait("vector", sid, CONST_LOAD)
        bank_last = {}
        for pi, (doff, soff, w) in enumerate(pieces):
            for bk_ in range(doff // 512, (doff + w - 1) // 512 + 1):
                bank_last[bk_] = piece_ticks[pi]
        dpi = None
        for pi, (doff, soff, w) in enumerate(pieces):
            if doff <= diag0 < doff + w:
                dpi = pi
                continue
            b.wait("vector", st,
                   max(bank_last[bk_] for bk_ in
                       range(doff // 512, (doff + w - 1) // 512 + 1)))
            b.op("vector", lambda e, pi=pi, doff=doff, w=w: e.reduce_max(
                pmax[:, pi:pi + 1], S5[:, doff:doff + w], axis=AX.X),
                inc=(sv, 1))
        b.wait("vector", st, s_mm)
        b.op("vector", lambda e, diag0=diag0: e.tensor_tensor(
            S5[:, diag0:diag0 + 128], S5[:, diag0:diag0 + 128],
            DIAG, ALU.add), inc=(sv, 1))
        doff_d, _, w_d = pieces[dpi]
        b.op("vector", lambda e, dpi=dpi, doff_d=doff_d, w_d=w_d: e.reduce_max(
            pmax[:, dpi:dpi + 1], S5[:, doff_d:doff_d + w_d], axis=AX.X),
            inc=(sv, 1))
        b.wait("vector", ss, exp_tick5[up])
        b.op("vector", lambda e, np_=len(pieces), up=up: e.reduce_max(
            mneg[:, up:up + 1], pmax[:, 0:np_], axis=AX.X, negate=True),
            inc=(sv, 1))
        mneg_v = b.tick(sv)
        b.wait("scalar", sv, mneg_v)
        b.wait("scalar", sv, recip_tick5[up])
        b.wait("scalar", sv, pn_tick5[up])
        w1_ = min(128 * nblk, 1024)
        w2_ = 128 * nblk - w1_
        b.op("scalar", lambda e, w1_=w1_, up=up: e.activation(
            p_sb[:, up * SEQ + 0:up * SEQ + w1_],
            S5[:, 0:w1_], AF.Exp, bias=mneg[:, up:up + 1],
            accum_out=l_col[:, 2 * up:2 * up + 1]), inc=(ss, 1))
        exp_half5[up] = b.tick(ss)
        if w2_ > 0:
            b.op("scalar", lambda e, w1_=w1_, w2_=w2_, up=up: e.activation(
                p_sb[:, up * SEQ + w1_:up * SEQ + w1_ + w2_],
                S5[:, w1_:w1_ + w2_], AF.Exp, bias=mneg[:, up:up + 1],
                accum_out=l_col[:, 2 * up + 1:2 * up + 2]), inc=(ss, 1))
        exp_tick5[up] = b.tick(ss)
        exp_full_prev[0] = b.tick(ss)
        b.wait("vector", ss, b.tick(ss))
        if w2_ > 0:
            b.op("vector", lambda e, up=up: e.tensor_add(
                l_col[:, 2 * up:2 * up + 1], l_col[:, 2 * up:2 * up + 1],
                l_col[:, 2 * up + 1:2 * up + 2]), inc=(sv, 1))
            b.guard("vector", sv)
        b.op("vector", lambda e, up=up: e.reciprocal(
            linv_col[:, up:up + 1], l_col[:, 2 * up:2 * up + 1]), inc=(sv, 1))
        recip_tick5[up] = b.tick(sv)
        b.guard("vector", sv)
        b.wait("vector", st, trdone_tick5[up])
        b.op("vector", lambda e, nblk=nblk, up=up: e.tensor_scalar_mul(
            pn_sb[:, up * SEQ:up * SEQ + 128 * nblk],
            p_sb[:, up * SEQ:up * SEQ + 128 * nblk],
            linv_col[:, up:up + 1]), inc=(sv, 1))
        pn_tick5[up] = b.tick(sv)
        pn_v = b.tick(sv)

        def tail(h=h, qb=qb, hp=hp, lks=lks, nblk=nblk, up=up, pn_v=pn_v,
                 vsem=vsem, vtk=vtk, av_lastt=av_lastt, u=u):
            tp = u % 2
            b.wait("tensor", sv, pn_v)
            if qb == 0:
                b.wait("tensor", vsem, vtk)
            b.wait("tensor", ss, ot_evac_tick[tp])
            grp_tick = {}
            ngr = ceil_div(nblk, 4)

            def emit_av(g):
                b.wait("tensor", ss, grp_tick[g])
                for p in range(4 * g, min(4 * g + 4, nblk)):
                    lk = lks[p]
                    b.op("tensor", lambda e, p=p, lk=lk, tp=tp, hp=hp,
                         nblk=nblk: e.matmul(
                        OT5(tp), v_hb[:, hp * SEQ + 128 * lk:hp * SEQ + 128 * (lk + 1)],
                        pt_sb[:, 128 * p:128 * (p + 1)],
                        start=(p == 0), stop=(p == nblk - 1)),
                        inc=(st, 1) if p == nblk - 1 else None)

            for g in range(ngr):
                blocks = list(range(4 * g, min(4 * g + 4, nblk)))
                bj = g % 2
                b.wait("tensor", ss, pt_evac_tick[bj])
                for ip in blocks:
                    b.op("tensor", lambda e, ip=ip, bj=bj, up=up: e.transpose(
                        ptp[:, 1024 * bj + 128 * (ip % 4):
                            1024 * bj + 128 * (ip % 4) + 128],
                        pn_sb[:, up * SEQ + 128 * ip:up * SEQ + 128 * (ip + 1)],
                        ident[:, :]),
                        inc=(st, 1) if ip == blocks[-1] else None)
                tr_t = b.tick(st)
                b.wait("scalar", st, tr_t)
                nb = len(blocks)
                b.op("scalar", lambda e, g=g, bj=bj, nb=nb: e.copy(
                    pt_sb[:, 512 * g:512 * g + 128 * nb],
                    ptp[:, 1024 * bj:1024 * bj + 128 * nb]), inc=(ss, 1))
                pt_evac_tick[bj] = b.tick(ss)
                grp_tick[g] = b.tick(ss)
                if g >= 1:
                    emit_av(g - 1)
            trdone_tick5[up] = b.tick(st)
            emit_av(ngr - 1)
            av_t = b.tick(st)
            av_lastt[0] = av_t
            b.wait("scalar", st, av_t)
            b.op("scalar", lambda e, h=h, qb=qb, tp=tp: e.copy(
                o_sb[:, h * TOK + 128 * qb:h * TOK + 128 * (qb + 1)],
                OT5(tp)), inc=(ss, 1))
            ot_evac_tick[tp] = b.tick(ss)

        tails5.append((tail, (h, qb)))
        if len(tails5) > 2:
            flush_tail5()
    while tails5:
        flush_tail5()
    p5.close()

    fences()
    p6 = ExitStack()
    sbl = lambda n, s, d: p6.enter_context(nc.sbuf_tensor(n, s, d))
    wobuf = sbl("wobuf", [128, 2 * DM], BF16)
    g_buf6 = sbl("g_buf6", [128, 2 * TOK], F32)
    x_buf6 = sbl("x_buf6", [128, 2 * TOK], F32)
    tmp6 = sbl("tmp6", [128, 512], F32)
    x1_stg = sbl("x1_stg", [128, 2 * TOK], F32)
    wos = Stream(b, "wos", "gpsimd", 2)
    gs6 = Stream(b, "gs6", "sync", 2)
    xs6 = Stream(b, "xs6", "sync", 2)
    b.wait("sync", sf[0], b.tick(sf[0]))
    b.wait("sync", sf[1], b.tick(sf[1]))
    OLOAD = 0
    sbank_tick = [0, 0]
    x1dma_tick = [0, 0]
    for oc in range(NCH):
        par = oc % 2
        wsem, wtk, wslot = wos.load(
            wobuf[:, par * DM:(par + 1) * DM].rearrange("p (c n) -> p c n", n=128),
            wo0[:, 128 * oc:128 * (oc + 1)].rearrange("(c p) n -> p c n", p=128))
        gsem, gtk, gslot = gs6.load(g_buf6[:, par * TOK:(par + 1) * TOK],
                                    g_d[128 * oc:128 * (oc + 1), :])
        xsem, xtk, xslot = xs6.load(x_buf6[:, par * TOK:(par + 1) * TOK],
                                    xT[128 * oc:128 * (oc + 1), 0:TOK])
        b.wait("tensor", wsem, wtk)
        res_v = 0
        for tq in range(2):
            sbk = 2 * par + tq % 2
            b.wait("tensor", sv, sbank_tick[par])
            for dc in range(NCH):
                b.op("tensor", lambda e, sbk=sbk, par=par, dc=dc, tq=tq: e.matmul(
                    bank(sbk), wobuf[:, par * DM + 128 * dc:par * DM + 128 * (dc + 1)],
                    o_sb[:, dc * TOK + 512 * tq:dc * TOK + 512 * (tq + 1)],
                    start=(dc == 0), stop=(dc == NCH - 1)),
                    inc=(st, 1) if dc == NCH - 1 else None)
            s_t = b.tick(st)
            b.wait("vector", st, s_t)
            b.wait("vector", gsem, gtk)
            b.wait("vector", xsem, xtk)
            if tq == 0:
                b.wait("vector", so[par], x1dma_tick[par])
            b.guard("vector", sv)
            b.op("vector", lambda e, sbk=sbk, par=par, tq=tq: e.tensor_mul(
                tmp6[:, :], bank(sbk),
                g_buf6[:, par * TOK + 512 * tq:par * TOK + 512 * (tq + 1)]),
                inc=(sv, 1))
            b.guard("vector", sv)
            b.op("vector", lambda e, par=par, tq=tq: e.tensor_add(
                x1_stg[:, par * TOK + 512 * tq:par * TOK + 512 * (tq + 1)],
                tmp6[:, :],
                x_buf6[:, par * TOK + 512 * tq:par * TOK + 512 * (tq + 1)]),
                inc=(sv, 1))
            res_v = b.tick(sv)
        sbank_tick[par] = res_v
        wos.consumed(wslot, "st", b.tick(st))
        gs6.consumed(gslot, "sv", res_v)
        xs6.consumed(xslot, "sv", res_v)
        b.wait("sync", sv, res_v)
        b.wait("sync", so[par], b.tick(so[par]))
        b.op("sync", lambda e, oc=oc, par=par: e.dma_start(
            out=x1_d[128 * oc:128 * (oc + 1), :],
            in_=x1_stg[:, par * TOK:(par + 1) * TOK]), inc=(so[par], 16))
        x1dma_tick[par] = b.tick(so[par])
    p6.close()
    o_cm.__exit__(None, None, None)

    fences()
    xn1_cm = nc.sbuf_tensor("xn1", [128, NCH * TOK], BF16)
    xn1 = xn1_cm.__enter__()
    p7 = ExitStack()
    sbl = lambda n, s, d: p7.enter_context(nc.sbuf_tensor(n, s, d))
    rows = {k: sbl("r7" + k, [1, TOK], F32) for k in ("m_row", "ex2", "m2", "var", "sqv", "rstd")}
    rows["tmp32"] = sbl("r7tmp32", [128, 1024], F32)
    x_buf7 = sbl("x_buf7", [128, 2 * TOK], F32)
    xbf_buf7 = sbl("xbf_buf7", [128, 2 * TOK], BF16)
    sq_buf7 = sbl("sq_buf7", [128, 2 * TOK], BF16)
    x_stream7 = Stream(b, "lx7", "sync", 2)
    x_stream7.buf = x_buf7
    xbf_stream7 = Stream(b, "lb7", "gpsimd", 2)
    xbf_stream7.buf = xbf_buf7
    b.wait("sync", so[0], b.tick(so[0]))
    b.wait("sync", so[1], b.tick(so[1]))
    b.wait("gpsimd", so[0], b.tick(so[0]))
    b.wait("gpsimd", so[1], b.tick(so[1]))
    emit_ln_stats_and_xn(
        b, bank, consts, x1_d[:, :], x_stream7, xbf_stream7,
        sq_buf7, rows, xn1, TOK, sx, sg, st, sv, ss,
        xn_cols=lambda dc: xn1[:, dc * TOK:(dc + 1) * TOK])
    XN1_V = b.tick(sv)
    p7.close()

    fences()
    y1_cm = nc.sbuf_tensor("y1sb", [128, NCH * TOK], BF16)
    y1_sb = y1_cm.__enter__()
    k1_cm = nc.sbuf_tensor("k1sb", [128, 4 * SEQ], BF16)
    k1_sb = k1_cm.__enter__()
    v1_cm = nc.sbuf_tensor("v1sb", [128, 4 * SEQ], BF16)
    v1_sb = v1_cm.__enter__()
    q1_cm = nc.sbuf_tensor("q1sb", [128, NCH * TOK], BF16)
    q1_sb = q1_cm.__enter__()
    p8 = ExitStack()
    sbl = lambda n, s, d: p8.enter_context(nc.sbuf_tensor(n, s, d))
    wv1_sb = sbl("wv1_sb", [128, NCH * 512], BF16)
    vz8 = sbl("vz8", [128, 2 * 1024], F32)
    for dc in range(NCH):
        b.dma("gpsimd", wv1_sb[:, dc * 512:(dc + 1) * 512],
              wv1[128 * dc:128 * (dc + 1), :], swv1)
    WV1_G = b.tick(swv1)
    b.wait("tensor", swv1, WV1_G)
    b.wait("tensor", sv, XN1_V)
    b.wait("tensor", sid, CONST_LOAD)
    b.wait("vector", sid, CONST_LOAD)
    vevac8 = [0, 0]
    vspill8 = [0, 0]
    for tc in range(TOK // 128):
        par = tc % 2
        b.wait("tensor", sv, vevac8[par])
        for dc in range(NCH):
            b.op("tensor", lambda e, par=par, tc=tc, dc=dc: e.matmul(
                bank(par), xn1[:, dc * TOK + 128 * tc:dc * TOK + 128 * (tc + 1)],
                wv1_sb[:, dc * 512:(dc + 1) * 512],
                start=(dc == 0), stop=(dc == NCH - 1)),
                inc=(st, 1) if dc == NCH - 1 else None)
        v_t = b.tick(st)
        b.wait("vector", st, v_t)
        b.wait("vector", sf[par], vspill8[par])
        b.op("vector", lambda e, par=par: e.tensor_scalar_mul(
            vz8[:, par * 1024:par * 1024 + 512], bank(par), msel_sb[:, 0:1]),
            inc=(sv, 1))
        b.op("vector", lambda e, par=par: e.tensor_scalar_mul(
            vz8[:, par * 1024 + 512:par * 1024 + 1024], bank(par),
            msel_sb[:, 1:2]), inc=(sv, 1))
        vevac8[par] = b.tick(sv)
        b.wait("sync", sv, vevac8[par])
        b.wait("sync", sf[par], b.tick(sf[par]))
        b.op("sync", lambda e, tc=tc, par=par: e.dma_start(
            out=v1pad_d[128 * tc:128 * (tc + 1), :],
            in_=vz8[:, par * 1024:par * 1024 + 512]), inc=(sf[par], 16))
        b.op("sync", lambda e, tc=tc, par=par: e.dma_start(
            out=v1pad_d[TOK + 128 * tc:TOK + 128 * (tc + 1), :],
            in_=vz8[:, par * 1024 + 512:par * 1024 + 1024]), inc=(sf[par], 16))
        vspill8[par] = b.tick(sf[par])
    b.wait("gpsimd", sf[0], b.tick(sf[0]))
    b.wait("gpsimd", sf[1], b.tick(sf[1]))
    b.op("gpsimd", lambda e: e.collective_compute(
        "AllReduce", ALU.add,
        replica_groups=[[0, 1], [2, 3], [4, 5], [6, 7]],
        ins=[v1pad_d], outs=[v1ag_d]), inc=(scc, 1))
    p8.close()
    p8 = ExitStack()
    sbl = lambda n, s, d: p8.enter_context(nc.sbuf_tensor(n, s, d))
    wq1_buf = sbl("wq1_buf", [128, 2 * DM], BF16)
    qsq8 = sbl("qsq8", [128, TOK], BF16)
    rstd_bc8 = sbl("rstd_bc8", [128, TOK], F32)
    qrow8 = {k: sbl("q8" + k, [1, TOK], F32) for k in ("sqv", "rstd")}
    fences()
    wq1s = Stream(b, "wq1s", "gpsimd", 2)
    qmul8 = 0
    for qh in range(16):
        par = qh % 2
        wsem, wtk, wslot = wq1s.load(
            wq1_buf[:, par * DM:(par + 1) * DM].rearrange("p (c n) -> p c n", n=128),
            wq1[:, 128 * qh:128 * (qh + 1)].rearrange("(c p) n -> p c n", p=128))
        b.wait("tensor", wsem, wtk)
        b.wait("tensor", sv, qmul8)
        b.wait("tensor", ss, b.tick(ss))
        for h in range(2):
            for dc in range(NCH):
                b.op("tensor", lambda e, h=h, par=par, dc=dc: e.matmul(
                    bank(h), wq1_buf[:, par * DM + 128 * dc:par * DM + 128 * (dc + 1)],
                    xn1[:, dc * TOK + 512 * h:dc * TOK + 512 * (h + 1)],
                    start=(dc == 0), stop=(dc == NCH - 1)),
                    inc=(st, 1) if dc == NCH - 1 else None)
        q_t = b.tick(st)
        wq1s.consumed(wslot, "st", q_t)
        b.wait("scalar", st, q_t)
        b.op("scalar", lambda e: e.activation(qsq8[:, :], pall[:, 0:1024], AF.Square),
             inc=(ss, 1))
        qsq_t = b.tick(ss)
        b.wait("tensor", ss, qsq_t)
        for h in range(2):
            b.op("tensor", lambda e, h=h: e.matmul(
                pall[0:1, 512 * (2 + h):512 * (3 + h)], ones_bf[:, :],
                qsq8[:, 512 * h:512 * (h + 1)], start=True, stop=True),
                inc=(st, 1) if h == 1 else None)
        qs_t = b.tick(st)
        b.wait("scalar", st, qs_t)
        b.op("scalar", lambda e: e.activation(
            qrow8["sqv"][:, :], pall[0:1, 1024:2048], AF.Sqrt, scale=1.0 / 128.0,
            bias=epst[0:1, 1:2]), inc=(ss, 1))
        b.wait("vector", ss, b.tick(ss))
        b.op("vector", lambda e: e.reciprocal(qrow8["rstd"][:, :], qrow8["sqv"][:, :]),
             inc=(sv, 1))
        qr_v = b.tick(sv)
        b.wait("tensor", sv, qr_v)
        for h in range(2):
            b.op("tensor", lambda e, h=h: e.matmul(
                pall[:, 512 * (4 + h):512 * (5 + h)], ones_f[:, :],
                qrow8["rstd"][:, 512 * h:512 * (h + 1)], start=True, stop=True),
                inc=(st, 1) if h == 1 else None)
        bc_t = b.tick(st)
        b.wait("scalar", st, bc_t)
        b.wait("scalar", sv, qmul8)
        b.op("scalar", lambda e: e.copy(rstd_bc8[:, :], pall[:, 2048:3072]),
             inc=(ss, 1))
        bce_t = b.tick(ss)
        b.wait("vector", ss, bce_t)
        b.op("vector", lambda e, qh=qh: e.tensor_mul(
            q1_sb[:, qh * TOK:(qh + 1) * TOK], pall[:, 0:1024], rstd_bc8[:, :]),
            inc=(sv, 1))
        qmul8 = b.tick(sv)
    p8.close()

    p8 = ExitStack()
    sbl = lambda n, s, d: p8.enter_context(nc.sbuf_tensor(n, s, d))
    v_f8 = sbl("v_f8d", [128, 2 * 512], F32)
    sqv_f8 = sbl("sqv_f8", [128, 128], F32)
    vsum8 = sbl("vsum8", [128, 8], F32)
    vsd8 = sbl("vsd8", [128, 8], F32)
    vrstd8 = sbl("vrstd8", [128, 8], F32)
    k_tmp8 = sbl("k_tmp8", [128, 2 * 512], BF16)
    fences()
    vag = Stream(b, "vag", "sync", 2)
    b.wait("sync", scc, 1)
    kmul8 = [0, 0]
    ktr8 = [0, 0]
    trevac8 = [0, 0]
    for tc in range(SEQ // 128):
        par = tc % 2
        vsem, vtk, vslot = vag.load(v_f8[:, 512 * par:512 * (par + 1)],
                                    v1ag_d[128 * tc:128 * (tc + 1), :])
        b.wait("scalar", vsem, vtk)
        b.wait("scalar", sv, kmul8[par])
        for kvh in range(4):
            b.op("scalar", lambda e, par=par, tc=tc, kvh=kvh: e.activation(
                v1_sb[:, kvh * SEQ + 128 * tc:kvh * SEQ + 128 * (tc + 1)],
                v_f8[:, 512 * par + 128 * kvh:512 * par + 128 * (kvh + 1)],
                AF.Copy), inc=(ss, 1))
        VE8 = b.tick(ss)
        for kvh in range(4):
            col = 512 * par + 128 * kvh
            b.op("scalar", lambda e, col=col: e.activation(
                sqv_f8[:, 0:128], v_f8[:, col:col + 128], AF.Square), inc=(ss, 1))
            sq_t = b.tick(ss)
            b.wait("vector", ss, sq_t)
            b.op("vector", lambda e, kvh=kvh: e.reduce_sum(
                vsum8[:, kvh:kvh + 1], sqv_f8[:, 0:128], axis=AX.X), inc=(sv, 1))
            sum_v = b.tick(sv)
            b.wait("scalar", sv, sum_v)
            b.op("scalar", lambda e, kvh=kvh: e.activation(
                vsd8[:, kvh:kvh + 1], vsum8[:, kvh:kvh + 1], AF.Sqrt,
                scale=1.0 / 128.0, bias=epst[:, 1:2]), inc=(ss, 1))
            sd_t = b.tick(ss)
            b.wait("vector", ss, sd_t)
            b.op("vector", lambda e, kvh=kvh: e.reciprocal(
                vrstd8[:, kvh:kvh + 1], vsd8[:, kvh:kvh + 1]), inc=(sv, 1))
            b.guard("vector", sv)
            if kvh == 0:
                b.wait("vector", st, ktr8[par])
            b.op("vector", lambda e, col=col, kvh=kvh: e.tensor_scalar_mul(
                k_tmp8[:, col:col + 128], v_f8[:, col:col + 128],
                vrstd8[:, kvh:kvh + 1]), inc=(sv, 1))
        krow_v = b.tick(sv)
        kmul8[par] = krow_v
        vag.consumed(vslot, "sv", krow_v)
        b.wait("tensor", sv, krow_v)
        for kvh in range(4):
            j = kvh % 2
            b.wait("tensor", ss, trevac8[j])
            b.op("tensor", lambda e, par=par, kvh=kvh, j=j: e.transpose(
                ptp[:, 1024 * j:1024 * j + 128],
                k_tmp8[:, 512 * par + 128 * kvh:512 * par + 128 * (kvh + 1)],
                ident[:, :]), inc=(st, 1))
            tr_t = b.tick(st)
            b.wait("scalar", st, tr_t)
            b.op("scalar", lambda e, kvh=kvh, tc=tc, j=j: e.copy(
                k1_sb[:, kvh * SEQ + 128 * tc:kvh * SEQ + 128 * (tc + 1)],
                ptp[:, 1024 * j:1024 * j + 128]), inc=(ss, 1))
            trevac8[j] = b.tick(ss)
        ktr8[par] = b.tick(st)
    p8.close()

    fences()
    p9 = ExitStack()
    sbl = lambda n, s, d: p9.enter_context(nc.sbuf_tensor(n, s, d))
    p_sb9 = sbl("p_sb9", [128, 3 * SEQ], BF16)
    pn_sb9 = sbl("pn_sb9", [128, 3 * SEQ], BF16)
    pt_sb9 = sbl("pt_sb9", [128, SEQ], BF16)
    pt_evac9 = [0, 0]
    ot_evac9 = [0, 0]
    exp_tick9 = [0, 0, 0]
    exp_half9 = [0, 0, 0]
    exp_full9 = [0]
    recip_tick9 = [0, 0, 0]
    pn_tick9 = [0, 0, 0]
    trdone_tick9 = [0, 0, 0]
    tails9 = []
    b.wait("tensor", sv, b.tick(sv))
    b.wait("tensor", ss, b.tick(ss))
    units9 = [(qh, qc) for qh in range(16) for qc in range(TOK // 128)]
    for u, (qh, qc) in enumerate(units9):
        up = u % 3
        kvh = qh // 4
        kbase = kvh * SEQ
        b.wait("tensor", ss, exp_half9[(u - 1) % 3] if u >= 1 else 0)
        kg_ticks = []
        for kg in range(4):
            if kg == 2:
                b.wait("tensor", ss, exp_full9[0])
            b.op("tensor", lambda e, qh=qh, qc=qc, kg=kg, kbase=kbase: e.matmul(
                pall[:, 512 * kg:512 * (kg + 1)],
                q1_sb[:, qh * TOK + 128 * qc:qh * TOK + 128 * (qc + 1)],
                k1_sb[:, kbase + 512 * kg:kbase + 512 * (kg + 1)],
                start=True, stop=True), inc=(st, 1))
            kg_ticks.append(b.tick(st))
        s_mm = b.tick(st)
        for kg in range(4):
            b.wait("vector", st, kg_ticks[kg])
            b.op("vector", lambda e, kg=kg: e.reduce_max(
                pmax[:, kg:kg + 1], pall[:, 512 * kg:512 * (kg + 1)],
                axis=AX.X), inc=(sv, 1))
        b.wait("vector", ss, exp_tick9[up])
        b.op("vector", lambda e, up=up: e.reduce_max(
            mneg[:, up:up + 1], pmax[:, 0:4], axis=AX.X, negate=True),
            inc=(sv, 1))
        mneg_v = b.tick(sv)
        b.wait("scalar", sv, mneg_v)
        b.wait("scalar", sv, recip_tick9[up])
        b.wait("scalar", sv, pn_tick9[up])
        b.op("scalar", lambda e, up=up: e.activation(
            p_sb9[:, up * SEQ:up * SEQ + 1024], pall[:, 0:1024], AF.Exp,
            bias=mneg[:, up:up + 1], accum_out=l_col[:, 2 * up:2 * up + 1]),
            inc=(ss, 1))
        exp_half9[up] = b.tick(ss)
        b.op("scalar", lambda e, up=up: e.activation(
            p_sb9[:, up * SEQ + 1024:(up + 1) * SEQ], pall[:, 1024:2048],
            AF.Exp, bias=mneg[:, up:up + 1],
            accum_out=l_col[:, 2 * up + 1:2 * up + 2]), inc=(ss, 1))
        exp_tick9[up] = b.tick(ss)
        exp_full9[0] = b.tick(ss)
        b.wait("vector", ss, b.tick(ss))
        b.op("vector", lambda e, up=up: e.tensor_add(
            l_col[:, 2 * up:2 * up + 1], l_col[:, 2 * up:2 * up + 1],
            l_col[:, 2 * up + 1:2 * up + 2]), inc=(sv, 1))
        b.guard("vector", sv)
        b.op("vector", lambda e, up=up: e.reciprocal(
            linv_col[:, up:up + 1], l_col[:, 2 * up:2 * up + 1]), inc=(sv, 1))
        recip_tick9[up] = b.tick(sv)
        b.guard("vector", sv)
        b.wait("vector", st, trdone_tick9[up])
        b.op("vector", lambda e, up=up: e.tensor_scalar_mul(
            pn_sb9[:, up * SEQ:(up + 1) * SEQ],
            p_sb9[:, up * SEQ:(up + 1) * SEQ], linv_col[:, up:up + 1]),
            inc=(sv, 1))
        pn_tick9[up] = b.tick(sv)
        pn_v = b.tick(sv)

        def tail9(qh=qh, qc=qc, kbase=kbase, up=up, pn_v=pn_v, u=u):
            tp = u % 2
            b.wait("tensor", sv, pn_v)
            b.wait("tensor", ss, ot_evac9[tp])
            grp_tick = {}

            def emit_av(g):
                b.wait("tensor", ss, grp_tick[g])
                for p in range(4 * g, 4 * g + 4):
                    b.op("tensor", lambda e, p=p, tp=tp, kbase=kbase: e.matmul(
                        pall[:, (4 + tp) * 512:(4 + tp) * 512 + 128],
                        v1_sb[:, kbase + 128 * p:kbase + 128 * (p + 1)],
                        pt_sb9[:, 128 * p:128 * (p + 1)],
                        start=(p == 0), stop=(p == 15)),
                        inc=(st, 1) if p == 15 else None)

            for g in range(4):
                bj = g % 2
                b.wait("tensor", ss, pt_evac9[bj])
                for ip in range(4 * g, 4 * g + 4):
                    b.op("tensor", lambda e, ip=ip, bj=bj, up=up: e.transpose(
                        ptp[:, 1024 * bj + 128 * (ip % 4):
                            1024 * bj + 128 * (ip % 4) + 128],
                        pn_sb9[:, up * SEQ + 128 * ip:up * SEQ + 128 * (ip + 1)],
                        ident[:, :]),
                        inc=(st, 1) if ip % 4 == 3 else None)
                tr_t = b.tick(st)
                b.wait("scalar", st, tr_t)
                b.op("scalar", lambda e, g=g, bj=bj: e.copy(
                    pt_sb9[:, 512 * g:512 * (g + 1)],
                    ptp[:, 1024 * bj:1024 * bj + 512]), inc=(ss, 1))
                pt_evac9[bj] = b.tick(ss)
                grp_tick[g] = b.tick(ss)
                if g >= 1:
                    emit_av(g - 1)
            trdone_tick9[up] = b.tick(st)
            emit_av(3)
            av_t = b.tick(st)
            b.wait("scalar", st, av_t)
            b.op("scalar", lambda e, qh=qh, qc=qc, tp=tp: e.copy(
                y1_sb[:, qh * TOK + 128 * qc:qh * TOK + 128 * (qc + 1)],
                pall[:, (4 + tp) * 512:(4 + tp) * 512 + 128]), inc=(ss, 1))
            ot_evac9[tp] = b.tick(ss)

        tails9.append(tail9)
        if len(tails9) > 2:
            tails9.pop(0)()
    while tails9:
        tails9.pop(0)()
    p9.close()
    q1_cm.__exit__(None, None, None)
    v1_cm.__exit__(None, None, None)
    k1_cm.__exit__(None, None, None)

    fences()
    p10 = ExitStack()
    sbl = lambda n, s, d: p10.enter_context(nc.sbuf_tensor(n, s, d))
    s_sb10 = sbl("s_sb10", [128, NCH * TOK], F32)
    wobuf10 = sbl("wobuf10", [128, 2 * DM], BF16)
    sq_bf10 = sbl("sq_bf10", [128, TOK], BF16)
    s1n_stg = sbl("s1n_stg", [128, 2 * TOK], F32)
    rstd_bc10 = sbl("rstd_bc10", [128, TOK], F32)
    srow10 = {k: sbl("s10" + k, [1, TOK], F32) for k in ("sqv", "rstd")}
    wos10 = Stream(b, "wos10", "gpsimd", 2)
    sevac10 = [0, 0]
    for oc in range(NCH):
        par = oc % 2
        wsem, wtk, wslot = wos10.load(
            wobuf10[:, par * DM:(par + 1) * DM].rearrange("p (c n) -> p c n", n=128),
            wo1[:, 128 * oc:128 * (oc + 1)].rearrange("(c p) n -> p c n", p=128))
        sb0 = 2 * par
        b.wait("tensor", wsem, wtk)
        b.wait("tensor", ss, sevac10[par])
        for h in range(2):
            for dc in range(NCH):
                b.op("tensor", lambda e, sbk=sb0 + h, par=par, dc=dc, h=h: e.matmul(
                    pall[:, 512 * sbk:512 * (sbk + 1)],
                    wobuf10[:, par * DM + 128 * dc:par * DM + 128 * (dc + 1)],
                    y1_sb[:, dc * TOK + 512 * h:dc * TOK + 512 * (h + 1)],
                    start=(dc == 0), stop=(dc == NCH - 1)),
                    inc=(st, 1) if dc == NCH - 1 else None)
        s_t = b.tick(st)
        wos10.consumed(wslot, "st", s_t)
        b.wait("scalar", st, s_t)
        for h in range(2):
            b.op("scalar", lambda e, sbk=sb0 + h, oc=oc, h=h: e.copy(
                s_sb10[:, oc * TOK + 512 * h:oc * TOK + 512 * (h + 1)],
                pall[:, 512 * sbk:512 * (sbk + 1)]), inc=(ss, 1))
            b.op("scalar", lambda e, sbk=sb0 + h, h=h: e.activation(
                sq_bf10[:, 512 * h:512 * (h + 1)],
                pall[:, 512 * sbk:512 * (sbk + 1)], AF.Square), inc=(ss, 1))
        sevac10[par] = b.tick(ss)
        b.wait("tensor", ss, sevac10[par])
        b.op("tensor", lambda e, oc=oc: e.matmul(
            pall[0:1, 4 * 512:4 * 512 + 512], ones_bf[:, :], sq_bf10[:, 0:512],
            start=(oc == 0), stop=(oc == NCH - 1)))
        b.op("tensor", lambda e, oc=oc: e.matmul(
            pall[0:1, 5 * 512:5 * 512 + 512], ones_bf[:, :], sq_bf10[:, 512:1024],
            start=(oc == 0), stop=(oc == NCH - 1)),
            inc=(st, 1) if oc == NCH - 1 else None)
    STATS10 = b.tick(st)
    b.wait("scalar", st, STATS10)
    b.op("scalar", lambda e: e.activation(
        srow10["sqv"][:, 0:512], pall[0:1, 4 * 512:4 * 512 + 512], AF.Sqrt,
        scale=1.0 / DM, bias=epst[0:1, 1:2]), inc=(ss, 1))
    b.op("scalar", lambda e: e.activation(
        srow10["sqv"][:, 512:1024], pall[0:1, 5 * 512:5 * 512 + 512], AF.Sqrt,
        scale=1.0 / DM, bias=epst[0:1, 1:2]), inc=(ss, 1))
    b.wait("vector", ss, b.tick(ss))
    b.op("vector", lambda e: e.reciprocal(srow10["rstd"][:, :], srow10["sqv"][:, :]),
         inc=(sv, 1))
    RST10 = b.tick(sv)
    b.wait("tensor", sv, RST10)
    b.wait("tensor", ss, b.tick(ss))
    for h in range(2):
        b.op("tensor", lambda e, h=h: e.matmul(
            pall[:, 512 * h:512 * (h + 1)], ones_f[:, :],
            srow10["rstd"][:, 512 * h:512 * (h + 1)], start=True, stop=True),
            inc=(st, 1) if h == 1 else None)
    BC10 = b.tick(st)
    b.wait("scalar", st, BC10)
    b.op("scalar", lambda e: e.copy(rstd_bc10[:, :], pall[:, 0:1024]), inc=(ss, 1))
    BCE10 = b.tick(ss)
    b.wait("vector", ss, BCE10)
    s1dma10 = [0, 0]
    for oc in range(NCH):
        par = oc % 2
        b.wait("vector", sf[par], s1dma10[par])
        b.op("vector", lambda e, oc=oc, par=par: e.tensor_mul(
            s1n_stg[:, par * TOK:(par + 1) * TOK],
            s_sb10[:, oc * TOK:(oc + 1) * TOK], rstd_bc10[:, :]), inc=(sv, 1))
        mul_v = b.tick(sv)
        b.wait("sync", sv, mul_v)
        s1dma10[par] = sp_out("sync", par, s1n_d[128 * oc:128 * (oc + 1), :],
                              s1n_stg[:, par * TOK:(par + 1) * TOK])
    p10.close()
    y1_cm.__exit__(None, None, None)

    fences()
    p11 = ExitStack()
    sbl = lambda n, s, d: p11.enter_context(nc.sbuf_tensor(n, s, d))
    wg11 = sbl("wg11", [128, 2 * DM], BF16)
    wu11 = sbl("wu11", [128, 2 * DM], BF16)
    gate11 = sbl("gate11", [128, 2 * 512], F32)
    hd11 = sbl("hd11", [128, 2 * 512], BF16)
    wgs11 = Stream(b, "wgs11", "gpsimd", 2)
    wus11 = Stream(b, "wus11", "gpsimd", 2)
    b.wait("sync", sf[0], b.tick(sf[0]))
    b.wait("sync", sf[1], b.tick(sf[1]))
    silu11 = [0, 0]
    hmul11 = [0, 0]
    hddma11 = [0, 0]
    p11_pending = [None]
    for hc in range(NHC):
        wpar = hc % 2
        wgsem, wgtk, wgslot = wgs11.load(
            wg11[:, wpar * DM:(wpar + 1) * DM].rearrange("p (c n) -> p c n", n=128),
            wgt[:, 128 * hc:128 * (hc + 1)].rearrange("(c p) n -> p c n", p=128))
        wusem, wutk, wuslot = wus11.load(
            wu11[:, wpar * DM:(wpar + 1) * DM].rearrange("p (c n) -> p c n", n=128),
            wut[:, 128 * hc:128 * (hc + 1)].rearrange("(c p) n -> p c n", p=128))
        b.wait("tensor", wgsem, wgtk)
        b.wait("tensor", wusem, wutk)
        last_mm = 0
        for tq in range(2):
            par = tq % 2
            gb, ub = 3 * par, 3 * par + 1
            b.wait("tensor", ss, silu11[par])
            b.wait("tensor", sv, hmul11[par])
            for dc in range(NCH):
                b.op("tensor", lambda e, gb=gb, wpar=wpar, dc=dc, tq=tq: e.matmul(
                    bank(gb), wg11[:, wpar * DM + 128 * dc:wpar * DM + 128 * (dc + 1)],
                    xn1[:, dc * TOK + 512 * tq:dc * TOK + 512 * (tq + 1)],
                    start=(dc == 0), stop=(dc == NCH - 1)))
            for dc in range(NCH):
                b.op("tensor", lambda e, ub=ub, wpar=wpar, dc=dc, tq=tq: e.matmul(
                    bank(ub), wu11[:, wpar * DM + 128 * dc:wpar * DM + 128 * (dc + 1)],
                    xn1[:, dc * TOK + 512 * tq:dc * TOK + 512 * (tq + 1)],
                    start=(dc == 0), stop=(dc == NCH - 1)),
                    inc=(st, 1) if dc == NCH - 1 else None)
            up_t = b.tick(st)
            last_mm = up_t
            b.wait("scalar", st, up_t)
            b.op("scalar", lambda e, gb=gb, par=par: e.activation(
                gate11[:, 512 * par:512 * (par + 1)], bank(gb), AF.Silu),
                inc=(ss, 1))
            silu11[par] = b.tick(ss)
            b.wait("vector", ss, silu11[par])
            b.wait("vector", sf[par], hddma11[par])
            b.op("vector", lambda e, ub=ub, par=par: e.tensor_mul(
                hd11[:, 512 * par:512 * (par + 1)], bank(ub),
                gate11[:, 512 * par:512 * (par + 1)]), inc=(sv, 1))
            hmul11[par] = b.tick(sv)
            b.wait("sync", sv, hmul11[par])
            hddma11[par] = sp_out("sync", par,
                                  hd_d[128 * hc:128 * (hc + 1), 512 * tq:512 * (tq + 1)],
                                  hd11[:, 512 * par:512 * (par + 1)])
        wgs11.consumed(wgslot, "st", last_mm)
        wus11.consumed(wuslot, "st", last_mm)
    p11.close()
    xn1_cm.__exit__(None, None, None)

    fences()
    p11 = ExitStack()
    sbl = lambda n, s, d: p11.enter_context(nc.sbuf_tensor(n, s, d))
    hw11 = sbl("hw11", [128, NHC * 1024], BF16)
    wd11 = sbl("wd11", [128, 2 * DH], BF16)
    g11 = sbl("g11", [128, 2 * 512], F32)
    s11 = sbl("s11", [128, 4 * 512], F32)
    x11 = sbl("x11", [128, 4 * 512], F32)
    t11 = sbl("t11", [128, 2 * 512], F32)
    yb11 = sbl("yb11", [128, 2 * 512], F32)
    wds11 = Stream(b, "wds11", "gpsimd", 2)
    sin11 = Stream(b, "sin11", "sync", 2)
    xin11 = Stream(b, "xin11", "sync", 2)
    b.wait("sync", sf[0], b.tick(sf[0]))
    b.wait("sync", sf[1], b.tick(sf[1]))
    for hcc in range(NHC):
        b.dma("sync", hw11[:, hcc * 1024:(hcc + 1) * 1024],
              hd_d[128 * hcc:128 * (hcc + 1), :], shw[0])
    HDL = b.tick(shw[0])
    b.wait("tensor", shw[0], HDL)
    gsil11 = [0, 0]
    res11 = [0, 0]
    ydma11 = [0, 0]
    for dc in range(NCH):
        par = dc % 2
        wdsem, wdtk, wdslot = wds11.load(
            wd11[:, par * DH:(par + 1) * DH].rearrange("p (c n) -> p c n", n=128),
            wdt[:, 128 * dc:128 * (dc + 1)].rearrange("(c p) n -> p c n", p=128))
        rA = 2 * par
        rB = 2 * par + 1
        ssem, stk, sslot = sin11.load(s11[:, rA * 512:(rA + 1) * 512],
                                      s1n_d[128 * dc:128 * (dc + 1), 0:512])
        xsem, xtk, xslot = xin11.load(x11[:, rA * 512:(rA + 1) * 512],
                                      x1_d[128 * dc:128 * (dc + 1), 0:512])
        b.wait("tensor", wdsem, wdtk)
        b.wait("tensor", ss, gsil11[par])
        for tb in range(2):
            dbk = 2 * par + tb
            for hcc in range(NHC):
                b.op("tensor", lambda e, dbk=dbk, hcc=hcc, par=par, tb=tb: e.matmul(
                    bank(dbk), wd11[:, par * DH + hcc * 128:par * DH + hcc * 128 + 128],
                    hw11[:, hcc * 1024 + 512 * tb:hcc * 1024 + 512 * (tb + 1)],
                    start=(hcc == 0), stop=(hcc == NHC - 1)),
                    inc=(st, 1) if hcc == NHC - 1 else None)
        down_t = b.tick(st)
        wds11.consumed(wdslot, "st", down_t)
        b.wait("scalar", st, down_t)
        b.wait("scalar", sv, res11[par])
        b.op("scalar", lambda e, par=par: e.activation(
            g11[:, 512 * par:512 * (par + 1)],
            pall[:, 512 * 2 * par:512 * (2 * par + 1)], AF.Silu), inc=(ss, 1))
        gsil_half = b.tick(ss)
        b.wait("vector", ss, gsil_half)
        b.wait("vector", ssem, stk)
        b.wait("vector", xsem, xtk)
        b.wait("vector", so[par], ydma11[par])
        b.guard("vector", sv)
        b.op("vector", lambda e, par=par, rA=rA: e.tensor_mul(
            t11[:, 512 * par:512 * (par + 1)], g11[:, 512 * par:512 * (par + 1)],
            s11[:, rA * 512:(rA + 1) * 512]), inc=(sv, 1))
        b.guard("vector", sv)
        b.op("vector", lambda e, par=par, rA=rA: e.tensor_add(
            yb11[:, 512 * par:512 * (par + 1)], t11[:, 512 * par:512 * (par + 1)],
            x11[:, rA * 512:(rA + 1) * 512]), inc=(sv, 1))
        res_v = b.tick(sv)
        b.wait("sync", sv, res_v)
        b.wait("sync", so[par], b.tick(so[par]))
        b.op("sync", lambda e, dc=dc, par=par: e.dma_start(
            out=y_o[128 * dc:128 * (dc + 1), 0:512],
            in_=yb11[:, 512 * par:512 * (par + 1)]), inc=(so[par], 16))
        ydma11[par] = b.tick(so[par])
        ssemB, stkB, sslotB = sin11.load(s11[:, rB * 512:(rB + 1) * 512],
                                         s1n_d[128 * dc:128 * (dc + 1), 512:1024])
        xsemB, xtkB, xslotB = xin11.load(x11[:, rB * 512:(rB + 1) * 512],
                                         x1_d[128 * dc:128 * (dc + 1), 512:1024])
        b.wait("scalar", sv, res_v)
        b.op("scalar", lambda e, par=par: e.activation(
            g11[:, 512 * par:512 * (par + 1)],
            pall[:, 512 * (2 * par + 1):512 * (2 * par + 2)], AF.Silu), inc=(ss, 1))
        gsil11[par] = b.tick(ss)
        b.wait("vector", ss, gsil11[par])
        b.wait("vector", ssemB, stkB)
        b.wait("vector", xsemB, xtkB)
        b.wait("vector", so[par], ydma11[par])
        b.guard("vector", sv)
        b.op("vector", lambda e, par=par, rB=rB: e.tensor_mul(
            t11[:, 512 * par:512 * (par + 1)], g11[:, 512 * par:512 * (par + 1)],
            s11[:, rB * 512:(rB + 1) * 512]), inc=(sv, 1))
        b.guard("vector", sv)
        b.op("vector", lambda e, par=par, rB=rB: e.tensor_add(
            yb11[:, 512 * par:512 * (par + 1)], t11[:, 512 * par:512 * (par + 1)],
            x11[:, rB * 512:(rB + 1) * 512]), inc=(sv, 1))
        res_vB = b.tick(sv)
        res11[par] = res_vB
        sin11.consumed(sslot, "sv", res_v)
        sin11.consumed(sslotB, "sv", res_vB)
        xin11.consumed(xslot, "sv", res_v)
        xin11.consumed(xslotB, "sv", res_vB)
        b.wait("sync", sv, res_vB)
        b.wait("sync", so[par], b.tick(so[par]))
        b.op("sync", lambda e, dc=dc, par=par: e.dma_start(
            out=y_o[128 * dc:128 * (dc + 1), 512:1024],
            in_=yb11[:, 512 * par:512 * (par + 1)]), inc=(so[par], 16))
        ydma11[par] = b.tick(so[par])
    p11.close()

    p11.close()

    b.wait("sync", sx, b.tick(sx))
    b.wait("sync", so[0], b.tick(so[0]))
    b.wait("sync", so[1], b.tick(so[1]))
    b.wait("sync", sf[0], b.tick(sf[0]))
    b.wait("sync", sf[1], b.tick(sf[1]))
    b.wait("gpsimd", sg, b.tick(sg))
    b.realize(ctx)
    nc._ctx_keepalive = ctx
    return nc


def host_fused_masks():
    diag = np.where(np.arange(128)[:, None] >= np.arange(128)[None, :], 0.0, NEG)
    zero = np.zeros((128, 128), np.float32)
    neg = np.full((128, 128), NEG, np.float32)
    out = {}
    for par in (0, 1):
        a = neg if par == 0 else zero
        bb = zero if par == 0 else neg
        out[par] = np.concatenate([diag] + [a] * 8 + [bb] * 8, axis=1).astype(BF)
    return out


def kernel_fused(**inputs):
    x = np.asarray(inputs["L0_attn_x"], np.float32)
    masks = host_fused_masks()
    h128, h64, perm = host_fwht_consts()
    ident = np.eye(128, dtype=np.float32).astype(BF)
    f32c = lambda a: np.ascontiguousarray(np.asarray(a, np.float32))
    in_maps = []
    common = {
        "wq0": f32c(inputs["L0_attn_Wq"]), "wk0": f32c(inputs["L0_attn_Wk"]),
        "wv0": f32c(inputs["L0_attn_Wv"]), "wo0": f32c(inputs["L0_attn_Wo"]),
        "w1t": f32c(inputs["L0_ffn_W1"].T), "w3t": f32c(inputs["L0_ffn_W3"].T),
        "w2tp": f32c(np.asarray(inputs["L0_ffn_W2"], np.float32).T[perm]),
        "wq1": f32c(inputs["L1_attn_Wq"]), "wv1": f32c(inputs["L1_attn_Wv"]),
        "wo1": f32c(inputs["L1_attn_Wo"]),
        "wgt": f32c(inputs["L1_ffn_w_gate"].T), "wut": f32c(inputs["L1_ffn_w_up"].T),
        "wdt": f32c(inputs["L1_ffn_w_down"].T),
        "ident_i": ident, "h128_i": np.asarray(h128),
        "h64_i": np.concatenate([np.asarray(h64), np.asarray(h64)], axis=0),
    }
    for c in range(8):
        seq, half = c // 2, c % 2
        own = x[seq, half * TOK:(half + 1) * TOK, :]
        par = x[seq, (1 - half) * TOK:(2 - half) * TOK, :]
        xf = np.ascontiguousarray(np.concatenate([own, par], axis=0).T)
        m = dict(common)
        m["xT"] = xf
        m["mask_i"] = masks[half]
        m["msel_i"] = np.tile(np.array([[1.0 - half, float(half)]], np.float32),
                              (128, 1))
        m["mrow_i"] = np.full((1, 128), NEG if half == 0 else 0.0,
                              np.float32).astype(BF)
        in_maps.append(m)
    res = _run(build_fused(), in_maps)
    out = np.empty((BSZ, SEQ, DM), np.float32)
    for c in range(8):
        seq, half = c // 2, c % 2
        out[seq, half * TOK:(half + 1) * TOK, :] = np.asarray(
            res.results[c]["y_o"], np.float32).T
    return out
